# version 18
# baseline (speedup 1.0000x reference)
import contextlib
import math
import numpy as np
import ml_dtypes
import concourse.bass as bass
import concourse.mybir as mybir
from concourse.bass_utils import run_bass_kernel_spmd

F32 = mybir.dt.float32
BF16 = mybir.dt.bfloat16
AF = mybir.ActivationFunctionType
ALU = mybir.AluOpType
AX = mybir.AxisListType

T = 2048
D = 1024
DC = D // 128
NSL = 4
SL = 512
H, Kd, Vd = 4, 256, 512
VAL_DIM = H * Vd
EPS_L2 = 1e-6
LN_EPS = 1e-5

ENGINES = ("sync", "tensor", "vector", "scalar", "gpsimd")



DMA_POOL = 8


class Op:
    __slots__ = ("eng", "fn", "deps", "is_dma", "pos", "needs_inc", "inc_value", "waits",
                 "dma_sem", "dma_val")

    def __init__(self, eng, fn, deps, is_dma):
        self.eng = eng
        self.fn = fn
        self.deps = deps
        self.is_dma = is_dma
        self.needs_inc = is_dma
        self.inc_value = None
        self.waits = []
        self.dma_sem = None
        self.dma_val = None


class Buf:
    __slots__ = ("name", "last_writer", "readers")

    def __init__(self, name):
        self.name = name
        self.last_writer = None
        self.readers = []


class G:
    def __init__(self, nc):
        self.nc = nc
        self.ops = {e: [] for e in ENGINES}
        self.order = []
        self._fence = None
        self._fence_pending = {e: False for e in ENGINES}
        self._last_dma = {}
        self._dma_counts = {}
        self._all_dmas = {}

    def barrier(self):
        join = self.op("vector", lambda e: e.engine_nop(), deps=self.barrier_deps())
        self._fence = join
        self._fence_pending = {e: True for e in ENGINES}
        self._fence_pending["vector"] = False

    def op(self, eng, fn, reads=(), writes=(), deps=(), dma=False):
        dd = []
        if self._fence is not None and self._fence_pending.get(eng):
            dd.append(self._fence)
            self._fence_pending[eng] = False
        for b in reads:
            if b.last_writer is not None:
                dd.append(b.last_writer)
        for b in writes:
            if b.last_writer is not None:
                dd.append(b.last_writer)
            dd.extend(b.readers)
        dd.extend(deps)
        seen, deps_u = set(), []
        for d in dd:
            if d is not None and id(d) not in seen:
                seen.add(id(d))
                deps_u.append(d)
        o = Op(eng, fn, deps_u, dma)
        o.pos = len(self.order)
        self.order.append(o)
        self.ops[eng].append(o)
        if dma:
            i = self._dma_counts.get(eng, 0)
            self._dma_counts[eng] = i + 1
            o.dma_sem = f"dq_{eng}_{i % DMA_POOL}"
            o.dma_val = 16 * (i // DMA_POOL + 1)
            self._last_dma[eng] = o
        for b in reads:
            b.readers.append(o)
        for b in writes:
            b.last_writer = o
            b.readers = []
        return o

    def dma(self, queue, out_ap_fn, in_ap_fn, reads=(), writes=(), deps=()):
        def fn(eng):
            return eng.dma_start(out_ap_fn() if callable(out_ap_fn) else out_ap_fn,
                                 in_ap_fn() if callable(in_ap_fn) else in_ap_fn)
        return self.op(queue, fn, reads=reads, writes=writes, deps=deps, dma=True)

    def barrier_deps(self):
        dd = [self.ops[e][-1] for e in ENGINES if self.ops[e]]
        for e in ENGINES:
            dmas = [o for o in self.ops[e] if o.is_dma][-DMA_POOL:]
            dd.extend(dmas)
        return dd

    @staticmethod
    def _skip_edge(d, o):
        return d.eng == o.eng == "tensor" and not d.is_dma

    def finalize_and_emit(self, ctx):
        for o in self.order:
            for d in o.deps:
                if not self._skip_edge(d, o):
                    d.needs_inc = True
        counts = {e: 0 for e in ENGINES}
        for o in self.order:
            if o.needs_inc and not o.is_dma:
                counts[o.eng] += 1
                o.inc_value = counts[o.eng]
        for o in self.order:
            need = {}
            for d in o.deps:
                if self._skip_edge(d, o):
                    continue
                if d.is_dma:
                    need[d.dma_sem] = max(need.get(d.dma_sem, 0), d.dma_val)
                else:
                    assert d.inc_value is not None
                    need[d.eng] = max(need.get(d.eng, 0), d.inc_value)
            if o.is_dma and o.dma_val > 16:
                need[o.dma_sem] = max(need.get(o.dma_sem, 0), o.dma_val - 16)
            o.waits = sorted(need.items())
        final_counts = dict(counts)
        for e, n in self._dma_counts.items():
            for i in range(min(n, DMA_POOL)):
                nm = f"dq_{e}_{i}"
                nuses = (n - i + DMA_POOL - 1) // DMA_POOL
                final_counts[nm] = 16 * nuses

        sems = {e: ctx.enter_context(self.nc.semaphore("sem_" + e)) for e in ENGINES}
        for e, n in self._dma_counts.items():
            for i in range(min(n, DMA_POOL)):
                nm = f"dq_{e}_{i}"
                sems[nm] = ctx.enter_context(self.nc.semaphore(nm))

        block = ctx.enter_context(self.nc.Block())

        def make_body(ename):
            def body(eng):
                watermark = {k: 0 for k in sems}
                for o in self.ops[ename]:
                    for se, v in o.waits:
                        if v > watermark[se]:
                            eng.wait_ge(sems[se], v)
                            watermark[se] = v
                    inst = o.fn(eng)
                    if o.is_dma:
                        inst.then_inc(sems[o.dma_sem], 16)
                    elif o.needs_inc:
                        inst.then_inc(sems[o.eng], 1)
                if ename == "sync":
                    for se, v in final_counts.items():
                        if v > watermark.get(se, 0) and v > 0:
                            eng.wait_ge(sems[se], v)
            return body

        for ename in ENGINES:
            getattr(block, ename)(make_body(ename))



_NAME_SEQ = [0]


def _uniq(name):
    _NAME_SEQ[0] += 1
    return f"{name}_{_NAME_SEQ[0]}"


class Tile:

    def __init__(self, nc, ctx, name, shape, dtype, nbufs=1, alias=None):
        if alias is not None:
            assert alias.shape[1] >= shape[1] and alias.t.dtype == dtype
            self.t = alias.t
            self.shape = shape
            self.buf = alias.buf
            self.bufs = alias.bufs
            return
        name = _uniq(name)
        self.t = ctx.enter_context(nc.sbuf_tensor(name, shape, dtype))
        self.shape = shape
        if nbufs == 1:
            self.buf = Buf(name)
            self.bufs = [self.buf]
        else:
            self.bufs = [Buf(f"{name}#{i}") for i in range(nbufs)]
            self.buf = self.bufs[0]

    def __getitem__(self, idx):
        return self.t[idx]

    @property
    def ap(self):
        return self.t[:, :]


class _RowIndexer:
    def __init__(self, t, part, c0, width):
        self._t = t
        self._part = part
        self._c0 = c0
        self._w = width

    def __getitem__(self, idx):
        rows, cols = idx
        a = cols.start if cols.start is not None else 0
        b = cols.stop if cols.stop is not None else self._w
        return self._t[self._part:self._part + 1, self._c0 + a:self._c0 + b]


class RowTile:

    def __init__(self, parent, part, name, c0=0, width=None):
        self.buf = Buf(name)
        if width is None:
            width = parent.shape[1] - c0
        self.t = _RowIndexer(parent.t, part, c0, width)


class Psum:

    def __init__(self, nc, ctx, name, shape, dtype):
        name = _uniq(name)
        self.t = ctx.enter_context(nc.psum_tensor(name, shape, dtype))
        self.shape = shape
        elem = 4 if dtype == F32 else 2
        self.bank_cols = 2048 // elem
        self.nbanks = (shape[1] * elem + 2047) // 2048
        self.bufs = [Buf(f"{name}@b{i}") for i in range(self.nbanks)]

    def slice(self, c0, c1, p0=0, p1=128):
        ap = self.t[p0:p1, c0:c1]
        b0 = c0 // self.bank_cols
        b1 = (c1 - 1) // self.bank_cols
        return ap, self.bufs[b0:b1 + 1]


class Cells:

    def __init__(self, nc, ctx, g):
        self.tile = Tile(nc, ctx, "cells", [128, 16], F32)
        self.g = g
        self.map = {}
        self.n = 0

    def _col(self, value):
        v = float(value)
        if v not in self.map:
            k = self.n
            self.n += 1
            assert k < 16
            self.g.op("vector", lambda e, k=k, v=v: e.memset(self.tile.t[:, k:k + 1], v),
                      writes=[self.tile.buf])
            self.map[v] = k
        return self.map[v]

    def col(self, value):
        k = self._col(value)
        return self.tile.t[:, k:k + 1], self.tile.buf

    def cell(self, value):
        k = self._col(value)
        return self.tile.t[0:1, k:k + 1], self.tile.buf


def fm_cols(c, t0, t1, period, pad=0):
    return (c * period + pad + t0, c * period + pad + t1)



def host_consts():
    c = {}
    c["ident"] = np.eye(128, dtype=np.float32)
    c["allones"] = np.ones((128, 128), np.float32)
    c["ones_col"] = np.ones((128, 1), np.float32)
    c["ones_row"] = np.ones((1, 128), np.float32)
    c["neg_ones_row"] = -np.ones((1, 128), np.float32)
    i = np.arange(128)
    c["causal_neg"] = np.where(i[:, None] >= i[None, :], 0.0, -1e30).astype(np.float32)
    ii = np.arange(D // 2, dtype=np.float64)
    theta = np.power(10000.0, -2.0 * (ii - 1.0) / D)
    pos = np.arange(T, dtype=np.float64)
    ang = theta[:, None] * pos[None, :]
    c["rope_cos"] = np.cos(ang).astype(np.float32)
    c["rope_sin"] = np.sin(ang).astype(np.float32)
    j = np.arange(128)
    cc, cp = j[:, None], j[None, :]
    c["tri_incl"] = (cc <= cp).astype(np.float32)
    big = np.float32(1e30)
    c["mask1"] = np.tile((cp >= cc) * -big, (1, 4)).astype(np.float32)
    c["mask2"] = np.tile((cp <= cc) * big, (1, 4)).astype(np.float32)
    c["mask3"] = np.tile((cp < cc) * big, (1, 4)).astype(np.float32)
    return c


def const_tensors(nc, g, ctx, consts, names):
    out = {}
    if not hasattr(g, "_const_drams"):
        g._const_drams = {}
    for name, want in names.items():
        arr = consts[name]
        if name in g._const_drams:
            dt = g._const_drams[name]
        else:
            dt = nc.dram_tensor("c_" + name, list(arr.shape), F32, kind="ExternalInput")
            g._const_drams[name] = dt
        for w in want:
            dtype = F32 if w == "f32" else BF16
            tl = Tile(nc, ctx, f"ct_{name}_{w}", list(arr.shape), dtype)
            q = "gpsimd" if w == "bf16" else "sync"
            g.dma(q, tl.t[:, :], dt[:, :], writes=[tl.buf])
            out[(name, w)] = tl
    return out



def emit_ln_stats_and_apply(nc, g, C, xsrc_dram, xbf, xn, ps_b, sq, t1, t2, mb_t, rb_t,
                            cells, eps=LN_EPS):
    ones_m = C[("allones", "bf16")]
    eps_col, eps_col_b = cells.col(eps)
    for sl in range(NSL):
        for c in range(DC):
            g.dma("gpsimd",
                  xbf.t[:, c * SL:(c + 1) * SL],
                  xsrc_dram[c * 128:(c + 1) * 128, sl * SL:(sl + 1) * SL],
                  writes=[xbf.bufs[c]])
        sum_ap, sum_bufs = ps_b.slice(0, SL)
        sq_ap, sq_bufs = ps_b.slice(512, 512 + SL)
        for c in range(DC):
            g.op("tensor",
                 lambda e, c=c, ap=sum_ap: e.matmul(ap, ones_m.t[:, :], xbf.t[:, c * SL:(c + 1) * SL],
                                                    start=(c == 0), stop=(c == DC - 1)),
                 reads=[xbf.bufs[c], ones_m.buf], writes=sum_bufs)
        for c in range(DC):
            g.op("vector",
                 lambda e, c=c: e.tensor_mul(sq.t[:, :], xbf.t[:, c * SL:(c + 1) * SL],
                                             xbf.t[:, c * SL:(c + 1) * SL]),
                 reads=[xbf.bufs[c]], writes=[sq.buf])
            g.op("tensor",
                 lambda e, c=c, ap=sq_ap: e.matmul(ap, ones_m.t[:, :], sq.t[:, :],
                                                   start=(c == 0), stop=(c == DC - 1)),
                 reads=[sq.buf, ones_m.buf], writes=sq_bufs)
        g.op("vector", lambda e, ap=sum_ap: e.tensor_scalar(mb_t.t[:, :], ap, 1.0 / D, None, ALU.mult),
             reads=sum_bufs, writes=[mb_t.buf])
        g.op("vector", lambda e: e.tensor_mul(t1.t[:, :], mb_t.t[:, :], mb_t.t[:, :]),
             reads=[mb_t.buf], writes=[t1.buf])
        g.op("vector", lambda e, ap=sq_ap: e.scalar_tensor_tensor(t2.t[:, :], ap, 1.0 / D,
                                                                  t1.t[:, :], ALU.mult, ALU.subtract),
             reads=sq_bufs + [t1.buf], writes=[t2.buf])
        g.op("scalar", lambda e: e.activation(t2.t[:, :], t2.t[:, :], AF.Ln, bias=eps_col),
             reads=[t2.buf, eps_col_b], writes=[t2.buf])
        g.op("scalar", lambda e: e.activation(rb_t.t[:, :], t2.t[:, :], AF.Exp, scale=-0.5),
             reads=[t2.buf], writes=[rb_t.buf])
        for c in range(DC):
            lo, hi = fm_cols(c, sl * SL, (sl + 1) * SL, T + 1, pad=1)
            g.op("vector", lambda e, c=c: e.tensor_tensor(t1.t[:, :], xbf.t[:, c * SL:(c + 1) * SL],
                                                          mb_t.t[:, :], ALU.subtract),
                 reads=[xbf.bufs[c], mb_t.buf], writes=[t1.buf])
            g.op("vector", lambda e, lo=lo, hi=hi: e.tensor_tensor(xn.t[:, lo:hi], t1.t[:, :],
                                                                   rb_t.t[:, :], ALU.mult),
                 reads=[t1.buf, rb_t.buf], writes=[xn.bufs[0]])



def emit_l0_attn(nc, gr, consts, cells, dr, debug=False, xn_shared=None):
    x_d = dr["xT"]
    wq_d = dr["wq_l"]
    wk_d = dr["wk_l"]
    wv_d = dr["wv_l"]
    cos_d = dr["rope_cos"]
    sin_d = dr["rope_sin"]
    o_d = dr["o0T"]
    if debug:
        dbg_xn = nc.dram_tensor("dbg_xn", [128, DC * (T + 1)], F32, kind="ExternalOutput")
        dbg_krot = nc.dram_tensor("dbg_krot", [128, DC * T], F32, kind="ExternalOutput")
        dbg_qs = nc.dram_tensor("dbg_qs", [128, DC * SL], F32, kind="ExternalOutput")
        dbg_pb = nc.dram_tensor("dbg_pb", [128, T], F32, kind="ExternalOutput")
        dbg_vtok = nc.dram_tensor("dbg_vtok", [128, 16 * 1024], F32, kind="ExternalOutput")

    ctx = contextlib.ExitStack()
    with ctx:
        C = const_tensors(nc, gr, ctx, consts, {"ident": ["bf16"], "ones_col": ["bf16"], "ones_row": ["f32"], "causal_neg": ["bf16"], "allones": ["bf16"]})
        xn = xn_shared if xn_shared is not None else Tile(nc, ctx, "xn", [128, DC * (T + 1)], BF16)
        krot = Tile(nc, ctx, "krot", [128, DC * T], BF16, nbufs=DC)
        vtok = Tile(nc, ctx, "vtok", [128, 16 * 1024], BF16, nbufs=16)
        qs = Tile(nc, ctx, "qs", [128, DC * SL], BF16)
        xbf = Tile(nc, ctx, "xbf", [128, DC * SL], BF16, nbufs=DC)
        wA = Tile(nc, ctx, "wA", [128, DC * 1024], BF16)
        wB = Tile(nc, ctx, "wB", [128, DC * 1024], BF16)
        tabc = Tile(nc, ctx, "tabc", [128, 4 * SL], BF16)
        tabs = Tile(nc, ctx, "tabs", [128, 4 * SL], BF16)
        sq = Tile(nc, ctx, "sqs", [128, SL], BF16)
        t1 = Tile(nc, ctx, "t1", [128, SL], F32)
        t2 = Tile(nc, ctx, "t2", [128, SL], F32)
        mb_t = Tile(nc, ctx, "mb_t", [128, SL], F32)
        rb_t = Tile(nc, ctx, "rb_t", [128, SL], F32)
        pb = Tile(nc, ctx, "pb", [128, T], BF16)
        pbn = Tile(nc, ctx, "pbn", [128, T], BF16)
        pbt = Tile(nc, ctx, "pbt", [128, T], BF16)
        ost = Tile(nc, ctx, "ost", [128, DC * SL], F32)
        rowred = Tile(nc, ctx, "rowred", [128, 4], F32)

        ps_s = Psum(nc, ctx, "ps_s", [128, 2048], F32)
        ps_o = Psum(nc, ctx, "ps_o", [128, 1024], F32)
        ps_a = Psum(nc, ctx, "ps_a", [128, 512], F32)
        ps_t = Psum(nc, ctx, "ps_t", [128, 1024], BF16)

        for c in range(DC):
            gr.op("vector", lambda e, c=c: e.memset(xn.t[:, c * (T + 1):c * (T + 1) + 1], 0.0),
                  writes=[xn.bufs[0]])
        emit_ln_stats_and_apply(nc, gr, C, x_d, xbf, xn, ps_o, sq, t1, t2, mb_t, rb_t, cells)

        def load_w(dst, src):
            gr.dma("gpsimd", dst.t[:, :], src[:, :], writes=[dst.buf])

        load_w(wA, wk_d)

        def load_tables(sl):
            gr.dma("gpsimd", tabc.t[:, :], cos_d[sl * 128:(sl + 1) * 128, :], writes=[tabc.buf])
            gr.dma("gpsimd", tabs.t[:, :], sin_d[sl * 128:(sl + 1) * 128, :], writes=[tabs.buf])

        def proj_rope(w, sl, outtile, outbufs, out_period, t0):
            for j in range(4):
                apA, bufA = ps_a.slice(0, SL)
                apB, bufB = ps_o.slice(0, SL)
                for c in range(DC):
                    lo, hi = fm_cols(c, sl * SL, (sl + 1) * SL, T + 1, pad=1)
                    gr.op("tensor", lambda e, c=c, j=j, lo=lo, hi=hi, ap=apA:
                          e.matmul(ap, w.t[:, c * 1024 + 128 * j: c * 1024 + 128 * j + 128],
                                   xn.t[:, lo:hi], start=(c == 0), stop=(c == DC - 1)),
                          reads=[w.buf, xn.bufs[0]], writes=bufA)
                for c in range(DC):
                    lo, hi = fm_cols(c, sl * SL, (sl + 1) * SL, T + 1, pad=1)
                    gr.op("tensor", lambda e, c=c, j=j, lo=lo, hi=hi, ap=apB:
                          e.matmul(ap, w.t[:, c * 1024 + 128 * (j + 4): c * 1024 + 128 * (j + 4) + 128],
                                   xn.t[:, lo:hi], start=(c == 0), stop=(c == DC - 1)),
                          reads=[w.buf, xn.bufs[0]], writes=bufB)
                ctab = tabc.t[:, j * SL:(j + 1) * SL]
                stab = tabs.t[:, j * SL:(j + 1) * SL]
                def ocols(cc):
                    return (cc * out_period + t0, cc * out_period + t0 + SL)
                r0, r1 = ocols(j)
                o0c, o1c = ocols(j + 4)
                gr.op("vector", lambda e, ap=apA, ctab=ctab: e.tensor_tensor(t1.t[:, :], ap, ctab, ALU.mult),
                      reads=bufA + [tabc.buf], writes=[t1.buf])
                gr.op("vector", lambda e, ap=apB, stab=stab: e.tensor_tensor(t2.t[:, :], ap, stab, ALU.mult),
                      reads=bufB + [tabs.buf], writes=[t2.buf])
                gr.op("vector", lambda e, r0=r0, r1=r1: e.tensor_add(outtile.t[:, r0:r1], t1.t[:, :], t2.t[:, :]),
                      reads=[t1.buf, t2.buf], writes=[outbufs[j]])
                gr.op("vector", lambda e, ap=apB, ctab=ctab: e.tensor_tensor(t1.t[:, :], ap, ctab, ALU.mult),
                      reads=bufB + [tabc.buf], writes=[t1.buf])
                gr.op("vector", lambda e, ap=apA, stab=stab: e.tensor_tensor(t2.t[:, :], ap, stab, ALU.mult),
                      reads=bufA + [tabs.buf], writes=[t2.buf])
                gr.op("vector", lambda e, o0c=o0c, o1c=o1c: e.tensor_tensor(outtile.t[:, o0c:o1c], t1.t[:, :],
                                                                            t2.t[:, :], ALU.subtract),
                      reads=[t1.buf, t2.buf], writes=[outbufs[j + 4]])

        for sl in range(NSL):
            load_tables(sl)
            proj_rope(wA, sl, krot, krot.bufs, T, sl * SL)

        load_w(wB, wv_d)
        for i in range(16):
            for hhalf in range(2):
                apV, bufV = ps_a.slice(0, SL)
                for c in range(DC):
                    lo, hi = fm_cols(c, i * 128, (i + 1) * 128, T + 1, pad=1)
                    gr.op("tensor", lambda e, c=c, lo=lo, hi=hi, hh=hhalf, ap=apV:
                          e.matmul(ap, xn.t[:, lo:hi], wB.t[:, c * 1024 + 512 * hh: c * 1024 + 512 * hh + 512],
                                   start=(c == 0), stop=(c == DC - 1)),
                          reads=[wB.buf, xn.bufs[0]], writes=bufV)
                gr.op("vector", lambda e, i=i, hh=hhalf, ap=apV:
                      e.tensor_copy(vtok.t[:, i * 1024 + 512 * hh: i * 1024 + 512 * hh + 512], ap),
                      reads=bufV, writes=[vtok.bufs[i]])

        load_w(wA, wq_d)
        for sl in range(NSL):
            load_tables(sl)
            proj_rope(wA, sl, qs, [qs.buf] * 8, SL, 0)
            for qq in range(4):
                qi = sl * 4 + qq
                W = (qi + 1) * 128
                ngroups = (W + 511) // 512
                s_bufs_all = []
                for gg in range(ngroups):
                    c0, c1 = gg * 512, min(W, (gg + 1) * 512)
                    apS, bufS = ps_s.slice(c0, c1)
                    s_bufs_all.extend(bufS)
                    for c in range(DC):
                        gr.op("tensor", lambda e, c=c, qq=qq, c0=c0, c1=c1, ap=apS:
                              e.matmul(ap, qs.t[:, c * SL + 128 * qq: c * SL + 128 * qq + 128],
                                       krot.t[:, c * T + c0: c * T + c1],
                                       start=(c == 0), stop=(c == DC - 1)),
                              reads=[qs.buf, krot.bufs[c]], writes=bufS)
                apD, bufD = ps_s.slice(W - 128, W)
                gr.op("tensor", lambda e, ap=apD: e.matmul(ap, C[("ident", "bf16")].t[:, :],
                                                           C[("causal_neg", "bf16")].t[:, :],
                                                           start=False, stop=True),
                      reads=[C[("ident", "bf16")].buf, C[("causal_neg", "bf16")].buf], writes=bufD)
                s_bufs_all = list({id(b): b for b in s_bufs_all}.values())
                apSfull, _ = ps_s.slice(0, W)
                gr.op("vector", lambda e, ap=apSfull: e.reduce_max(rowred.t[:, 0:1], ap, axis=AX.X, negate=True),
                      reads=s_bufs_all, writes=[rowred.buf])
                gr.op("vector", lambda e: e.tensor_scalar(rowred.t[:, 1:2], rowred.t[:, 0:1], 1.0 / 32.0, None, ALU.mult),
                      reads=[rowred.buf], writes=[rowred.buf])
                gr.op("scalar", lambda e, W=W, ap=apSfull: e.activation(pb.t[:, 0:W], ap, AF.Exp,
                                                                        bias=rowred.t[:, 1:2], scale=1.0 / 32.0),
                      reads=s_bufs_all + [rowred.buf], writes=[pb.buf])
                gr.op("vector", lambda e, W=W: e.reduce_sum(rowred.t[:, 2:3], pb.t[:, 0:W], axis=AX.X),
                      reads=[pb.buf], writes=[rowred.buf])
                gr.op("vector", lambda e: e.reciprocal(rowred.t[:, 3:4], rowred.t[:, 2:3]),
                      reads=[rowred.buf], writes=[rowred.buf])
                gr.op("vector", lambda e, W=W: e.tensor_scalar(pbn.t[:, 0:W], pb.t[:, 0:W], rowred.t[:, 3:4],
                                                               None, ALU.mult),
                      reads=[pb.buf, rowred.buf], writes=[pbn.buf])
                nt = qi + 1
                for j0 in range(0, nt, 8):
                    jn = min(8, nt - j0)
                    apT, bufT = ps_t.slice(0, jn * 128)
                    for j in range(j0, j0 + jn):
                        ap1, _ = ps_t.slice((j - j0) * 128, (j - j0 + 1) * 128)
                        gr.op("tensor", lambda e, j=j, ap1=ap1: e.transpose(ap1, pbn.t[:, j * 128:(j + 1) * 128],
                                                                            C[("ident", "bf16")].t[:, :]),
                              reads=[pbn.buf, C[("ident", "bf16")].buf], writes=bufT)
                    gr.op("vector", lambda e, j0=j0, jn=jn, ap=apT:
                          e.tensor_copy(pbt.t[:, j0 * 128:(j0 + jn) * 128], ap),
                          reads=bufT, writes=[pbt.buf])
                for c in range(DC):
                    apO, bufO = ps_o.slice(c * 128, (c + 1) * 128)
                    for j in range(nt):
                        gr.op("tensor", lambda e, c=c, j=j, nt=nt, ap=apO:
                              e.matmul(ap, vtok.t[:, j * 1024 + 128 * c: j * 1024 + 128 * c + 128],
                                       pbt.t[:, j * 128:(j + 1) * 128],
                                       start=(j == 0), stop=(j == nt - 1)),
                              reads=[vtok.bufs[j], pbt.buf], writes=bufO)
                    gr.op("vector", lambda e, c=c, qq=qq, ap=apO:
                          e.tensor_copy(ost.t[:, c * SL + 128 * qq: c * SL + 128 * qq + 128], ap),
                          reads=bufO, writes=[ost.buf])
            for c in range(DC):
                gr.dma("sync", o_d[c * 128:(c + 1) * 128, sl * SL:(sl + 1) * SL],
                       ost.t[:, c * SL:(c + 1) * SL], reads=[ost.buf])

        if debug:
            gr.dma("gpsimd", dbg_xn[:, :], xn.t[:, :], reads=[xn.bufs[0]])
            for c in range(DC):
                gr.dma("gpsimd", dbg_krot[:, c * T:(c + 1) * T], krot.t[:, c * T:(c + 1) * T],
                       reads=[krot.bufs[c]])
            gr.dma("gpsimd", dbg_qs[:, :], qs.t[:, :], reads=[qs.buf])
            gr.dma("gpsimd", dbg_pb[:, :], pb.t[:, :], reads=[pb.buf])
            for i in range(16):
                gr.dma("gpsimd", dbg_vtok[:, i * 1024:(i + 1) * 1024],
                       vtok.t[:, i * 1024:(i + 1) * 1024], reads=[vtok.bufs[i]])



def build_l0_attn(debug=False):
    nc = bass.Bass("TRN2", target_bir_lowering=False)
    gr = G(nc)
    consts = host_consts()
    x_d = nc.dram_tensor("xT", [D, T], F32, kind="ExternalInput")
    wq_d = nc.dram_tensor("wq_l", [128, DC * D], F32, kind="ExternalInput")
    wk_d = nc.dram_tensor("wk_l", [128, DC * D], F32, kind="ExternalInput")
    wv_d = nc.dram_tensor("wv_l", [128, DC * D], F32, kind="ExternalInput")
    cos_d = nc.dram_tensor("rope_cos", [NSL * 128, 4 * SL], F32, kind="ExternalInput")
    sin_d = nc.dram_tensor("rope_sin", [NSL * 128, 4 * SL], F32, kind="ExternalInput")
    o_d = nc.dram_tensor("o0T", [D, T], F32, kind="ExternalOutput")
    dr = {"xT": x_d, "wq_l": wq_d, "wk_l": wk_d, "wv_l": wv_d, "rope_cos": cos_d, "rope_sin": sin_d, "o0T": o_d}
    with contextlib.ExitStack() as ctx0:
        cells = Cells(nc, ctx0, gr)
        emit_l0_attn(nc, gr, consts, cells, dr, debug=debug)
        gr.finalize_and_emit(ctx0)
    return nc, consts


def emit_l0_ffn(nc, gr, consts, cells, dr, debug=False, xn_shared=None):
    if debug:
        dbgf = {}
        for nm, shp in [("xr", [128, DC * SL]), ("xk", [128, DC * SL]), ("r", [128, DC * SL]),
                        ("kact", [128, DC * SL]), ("xn", [128, DC * (T + 1)])]:
            dbgf[nm] = nc.dram_tensor("dbgf_" + nm, shp, F32, kind="ExternalOutput")
    x_d = dr["xT"]
    o_d = dr["o0T"]
    wr_d = dr["wr_l"]
    wk_d = dr["wk2_l"]
    wv_d = dr["wv2_l"]
    mur_d = dr["mu_r"]
    muk_d = dr["mu_k"]
    x1_d = dr["x1T"]
    ctx = contextlib.ExitStack()
    with ctx:
        C = const_tensors(nc, gr, ctx, consts, {"ones_col": ["bf16"], "ones_row": ["f32"], "allones": ["bf16"]})
        xn = xn_shared if xn_shared is not None else Tile(nc, ctx, "xn", [128, DC * (T + 1)], BF16)
        wR = Tile(nc, ctx, "wR", [128, DC * 1024], BF16)
        wK = Tile(nc, ctx, "wK", [128, DC * 1024], BF16)
        wV = Tile(nc, ctx, "wV", [128, DC * 1024], BF16)
        mur = Tile(nc, ctx, "mur", [128, DC], F32)
        muk = Tile(nc, ctx, "muk", [128, DC], F32)
        t1 = Tile(nc, ctx, "t1", [128, SL], F32)
        if xn_shared is None:
            xbf = Tile(nc, ctx, "xbf", [128, DC * SL], BF16, nbufs=DC)
            sq = Tile(nc, ctx, "sqs", [128, SL], BF16)
            t2 = Tile(nc, ctx, "t2", [128, SL], F32)
            mb_t = Tile(nc, ctx, "mb_t", [128, SL], F32)
            rb_t = Tile(nc, ctx, "rb_t", [128, SL], F32)
        xr = Tile(nc, ctx, "xr", [128, DC * SL], BF16)
        xk = Tile(nc, ctx, "xk", [128, DC * SL], BF16)
        rbuf = Tile(nc, ctx, "rbuf", [128, DC * SL], BF16)
        kact = Tile(nc, ctx, "kact", [128, DC * SL], BF16)
        relu = Tile(nc, ctx, "relu", [128, SL], BF16)
        xf32 = Tile(nc, ctx, "xf32", [128, DC * SL], F32, nbufs=DC)
        of32 = Tile(nc, ctx, "of32", [128, DC * SL], F32, nbufs=DC)
        x1st = Tile(nc, ctx, "x1st", [128, DC * SL], F32)

        ps_s = Psum(nc, ctx, "ps_s", [128, 2048], F32)
        ps_b = Psum(nc, ctx, "ps_b", [128, 1024], F32)
        ps_m = Psum(nc, ctx, "ps_m", [128, 1024], F32)

        if xn_shared is None:
            for c in range(DC):
                gr.op("vector", lambda e, c=c: e.memset(xn.t[:, c * (T + 1):c * (T + 1) + 1], 0.0),
                      writes=[xn.bufs[0]])
            emit_ln_stats_and_apply(nc, gr, C, x_d, xbf, xn, ps_b, sq, t1, t2, mb_t, rb_t, cells)

        def load_w(dst, src):
            gr.dma("gpsimd", dst.t[:, :], src[:, :], writes=[dst.buf])
        load_w(wR, wr_d)
        load_w(wK, wk_d)
        load_w(wV, wv_d)
        for c in range(DC):
            gr.dma("sync", mur.t[:, c:c + 1], mur_d[c * 128:(c + 1) * 128, :], writes=[mur.buf])
            gr.dma("sync", muk.t[:, c:c + 1], muk_d[c * 128:(c + 1) * 128, :], writes=[muk.buf])

        for sl in range(NSL):
            for c in range(DC):
                lo, hi = fm_cols(c, sl * SL, (sl + 1) * SL, T + 1, pad=1)
                cur = xn.t[:, lo:hi]
                prev = xn.t[:, lo - 1:hi - 1]
                gr.op("vector", lambda e, cur=cur, prev=prev: e.tensor_tensor(t1.t[:, :], cur, prev, ALU.subtract),
                      reads=[xn.bufs[0]], writes=[t1.buf])
                gr.op("vector", lambda e, c=c, prev=prev: e.scalar_tensor_tensor(
                    xr.t[:, c * SL:(c + 1) * SL], t1.t[:, :], mur.t[:, c:c + 1], prev, ALU.mult, ALU.add),
                      reads=[t1.buf, mur.buf, xn.bufs[0]], writes=[xr.buf])
                gr.op("vector", lambda e, c=c, prev=prev: e.scalar_tensor_tensor(
                    xk.t[:, c * SL:(c + 1) * SL], t1.t[:, :], muk.t[:, c:c + 1], prev, ALU.mult, ALU.add),
                      reads=[t1.buf, muk.buf, xn.bufs[0]], writes=[xk.buf])
            for j in range(DC):
                apR, bufR = ps_s.slice((j % 4) * 512, (j % 4) * 512 + 512)
                for c in range(DC):
                    gr.op("tensor", lambda e, c=c, j=j, ap=apR:
                          e.matmul(ap, wR.t[:, c * 1024 + 128 * j: c * 1024 + 128 * j + 128],
                                   xr.t[:, c * SL:(c + 1) * SL], start=(c == 0), stop=(c == DC - 1)),
                          reads=[wR.buf, xr.buf], writes=bufR)
                gr.op("scalar", lambda e, j=j, ap=apR: e.activation(rbuf.t[:, j * SL:(j + 1) * SL], ap, AF.Sigmoid),
                      reads=bufR, writes=[rbuf.buf])
            for j in range(DC):
                apK, bufK = ps_s.slice((j % 4) * 512, (j % 4) * 512 + 512)
                for c in range(DC):
                    gr.op("tensor", lambda e, c=c, j=j, ap=apK:
                          e.matmul(ap, wK.t[:, c * 1024 + 128 * j: c * 1024 + 128 * j + 128],
                                   xk.t[:, c * SL:(c + 1) * SL], start=(c == 0), stop=(c == DC - 1)),
                          reads=[wK.buf, xk.buf], writes=bufK)
                gr.op("scalar", lambda e, j=j, ap=apK: e.activation(relu.t[:, :], ap, AF.Relu),
                      reads=bufK, writes=[relu.buf])
                gr.op("vector", lambda e, j=j, ap=apK: e.tensor_tensor(kact.t[:, j * SL:(j + 1) * SL],
                                                                       relu.t[:, :], ap, ALU.mult),
                      reads=[relu.buf] + bufK, writes=[kact.buf])
            for c in range(DC):
                gr.dma("sync", xf32.t[:, c * SL:(c + 1) * SL],
                       x_d[c * 128:(c + 1) * 128, sl * SL:(sl + 1) * SL], writes=[xf32.bufs[c]])
                gr.dma("scalar", of32.t[:, c * SL:(c + 1) * SL],
                       o_d[c * 128:(c + 1) * 128, sl * SL:(sl + 1) * SL], writes=[of32.bufs[c]])
            if debug and sl == NSL - 1:
                gr.dma("gpsimd", dbgf["xr"][:, :], xr.t[:, :], reads=[xr.buf])
                gr.dma("gpsimd", dbgf["xk"][:, :], xk.t[:, :], reads=[xk.buf])
                gr.dma("gpsimd", dbgf["r"][:, :], rbuf.t[:, :], reads=[rbuf.buf])
                gr.dma("gpsimd", dbgf["kact"][:, :], kact.t[:, :], reads=[kact.buf])
                gr.dma("gpsimd", dbgf["xn"][:, :], xn.t[:, :], reads=[xn.bufs[0]])
            for j in range(DC):
                apV, bufV = ps_m.slice((j % 2) * 512, (j % 2) * 512 + 512)
                for c in range(DC):
                    gr.op("tensor", lambda e, c=c, j=j, ap=apV:
                          e.matmul(ap, wV.t[:, c * 1024 + 128 * j: c * 1024 + 128 * j + 128],
                                   kact.t[:, c * SL:(c + 1) * SL], start=(c == 0), stop=(c == DC - 1)),
                          reads=[wV.buf, kact.buf], writes=bufV)
                gr.op("vector", lambda e, j=j, ap=apV: e.tensor_tensor(t1.t[:, :], rbuf.t[:, j * SL:(j + 1) * SL],
                                                                       ap, ALU.mult),
                      reads=[rbuf.buf] + bufV, writes=[t1.buf])
                gr.op("vector", lambda e, j=j: e.tensor_tensor(x1st.t[:, j * SL:(j + 1) * SL],
                                                               xf32.t[:, j * SL:(j + 1) * SL],
                                                               of32.t[:, j * SL:(j + 1) * SL], ALU.add),
                      reads=[xf32.bufs[j], of32.bufs[j]], writes=[x1st.buf])
                gr.op("vector", lambda e, j=j: e.tensor_tensor(x1st.t[:, j * SL:(j + 1) * SL],
                                                               x1st.t[:, j * SL:(j + 1) * SL],
                                                               t1.t[:, :], ALU.add),
                      reads=[t1.buf, x1st.buf], writes=[x1st.buf])
                gr.dma("sync", x1_d[j * 128:(j + 1) * 128, sl * SL:(sl + 1) * SL],
                       x1st.t[:, j * SL:(j + 1) * SL], reads=[x1st.buf])




def build_l0_ffn(debug=False):
    nc = bass.Bass("TRN2", target_bir_lowering=False)
    gr = G(nc)
    consts = host_consts()
    x_d = nc.dram_tensor("xT", [D, T], F32, kind="ExternalInput")
    o_d = nc.dram_tensor("o0T", [D, T], F32, kind="ExternalInput")
    wr_d = nc.dram_tensor("wr_l", [128, DC * D], F32, kind="ExternalInput")
    wk_d = nc.dram_tensor("wk2_l", [128, DC * D], F32, kind="ExternalInput")
    wv_d = nc.dram_tensor("wv2_l", [128, DC * D], F32, kind="ExternalInput")
    mur_d = nc.dram_tensor("mu_r", [D, 1], F32, kind="ExternalInput")
    muk_d = nc.dram_tensor("mu_k", [D, 1], F32, kind="ExternalInput")
    x1_d = nc.dram_tensor("x1T", [D, T], F32, kind="ExternalOutput")
    dr = {"xT": x_d, "o0T": o_d, "wr_l": wr_d, "wk2_l": wk_d, "wv2_l": wv_d, "mu_r": mur_d, "mu_k": muk_d, "x1T": x1_d}
    with contextlib.ExitStack() as ctx0:
        cells = Cells(nc, ctx0, gr)
        emit_l0_ffn(nc, gr, consts, cells, dr, debug=debug)
        gr.finalize_and_emit(ctx0)
    return nc


DBG_SLICE_BARRIER = False
DBG_PARITY_KEYS = None
DBG_DUMP_SLICE = 0


def emit_l1_attn(nc, gr, consts, cells, dr, debug=False, LS=256):
    x1_d = dr["x1T"]
    wq_d = dr["Wq"]
    wk_d = dr["Wk"]
    wv_d = dr["Wv"]
    wg_d = dr["Wg"]
    wo_d = dr["Wo"]
    w2_d = dr["W2"]
    wa_d = dr["Wa"]
    wb_d = dr["Wb"]
    cq_d = dr["conv_q"]
    ck_d = dr["conv_k"]
    cv_d = dr["conv_v"]
    ln1g_d = dr["ln1_g"]
    ln1b_d = dr["ln1_b"]
    ln2g_d = dr["ln2_g"]
    ln2b_d = dr["ln2_b"]
    rmsw_d = dr["rms_in_w"]
    b2_d = dr["b2"]
    onw_d = dr["o_norm_w"]
    dtb_d = dr["dt_bias"]
    alog_d = dr["A_log"]
    y_d = dr["yT"]
    if debug:
        dbg = {}
        for nm, shp in [("h", [128, DC * 256]), ("qn", [128, DC * 256]), ("kn", [128, DC * 256]),
                        ("vn", [128, 16 * 256]), ("gb", [128, 8]), ("bb", [128, 8]),
                        ("btok", [128, 4]), ("Ot", [128, 2048]), ("U", [128, 2048]),
                        ("xa", [128, DC * 256]), ("ofm", [128, 16 * 256]),
                        ("Dlow", [128, 512]), ("DlowT", [128, 512]), ("DincT", [128, 512]),
                        ("Lb", [128, 512]), ("Ltb", [128, 512]), ("Rfin", [128, 512]),
                        ("AttT", [128, 512]), ("Wt", [128, 2048]), ("Vtok", [128, 2048]),
                        ("Snew", [128, 4096]), ("bT", [1, 512]), ("ogc", [128, 2048]),
                        ("gsil", [128, 2048]), ("h0", [128, DC * 256]), ("xn10", [128, DC * 256]),
                        ("x1bf0", [128, DC * 256]), ("m0b0", [128, 256]), ("r0b0", [128, 256])]:
            dbg[nm] = nc.dram_tensor("dbg_" + nm, shp, F32, kind="ExternalOutput")

    NCH = LS // 128
    LNS = T // LS
    RP = LS + 3

    ctx = contextlib.ExitStack()
    with ctx:
        C = const_tensors(nc, gr, ctx, consts, {"ident": ["bf16", "f32"], "ones_col": ["bf16"], "ones_row": ["f32", "bf16"], "neg_ones_row": ["f32"], "tri_incl": ["bf16"], "mask1": ["f32"], "mask2": ["f32"], "mask3": ["f32"], "allones": ["bf16"]})
        PAR = []
        qraw_sh = Tile(nc, ctx, "qraw_sh", [128, DC * RP], BF16)
        kraw_sh = Tile(nc, ctx, "kraw_sh", [128, DC * RP], BF16)
        for pk in range(2):
            d = {}
            d["x1bf"] = Tile(nc, ctx, f"x1bf{pk}", [128, DC * LS], BF16, nbufs=DC)
            d["xn1"] = Tile(nc, ctx, f"xn1{pk}", [128, DC * LS], BF16)
            d["a1t"] = Tile(nc, ctx, f"a1t{pk}", [128, LS], BF16)
            d["h"] = Tile(nc, ctx, f"h{pk}", [128, DC * LS], BF16)
            d["qraw"] = qraw_sh
            d["kraw"] = kraw_sh
            d["vraw"] = Tile(nc, ctx, f"vraw{pk}", [128, 16 * RP], BF16)
            d["qn"] = Tile(nc, ctx, f"qn{pk}", [128, DC * LS], BF16)
            d["kn"] = Tile(nc, ctx, f"kn{pk}", [128, DC * LS], BF16)
            d["vn"] = Tile(nc, ctx, f"vn{pk}", [128, 16 * LS], BF16)
            d["gate_sil"] = Tile(nc, ctx, f"gate_sil{pk}", [128, 2 * 2048], BF16)
            d["o_fm"] = Tile(nc, ctx, f"o_fm{pk}", [128, 16 * LS], BF16, alias=d["vraw"])
            d["h2t"] = Tile(nc, ctx, f"h2t{pk}", [128, DC * LS], BF16, alias=d["x1bf"])
            d["g_bf"] = Tile(nc, ctx, f"g_bf{pk}", [128, NCH * 4], BF16)
            d["beta_bf"] = Tile(nc, ctx, f"beta_bf{pk}", [128, NCH * 4], BF16)
            d["beta_f"] = Tile(nc, ctx, f"beta_f{pk}", [128, NCH * 4], F32)
            d["nbeta_f"] = Tile(nc, ctx, f"nbeta_f{pk}", [128, NCH * 4], F32)
            d["m0b"] = Tile(nc, ctx, f"m0b{pk}", [128, LS], F32)
            d["r0b"] = Tile(nc, ctx, f"r0b{pk}", [128, LS], F32)
            d["r01b"] = Tile(nc, ctx, f"r01b{pk}", [128, LS], F32)
            d["vbb"] = Tile(nc, ctx, f"vbb{pk}", [128, LS], F32)
            d["lnb"] = Tile(nc, ctx, f"lnb{pk}", [128, LS], F32)
            d["sq"] = Tile(nc, ctx, f"sq{pk}", [128, LS], BF16)
            d["t1"] = Tile(nc, ctx, f"t1{pk}", [128, LS], F32)
            d["t2"] = Tile(nc, ctx, f"t2{pk}", [128, LS], F32)
            d["ca"] = Tile(nc, ctx, f"ca{pk}", [128, LS], BF16)
            d["cb"] = Tile(nc, ctx, f"cb{pk}", [128, LS], BF16)
            d["gelc"] = Tile(nc, ctx, f"gelc{pk}", [128, LS], BF16, alias=d["ca"])
            d["x1f"] = Tile(nc, ctx, f"x1f{pk}", [128, 2 * LS], F32, nbufs=2)
            d["yst"] = Tile(nc, ctx, f"yst{pk}", [128, 2 * LS], F32, nbufs=2)
            PAR.append(d)
        if DBG_PARITY_KEYS is not None:
            for kk in list(PAR[1].keys()):
                if kk not in DBG_PARITY_KEYS:
                    PAR[1][kk] = PAR[0][kk]
        WPN = 4
        wps = [Tile(nc, ctx, f"wp{i}", [128, 2048], BF16) for i in range(WPN)]
        wab = Tile(nc, ctx, "wab", [128, DC * 8], BF16)
        cwq = Tile(nc, ctx, "cwq", [128, DC * 4], F32)
        cwk = Tile(nc, ctx, "cwk", [128, DC * 4], F32)
        cwv = Tile(nc, ctx, "cwv", [128, 16 * 4], F32)
        dtb_row = Tile(nc, ctx, "dtb_row", [1, 4], BF16)
        alog_row = Tile(nc, ctx, "alog_row", [1, 4], F32)
        negexpA = Tile(nc, ctx, "negexpA", [128, 4], F32)
        onw_row = Tile(nc, ctx, "onw_row", [1, Vd], F32)
        normw_bc = Tile(nc, ctx, "normw_bc", [128, Vd], BF16)
        vcols = {}
        for nm in ["ln1g", "ln1b", "ln2g", "ln2b", "rmsw", "b2a"]:
            vcols[nm] = Tile(nc, ctx, "vc_" + nm, [128, DC], F32)
        sc4a = Tile(nc, ctx, "sc4a", [128, 4], F32)
        sc4b = Tile(nc, ctx, "sc4b", [128, 4], F32)
        sc4c = Tile(nc, ctx, "sc4c", [128, 4], F32)
        b_tok = Tile(nc, ctx, "b_tok", [128, 4], F32)
        Acol = Tile(nc, ctx, "Acol", [128, 4], F32)
        aC_col = Tile(nc, ctx, "aC_col", [128, 4], F32)
        bC_col = Tile(nc, ctx, "bC_col", [128, 4], F32)
        ebar = Tile(nc, ctx, "ebar", [128, 4], F32)
        btT_sb = Tile(nc, ctx, "btT_sb", [1, 512], BF16)
        Dlow = Tile(nc, ctx, "Dlow", [128, 512], BF16)
        DlowT = Tile(nc, ctx, "DlowT", [128, 512], BF16)
        DincT = Tile(nc, ctx, "DincT", [128, 512], BF16)
        zs = Tile(nc, ctx, "zs", [128, 512], F32)
        Lb = Tile(nc, ctx, "Lb", [128, 512], BF16)
        Ltb = Tile(nc, ctx, "Ltb", [128, 512], BF16)
        Apw = [Tile(nc, ctx, f"Apw{i}", [128, 512], BF16) for i in range(2)]
        Atw = [Tile(nc, ctx, f"Atw{i}", [128, 512], BF16) for i in range(2)]
        Rw = [Tile(nc, ctx, f"Rw{i}", [128, 512], BF16) for i in range(2)]
        AttT = Tile(nc, ctx, "AttT", [128, 512], BF16)
        Vtok = Tile(nc, ctx, "Vtok", [128, 2048], BF16)
        Ktok = Tile(nc, ctx, "Ktok", [128, 1024], BF16)
        Kbar = Tile(nc, ctx, "Kbar", [128, 1024], BF16)
        Wt = Tile(nc, ctx, "Wt", [128, 2048], BF16)
        W1t = Tile(nc, ctx, "W1t", [128, 512], F32)
        Ut = Tile(nc, ctx, "Ut", [128, 2048], BF16)
        attu = Tile(nc, ctx, "attu", [128, 512], BF16)
        O_t = Tile(nc, ctx, "O_t", [128, 2048], BF16)
        ogc = Tile(nc, ctx, "ogc", [128, 2048], BF16, alias=O_t)
        ostd = Tile(nc, ctx, "ostd", [128, 4], F32)
        Sb = [Tile(nc, ctx, f"Sb{i}", [128, DC * SL], BF16) for i in range(2)]
        bT_sb = Tile(nc, ctx, "bT_sb", [1, 512], F32)

        P1 = Psum(nc, ctx, "P1", [128, 512], F32)
        P2 = Psum(nc, ctx, "P2", [128, 512], F32)
        P3a = Psum(nc, ctx, "P3a", [128, 512], F32)
        P3b = Psum(nc, ctx, "P3b", [128, 512], F32)
        P4a = Psum(nc, ctx, "P4a", [128, 512], F32)
        P4b = Psum(nc, ctx, "P4b", [128, 512], F32)
        P5 = Psum(nc, ctx, "P5", [128, 512], F32)
        PT = Psum(nc, ctx, "PT", [128, 1024], BF16)

        ID_BF = C[("ident", "bf16")]
        ID_F = C[("ident", "f32")]
        OC_BF = C[("ones_col", "bf16")]
        OR_F = C[("ones_row", "f32")]
        OR_BF = C[("ones_row", "bf16")]
        NOR_F = C[("neg_ones_row", "f32")]
        TRI = C[("tri_incl", "bf16")]
        MASKS = [C[("mask1", "f32")], C[("mask2", "f32")], C[("mask3", "f32")]]
        eps5, eps5_b = cells.cell(LN_EPS)
        eps5c, eps5c_b = cells.col(LN_EPS)
        eps6, eps6_b = cells.cell(EPS_L2)
        eps6c, eps6c_b = cells.col(EPS_L2)
        lnq, lnq_b = cells.col(float(-np.log(16.0)))
        AO_BF = C[("allones", "bf16")]

        for c in range(DC):
            gr.dma("gpsimd", wab.t[:, c * 8:c * 8 + 4], wa_d[c * 128:(c + 1) * 128, :], writes=[wab.buf])
            gr.dma("gpsimd", wab.t[:, c * 8 + 4:c * 8 + 8], wb_d[c * 128:(c + 1) * 128, :], writes=[wab.buf])
            gr.dma("sync", cwq.t[:, c * 4:(c + 1) * 4], cq_d[c * 128:(c + 1) * 128, :], writes=[cwq.buf])
            gr.dma("sync", cwk.t[:, c * 4:(c + 1) * 4], ck_d[c * 128:(c + 1) * 128, :], writes=[cwk.buf])
        for c in range(16):
            gr.dma("sync", cwv.t[:, c * 4:(c + 1) * 4], cv_d[c * 128:(c + 1) * 128, :], writes=[cwv.buf])
        for nm, dsrc in [("ln1g", ln1g_d), ("ln1b", ln1b_d), ("ln2g", ln2g_d),
                         ("ln2b", ln2b_d), ("rmsw", rmsw_d), ("b2a", b2_d)]:
            for c in range(DC):
                gr.dma("sync", vcols[nm].t[:, c:c + 1], dsrc[c * 128:(c + 1) * 128, :], writes=[vcols[nm].buf])
        gr.dma("gpsimd", dtb_row.t[:, :], dtb_d[:, :], writes=[dtb_row.buf])
        gr.dma("sync", alog_row.t[:, :], alog_d[:, :], writes=[alog_row.buf])
        gr.dma("sync", onw_row.t[:, :], onw_d[:, :], writes=[onw_row.buf])
        gr.op("scalar", lambda e: e.activation(bT_sb.t[0:1, 0:4], alog_row.t[:, :], AF.Exp),
              reads=[alog_row.buf], writes=[bT_sb.buf])
        gr.op("vector", lambda e: e.tensor_scalar(bT_sb.t[0:1, 4:8], bT_sb.t[0:1, 0:4], -1.0, None, ALU.mult),
              reads=[bT_sb.buf], writes=[bT_sb.buf])
        apN, bufN = P5.slice(0, 4)
        gr.op("tensor", lambda e, ap=apN: e.matmul(ap, OR_F.t[:, :], bT_sb.t[0:1, 4:8], start=True, stop=True),
              reads=[bT_sb.buf, OR_F.buf], writes=bufN)
        gr.op("vector", lambda e, ap=apN: e.tensor_copy(negexpA.t[:, :], ap),
              reads=bufN, writes=[negexpA.buf])
        apW, bufW = P5.slice(0, Vd)
        gr.op("tensor", lambda e, ap=apW: e.matmul(ap, OR_F.t[:, :], onw_row.t[:, :], start=True, stop=True),
              reads=[onw_row.buf, OR_F.buf], writes=bufW)
        gr.op("vector", lambda e, ap=apW: e.tensor_copy(normw_bc.t[:, :], ap),
              reads=bufW, writes=[normw_bc.buf])
        for i in range(2):
            gr.op("vector", lambda e, i=i: e.memset(Sb[i].t[:, :], 0.0), writes=[Sb[i].buf])
        for pk in range(2):
            qrw, krw, vrw = PAR[pk]["qraw"], PAR[pk]["kraw"], PAR[pk]["vraw"]
            for c in range(DC):
                gr.op("vector", lambda e, c=c, t_=qrw: e.memset(t_.t[:, c * RP:c * RP + 3], 0.0),
                      writes=[qrw.buf])
                gr.op("vector", lambda e, c=c, t_=krw: e.memset(t_.t[:, c * RP:c * RP + 3], 0.0),
                      writes=[krw.buf])
            for c in range(16):
                gr.op("vector", lambda e, c=c, t_=vrw: e.memset(t_.t[:, c * RP:c * RP + 3], 0.0),
                      writes=[vrw.buf])

        piece_idx = [0]

        def wpiece():
            wp = wps[piece_idx[0] % WPN]
            piece_idx[0] += 1
            return wp

        def load_piece(wdram, pidx):
            wp = wpiece()
            gr.dma("gpsimd", wp.t[:, :], wdram[pidx * 128:(pidx + 1) * 128, :], writes=[wp.buf])
            return wp

        def psum_pick(j):
            return (P3a, P3b)[j % 2]

        def emit_slice(sl):
            tsl = slice(sl * LS, (sl + 1) * LS)
            PD = PAR[sl % 2]
            PDp = PAR[(sl + 1) % 2]
            x1bf = PD["x1bf"]; xn1 = PD["xn1"]; a1t = PD["a1t"]; h = PD["h"]
            qraw = PD["qraw"]; kraw = PD["kraw"]; vraw = PD["vraw"]
            qn = PD["qn"]; kn = PD["kn"]; vn = PD["vn"]
            gate_sil = PD["gate_sil"]; o_fm = PD["o_fm"]; h2t = PD["h2t"]
            g_bf = PD["g_bf"]; beta_bf = PD["beta_bf"]; beta_f = PD["beta_f"]; nbeta_f = PD["nbeta_f"]
            m0b = PD["m0b"]; r0b = PD["r0b"]; r01b = PD["r01b"]; vbb = PD["vbb"]; lnb = PD["lnb"]
            sq = PD["sq"]; t1 = PD["t1"]; t2 = PD["t2"]; ca = PD["ca"]; cb = PD["cb"]
            gelc = PD["gelc"]; x1f = PD["x1f"]; yst = PD["yst"]
            for c in range(DC):
                gr.dma("gpsimd", x1bf.t[:, c * LS:(c + 1) * LS],
                       x1_d[c * 128:(c + 1) * 128, sl * LS:(sl + 1) * LS], writes=[x1bf.bufs[c]])
            sum_ap, sum_bufs = P3a.slice(0, LS)
            sq_ap, sq_bufs = P3b.slice(0, LS)
            for c in range(DC):
                gr.op("tensor", lambda e, c=c, ap=sum_ap: e.matmul(ap, AO_BF.t[:, :],
                                                                   x1bf.t[:, c * LS:(c + 1) * LS],
                                                                   start=(c == 0), stop=(c == DC - 1)),
                      reads=[x1bf.bufs[c], AO_BF.buf], writes=sum_bufs)
            for c in range(DC):
                gr.op("vector", lambda e, c=c: e.tensor_mul(sq.t[:, :], x1bf.t[:, c * LS:(c + 1) * LS],
                                                            x1bf.t[:, c * LS:(c + 1) * LS]),
                      reads=[x1bf.bufs[c]], writes=[sq.buf])
                gr.op("tensor", lambda e, c=c, ap=sq_ap: e.matmul(ap, AO_BF.t[:, :], sq.t[:, :],
                                                                  start=(c == 0), stop=(c == DC - 1)),
                      reads=[sq.buf, AO_BF.buf], writes=sq_bufs)
            gr.op("vector", lambda e, ap=sum_ap: e.tensor_scalar(m0b.t[:, :], ap, 1.0 / D, None, ALU.mult),
                  reads=sum_bufs, writes=[m0b.buf])
            gr.op("vector", lambda e: e.tensor_mul(t1.t[:, :], m0b.t[:, :], m0b.t[:, :]),
                  reads=[m0b.buf], writes=[t1.buf])
            gr.op("vector", lambda e, ap=sq_ap: e.scalar_tensor_tensor(vbb.t[:, :], ap, 1.0 / D,
                                                                       t1.t[:, :], ALU.mult, ALU.subtract),
                  reads=sq_bufs + [t1.buf], writes=[vbb.buf])
            gr.op("scalar", lambda e: e.activation(lnb.t[:, :], vbb.t[:, :], AF.Ln, bias=eps5c),
                  reads=[vbb.buf, eps5c_b], writes=[lnb.buf])
            gr.op("scalar", lambda e: e.activation(r0b.t[:, :], lnb.t[:, :], AF.Exp, scale=-0.5),
                  reads=[lnb.buf], writes=[r0b.buf])
            gr.op("vector", lambda e: e.tensor_mul(t1.t[:, :], r0b.t[:, :], r0b.t[:, :]),
                  reads=[r0b.buf], writes=[t1.buf])
            gr.op("vector", lambda e: e.tensor_mul(t1.t[:, :], t1.t[:, :], vbb.t[:, :]),
                  reads=[t1.buf, vbb.buf], writes=[t1.buf])
            gr.op("scalar", lambda e: e.activation(t1.t[:, :], t1.t[:, :], AF.Ln, bias=eps5c),
                  reads=[t1.buf, eps5c_b], writes=[t1.buf])
            gr.op("scalar", lambda e: e.activation(t1.t[:, :], t1.t[:, :], AF.Exp, scale=-0.5),
                  reads=[t1.buf], writes=[t1.buf])
            gr.op("vector", lambda e: e.tensor_mul(r01b.t[:, :], t1.t[:, :], r0b.t[:, :]),
                  reads=[t1.buf, r0b.buf], writes=[r01b.buf])
            for c in range(DC):
                csl = slice(c * LS, (c + 1) * LS)
                gr.op("vector", lambda e, csl=csl: e.tensor_tensor(t1.t[:, :], x1bf.t[:, csl],
                                                                   m0b.t[:, :], ALU.subtract),
                      reads=[x1bf.bufs[c], m0b.buf], writes=[t1.buf])
                gr.op("vector", lambda e, csl=csl: e.tensor_tensor(xn1.t[:, csl], t1.t[:, :],
                                                                   r0b.t[:, :], ALU.mult),
                      reads=[t1.buf, r0b.buf], writes=[xn1.buf])
                gr.op("vector", lambda e: e.tensor_tensor(t2.t[:, :], t1.t[:, :], r01b.t[:, :], ALU.mult),
                      reads=[t1.buf, r01b.buf], writes=[t2.buf])
                gr.op("vector", lambda e, c=c: e.tensor_scalar(a1t.t[:, :], t2.t[:, :],
                                                               vcols["ln1g"].t[:, c:c + 1],
                                                               vcols["ln1b"].t[:, c:c + 1],
                                                               ALU.mult, ALU.add),
                      reads=[t2.buf, vcols["ln1g"].buf, vcols["ln1b"].buf], writes=[a1t.buf])
                rs_ap, rs_bufs = P5.slice(0, LS)
                gr.op("vector", lambda e: e.tensor_mul(sq.t[:, :], a1t.t[:, :], a1t.t[:, :]),
                      reads=[a1t.buf], writes=[sq.buf])
                gr.op("tensor", lambda e, c=c, ap=rs_ap: e.matmul(ap, AO_BF.t[:, :], sq.t[:, :],
                                                                  start=(c == 0), stop=(c == DC - 1)),
                      reads=[sq.buf, AO_BF.buf], writes=rs_bufs)
                gr.op("vector", lambda e, csl=csl: e.tensor_copy(h.t[:, csl], a1t.t[:, :]),
                      reads=[a1t.buf], writes=[h.buf])
            rs_ap, rs_bufs = P5.slice(0, LS)
            gr.op("scalar", lambda e, ap=rs_ap: e.activation(lnb.t[:, :], ap, AF.Ln, bias=eps6c,
                                                             scale=1.0 / D),
                  reads=rs_bufs + [eps6c_b], writes=[lnb.buf])
            gr.op("scalar", lambda e: e.activation(lnb.t[:, :], lnb.t[:, :], AF.Exp, scale=-0.5),
                  reads=[lnb.buf], writes=[lnb.buf])
            for c in range(DC):
                csl = slice(c * LS, (c + 1) * LS)
                gr.op("vector", lambda e, csl=csl: e.tensor_tensor(t1.t[:, :], h.t[:, csl],
                                                                   lnb.t[:, :], ALU.mult),
                      reads=[h.buf, lnb.buf], writes=[t1.buf])
                gr.op("vector", lambda e, c=c, csl=csl: e.tensor_scalar(h.t[:, csl], t1.t[:, :],
                                                                        vcols["rmsw"].t[:, c:c + 1], None,
                                                                        ALU.mult),
                      reads=[t1.buf, vcols["rmsw"].buf], writes=[h.buf])

            if debug and sl == DBG_DUMP_SLICE:
                gr.dma("gpsimd", dbg["h0"][:, :], h.t[:, :], reads=[h.buf])
                gr.dma("gpsimd", dbg["xn10"][:, :], xn1.t[:, :], reads=[xn1.buf])
                for c in range(DC):
                    gr.dma("gpsimd", dbg["x1bf0"][:, c * LS:(c + 1) * LS],
                           x1bf.t[:, c * LS:(c + 1) * LS], reads=[x1bf.bufs[c]])
                gr.dma("gpsimd", dbg["m0b0"][:, :], m0b.t[:, :], reads=[m0b.buf])
                gr.dma("gpsimd", dbg["r0b0"][:, :], r0b.t[:, :], reads=[r0b.buf])

            def proj(wdram, ncols, dst, dstper):
                for p in range(ncols // 256):
                    wp = load_piece(wdram, p)
                    for jj in range(2):
                        j = 2 * p + jj
                        PS = psum_pick(j)
                        ap, bufs = PS.slice(0, LS)
                        for c in range(DC):
                            gr.op("tensor", lambda e, c=c, jj=jj, wp=wp, ap=ap:
                                  e.matmul(ap, wp.t[:, c * 256 + 128 * jj:c * 256 + 128 * jj + 128],
                                           h.t[:, c * LS:(c + 1) * LS], start=(c == 0), stop=(c == DC - 1)),
                                  reads=[wp.buf, h.buf], writes=bufs)
                        lo = j * dstper + 3
                        gr.op("vector", lambda e, lo=lo, ap=ap: e.tensor_copy(dst.t[:, lo:lo + LS], ap),
                              reads=bufs, writes=[dst.buf])
            proj(wq_d, D, qraw, RP)
            proj(wk_d, D, kraw, RP)
            proj(wv_d, VAL_DIM, vraw, RP)

            for tt_ in range(NCH):
                ab_ap, ab_bufs = P5.slice(0, 8)
                for c in range(DC):
                    gr.op("tensor", lambda e, c=c, tt_=tt_, ap=ab_ap:
                          e.matmul(ap, h.t[:, c * LS + 128 * tt_:c * LS + 128 * tt_ + 128],
                                   wab.t[:, c * 8:(c + 1) * 8], start=(c == 0), stop=False),
                          reads=[h.buf, wab.buf], writes=ab_bufs)
                ab4_ap, _ = P5.slice(0, 4)
                gr.op("tensor", lambda e, ap=ab4_ap: e.matmul(ap, OR_BF.t[:, :], dtb_row.t[:, :],
                                                              start=False, stop=True),
                      reads=[dtb_row.buf, OR_BF.buf], writes=ab_bufs)
                gr.op("scalar", lambda e, ap=ab4_ap: e.activation(sc4a.t[:, :], ap, AF.Abs),
                      reads=ab_bufs, writes=[sc4a.buf])
                gr.op("scalar", lambda e: e.activation(sc4b.t[:, :], sc4a.t[:, :], AF.Exp, scale=-1.0),
                      reads=[sc4a.buf], writes=[sc4b.buf])
                gr.op("scalar", lambda e: e.activation(sc4b.t[:, :], sc4b.t[:, :], AF.Ln, bias=1.0),
                      reads=[sc4b.buf], writes=[sc4b.buf])
                gr.op("scalar", lambda e, ap=ab4_ap: e.activation(sc4c.t[:, :], ap, AF.Relu),
                      reads=ab_bufs, writes=[sc4c.buf])
                gr.op("vector", lambda e: e.tensor_add(sc4a.t[:, :], sc4b.t[:, :], sc4c.t[:, :]),
                      reads=[sc4b.buf, sc4c.buf], writes=[sc4a.buf])
                gr.op("vector", lambda e, tt_=tt_: e.tensor_mul(g_bf.t[:, 4 * tt_:4 * tt_ + 4],
                                                                sc4a.t[:, :], negexpA.t[:, :]),
                      reads=[sc4a.buf, negexpA.buf], writes=[g_bf.buf])
                b_ap = P5.t[0:128, 4:8]
                gr.op("scalar", lambda e, tt_=tt_, ap=b_ap: e.activation(beta_f.t[:, 4 * tt_:4 * tt_ + 4],
                                                                         ap, AF.Sigmoid),
                      reads=ab_bufs, writes=[beta_f.buf])
                gr.op("vector", lambda e, tt_=tt_: e.tensor_copy(beta_bf.t[:, 4 * tt_:4 * tt_ + 4],
                                                                 beta_f.t[:, 4 * tt_:4 * tt_ + 4]),
                      reads=[beta_f.buf], writes=[beta_bf.buf])
                gr.op("vector", lambda e, tt_=tt_: e.tensor_scalar(nbeta_f.t[:, 4 * tt_:4 * tt_ + 4],
                                                                   beta_f.t[:, 4 * tt_:4 * tt_ + 4],
                                                                   -1.0, None, ALU.mult),
                      reads=[beta_f.buf], writes=[nbeta_f.buf])

            def conv(raw, cw, nch, dst):
                for c in range(nch):
                    base = c * RP
                    gr.op("vector", lambda e, base=base, c=c, raw=raw, cw=cw:
                          e.tensor_scalar(ca.t[:, :], raw.t[:, base:base + LS], cw.t[:, c * 4:c * 4 + 1],
                                          None, ALU.mult),
                          reads=[raw.buf, cw.buf], writes=[ca.buf])
                    gr.op("vector", lambda e, base=base, c=c, raw=raw, cw=cw:
                          e.scalar_tensor_tensor(cb.t[:, :], raw.t[:, base + 1:base + 1 + LS],
                                                 cw.t[:, c * 4 + 1:c * 4 + 2], ca.t[:, :], ALU.mult, ALU.add),
                          reads=[raw.buf, cw.buf, ca.buf], writes=[cb.buf])
                    gr.op("vector", lambda e, base=base, c=c, raw=raw, cw=cw:
                          e.scalar_tensor_tensor(ca.t[:, :], raw.t[:, base + 2:base + 2 + LS],
                                                 cw.t[:, c * 4 + 2:c * 4 + 3], cb.t[:, :], ALU.mult, ALU.add),
                          reads=[raw.buf, cw.buf, cb.buf], writes=[ca.buf])
                    gr.op("vector", lambda e, base=base, c=c, raw=raw, cw=cw:
                          e.scalar_tensor_tensor(cb.t[:, :], raw.t[:, base + 3:base + 3 + LS],
                                                 cw.t[:, c * 4 + 3:c * 4 + 4], ca.t[:, :], ALU.mult, ALU.add),
                          reads=[raw.buf, cw.buf, ca.buf], writes=[cb.buf])
                    gr.op("scalar", lambda e, c=c, dst=dst: e.activation(dst.t[:, c * LS:(c + 1) * LS],
                                                                         cb.t[:, :], AF.Silu),
                          reads=[cb.buf], writes=[dst.buf])
            conv(qraw, cwq, DC, qn)
            conv(kraw, cwk, DC, kn)
            conv(vraw, cwv, 16, vn)
            if sl < LNS - 1:
                for c in range(DC):
                    gr.op("vector", lambda e, c=c, a_=qraw, b_=PDp["qraw"]: e.tensor_copy(
                        b_.t[:, c * RP:c * RP + 3], a_.t[:, c * RP + LS:c * RP + LS + 3]),
                          reads=[qraw.buf], writes=[PDp["qraw"].buf])
                    gr.op("vector", lambda e, c=c, a_=kraw, b_=PDp["kraw"]: e.tensor_copy(
                        b_.t[:, c * RP:c * RP + 3], a_.t[:, c * RP + LS:c * RP + LS + 3]),
                          reads=[kraw.buf], writes=[PDp["kraw"].buf])
                for c in range(16):
                    gr.op("vector", lambda e, c=c, a_=vraw, b_=PDp["vraw"]: e.tensor_copy(
                        b_.t[:, c * RP:c * RP + 3], a_.t[:, c * RP + LS:c * RP + LS + 3]),
                          reads=[vraw.buf], writes=[PDp["vraw"].buf])

            assert NCH == 2
            cc0 = 0
            if True:
                for p in range(VAL_DIM // 256):
                    wp = load_piece(wg_d, p)
                    for jj in range(2):
                        jhv = 2 * p + jj
                        PS = psum_pick(jj)
                        for cc in range(cc0, cc0 + 2):
                            ap = PS.t[0:128, 128 * (cc - cc0):128 * (cc - cc0) + 128]
                            _, bufs = PS.slice(128 * (cc - cc0), 128 * (cc - cc0) + 128)
                            for c in range(DC):
                                gr.op("tensor", lambda e, c=c, cc=cc, jj=jj, wp=wp, ap=ap:
                                      e.matmul(ap, h.t[:, c * LS + 128 * cc:c * LS + 128 * cc + 128],
                                               wp.t[:, c * 256 + 128 * jj:c * 256 + 128 * jj + 128],
                                               start=(c == 0), stop=(c == DC - 1)),
                                      reads=[wp.buf, h.buf], writes=bufs)
                            gr.op("scalar", lambda e, cc=cc, cc0=cc0, jhv=jhv, ap=ap:
                                  e.activation(gate_sil.t[:, (cc - cc0) * 2048 + jhv * 128: (cc - cc0) * 2048 + jhv * 128 + 128],
                                               ap, AF.Silu),
                                  reads=bufs, writes=[gate_sil.buf])

                if cc0 == 0:
                    def l2n(dst, qscale):
                        for hh in range(H):
                            rs_ap, rs_bufs = P5.slice(0, LS)
                            for kc in range(2):
                                csl = slice((2 * hh + kc) * LS, (2 * hh + kc + 1) * LS)
                                gr.op("vector", lambda e, csl=csl, dst=dst: e.tensor_mul(sq.t[:, :], dst.t[:, csl],
                                                                                         dst.t[:, csl]),
                                      reads=[dst.buf], writes=[sq.buf])
                                gr.op("tensor", lambda e, kc=kc, ap=rs_ap: e.matmul(ap, AO_BF.t[:, :], sq.t[:, :],
                                                                                    start=(kc == 0), stop=(kc == 1)),
                                      reads=[sq.buf, AO_BF.buf], writes=rs_bufs)
                            gr.op("scalar", lambda e, ap=rs_ap: e.activation(lnb.t[:, :], ap, AF.Ln, bias=eps6c),
                                  reads=rs_bufs + [eps6c_b], writes=[lnb.buf])
                            if qscale:
                                gr.op("scalar", lambda e: e.activation(lnb.t[:, :], lnb.t[:, :], AF.Exp,
                                                                       scale=-0.5, bias=lnq),
                                      reads=[lnb.buf, lnq_b], writes=[lnb.buf])
                            else:
                                gr.op("scalar", lambda e: e.activation(lnb.t[:, :], lnb.t[:, :], AF.Exp, scale=-0.5),
                                      reads=[lnb.buf], writes=[lnb.buf])
                            for kc in range(2):
                                csl = slice((2 * hh + kc) * LS, (2 * hh + kc + 1) * LS)
                                gr.op("vector", lambda e, csl=csl, dst=dst:
                                      e.tensor_tensor(dst.t[:, csl], dst.t[:, csl], lnb.t[:, :], ALU.mult),
                                      reads=[dst.buf, lnb.buf], writes=[dst.buf])
                    l2n(qn, True)
                    l2n(kn, False)

        def emit_back(sl):
            tsl = slice(sl * LS, (sl + 1) * LS)
            PD = PAR[sl % 2]
            PDp = PAR[(sl + 1) % 2]
            x1bf = PD["x1bf"]; xn1 = PD["xn1"]; a1t = PD["a1t"]; h = PD["h"]
            qraw = PD["qraw"]; kraw = PD["kraw"]; vraw = PD["vraw"]
            qn = PD["qn"]; kn = PD["kn"]; vn = PD["vn"]
            gate_sil = PD["gate_sil"]; o_fm = PD["o_fm"]; h2t = PD["h2t"]
            g_bf = PD["g_bf"]; beta_bf = PD["beta_bf"]; beta_f = PD["beta_f"]; nbeta_f = PD["nbeta_f"]
            m0b = PD["m0b"]; r0b = PD["r0b"]; r01b = PD["r01b"]; vbb = PD["vbb"]; lnb = PD["lnb"]
            sq = PD["sq"]; t1 = PD["t1"]; t2 = PD["t2"]; ca = PD["ca"]; cb = PD["cb"]
            gelc = PD["gelc"]; x1f = PD["x1f"]; yst = PD["yst"]
            cc0 = 0
            if True:
                for cc in (cc0, cc0 + 1):
                    gsl = slice(4 * cc, 4 * cc + 4)
                    bt_ap, bt_bufs = P1.slice(0, 4)
                    gr.op("tensor", lambda e, gsl=gsl, ap=bt_ap: e.matmul(ap, TRI.t[:, :], g_bf.t[:, gsl],
                                                                          start=True, stop=True),
                          reads=[g_bf.buf, TRI.buf], writes=bt_bufs)
                    gr.op("vector", lambda e, ap=bt_ap: e.tensor_copy(b_tok.t[:, :], ap),
                          reads=bt_bufs, writes=[b_tok.buf])
                    gr.op("scalar", lambda e: e.activation(Acol.t[:, :], b_tok.t[:, :], AF.Exp),
                          reads=[b_tok.buf], writes=[Acol.buf])
                    for hh in range(H):
                        tp_ap = P2.t[0:1, 128 * hh:128 * hh + 128]
                        _, tp_bufs = P2.slice(128 * hh, 128 * hh + 128)
                        gr.op("tensor", lambda e, hh=hh, ap=tp_ap: e.transpose(ap, b_tok.t[:, hh:hh + 1],
                                                                               ID_F.t[:, :]),
                              reads=[b_tok.buf, ID_F.buf], writes=tp_bufs)
                    ball_ap, ball_bufs = P2.slice(0, 512, 0, 1)
                    gr.op("vector", lambda e, ap=ball_ap: e.tensor_copy(bT_sb.t[:, :], ap),
                          reads=ball_bufs, writes=[bT_sb.buf])
                    bC_ap, bC_bufs = P2.slice(0, 4)
                    lastv = bass.AP(bT_sb.t, 127, [[512, 1], [128, 4]])
                    gr.op("tensor", lambda e, ap=bC_ap, lastv=lastv: e.matmul(ap, OR_F.t[:, :], lastv,
                                                                              start=True, stop=True),
                          reads=[bT_sb.buf, OR_F.buf], writes=bC_bufs)
                    gr.op("vector", lambda e, ap=bC_ap: e.tensor_copy(bC_col.t[:, :], ap),
                          reads=bC_bufs, writes=[bC_col.buf])
                    gr.op("scalar", lambda e: e.activation(aC_col.t[:, :], bC_col.t[:, :], AF.Exp),
                          reads=[bC_col.buf], writes=[aC_col.buf])
                    for hh in range(H):
                        gr.op("scalar", lambda e, hh=hh: e.activation(ebar.t[:, hh:hh + 1], b_tok.t[:, hh:hh + 1],
                                                                      AF.Exp, scale=-1.0,
                                                                      bias=bC_col.t[:, hh:hh + 1]),
                              reads=[b_tok.buf, bC_col.buf], writes=[ebar.buf])
                    zp_ap, zp_bufs = P2.slice(0, 512)
                    for hh in range(H):
                        blk = P2.t[0:128, 128 * hh:128 * hh + 128]
                        bseg = bT_sb.t[0:1, 128 * hh:128 * hh + 128]
                        gr.op("tensor", lambda e, blk=blk, bseg=bseg, hh=hh:
                              e.matmul(blk, bseg, OR_F.t[:, :], start=(hh == 0), stop=False),
                              reads=[bT_sb.buf, OR_F.buf], writes=zp_bufs)
                    gr.op("tensor", lambda e, ap=zp_ap: e.matmul(ap, NOR_F.t[:, :], bT_sb.t[:, :],
                                                                 start=False, stop=True),
                          reads=[bT_sb.buf, NOR_F.buf], writes=zp_bufs)
                    for Dt, mk, sc in [(Dlow, MASKS[0], 1.0), (DlowT, MASKS[1], -1.0), (DincT, MASKS[2], -1.0)]:
                        gr.op("vector", lambda e, mk=mk, ap=zp_ap: e.tensor_tensor(zs.t[:, :], ap, mk.t[:, :],
                                                                                   ALU.add),
                              reads=zp_bufs + [mk.buf], writes=[zs.buf])
                        gr.op("scalar", lambda e, Dt=Dt, sc=sc: e.activation(Dt.t[:, :], zs.t[:, :], AF.Exp,
                                                                             scale=sc),
                              reads=[zs.buf], writes=[Dt.buf])
                    g_ap, g_bufs = P1.slice(0, 512)
                    for hh in range(H):
                        blk = P1.t[0:128, 128 * hh:128 * hh + 128]
                        for kc in range(2):
                            kcol = slice((2 * hh + kc) * LS + 128 * cc, (2 * hh + kc) * LS + 128 * cc + 128)
                            gr.op("tensor", lambda e, blk=blk, kcol=kcol, hh=hh, kc=kc:
                                  e.matmul(blk, kn.t[:, kcol], kn.t[:, kcol],
                                           start=(hh == 0 and kc == 0), stop=False),
                                  reads=[kn.buf], writes=g_bufs)
                    bcolv = bass.AP(beta_bf.t, 4 * cc, [[NCH * 4, 128], [1, 4], [0, 128]])
                    gr.op("vector", lambda e, ap=g_ap: e.tensor_tensor(Lb.t[:, :], ap, Dlow.t[:, :], ALU.mult),
                          reads=g_bufs + [Dlow.buf], writes=[Lb.buf])
                    gr.op("vector", lambda e, bcolv=bcolv: e.tensor_tensor(
                        Lb.t[:, :].rearrange("p (a b) -> p a b", a=4), Lb.t[:, :].rearrange("p (a b) -> p a b", a=4),
                        bcolv, ALU.mult),
                          reads=[Lb.buf, beta_bf.buf], writes=[Lb.buf])
                    for hh in range(H):
                        tp_ap = P2.t[0:1, 128 * hh:128 * hh + 128]
                        _, tp_bufs = P2.slice(128 * hh, 128 * hh + 128)
                        gr.op("tensor", lambda e, hh=hh, cc=cc, ap=tp_ap:
                              e.matmul(ap, beta_bf.t[:, 4 * cc + hh:4 * cc + hh + 1], ID_BF.t[:, :],
                                       start=True, stop=True),
                              reads=[beta_bf.buf, ID_BF.buf], writes=tp_bufs)
                    btall_ap, btall_bufs = P2.slice(0, 512, 0, 1)
                    gr.op("vector", lambda e, ap=btall_ap: e.tensor_copy(btT_sb.t[:, :], ap),
                          reads=btall_bufs, writes=[btT_sb.buf])
                    brow_ap, brow_bufs = P4a.slice(0, 512)
                    gr.op("tensor", lambda e, ap=brow_ap: e.matmul(ap, OR_BF.t[:, :], btT_sb.t[:, :],
                                                                   start=True, stop=True),
                          reads=[btT_sb.buf, OR_BF.buf], writes=brow_bufs)
                    gr.op("vector", lambda e, ap=g_ap: e.tensor_tensor(Ltb.t[:, :], ap, DlowT.t[:, :], ALU.mult),
                          reads=g_bufs + [DlowT.buf], writes=[Ltb.buf])
                    gr.op("vector", lambda e, ap=brow_ap: e.tensor_tensor(Ltb.t[:, :], Ltb.t[:, :], ap, ALU.mult),
                          reads=[Ltb.buf] + brow_bufs, writes=[Ltb.buf])
                    kq_ap, kq_bufs = P1.slice(0, 512)
                    for hh in range(H):
                        blk = P1.t[0:128, 128 * hh:128 * hh + 128]
                        for kc in range(2):
                            kcol = slice((2 * hh + kc) * LS + 128 * cc, (2 * hh + kc) * LS + 128 * cc + 128)
                            gr.op("tensor", lambda e, blk=blk, kcol=kcol, hh=hh, kc=kc:
                                  e.matmul(blk, kn.t[:, kcol], qn.t[:, kcol],
                                           start=(hh == 0 and kc == 0), stop=False),
                                  reads=[kn.buf, qn.buf], writes=kq_bufs)
                    gr.op("vector", lambda e, ap=kq_ap: e.tensor_tensor(AttT.t[:, :], ap, DincT.t[:, :], ALU.mult),
                          reads=kq_bufs + [DincT.buf], writes=[AttT.buf])
                    Sold, Snew = Sb[cc % 2], Sb[(cc + 1) % 2]
                    for hh in range(H):
                        pt_ap4, pt_bufs4 = PT.slice(0, 512)
                        for vc in range(4):
                            vcol = slice((4 * hh + vc) * LS + 128 * cc, (4 * hh + vc) * LS + 128 * cc + 128)
                            tp = PT.t[0:128, 128 * vc:128 * vc + 128]
                            gr.op("tensor", lambda e, tp=tp, vcol=vcol: e.transpose(tp, vn.t[:, vcol],
                                                                                    ID_BF.t[:, :]),
                                  reads=[vn.buf, ID_BF.buf], writes=pt_bufs4)
                        gr.op("vector", lambda e, hh=hh, ap=pt_ap4: e.tensor_copy(
                            Vtok.t[:, 512 * hh:512 * hh + 512], ap),
                              reads=pt_bufs4, writes=[Vtok.buf])
                    for hh in range(H):
                        ks_ap, ks_bufs = P4a.slice(0, 512)
                        for kc in range(2):
                            kcol = slice((2 * hh + kc) * LS + 128 * cc, (2 * hh + kc) * LS + 128 * cc + 128)
                            gr.op("tensor", lambda e, kcol=kcol, kc=kc, hh=hh, ap=ks_ap, Sold=Sold:
                                  e.matmul(ap, kn.t[:, kcol], Sold.t[:, (2 * hh + kc) * 512:(2 * hh + kc + 1) * 512],
                                           start=(kc == 0), stop=(kc == 1)),
                                  reads=[kn.buf, Sold.buf], writes=ks_bufs)
                        gr.op("vector", lambda e, hh=hh, ap=ks_ap: e.scalar_tensor_tensor(
                            W1t.t[:, :], ap, Acol.t[:, hh:hh + 1], Vtok.t[:, 512 * hh:512 * hh + 512],
                            ALU.mult, ALU.subtract),
                              reads=ks_bufs + [Acol.buf, Vtok.buf], writes=[W1t.buf])
                        gr.op("vector", lambda e, hh=hh, cc=cc: e.tensor_scalar(
                            Wt.t[:, 512 * hh:512 * hh + 512], W1t.t[:, :],
                            nbeta_f.t[:, 4 * cc + hh:4 * cc + hh + 1], None, ALU.mult),
                              reads=[W1t.buf, nbeta_f.buf], writes=[Wt.buf])
                    gr.op("vector", lambda e: e.tensor_tensor(
                        Rw[0].t[:, :].rearrange("p (a b) -> p a b", a=4),
                        bass.AP(ID_BF.t, 0, [[128, 128], [0, 4], [1, 128]]),
                        Ltb.t[:, :].rearrange("p (a b) -> p a b", a=4), ALU.subtract),
                          reads=[Ltb.buf, ID_BF.buf], writes=[Rw[0].buf])
                    Acur, Atcur, Rcur = Lb, Ltb, Rw[0]
                    Anew, Atnew, Rnew = Apw[0], Atw[0], Rw[1]
                    Aspare, Atspare = Apw[1], Atw[1]
                    for k in range(1, 7):
                        a_ap, a_bufs = P1.slice(0, 512)
                        for hh in range(H):
                            blk = P1.t[0:128, 128 * hh:128 * hh + 128]
                            hs = slice(128 * hh, 128 * hh + 128)
                            gr.op("tensor", lambda e, blk=blk, hs=hs, hh=hh, Atcur=Atcur, Acur=Acur:
                                  e.matmul(blk, Atcur.t[:, hs], Acur.t[:, hs], start=(hh == 0), stop=(hh == 3)),
                                  reads=[Atcur.buf, Acur.buf], writes=a_bufs)
                        gr.op("vector", lambda e, ap=a_ap, Anew=Anew: e.tensor_copy(Anew.t[:, :], ap),
                              reads=a_bufs, writes=[Anew.buf])
                        if k < 6:
                            at_ap, at_bufs = P2.slice(0, 512)
                            for hh in range(H):
                                blk = P2.t[0:128, 128 * hh:128 * hh + 128]
                                hs = slice(128 * hh, 128 * hh + 128)
                                gr.op("tensor", lambda e, blk=blk, hs=hs, hh=hh, Atcur=Atcur, Acur=Acur:
                                      e.matmul(blk, Acur.t[:, hs], Atcur.t[:, hs], start=(hh == 0), stop=(hh == 3)),
                                      reads=[Atcur.buf, Acur.buf], writes=at_bufs)
                            gr.op("vector", lambda e, ap=at_ap, Atnew=Atnew: e.tensor_copy(Atnew.t[:, :], ap),
                                  reads=at_bufs, writes=[Atnew.buf])
                        r_ap, r_bufs = P1.slice(0, 512)
                        for hh in range(H):
                            blk = P1.t[0:128, 128 * hh:128 * hh + 128]
                            hs = slice(128 * hh, 128 * hh + 128)
                            gr.op("tensor", lambda e, blk=blk, hs=hs, hh=hh, Anew=Anew, Rcur=Rcur:
                                  e.matmul(blk, Anew.t[:, hs], Rcur.t[:, hs], start=(hh == 0), stop=(hh == 3)),
                                  reads=[Anew.buf, Rcur.buf], writes=r_bufs)
                        gr.op("vector", lambda e, ap=r_ap, Rcur=Rcur, Rnew=Rnew:
                              e.tensor_tensor(Rnew.t[:, :], Rcur.t[:, :], ap, ALU.add),
                              reads=[Rcur.buf] + r_bufs, writes=[Rnew.buf])
                        if k == 1:
                            Acur, Anew = Anew, Aspare
                            Atcur, Atnew = Atnew, Atspare
                        else:
                            Acur, Anew = Anew, Acur
                            Atcur, Atnew = Atnew, Atcur
                        Rcur, Rnew = Rnew, Rcur
                    Rfin = Rcur
                    for hh in range(H):
                        u_ap, u_bufs = (P4a if hh % 2 == 0 else P4b).slice(0, 512)
                        hs = slice(128 * hh, 128 * hh + 128)
                        gr.op("tensor", lambda e, hs=hs, hh=hh, ap=u_ap, Rfin=Rfin:
                              e.matmul(ap, Rfin.t[:, hs], Wt.t[:, 512 * hh:512 * hh + 512], start=True, stop=True),
                              reads=[Rfin.buf, Wt.buf], writes=u_bufs)
                        gr.op("vector", lambda e, hh=hh, ap=u_ap: e.tensor_copy(Ut.t[:, 512 * hh:512 * hh + 512],
                                                                                ap),
                              reads=u_bufs, writes=[Ut.buf])
                    for hh in range(H):
                        qs_ap, qs_bufs = P4a.slice(0, 512)
                        au_ap, au_bufs = P4b.slice(0, 512)
                        for kc in range(2):
                            kcol = slice((2 * hh + kc) * LS + 128 * cc, (2 * hh + kc) * LS + 128 * cc + 128)
                            gr.op("tensor", lambda e, kcol=kcol, kc=kc, hh=hh, ap=qs_ap, Sold=Sold:
                                  e.matmul(ap, qn.t[:, kcol], Sold.t[:, (2 * hh + kc) * 512:(2 * hh + kc + 1) * 512],
                                           start=(kc == 0), stop=(kc == 1)),
                                  reads=[qn.buf, Sold.buf], writes=qs_bufs)
                        hs = slice(128 * hh, 128 * hh + 128)
                        gr.op("tensor", lambda e, hs=hs, hh=hh, ap=au_ap:
                              e.matmul(ap, AttT.t[:, hs], Ut.t[:, 512 * hh:512 * hh + 512], start=True, stop=True),
                              reads=[AttT.buf, Ut.buf], writes=au_bufs)
                        gr.op("vector", lambda e, ap=au_ap: e.tensor_copy(attu.t[:, :], ap),
                              reads=au_bufs, writes=[attu.buf])
                        gr.op("vector", lambda e, hh=hh, ap=qs_ap: e.scalar_tensor_tensor(
                            O_t.t[:, 512 * hh:512 * hh + 512], ap, Acol.t[:, hh:hh + 1], attu.t[:, :],
                            ALU.mult, ALU.add),
                              reads=qs_bufs + [Acol.buf, attu.buf], writes=[O_t.buf])
                    for hh in range(H):
                        pt_ap2, pt_bufs2 = PT.slice(0, 256)
                        for kc in range(2):
                            kcol = slice((2 * hh + kc) * LS + 128 * cc, (2 * hh + kc) * LS + 128 * cc + 128)
                            tp = PT.t[0:128, 128 * kc:128 * kc + 128]
                            gr.op("tensor", lambda e, tp=tp, kcol=kcol: e.transpose(tp, kn.t[:, kcol],
                                                                                    ID_BF.t[:, :]),
                                  reads=[kn.buf, ID_BF.buf], writes=pt_bufs2)
                        gr.op("vector", lambda e, hh=hh, ap=PT.t[0:128, 0:256]: e.tensor_copy(
                            Ktok.t[:, 256 * hh:256 * hh + 256], ap),
                              reads=pt_bufs2, writes=[Ktok.buf])
                        gr.op("vector", lambda e, hh=hh: e.tensor_scalar(Kbar.t[:, 256 * hh:256 * hh + 256],
                                                                         Ktok.t[:, 256 * hh:256 * hh + 256],
                                                                         ebar.t[:, hh:hh + 1], None, ALU.mult),
                              reads=[Ktok.buf, ebar.buf], writes=[Kbar.buf])
                        for kc in range(2):
                            st_ap, st_bufs = (P4a, P4b)[kc].slice(0, 512)
                            gr.op("tensor", lambda e, hh=hh, kc=kc, ap=st_ap:
                                  e.matmul(ap, Kbar.t[:, 256 * hh + 128 * kc:256 * hh + 128 * kc + 128],
                                           Ut.t[:, 512 * hh:512 * hh + 512], start=True, stop=True),
                                  reads=[Kbar.buf, Ut.buf], writes=st_bufs)
                            scol = slice((2 * hh + kc) * 512, (2 * hh + kc + 1) * 512)
                            gr.op("vector", lambda e, hh=hh, scol=scol, ap=st_ap, Sold=Sold, Snew=Snew:
                                  e.scalar_tensor_tensor(Snew.t[:, scol], Sold.t[:, scol],
                                                         aC_col.t[:, hh:hh + 1], ap, ALU.mult, ALU.add),
                                  reads=[Sold.buf, aC_col.buf] + st_bufs, writes=[Snew.buf])
                    gr.op("vector", lambda e: e.tensor_mul(Wt.t[:, :], O_t.t[:, :], O_t.t[:, :]),
                          reads=[O_t.buf], writes=[Wt.buf])
                    gr.op("vector", lambda e: e.reduce_sum(ostd.t[:, :],
                                                           Wt.t[:, :].rearrange("p (a b) -> p a b", a=4),
                                                           axis=AX.X),
                          reads=[Wt.buf], writes=[ostd.buf])
                    gr.op("scalar", lambda e: e.activation(ostd.t[:, :], ostd.t[:, :], AF.Ln, scale=1.0 / Vd,
                                                           bias=eps6c),
                          reads=[ostd.buf, eps6c_b], writes=[ostd.buf])
                    gr.op("scalar", lambda e: e.activation(ostd.t[:, :], ostd.t[:, :], AF.Exp, scale=-0.5),
                          reads=[ostd.buf], writes=[ostd.buf])
                    for hh in range(H):
                        gr.op("vector", lambda e, hh=hh: e.tensor_scalar(
                            ogc.t[:, 512 * hh:512 * hh + 512], O_t.t[:, 512 * hh:512 * hh + 512],
                            ostd.t[:, hh:hh + 1], None, ALU.mult),
                              reads=[O_t.buf, ostd.buf], writes=[ogc.buf])
                    nwv = bass.AP(normw_bc.t, 0, [[512, 128], [0, 4], [1, 512]])
                    gr.op("vector", lambda e, nwv=nwv: e.tensor_tensor(
                        ogc.t[:, :].rearrange("p (a b) -> p a b", a=4),
                        ogc.t[:, :].rearrange("p (a b) -> p a b", a=4), nwv, ALU.mult),
                          reads=[ogc.buf, normw_bc.buf], writes=[ogc.buf])
                    gr.op("vector", lambda e, cc=cc, cc0=cc0: e.tensor_tensor(ogc.t[:, :], ogc.t[:, :],
                                                                     gate_sil.t[:, (cc - cc0) * 2048:(cc - cc0 + 1) * 2048],
                                                                     ALU.mult),
                          reads=[ogc.buf, gate_sil.buf], writes=[ogc.buf])
                    for b8 in range(2):
                        pt8_ap, pt8_bufs = PT.slice(0, 1024)
                        for j in range(8):
                            jhv = b8 * 8 + j
                            tp = PT.t[0:128, 128 * j:128 * j + 128]
                            gr.op("tensor", lambda e, tp=tp, jhv=jhv: e.transpose(
                                tp, ogc.t[:, 128 * jhv:128 * jhv + 128], ID_BF.t[:, :]),
                                  reads=[ogc.buf, ID_BF.buf], writes=pt8_bufs)
                        for j in range(8):
                            jhv = b8 * 8 + j
                            gr.op("vector", lambda e, j=j, jhv=jhv, cc=cc: e.tensor_copy(
                                o_fm.t[:, jhv * LS + 128 * cc: jhv * LS + 128 * cc + 128],
                                PT.t[0:128, 128 * j:128 * j + 128]),
                                  reads=pt8_bufs, writes=[o_fm.buf])

                    if debug and sl == 0 and cc == 1:
                        for nm, tl in [("Dlow", Dlow), ("DlowT", DlowT), ("DincT", DincT),
                                       ("Lb", Lb), ("Ltb", Ltb), ("AttT", AttT), ("Wt", Wt),
                                       ("Vtok", Vtok), ("ogc", ogc)]:
                            gr.dma("gpsimd", dbg[nm][:, :], tl.t[:, :], reads=[tl.buf])
                        gr.dma("gpsimd", dbg["Rfin"][:, :], Rfin.t[:, :], reads=[Rfin.buf])
                        gr.dma("gpsimd", dbg["Snew"][:, :], Snew.t[:, :], reads=[Snew.buf])
                        gr.dma("gpsimd", dbg["bT"][:, :], bT_sb.t[:, :], reads=[bT_sb.buf])
                        gr.dma("gpsimd", dbg["gsil"][:, :], gate_sil.t[:, (cc - cc0) * 2048:(cc - cc0 + 1) * 2048],
                               reads=[gate_sil.buf])


            for grp in range(2):
                psl = [P3a.slice(0, LS), P3b.slice(0, LS), P4a.slice(0, LS), P4b.slice(0, LS)]
                for p in range(8):
                    wp = load_piece(wo_d, p)
                    for i in range(2):
                        jhv = 2 * p + i
                        for dci in range(4):
                            dc = grp * 4 + dci
                            ap, bufs = psl[dci]
                            gr.op("tensor", lambda e, wp=wp, i=i, dc=dc, jhv=jhv, ap=ap:
                                  e.matmul(ap, wp.t[:, i * D + 128 * dc:i * D + 128 * dc + 128],
                                           o_fm.t[:, jhv * LS:(jhv + 1) * LS],
                                           start=(jhv == 0), stop=(jhv == 15)),
                                  reads=[wp.buf, o_fm.buf], writes=bufs)
                for dci in range(4):
                    dc = grp * 4 + dci
                    ap, bufs = psl[dci]
                    csl = slice(dc * LS, (dc + 1) * LS)
                    gr.op("vector", lambda e, csl=csl, ap=ap: e.tensor_tensor(xn1.t[:, csl], xn1.t[:, csl],
                                                                              ap, ALU.add),
                          reads=[xn1.buf] + bufs, writes=[xn1.buf])
            s2_ap, s2_bufs = P3a.slice(0, LS)
            q2_ap, q2_bufs = P3b.slice(0, LS)
            for c in range(DC):
                gr.op("tensor", lambda e, c=c, ap=s2_ap: e.matmul(ap, AO_BF.t[:, :],
                                                                  xn1.t[:, c * LS:(c + 1) * LS],
                                                                  start=(c == 0), stop=(c == DC - 1)),
                      reads=[xn1.buf, AO_BF.buf], writes=s2_bufs)
            for c in range(DC):
                gr.op("vector", lambda e, c=c: e.tensor_mul(sq.t[:, :], xn1.t[:, c * LS:(c + 1) * LS],
                                                            xn1.t[:, c * LS:(c + 1) * LS]),
                      reads=[xn1.buf], writes=[sq.buf])
                gr.op("tensor", lambda e, c=c, ap=q2_ap: e.matmul(ap, AO_BF.t[:, :], sq.t[:, :],
                                                                  start=(c == 0), stop=(c == DC - 1)),
                      reads=[sq.buf, AO_BF.buf], writes=q2_bufs)
            gr.op("vector", lambda e, ap=s2_ap: e.tensor_scalar(m0b.t[:, :], ap, 1.0 / D, None, ALU.mult),
                  reads=s2_bufs, writes=[m0b.buf])
            gr.op("vector", lambda e: e.tensor_mul(t1.t[:, :], m0b.t[:, :], m0b.t[:, :]),
                  reads=[m0b.buf], writes=[t1.buf])
            gr.op("vector", lambda e, ap=q2_ap: e.scalar_tensor_tensor(t2.t[:, :], ap, 1.0 / D,
                                                                       t1.t[:, :], ALU.mult, ALU.subtract),
                  reads=q2_bufs + [t1.buf], writes=[t2.buf])
            gr.op("scalar", lambda e: e.activation(t2.t[:, :], t2.t[:, :], AF.Ln, bias=eps5c),
                  reads=[t2.buf, eps5c_b], writes=[t2.buf])
            gr.op("scalar", lambda e: e.activation(r0b.t[:, :], t2.t[:, :], AF.Exp, scale=-0.5),
                  reads=[t2.buf], writes=[r0b.buf])
            for c in range(DC):
                csl = slice(c * LS, (c + 1) * LS)
                gr.op("vector", lambda e, csl=csl: e.tensor_tensor(t1.t[:, :], xn1.t[:, csl],
                                                                   m0b.t[:, :], ALU.subtract),
                      reads=[xn1.buf, m0b.buf], writes=[t1.buf])
                gr.op("vector", lambda e: e.tensor_tensor(t2.t[:, :], t1.t[:, :], r0b.t[:, :], ALU.mult),
                      reads=[t1.buf, r0b.buf], writes=[t2.buf])
                gr.op("vector", lambda e, c=c, csl=csl: e.tensor_scalar(h2t.t[:, csl], t2.t[:, :],
                                                                        vcols["ln2g"].t[:, c:c + 1],
                                                                        vcols["ln2b"].t[:, c:c + 1],
                                                                        ALU.mult, ALU.add),
                      reads=[t2.buf, vcols["ln2g"].buf, vcols["ln2b"].buf], writes=[h2t.bufs[c]])
            for p in range(4):
                wp = load_piece(w2_d, p)
                for jj in range(2):
                    j = 2 * p + jj
                    PS = psum_pick(jj)
                    ap, bufs = PS.slice(0, LS)
                    for c in range(DC):
                        gr.op("tensor", lambda e, c=c, jj=jj, wp=wp, ap=ap:
                              e.matmul(ap, wp.t[:, c * 256 + 128 * jj:c * 256 + 128 * jj + 128],
                                       h2t.t[:, c * LS:(c + 1) * LS], start=(c == 0), stop=(c == DC - 1)),
                              reads=[wp.buf, h2t.bufs[c]], writes=bufs)
                    gr.op("scalar", lambda e, j=j, ap=ap: e.activation(gelc.t[:, :], ap, AF.Gelu,
                                                                       bias=vcols["b2a"].t[:, j:j + 1]),
                          reads=bufs + [vcols["b2a"].buf], writes=[gelc.buf])
                    xbuf = x1f.bufs[j % 2]
                    xap = x1f.t[:, (j % 2) * LS:(j % 2 + 1) * LS]
                    gr.dma("sync", xap, x1_d[j * 128:(j + 1) * 128, sl * LS:(sl + 1) * LS], writes=[xbuf])
                    ybuf = yst.bufs[j % 2]
                    yap = yst.t[:, (j % 2) * LS:(j % 2 + 1) * LS]
                    jsl = slice(j * LS, (j + 1) * LS)
                    gr.op("vector", lambda e, jsl=jsl, yap=yap: e.tensor_tensor(yap, xn1.t[:, jsl],
                                                                                gelc.t[:, :], ALU.add),
                          reads=[xn1.buf, gelc.buf], writes=[ybuf])
                    gr.op("vector", lambda e, yap=yap, xap=xap: e.tensor_tensor(yap, yap, xap, ALU.add),
                          reads=[xbuf, ybuf], writes=[ybuf])
                    gr.dma("sync", y_d[j * 128:(j + 1) * 128, sl * LS:(sl + 1) * LS], yap, reads=[ybuf])

            if debug and sl == 0:
                gr.dma("gpsimd", dbg["h"][:, :], h.t[:, :], reads=[h.buf])
                gr.dma("gpsimd", dbg["qn"][:, :], qn.t[:, :], reads=[qn.buf])
                gr.dma("gpsimd", dbg["kn"][:, :], kn.t[:, :], reads=[kn.buf])
                gr.dma("gpsimd", dbg["vn"][:, :], vn.t[:, :], reads=[vn.buf])
                gr.dma("gpsimd", dbg["gb"][:, :], g_bf.t[:, :], reads=[g_bf.buf])
                gr.dma("gpsimd", dbg["bb"][:, :], beta_bf.t[:, :], reads=[beta_bf.buf])
                gr.dma("gpsimd", dbg["btok"][:, :], b_tok.t[:, :], reads=[b_tok.buf])
                gr.dma("gpsimd", dbg["Ot"][:, :], O_t.t[:, :], reads=[O_t.buf])
                gr.dma("gpsimd", dbg["U"][:, :], Ut.t[:, :], reads=[Ut.buf])
                gr.dma("gpsimd", dbg["xa"][:, :], xn1.t[:, :], reads=[xn1.buf])
                gr.dma("gpsimd", dbg["ofm"][:, :], o_fm.t[:, 0:16 * LS], reads=[o_fm.buf])




        emit_slice(0)
        for sl in range(LNS):
            if DBG_SLICE_BARRIER:
                gr.barrier()
            if sl + 1 < LNS:
                emit_slice(sl + 1)
            emit_back(sl)


def build_l1_attn(debug=False, LS=256):
    nc = bass.Bass("TRN2", target_bir_lowering=False)
    gr = G(nc)
    consts = host_consts()
    x1_d = nc.dram_tensor("x1T", [D, T], F32, kind="ExternalInput")
    wq_d = nc.dram_tensor("Wq", [4 * 128, 2048], F32, kind="ExternalInput")
    wk_d = nc.dram_tensor("Wk", [4 * 128, 2048], F32, kind="ExternalInput")
    wv_d = nc.dram_tensor("Wv", [8 * 128, 2048], F32, kind="ExternalInput")
    wg_d = nc.dram_tensor("Wg", [8 * 128, 2048], F32, kind="ExternalInput")
    wo_d = nc.dram_tensor("Wo", [8 * 128, 2048], F32, kind="ExternalInput")
    w2_d = nc.dram_tensor("W2", [4 * 128, 2048], F32, kind="ExternalInput")
    wa_d = nc.dram_tensor("Wa", [D, 4], F32, kind="ExternalInput")
    wb_d = nc.dram_tensor("Wb", [D, 4], F32, kind="ExternalInput")
    cq_d = nc.dram_tensor("conv_q", [D, 4], F32, kind="ExternalInput")
    ck_d = nc.dram_tensor("conv_k", [D, 4], F32, kind="ExternalInput")
    cv_d = nc.dram_tensor("conv_v", [VAL_DIM, 4], F32, kind="ExternalInput")
    ln1g_d = nc.dram_tensor("ln1_g", [D, 1], F32, kind="ExternalInput")
    ln1b_d = nc.dram_tensor("ln1_b", [D, 1], F32, kind="ExternalInput")
    ln2g_d = nc.dram_tensor("ln2_g", [D, 1], F32, kind="ExternalInput")
    ln2b_d = nc.dram_tensor("ln2_b", [D, 1], F32, kind="ExternalInput")
    rmsw_d = nc.dram_tensor("rms_in_w", [D, 1], F32, kind="ExternalInput")
    b2_d = nc.dram_tensor("b2", [D, 1], F32, kind="ExternalInput")
    onw_d = nc.dram_tensor("o_norm_w", [1, Vd], F32, kind="ExternalInput")
    dtb_d = nc.dram_tensor("dt_bias", [1, 4], F32, kind="ExternalInput")
    alog_d = nc.dram_tensor("A_log", [1, 4], F32, kind="ExternalInput")
    y_d = nc.dram_tensor("yT", [D, T], F32, kind="ExternalOutput")
    dr = {"x1T": x1_d, "Wq": wq_d, "Wk": wk_d, "Wv": wv_d, "Wg": wg_d, "Wo": wo_d, "W2": w2_d, "Wa": wa_d, "Wb": wb_d, "conv_q": cq_d, "conv_k": ck_d, "conv_v": cv_d, "ln1_g": ln1g_d, "ln1_b": ln1b_d, "ln2_g": ln2g_d, "ln2_b": ln2b_d, "rms_in_w": rmsw_d, "b2": b2_d, "o_norm_w": onw_d, "dt_bias": dtb_d, "A_log": alog_d, "yT": y_d}
    with contextlib.ExitStack() as ctx0:
        cells = Cells(nc, ctx0, gr)
        emit_l1_attn(nc, gr, consts, cells, dr, debug=debug, LS=LS)
        gr.finalize_and_emit(ctx0)
    return nc


def emit_l1_moe(nc, gr, consts, cells, dr, debug=False):
    x1_d = dr["x1T"]
    y_d = dr["yT"]
    gw_d = dr["gate_W"]
    w1_d = dr["W1e"]
    b1_d = dr["b1e"]
    w2_d = dr["W2e"]
    b2_d = dr["b2e"]
    ws1_d = dr["Ws1"]
    bs1_d = dr["bs1"]
    ws2_d = dr["Ws2"]
    bs2_d = dr["bs2"]
    rs_d = dr["route_scale"]
    out_d = dr["outT"]
    E = 8
    if debug:
        dbg = {}
        for nm, shp in [("gf", [128, E]), ("sc", [128, E]), ("acc", [128, DC * SL]),
                        ("xn1", [128, DC * SL])]:
            dbg[nm] = nc.dram_tensor("dbg_" + nm, shp, F32, kind="ExternalOutput")

    ctx = contextlib.ExitStack()
    with ctx:
        C = const_tensors(nc, gr, ctx, consts, {"ident": ["f32"], "ones_col": ["bf16"], "ones_row": ["f32", "bf16"]})
        OC_BF = C[("ones_col", "bf16")]
        OR_F = C[("ones_row", "f32")]
        OR_BF = C[("ones_row", "bf16")]
        ID_F = C[("ident", "f32")]
        eps5, eps5_b = cells.cell(LN_EPS)

        x1bf = Tile(nc, ctx, "x1bf", [128, DC * SL], BF16, nbufs=DC)
        xn1 = Tile(nc, ctx, "xn1", [128, DC * SL], BF16)
        acc = Tile(nc, ctx, "acc", [128, DC * SL], F32)
        Hg = Tile(nc, ctx, "Hg", [128, SL], F32)
        Hw = Tile(nc, ctx, "Hw", [128, 4 * SL], BF16)
        WPN = 4
        wps = [Tile(nc, ctx, f"wp{i}", [128, 2048], BF16) for i in range(WPN)]
        gw_sb = Tile(nc, ctx, "gw_sb", [128, DC * E], BF16)
        b2_sb = Tile(nc, ctx, "b2_sb", [8, D], BF16)
        b1col = Tile(nc, ctx, "b1col", [128, E * 4], F32)
        bs1col = Tile(nc, ctx, "bs1col", [128, 4], F32)
        bs2col = Tile(nc, ctx, "bs2col", [128, DC], F32)
        rs_sb = Tile(nc, ctx, "rs_sb", [1, 1], F32)
        rs_col = Tile(nc, ctx, "rs_col", [128, 1], F32)
        sc = Tile(nc, ctx, "sc", [128, E], F32)
        gf = Tile(nc, ctx, "gf", [128, E], F32)
        mm1 = Tile(nc, ctx, "mm1", [128, 4], F32)
        eqm = Tile(nc, ctx, "eqm", [128, E], F32)
        gfT8 = Tile(nc, ctx, "gfT8", [8, 4 * 128], BF16)
        ge_row = Tile(nc, ctx, "ge_row", [1, SL], BF16)
        yf = Tile(nc, ctx, "yf", [128, SL], F32)
        ost = Tile(nc, ctx, "ost", [128, SL], F32)
        sq = Tile(nc, ctx, "sqs", [128, SL], BF16)
        t1 = Tile(nc, ctx, "t1", [128, SL], F32)
        rows_all = Tile(nc, ctx, "rows_all", [1, 2048], F32)
        m_row = RowTile(rows_all, 0, "m_row", 0, SL)
        rstd0_row = RowTile(rows_all, 0, "rstd0_row", 512, SL)
        rows = RowTile(rows_all, 0, "rows", 1024, 2 * SL)

        PH = Psum(nc, ctx, "PH", [128, 1024], F32)
        PW = Psum(nc, ctx, "PW", [128, 1024], F32)
        PG = Psum(nc, ctx, "PG", [128, 512], F32)
        PS5 = Psum(nc, ctx, "PS5", [128, 512], F32)

        for c in range(DC):
            gr.dma("gpsimd", gw_sb.t[:, c * E:(c + 1) * E], gw_d[c * 128:(c + 1) * 128, :],
                   writes=[gw_sb.buf])
        gr.dma("gpsimd", b2_sb.t[:, :], b2_d[:, :], writes=[b2_sb.buf])
        for e in range(E):
            for hc in range(4):
                ap_src = bass.AP(b1_d, e * 512 + hc * 128, [[1, 128], [1, 1]])
                gr.dma("sync", b1col.t[:, e * 4 + hc:e * 4 + hc + 1], ap_src, writes=[b1col.buf])
        for hc in range(4):
            gr.dma("sync", bs1col.t[:, hc:hc + 1], bs1_d[hc * 128:(hc + 1) * 128, :], writes=[bs1col.buf])
        for c in range(DC):
            gr.dma("sync", bs2col.t[:, c:c + 1], bs2_d[c * 128:(c + 1) * 128, :], writes=[bs2col.buf])
        gr.dma("sync", rs_sb.t[:, :], rs_d[:, :], writes=[rs_sb.buf])
        rsb_ap, rsb_bufs = PS5.slice(0, 1)
        gr.op("tensor", lambda e, ap=rsb_ap: e.matmul(ap, OR_F.t[:, :], rs_sb.t[0:1, 0:1],
                                                      start=True, stop=True),
              reads=[rs_sb.buf, OR_F.buf], writes=rsb_bufs)
        gr.op("vector", lambda e, ap=rsb_ap: e.tensor_copy(rs_col.t[:, :], ap),
              reads=rsb_bufs, writes=[rs_col.buf])

        piece_idx = [0]

        def wpiece():
            wp = wps[piece_idx[0] % WPN]
            piece_idx[0] += 1
            return wp

        def load_piece(wdram, pidx):
            wp = wpiece()
            gr.dma("gpsimd", wp.t[:, :], wdram[pidx * 128:(pidx + 1) * 128, :], writes=[wp.buf])
            return wp

        for sl in range(NSL):
            for c in range(DC):
                gr.dma("gpsimd", x1bf.t[:, c * SL:(c + 1) * SL],
                       x1_d[c * 128:(c + 1) * 128, sl * SL:(sl + 1) * SL], writes=[x1bf.bufs[c]])
            sum_ap, sum_bufs = PH.slice(0, SL, 0, 1)
            sq_ap, sq_bufs = PH.slice(512, 1024, 0, 1)
            for c in range(DC):
                gr.op("tensor", lambda e, c=c, ap=sum_ap: e.matmul(ap, OC_BF.t[:, 0:1],
                                                                   x1bf.t[:, c * SL:(c + 1) * SL],
                                                                   start=(c == 0), stop=(c == DC - 1)),
                      reads=[x1bf.bufs[c], OC_BF.buf], writes=sum_bufs)
            for c in range(DC):
                gr.op("vector", lambda e, c=c: e.tensor_mul(sq.t[:, :], x1bf.t[:, c * SL:(c + 1) * SL],
                                                            x1bf.t[:, c * SL:(c + 1) * SL]),
                      reads=[x1bf.bufs[c]], writes=[sq.buf])
                gr.op("tensor", lambda e, c=c, ap=sq_ap: e.matmul(ap, OC_BF.t[:, 0:1], sq.t[:, :],
                                                                  start=(c == 0), stop=(c == DC - 1)),
                      reads=[sq.buf, OC_BF.buf], writes=sq_bufs)
            gr.op("vector", lambda e, ap=sum_ap: e.tensor_scalar(m_row.t[0:1, :], ap, 1.0 / D, None, ALU.mult),
                  reads=sum_bufs, writes=[m_row.buf])
            gr.op("vector", lambda e: e.tensor_mul(rows.t[0:1, 0:SL], m_row.t[0:1, :], m_row.t[0:1, :]),
                  reads=[m_row.buf], writes=[rows.buf])
            gr.op("vector", lambda e, ap=sq_ap: e.scalar_tensor_tensor(rows.t[0:1, SL:2 * SL], ap, 1.0 / D,
                                                                       rows.t[0:1, 0:SL], ALU.mult,
                                                                       ALU.subtract),
                  reads=sq_bufs + [rows.buf], writes=[rows.buf])
            gr.op("scalar", lambda e: e.activation(rows.t[0:1, 0:SL], rows.t[0:1, SL:2 * SL], AF.Ln,
                                                   bias=eps5),
                  reads=[rows.buf, eps5_b], writes=[rows.buf])
            gr.op("scalar", lambda e: e.activation(rstd0_row.t[0:1, :], rows.t[0:1, 0:SL], AF.Exp,
                                                   scale=-0.5),
                  reads=[rows.buf], writes=[rstd0_row.buf])
            mb_ap, mb_bufs = PW.slice(0, SL)
            rb_ap, rb_bufs = PW.slice(512, 1024)
            gr.op("tensor", lambda e, ap=mb_ap: e.matmul(ap, OR_F.t[:, :], m_row.t[0:1, :],
                                                         start=True, stop=True),
                  reads=[m_row.buf, OR_F.buf], writes=mb_bufs)
            gr.op("tensor", lambda e, ap=rb_ap: e.matmul(ap, OR_F.t[:, :], rstd0_row.t[0:1, :],
                                                         start=True, stop=True),
                  reads=[rstd0_row.buf, OR_F.buf], writes=rb_bufs)
            for c in range(DC):
                csl = slice(c * SL, (c + 1) * SL)
                gr.op("vector", lambda e, c=c, csl=csl, ap=mb_ap: e.tensor_tensor(
                    t1.t[:, :], x1bf.t[:, csl], ap, ALU.subtract),
                      reads=[x1bf.bufs[c]] + mb_bufs, writes=[t1.buf])
                gr.op("vector", lambda e, csl=csl, ap=rb_ap: e.tensor_tensor(xn1.t[:, csl], t1.t[:, :],
                                                                             ap, ALU.mult),
                      reads=[t1.buf] + rb_bufs, writes=[xn1.buf])
            gr.op("vector", lambda e: e.memset(acc.t[:, :], 0.0), writes=[acc.buf])

            for tt_ in range(4):
                sc_ap, sc_bufs = PS5.slice(0, E)
                for c in range(DC):
                    gr.op("tensor", lambda e, c=c, tt_=tt_, ap=sc_ap:
                          e.matmul(ap, xn1.t[:, c * SL + 128 * tt_:c * SL + 128 * tt_ + 128],
                                   gw_sb.t[:, c * E:(c + 1) * E], start=(c == 0), stop=(c == DC - 1)),
                          reads=[xn1.buf, gw_sb.buf], writes=sc_bufs)
                gr.op("scalar", lambda e, ap=sc_ap: e.activation(sc.t[:, :], ap, AF.Sigmoid),
                      reads=sc_bufs, writes=[sc.buf])
                gr.op("vector", lambda e: e.tensor_scalar(sc.t[:, :], sc.t[:, :], rs_col.t[:, 0:1],
                                                          None, ALU.mult),
                      reads=[sc.buf, rs_col.buf], writes=[sc.buf])
                gr.op("vector", lambda e: e.reduce_max(mm1.t[:, 0:1], sc.t[:, :], axis=AX.X),
                      reads=[sc.buf], writes=[mm1.buf])
                gr.op("vector", lambda e: e.tensor_scalar(eqm.t[:, :], sc.t[:, :], mm1.t[:, 0:1],
                                                          None, ALU.is_equal),
                      reads=[sc.buf, mm1.buf], writes=[eqm.buf])
                gr.op("vector", lambda e: e.scalar_tensor_tensor(eqm.t[:, :], eqm.t[:, :], -1e30,
                                                                 sc.t[:, :], ALU.mult, ALU.add),
                      reads=[eqm.buf, sc.buf], writes=[eqm.buf])
                gr.op("vector", lambda e: e.reduce_max(mm1.t[:, 1:2], eqm.t[:, :], axis=AX.X),
                      reads=[eqm.buf], writes=[mm1.buf])
                gr.op("vector", lambda e: e.tensor_add(mm1.t[:, 2:3], mm1.t[:, 0:1], mm1.t[:, 1:2]),
                      reads=[mm1.buf], writes=[mm1.buf])
                gr.op("vector", lambda e: e.reciprocal(mm1.t[:, 3:4], mm1.t[:, 2:3]),
                      reads=[mm1.buf], writes=[mm1.buf])
                gr.op("vector", lambda e: e.tensor_scalar(gf.t[:, :], sc.t[:, :], mm1.t[:, 1:2],
                                                          None, ALU.is_ge),
                      reads=[sc.buf, mm1.buf], writes=[gf.buf])
                gr.op("vector", lambda e: e.tensor_tensor(gf.t[:, :], gf.t[:, :], sc.t[:, :], ALU.mult),
                      reads=[gf.buf, sc.buf], writes=[gf.buf])
                gr.op("vector", lambda e: e.tensor_scalar(gf.t[:, :], gf.t[:, :], mm1.t[:, 3:4],
                                                          None, ALU.mult),
                      reads=[gf.buf, mm1.buf], writes=[gf.buf])
                gt_ap, gt_bufs = PS5.slice(0, 128, 0, 8)
                gr.op("tensor", lambda e, ap=gt_ap: e.transpose(ap, gf.t[:, :], ID_F.t[:, :]),
                      reads=[gf.buf, ID_F.buf], writes=gt_bufs)
                gr.op("vector", lambda e, tt_=tt_, ap=gt_ap: e.tensor_copy(
                    gfT8.t[:, 128 * tt_:128 * tt_ + 128], ap),
                      reads=gt_bufs, writes=[gfT8.buf])
                if debug and sl == 0 and tt_ == 0:
                    gr.dma("gpsimd", dbg["sc"][:, :], sc.t[:, :], reads=[sc.buf])
                    gr.dma("gpsimd", dbg["gf"][:, :], gf.t[:, :], reads=[gf.buf])

            for e_i in range(E):
                gr.dma("sync", ge_row.t[0:1, :],
                       bass.AP(gfT8.t, e_i * 4 * 128, [[4 * 128, 1], [1, SL]]),
                       reads=[gfT8.buf], writes=[ge_row.buf])
                gb_ap, gb_bufs = PG.slice(0, SL)
                gr.op("tensor", lambda e, ap=gb_ap: e.matmul(ap, OR_BF.t[:, :], ge_row.t[0:1, :],
                                                             start=True, stop=True),
                      reads=[ge_row.buf, OR_BF.buf], writes=gb_bufs)
                for p in range(2):
                    wp = load_piece(w1_d, e_i * 2 + p)
                    for jj in range(2):
                        hc = 2 * p + jj
                        h_ap = PH.t[0:128, 512 * (hc % 2):512 * (hc % 2) + 512]
                        _, h_bufs = PH.slice(512 * (hc % 2), 512 * (hc % 2) + 512)
                        for c in range(DC):
                            gr.op("tensor", lambda e, c=c, jj=jj, wp=wp, ap=h_ap:
                                  e.matmul(ap, wp.t[:, c * 256 + 128 * jj:c * 256 + 128 * jj + 128],
                                           xn1.t[:, c * SL:(c + 1) * SL], start=(c == 0),
                                           stop=(c == DC - 1)),
                                  reads=[wp.buf, xn1.buf], writes=h_bufs)
                        gr.op("scalar", lambda e, e_i=e_i, hc=hc, ap=h_ap: e.activation(
                            Hg.t[:, :], ap, AF.Gelu, bias=b1col.t[:, e_i * 4 + hc:e_i * 4 + hc + 1]),
                              reads=h_bufs + [b1col.buf], writes=[Hg.buf])
                        gr.op("vector", lambda e, hc=hc, ap=gb_ap: e.tensor_tensor(
                            Hw.t[:, hc * SL:(hc + 1) * SL], Hg.t[:, :], ap, ALU.mult),
                              reads=[Hg.buf] + gb_bufs, writes=[Hw.buf])
                for p in range(2):
                    wp = load_piece(w2_d, e_i * 2 + p)
                    for c in range(DC):
                        w_ap = PW.t[0:128, 512 * (c % 2):512 * (c % 2) + 512]
                        _, w_bufs = PW.slice(512 * (c % 2), 512 * (c % 2) + 512)
                        for i in range(2):
                            hc = 2 * p + i
                            gr.op("tensor", lambda e, wp=wp, i=i, c=c, hc=hc, ap=w_ap:
                                  e.matmul(ap, wp.t[:, i * D + 128 * c:i * D + 128 * c + 128],
                                           Hw.t[:, hc * SL:(hc + 1) * SL],
                                           start=(i == 0), stop=(i == 1)),
                                  reads=[wp.buf, Hw.buf], writes=w_bufs)
                        csl = slice(c * SL, (c + 1) * SL)
                        gr.op("vector", lambda e, csl=csl, ap=w_ap: e.tensor_tensor(
                            acc.t[:, csl], acc.t[:, csl], ap, ALU.add),
                              reads=gb_bufs + w_bufs + [acc.buf], writes=[acc.buf])
            for c in range(DC):
                b2_ap = PW.t[0:128, 512 * (c % 2):512 * (c % 2) + 512]
                _, b2_bufs = PW.slice(512 * (c % 2), 512 * (c % 2) + 512)
                gr.op("tensor", lambda e, c=c, ap=b2_ap: e.matmul(
                    ap, b2_sb.t[:, c * 128:(c + 1) * 128], gfT8.t[:, :], start=True, stop=True),
                      reads=[b2_sb.buf, gfT8.buf], writes=b2_bufs)
                csl = slice(c * SL, (c + 1) * SL)
                gr.op("vector", lambda e, csl=csl, ap=b2_ap: e.tensor_tensor(acc.t[:, csl],
                                                                             acc.t[:, csl], ap, ALU.add),
                      reads=b2_bufs + [acc.buf], writes=[acc.buf])
            for p in range(2):
                wp = load_piece(ws1_d, p)
                for jj in range(2):
                    hc = 2 * p + jj
                    h_ap = PH.t[0:128, 512 * (hc % 2):512 * (hc % 2) + 512]
                    _, h_bufs = PH.slice(512 * (hc % 2), 512 * (hc % 2) + 512)
                    for c in range(DC):
                        gr.op("tensor", lambda e, c=c, jj=jj, wp=wp, ap=h_ap:
                              e.matmul(ap, wp.t[:, c * 256 + 128 * jj:c * 256 + 128 * jj + 128],
                                       xn1.t[:, c * SL:(c + 1) * SL], start=(c == 0), stop=(c == DC - 1)),
                              reads=[wp.buf, xn1.buf], writes=h_bufs)
                    gr.op("scalar", lambda e, hc=hc, ap=h_ap: e.activation(
                        Hg.t[:, :], ap, AF.Gelu, bias=bs1col.t[:, hc:hc + 1]),
                          reads=h_bufs + [bs1col.buf], writes=[Hg.buf])
                    gr.op("vector", lambda e, hc=hc: e.tensor_copy(Hw.t[:, hc * SL:(hc + 1) * SL],
                                                                   Hg.t[:, :]),
                          reads=[Hg.buf], writes=[Hw.buf])
            for p in range(2):
                wp = load_piece(ws2_d, p)
                for c in range(DC):
                    w_ap = PW.t[0:128, 512 * (c % 2):512 * (c % 2) + 512]
                    _, w_bufs = PW.slice(512 * (c % 2), 512 * (c % 2) + 512)
                    for i in range(2):
                        hc = 2 * p + i
                        gr.op("tensor", lambda e, wp=wp, i=i, c=c, hc=hc, ap=w_ap:
                              e.matmul(ap, wp.t[:, i * D + 128 * c:i * D + 128 * c + 128],
                                       Hw.t[:, hc * SL:(hc + 1) * SL],
                                       start=(i == 0), stop=(i == 1)),
                              reads=[wp.buf, Hw.buf], writes=w_bufs)
                    csl = slice(c * SL, (c + 1) * SL)
                    if p == 0:
                        gr.op("vector", lambda e, c=c, csl=csl, ap=w_ap: e.scalar_tensor_tensor(
                            acc.t[:, csl], ap, bs2col.t[:, c:c + 1], acc.t[:, csl], ALU.add, ALU.add),
                              reads=w_bufs + [bs2col.buf, acc.buf], writes=[acc.buf])
                    else:
                        gr.op("vector", lambda e, csl=csl, ap=w_ap: e.tensor_tensor(
                            acc.t[:, csl], acc.t[:, csl], ap, ALU.add),
                              reads=w_bufs + [acc.buf], writes=[acc.buf])
            if debug and sl == 0:
                gr.dma("gpsimd", dbg["acc"][:, :], acc.t[:, :], reads=[acc.buf])
                gr.dma("gpsimd", dbg["xn1"][:, :], xn1.t[:, :], reads=[xn1.buf])
            for c in range(DC):
                gr.dma("sync", yf.t[:, :], y_d[c * 128:(c + 1) * 128, sl * SL:(sl + 1) * SL],
                       writes=[yf.buf])
                gr.op("vector", lambda e, c=c: e.tensor_tensor(ost.t[:, :], yf.t[:, :],
                                                               acc.t[:, c * SL:(c + 1) * SL], ALU.add),
                      reads=[yf.buf, acc.buf], writes=[ost.buf])
                gr.dma("sync", out_d[c * 128:(c + 1) * 128, sl * SL:(sl + 1) * SL], ost.t[:, :],
                       reads=[ost.buf])




def build_l1_moe(debug=False):
    nc = bass.Bass("TRN2", target_bir_lowering=False)
    gr = G(nc)
    consts = host_consts()
    x1_d = nc.dram_tensor("x1T", [D, T], F32, kind="ExternalInput")
    y_d = nc.dram_tensor("yT", [D, T], F32, kind="ExternalInput")
    gw_d = nc.dram_tensor("gate_W", [D, E], F32, kind="ExternalInput")
    w1_d = nc.dram_tensor("W1e", [16 * 128, 2048], F32, kind="ExternalInput")
    b1_d = nc.dram_tensor("b1e", [E, 512], F32, kind="ExternalInput")
    w2_d = nc.dram_tensor("W2e", [16 * 128, 2048], F32, kind="ExternalInput")
    b2_d = nc.dram_tensor("b2e", [E, D], F32, kind="ExternalInput")
    ws1_d = nc.dram_tensor("Ws1", [2 * 128, 2048], F32, kind="ExternalInput")
    bs1_d = nc.dram_tensor("bs1", [512, 1], F32, kind="ExternalInput")
    ws2_d = nc.dram_tensor("Ws2", [2 * 128, 2048], F32, kind="ExternalInput")
    bs2_d = nc.dram_tensor("bs2", [D, 1], F32, kind="ExternalInput")
    rs_d = nc.dram_tensor("route_scale", [1, 1], F32, kind="ExternalInput")
    out_d = nc.dram_tensor("outT", [D, T], F32, kind="ExternalOutput")
    dr = {"x1T": x1_d, "yT": y_d, "gate_W": gw_d, "W1e": w1_d, "b1e": b1_d, "W2e": w2_d, "b2e": b2_d, "Ws1": ws1_d, "bs1": bs1_d, "Ws2": ws2_d, "bs2": bs2_d, "route_scale": rs_d, "outT": out_d}
    with contextlib.ExitStack() as ctx0:
        cells = Cells(nc, ctx0, gr)
        emit_l1_moe(nc, gr, consts, cells, dr, debug=debug)
        gr.finalize_and_emit(ctx0)
    return nc



def build_fused(debug=False):
    nc = bass.Bass("TRN2", target_bir_lowering=False)
    gr = G(nc)
    consts = host_consts()
    E = 8
    dr = {}
    for name, shape in [("xT", [D, T]), ("wq_l", [128, DC * D]), ("wk_l", [128, DC * D]), ("wv_l", [128, DC * D]),
                        ("rope_cos", [NSL * 128, 4 * SL]), ("rope_sin", [NSL * 128, 4 * SL]),
                        ("wr_l", [128, DC * D]), ("wk2_l", [128, DC * D]), ("wv2_l", [128, DC * D]),
                        ("mu_r", [D, 1]), ("mu_k", [D, 1]),
                        ("Wq", [4 * 128, 2048]), ("Wk", [4 * 128, 2048]), ("Wv", [8 * 128, 2048]), ("Wg", [8 * 128, 2048]),
                        ("Wo", [8 * 128, 2048]), ("W2", [4 * 128, 2048]), ("Wa", [D, 4]), ("Wb", [D, 4]),
                        ("conv_q", [D, 4]), ("conv_k", [D, 4]), ("conv_v", [VAL_DIM, 4]),
                        ("ln1_g", [D, 1]), ("ln1_b", [D, 1]), ("ln2_g", [D, 1]), ("ln2_b", [D, 1]),
                        ("rms_in_w", [D, 1]), ("b2", [D, 1]), ("o_norm_w", [1, Vd]),
                        ("dt_bias", [1, 4]), ("A_log", [1, 4]),
                        ("gate_W", [D, E]), ("W1e", [16 * 128, 2048]), ("b1e", [E, 512]),
                        ("W2e", [16 * 128, 2048]), ("b2e", [E, D]), ("Ws1", [2 * 128, 2048]),
                        ("bs1", [512, 1]), ("Ws2", [2 * 128, 2048]), ("bs2", [D, 1]),
                        ("route_scale", [1, 1])]:
        dr[name] = nc.dram_tensor(name, shape, F32, kind="ExternalInput")
    ikind = "ExternalOutput" if debug else Ellipsis
    for name, shape in [("o0T", [D, T]), ("x1T", [D, T]), ("yT", [D, T])]:
        if debug:
            dr[name] = nc.dram_tensor(name, shape, F32, kind="ExternalOutput")
        else:
            dr[name] = nc.dram_tensor(name, shape, F32)
    dr["outT"] = nc.dram_tensor("outT", [D, T], F32, kind="ExternalOutput")

    with contextlib.ExitStack() as ctx0:
        cells = Cells(nc, ctx0, gr)
        with contextlib.ExitStack() as ctx01:
            xn_shared = Tile(nc, ctx01, "xn_shared", [128, DC * (T + 1)], BF16)
            emit_l0_attn(nc, gr, consts, cells, dr, xn_shared=xn_shared)
            gr.barrier()
            emit_l0_ffn(nc, gr, consts, cells, dr, xn_shared=xn_shared)
        gr.barrier()
        emit_l1_attn(nc, gr, consts, cells, dr, debug=debug)
        gr.barrier()
        emit_l1_moe(nc, gr, consts, cells, dr)
        gr.finalize_and_emit(ctx0)
    return nc, consts


def _host_inputs(inputs, consts):
    f32 = lambda k: np.ascontiguousarray(np.asarray(inputs[k], np.float32))
    def _tab_p(t):
        a = t.reshape(4, 128, 4, SL)
        return np.ascontiguousarray(a.transpose(2, 1, 0, 3).reshape(4 * 128, 4 * SL))

    base = {
        "wq_l": _whole_w(np.ascontiguousarray(f32("L0_attn_w_q")[_PERM].T)),
        "wk_l": _whole_w(np.ascontiguousarray(f32("L0_attn_w_k")[_PERM].T)),
        "wv_l": _whole_w(np.ascontiguousarray(f32("L0_attn_w_v").T)),
        "rope_cos": _tab_p(consts["rope_cos"]), "rope_sin": _tab_p(consts["rope_sin"]),
        "wr_l": _whole_w(np.ascontiguousarray(f32("L0_ffn_W_r").T)),
        "wk2_l": _whole_w(np.ascontiguousarray(f32("L0_ffn_W_k").T)),
        "wv2_l": _whole_w(np.ascontiguousarray(f32("L0_ffn_W_v").T)),
        "mu_r": f32("L0_ffn_mu_r").reshape(D, 1), "mu_k": f32("L0_ffn_mu_k").reshape(D, 1),
        "Wq": _pieces_cols(f32("L1_attn_Wq"), 256), "Wk": _pieces_cols(f32("L1_attn_Wk"), 256),
        "Wv": _pieces_cols(f32("L1_attn_Wv"), 256), "Wg": _pieces_cols(f32("L1_attn_Wg"), 256),
        "Wo": _pieces_rows(f32("L1_attn_Wo"), 2), "W2": _pieces_cols(f32("L1_attn_W2"), 256),
        "Wa": f32("L1_attn_Wa"), "Wb": f32("L1_attn_Wb"),
        "conv_q": f32("L1_attn_conv_q_w"), "conv_k": f32("L1_attn_conv_k_w"),
        "conv_v": f32("L1_attn_conv_v_w"),
        "ln1_g": f32("L1_attn_ln1_g").reshape(-1, 1), "ln1_b": f32("L1_attn_ln1_b").reshape(-1, 1),
        "ln2_g": f32("L1_attn_ln2_g").reshape(-1, 1), "ln2_b": f32("L1_attn_ln2_b").reshape(-1, 1),
        "rms_in_w": f32("L1_attn_rms_in_w").reshape(-1, 1), "b2": f32("L1_attn_b2").reshape(-1, 1),
        "o_norm_w": f32("L1_attn_o_norm_w").reshape(1, -1),
        "dt_bias": f32("L1_attn_dt_bias").reshape(1, -1), "A_log": f32("L1_attn_A_log").reshape(1, -1),
        "gate_W": f32("L1_ffn_gate_W"),
        "W1e": np.concatenate([_pieces_cols(f32("L1_ffn_W1")[e], 256) for e in range(8)], 0),
        "b1e": f32("L1_ffn_b1"),
        "W2e": np.concatenate([_pieces_rows(f32("L1_ffn_W2")[e], 2) for e in range(8)], 0),
        "b2e": f32("L1_ffn_b2"),
        "Ws1": _pieces_cols(f32("L1_ffn_Ws1"), 256), "bs1": f32("L1_ffn_bs1").reshape(-1, 1),
        "Ws2": _pieces_rows(f32("L1_ffn_Ws2"), 2), "bs2": f32("L1_ffn_bs2").reshape(-1, 1),
        "route_scale": f32("L1_ffn_route_scale").reshape(1, 1),
    }
    for name in ["ident", "ones_col", "ones_row", "neg_ones_row", "causal_neg", "tri_incl",
                 "mask1", "mask2", "mask3", "allones"]:
        base["c_" + name] = consts[name]
    return base


def kernel(**inputs):
    consts = host_consts()
    x = np.ascontiguousarray(np.asarray(inputs["L0_attn_x"], np.float32))
    B = x.shape[0]
    xT = np.ascontiguousarray(x.transpose(0, 2, 1))
    nc, _ = build_fused()
    base = _host_inputs(inputs, consts)
    res = run_bass_kernel_spmd(nc, [dict(base, xT=xT[b]) for b in range(B)],
                               core_ids=list(range(B)))
    out = np.stack([res.results[b]["outT"] for b in range(B)])
    return np.ascontiguousarray(out.transpose(0, 2, 1))



def _pieces_cols(w, ewid):
    K, N = w.shape
    npc, dck = N // ewid, K // 128
    a = w.reshape(dck, 128, npc, ewid)
    return np.ascontiguousarray(a.transpose(2, 1, 0, 3).reshape(npc * 128, dck * ewid))


def _pieces_rows(w, nrch):
    K, N = w.shape
    npc = K // (nrch * 128)
    a = w.reshape(npc, nrch, 128, N)
    return np.ascontiguousarray(a.transpose(0, 2, 1, 3).reshape(npc * 128, nrch * N))


def _whole_w(w):
    K, N = w.shape
    dck = K // 128
    return np.ascontiguousarray(w.reshape(dck, 128, N).transpose(1, 0, 2).reshape(128, dck * N))


_PERM = np.concatenate([np.arange(0, D, 2), np.arange(1, D, 2)])


def kernel_unfused(**inputs):
    consts = host_consts()
    x = np.ascontiguousarray(np.asarray(inputs["L0_attn_x"], np.float32))
    B = x.shape[0]
    xT = np.ascontiguousarray(x.transpose(0, 2, 1))
    f32 = lambda k: np.ascontiguousarray(np.asarray(inputs[k], np.float32))

    def cmap(base, names):
        for name in names:
            base["c_" + name] = consts[name]
        return base

    nc1, _ = build_l0_attn()
    base = cmap({"wq_l": np.ascontiguousarray(f32("L0_attn_w_q")[_PERM].T),
                 "wk_l": np.ascontiguousarray(f32("L0_attn_w_k")[_PERM].T),
                 "wv_l": np.ascontiguousarray(f32("L0_attn_w_v").T),
                 "rope_cos": consts["rope_cos"], "rope_sin": consts["rope_sin"]},
                ["ident", "ones_col", "ones_row", "causal_neg"])
    res1 = run_bass_kernel_spmd(nc1, [dict(base, xT=xT[b]) for b in range(B)],
                                core_ids=list(range(B)))
    o0 = [res1.results[b]["o0T"] for b in range(B)]

    nc2 = build_l0_ffn()
    base2 = cmap({"wr_l": np.ascontiguousarray(f32("L0_ffn_W_r").T),
                  "wk2_l": np.ascontiguousarray(f32("L0_ffn_W_k").T),
                  "wv2_l": np.ascontiguousarray(f32("L0_ffn_W_v").T),
                  "mu_r": f32("L0_ffn_mu_r").reshape(D, 1),
                  "mu_k": f32("L0_ffn_mu_k").reshape(D, 1)},
                 ["ones_col", "ones_row"])
    res2 = run_bass_kernel_spmd(nc2, [dict(base2, xT=xT[b], o0T=o0[b]) for b in range(B)],
                                core_ids=list(range(B)))
    x1 = [res2.results[b]["x1T"] for b in range(B)]

    nc3 = build_l1_attn()
    base3 = cmap({
        "Wq": _pieces_cols(f32("L1_attn_Wq"), 256), "Wk": _pieces_cols(f32("L1_attn_Wk"), 256),
        "Wv": _pieces_cols(f32("L1_attn_Wv"), 256), "Wg": _pieces_cols(f32("L1_attn_Wg"), 256),
        "Wo": _pieces_rows(f32("L1_attn_Wo"), 2), "W2": _pieces_cols(f32("L1_attn_W2"), 256),
        "Wa": f32("L1_attn_Wa"), "Wb": f32("L1_attn_Wb"),
        "conv_q": f32("L1_attn_conv_q_w"), "conv_k": f32("L1_attn_conv_k_w"),
        "conv_v": f32("L1_attn_conv_v_w"),
        "ln1_g": f32("L1_attn_ln1_g").reshape(-1, 1), "ln1_b": f32("L1_attn_ln1_b").reshape(-1, 1),
        "ln2_g": f32("L1_attn_ln2_g").reshape(-1, 1), "ln2_b": f32("L1_attn_ln2_b").reshape(-1, 1),
        "rms_in_w": f32("L1_attn_rms_in_w").reshape(-1, 1), "b2": f32("L1_attn_b2").reshape(-1, 1),
        "o_norm_w": f32("L1_attn_o_norm_w").reshape(1, -1),
        "dt_bias": f32("L1_attn_dt_bias").reshape(1, -1),
        "A_log": f32("L1_attn_A_log").reshape(1, -1)},
        ["ident", "ones_col", "ones_row", "neg_ones_row", "tri_incl", "mask1", "mask2", "mask3"])
    res3 = run_bass_kernel_spmd(nc3, [dict(base3, x1T=x1[b]) for b in range(B)],
                                core_ids=list(range(B)))
    y = [res3.results[b]["yT"] for b in range(B)]

    nc4 = build_l1_moe()
    base4 = cmap({
        "gate_W": f32("L1_ffn_gate_W"),
        "W1e": np.ascontiguousarray(f32("L1_ffn_W1").reshape(8 * D, 512)),
        "b1e": f32("L1_ffn_b1"),
        "W2e": np.ascontiguousarray(f32("L1_ffn_W2").reshape(8 * 512, D)),
        "b2e": f32("L1_ffn_b2"),
        "Ws1": f32("L1_ffn_Ws1"), "bs1": f32("L1_ffn_bs1").reshape(-1, 1),
        "Ws2": f32("L1_ffn_Ws2"), "bs2": f32("L1_ffn_bs2").reshape(-1, 1),
        "route_scale": f32("L1_ffn_route_scale").reshape(1, 1)},
        ["ident", "ones_col", "ones_row"])
    res4 = run_bass_kernel_spmd(nc4, [dict(base4, x1T=x1[b], yT=y[b]) for b in range(B)],
                                core_ids=list(range(B)))
    out = np.stack([res4.results[b]["outT"] for b in range(B)])
    return np.ascontiguousarray(out.transpose(0, 2, 1))


# version 19
# speedup vs baseline: 1.1401x; 1.1401x over previous
import contextlib
import math
import numpy as np
import ml_dtypes
import concourse.bass as bass
import concourse.mybir as mybir
from concourse.bass_utils import run_bass_kernel_spmd

F32 = mybir.dt.float32
BF16 = mybir.dt.bfloat16
AF = mybir.ActivationFunctionType
ALU = mybir.AluOpType
AX = mybir.AxisListType

T = 2048
D = 1024
DC = D // 128
NSL = 4
SL = 512
H, Kd, Vd = 4, 256, 512
VAL_DIM = H * Vd
EPS_L2 = 1e-6
LN_EPS = 1e-5

ENGINES = ("sync", "tensor", "vector", "scalar", "gpsimd")



DMA_POOL = 8


class Op:
    __slots__ = ("eng", "fn", "deps", "is_dma", "pos", "needs_inc", "inc_value", "waits",
                 "dma_sem", "dma_val")

    def __init__(self, eng, fn, deps, is_dma):
        self.eng = eng
        self.fn = fn
        self.deps = deps
        self.is_dma = is_dma
        self.needs_inc = is_dma
        self.inc_value = None
        self.waits = []
        self.dma_sem = None
        self.dma_val = None


class Buf:
    __slots__ = ("name", "last_writer", "readers")

    def __init__(self, name):
        self.name = name
        self.last_writer = None
        self.readers = []


class G:
    def __init__(self, nc):
        self.nc = nc
        self.ops = {e: [] for e in ENGINES}
        self.order = []
        self._fence = None
        self._fence_pending = {e: False for e in ENGINES}
        self._last_dma = {}
        self._dma_counts = {}
        self._all_dmas = {}

    def barrier(self):
        join = self.op("vector", lambda e: e.engine_nop(), deps=self.barrier_deps())
        self._fence = join
        self._fence_pending = {e: True for e in ENGINES}
        self._fence_pending["vector"] = False

    def op(self, eng, fn, reads=(), writes=(), deps=(), dma=False):
        dd = []
        if self._fence is not None and self._fence_pending.get(eng):
            dd.append(self._fence)
            self._fence_pending[eng] = False
        for b in reads:
            if b.last_writer is not None:
                dd.append(b.last_writer)
        for b in writes:
            if b.last_writer is not None:
                dd.append(b.last_writer)
            dd.extend(b.readers)
        dd.extend(deps)
        seen, deps_u = set(), []
        for d in dd:
            if d is not None and id(d) not in seen:
                seen.add(id(d))
                deps_u.append(d)
        o = Op(eng, fn, deps_u, dma)
        o.pos = len(self.order)
        self.order.append(o)
        self.ops[eng].append(o)
        if dma:
            i = self._dma_counts.get(eng, 0)
            self._dma_counts[eng] = i + 1
            o.dma_sem = f"dq_{eng}_{i % DMA_POOL}"
            o.dma_val = 16 * (i // DMA_POOL + 1)
            self._last_dma[eng] = o
        for b in reads:
            b.readers.append(o)
        for b in writes:
            b.last_writer = o
            b.readers = []
        return o

    def dma(self, queue, out_ap_fn, in_ap_fn, reads=(), writes=(), deps=()):
        def fn(eng):
            return eng.dma_start(out_ap_fn() if callable(out_ap_fn) else out_ap_fn,
                                 in_ap_fn() if callable(in_ap_fn) else in_ap_fn)
        return self.op(queue, fn, reads=reads, writes=writes, deps=deps, dma=True)

    def barrier_deps(self):
        dd = [self.ops[e][-1] for e in ENGINES if self.ops[e]]
        for e in ENGINES:
            dmas = [o for o in self.ops[e] if o.is_dma][-DMA_POOL:]
            dd.extend(dmas)
        return dd

    @staticmethod
    def _skip_edge(d, o):
        return d.eng == o.eng == "tensor" and not d.is_dma

    def finalize_and_emit(self, ctx):
        for o in self.order:
            for d in o.deps:
                if not self._skip_edge(d, o):
                    d.needs_inc = True
        counts = {e: 0 for e in ENGINES}
        for o in self.order:
            if o.needs_inc and not o.is_dma:
                counts[o.eng] += 1
                o.inc_value = counts[o.eng]
        for o in self.order:
            need = {}
            for d in o.deps:
                if self._skip_edge(d, o):
                    continue
                if d.is_dma:
                    need[d.dma_sem] = max(need.get(d.dma_sem, 0), d.dma_val)
                else:
                    assert d.inc_value is not None
                    need[d.eng] = max(need.get(d.eng, 0), d.inc_value)
            if o.is_dma and o.dma_val > 16:
                need[o.dma_sem] = max(need.get(o.dma_sem, 0), o.dma_val - 16)
            o.waits = sorted(need.items())
        final_counts = dict(counts)
        for e, n in self._dma_counts.items():
            for i in range(min(n, DMA_POOL)):
                nm = f"dq_{e}_{i}"
                nuses = (n - i + DMA_POOL - 1) // DMA_POOL
                final_counts[nm] = 16 * nuses

        sems = {e: ctx.enter_context(self.nc.semaphore("sem_" + e)) for e in ENGINES}
        for e, n in self._dma_counts.items():
            for i in range(min(n, DMA_POOL)):
                nm = f"dq_{e}_{i}"
                sems[nm] = ctx.enter_context(self.nc.semaphore(nm))

        block = ctx.enter_context(self.nc.Block())

        def make_body(ename):
            def body(eng):
                watermark = {k: 0 for k in sems}
                for o in self.ops[ename]:
                    for se, v in o.waits:
                        if v > watermark[se]:
                            eng.wait_ge(sems[se], v)
                            watermark[se] = v
                    inst = o.fn(eng)
                    if o.is_dma:
                        inst.then_inc(sems[o.dma_sem], 16)
                    elif o.needs_inc:
                        inst.then_inc(sems[o.eng], 1)
                if ename == "sync":
                    for se, v in final_counts.items():
                        if v > watermark.get(se, 0) and v > 0:
                            eng.wait_ge(sems[se], v)
            return body

        for ename in ENGINES:
            getattr(block, ename)(make_body(ename))



_NAME_SEQ = [0]


def _uniq(name):
    _NAME_SEQ[0] += 1
    return f"{name}_{_NAME_SEQ[0]}"


class Tile:

    def __init__(self, nc, ctx, name, shape, dtype, nbufs=1, alias=None):
        if alias is not None:
            assert alias.shape[1] >= shape[1] and alias.t.dtype == dtype
            self.t = alias.t
            self.shape = shape
            self.buf = alias.buf
            self.bufs = alias.bufs
            return
        name = _uniq(name)
        self.t = ctx.enter_context(nc.sbuf_tensor(name, shape, dtype))
        self.shape = shape
        if nbufs == 1:
            self.buf = Buf(name)
            self.bufs = [self.buf]
        else:
            self.bufs = [Buf(f"{name}#{i}") for i in range(nbufs)]
            self.buf = self.bufs[0]

    def __getitem__(self, idx):
        return self.t[idx]

    @property
    def ap(self):
        return self.t[:, :]


class _RowIndexer:
    def __init__(self, t, part, c0, width):
        self._t = t
        self._part = part
        self._c0 = c0
        self._w = width

    def __getitem__(self, idx):
        rows, cols = idx
        a = cols.start if cols.start is not None else 0
        b = cols.stop if cols.stop is not None else self._w
        return self._t[self._part:self._part + 1, self._c0 + a:self._c0 + b]


class RowTile:

    def __init__(self, parent, part, name, c0=0, width=None):
        self.buf = Buf(name)
        if width is None:
            width = parent.shape[1] - c0
        self.t = _RowIndexer(parent.t, part, c0, width)


class Psum:

    def __init__(self, nc, ctx, name, shape, dtype):
        name = _uniq(name)
        self.t = ctx.enter_context(nc.psum_tensor(name, shape, dtype))
        self.shape = shape
        elem = 4 if dtype == F32 else 2
        self.bank_cols = 2048 // elem
        self.nbanks = (shape[1] * elem + 2047) // 2048
        self.bufs = [Buf(f"{name}@b{i}") for i in range(self.nbanks)]

    def slice(self, c0, c1, p0=0, p1=128):
        ap = self.t[p0:p1, c0:c1]
        b0 = c0 // self.bank_cols
        b1 = (c1 - 1) // self.bank_cols
        return ap, self.bufs[b0:b1 + 1]


class Cells:

    def __init__(self, nc, ctx, g):
        self.tile = Tile(nc, ctx, "cells", [128, 16], F32)
        self.g = g
        self.map = {}
        self.n = 0

    def _col(self, value):
        v = float(value)
        if v not in self.map:
            k = self.n
            self.n += 1
            assert k < 16
            self.g.op("vector", lambda e, k=k, v=v: e.memset(self.tile.t[:, k:k + 1], v),
                      writes=[self.tile.buf])
            self.map[v] = k
        return self.map[v]

    def col(self, value):
        k = self._col(value)
        return self.tile.t[:, k:k + 1], self.tile.buf

    def cell(self, value):
        k = self._col(value)
        return self.tile.t[0:1, k:k + 1], self.tile.buf


def fm_cols(c, t0, t1, period, pad=0):
    return (c * period + pad + t0, c * period + pad + t1)



def host_consts():
    c = {}
    c["ident"] = np.eye(128, dtype=np.float32)
    c["allones"] = np.ones((128, 128), np.float32)
    c["ones_col"] = np.ones((128, 1), np.float32)
    c["ones_row"] = np.ones((1, 128), np.float32)
    c["neg_ones_row"] = -np.ones((1, 128), np.float32)
    i = np.arange(128)
    c["causal_neg"] = np.where(i[:, None] >= i[None, :], 0.0, -1e30).astype(np.float32)
    ii = np.arange(D // 2, dtype=np.float64)
    theta = np.power(10000.0, -2.0 * (ii - 1.0) / D)
    pos = np.arange(T, dtype=np.float64)
    ang = theta[:, None] * pos[None, :]
    c["rope_cos"] = np.cos(ang).astype(np.float32)
    c["rope_sin"] = np.sin(ang).astype(np.float32)
    j = np.arange(128)
    cc, cp = j[:, None], j[None, :]
    c["tri_incl"] = (cc <= cp).astype(np.float32)
    big = np.float32(1e30)
    c["mask1"] = np.tile((cp >= cc) * -big, (1, 4)).astype(np.float32)
    c["mask2"] = np.tile((cp <= cc) * big, (1, 4)).astype(np.float32)
    c["mask3"] = np.tile((cp < cc) * big, (1, 4)).astype(np.float32)
    return c


def const_tensors(nc, g, ctx, consts, names):
    out = {}
    if not hasattr(g, "_const_drams"):
        g._const_drams = {}
    for name, want in names.items():
        arr = consts[name]
        if name in g._const_drams:
            dt = g._const_drams[name]
        else:
            dt = nc.dram_tensor("c_" + name, list(arr.shape), F32, kind="ExternalInput")
            g._const_drams[name] = dt
        for w in want:
            dtype = F32 if w == "f32" else BF16
            tl = Tile(nc, ctx, f"ct_{name}_{w}", list(arr.shape), dtype)
            q = "gpsimd" if w == "bf16" else "sync"
            g.dma(q, tl.t[:, :], dt[:, :], writes=[tl.buf])
            out[(name, w)] = tl
    return out



def emit_ln_stats_and_apply(nc, g, C, xsrc_dram, xbf, xn, ps_b, sq, t1, t2, mb_t, rb_t,
                            cells, eps=LN_EPS):
    ones_m = C[("allones", "bf16")]
    eps_col, eps_col_b = cells.col(eps)
    for sl in range(NSL):
        for c in range(DC):
            g.dma("gpsimd",
                  xbf.t[:, c * SL:(c + 1) * SL],
                  xsrc_dram[c * 128:(c + 1) * 128, sl * SL:(sl + 1) * SL],
                  writes=[xbf.bufs[c]])
        sum_ap, sum_bufs = ps_b.slice(0, SL)
        sq_ap, sq_bufs = ps_b.slice(512, 512 + SL)
        for c in range(DC):
            g.op("tensor",
                 lambda e, c=c, ap=sum_ap: e.matmul(ap, ones_m.t[:, :], xbf.t[:, c * SL:(c + 1) * SL],
                                                    start=(c == 0), stop=(c == DC - 1)),
                 reads=[xbf.bufs[c], ones_m.buf], writes=sum_bufs)
        for c in range(DC):
            g.op("vector",
                 lambda e, c=c: e.tensor_mul(sq.t[:, :], xbf.t[:, c * SL:(c + 1) * SL],
                                             xbf.t[:, c * SL:(c + 1) * SL]),
                 reads=[xbf.bufs[c]], writes=[sq.buf])
            g.op("tensor",
                 lambda e, c=c, ap=sq_ap: e.matmul(ap, ones_m.t[:, :], sq.t[:, :],
                                                   start=(c == 0), stop=(c == DC - 1)),
                 reads=[sq.buf, ones_m.buf], writes=sq_bufs)
        g.op("vector", lambda e, ap=sum_ap: e.tensor_scalar(mb_t.t[:, :], ap, 1.0 / D, None, ALU.mult),
             reads=sum_bufs, writes=[mb_t.buf])
        g.op("vector", lambda e: e.tensor_mul(t1.t[:, :], mb_t.t[:, :], mb_t.t[:, :]),
             reads=[mb_t.buf], writes=[t1.buf])
        g.op("vector", lambda e, ap=sq_ap: e.scalar_tensor_tensor(t2.t[:, :], ap, 1.0 / D,
                                                                  t1.t[:, :], ALU.mult, ALU.subtract),
             reads=sq_bufs + [t1.buf], writes=[t2.buf])
        g.op("scalar", lambda e: e.activation(t2.t[:, :], t2.t[:, :], AF.Ln, bias=eps_col),
             reads=[t2.buf, eps_col_b], writes=[t2.buf])
        g.op("scalar", lambda e: e.activation(rb_t.t[:, :], t2.t[:, :], AF.Exp, scale=-0.5),
             reads=[t2.buf], writes=[rb_t.buf])
        for c in range(DC):
            lo, hi = fm_cols(c, sl * SL, (sl + 1) * SL, T + 1, pad=1)
            g.op("vector", lambda e, c=c: e.tensor_tensor(t1.t[:, :], xbf.t[:, c * SL:(c + 1) * SL],
                                                          mb_t.t[:, :], ALU.subtract),
                 reads=[xbf.bufs[c], mb_t.buf], writes=[t1.buf])
            g.op("vector", lambda e, lo=lo, hi=hi: e.tensor_tensor(xn.t[:, lo:hi], t1.t[:, :],
                                                                   rb_t.t[:, :], ALU.mult),
                 reads=[t1.buf, rb_t.buf], writes=[xn.bufs[0]])



def emit_l0_attn(nc, gr, consts, cells, dr, debug=False, xn_shared=None):
    x_d = dr["xT"]
    wq_d = dr["wq_l"]
    wk_d = dr["wk_l"]
    wv_d = dr["wv_l"]
    cos_d = dr["rope_cos"]
    sin_d = dr["rope_sin"]
    o_d = dr["o0T"]
    if debug:
        dbg_xn = nc.dram_tensor("dbg_xn", [128, DC * (T + 1)], F32, kind="ExternalOutput")
        dbg_krot = nc.dram_tensor("dbg_krot", [128, DC * T], F32, kind="ExternalOutput")
        dbg_qs = nc.dram_tensor("dbg_qs", [128, DC * SL], F32, kind="ExternalOutput")
        dbg_pb = nc.dram_tensor("dbg_pb", [128, T], F32, kind="ExternalOutput")
        dbg_vtok = nc.dram_tensor("dbg_vtok", [128, 16 * 1024], F32, kind="ExternalOutput")

    ctx = contextlib.ExitStack()
    with ctx:
        C = const_tensors(nc, gr, ctx, consts, {"ident": ["bf16"], "ones_col": ["bf16"], "ones_row": ["f32"], "causal_neg": ["bf16"], "allones": ["bf16"]})
        xn = xn_shared if xn_shared is not None else Tile(nc, ctx, "xn", [128, DC * (T + 1)], BF16)
        krot = Tile(nc, ctx, "krot", [128, DC * T], BF16, nbufs=DC)
        vtok = Tile(nc, ctx, "vtok", [128, 16 * 1024], BF16, nbufs=16)
        qs = Tile(nc, ctx, "qs", [128, DC * SL], BF16)
        xbf = Tile(nc, ctx, "xbf", [128, DC * SL], BF16, nbufs=DC)
        wA = Tile(nc, ctx, "wA", [128, DC * 1024], BF16)
        wB = Tile(nc, ctx, "wB", [128, DC * 1024], BF16)
        tabc = Tile(nc, ctx, "tabc", [128, 4 * SL], BF16)
        tabs = Tile(nc, ctx, "tabs", [128, 4 * SL], BF16)
        sq = Tile(nc, ctx, "sqs", [128, SL], BF16)
        t1 = Tile(nc, ctx, "t1", [128, SL], F32)
        t2 = Tile(nc, ctx, "t2", [128, SL], F32)
        mb_t = Tile(nc, ctx, "mb_t", [128, SL], F32)
        rb_t = Tile(nc, ctx, "rb_t", [128, SL], F32)
        pb = Tile(nc, ctx, "pb", [128, T], BF16)
        pbn = Tile(nc, ctx, "pbn", [128, T], BF16)
        pbt = Tile(nc, ctx, "pbt", [128, T], BF16)
        ost = Tile(nc, ctx, "ost", [128, DC * SL], F32)
        rowred = Tile(nc, ctx, "rowred", [128, 4], F32)

        ps_s = Psum(nc, ctx, "ps_s", [128, 2048], F32)
        ps_o = Psum(nc, ctx, "ps_o", [128, 1024], F32)
        ps_a = Psum(nc, ctx, "ps_a", [128, 512], F32)
        ps_t = Psum(nc, ctx, "ps_t", [128, 1024], BF16)

        for c in range(DC):
            gr.op("vector", lambda e, c=c: e.memset(xn.t[:, c * (T + 1):c * (T + 1) + 1], 0.0),
                  writes=[xn.bufs[0]])
        emit_ln_stats_and_apply(nc, gr, C, x_d, xbf, xn, ps_o, sq, t1, t2, mb_t, rb_t, cells)

        def load_w(dst, src):
            gr.dma("gpsimd", dst.t[:, :], src[:, :], writes=[dst.buf])

        load_w(wA, wk_d)

        def load_tables(sl):
            gr.dma("gpsimd", tabc.t[:, :], cos_d[sl * 128:(sl + 1) * 128, :], writes=[tabc.buf])
            gr.dma("gpsimd", tabs.t[:, :], sin_d[sl * 128:(sl + 1) * 128, :], writes=[tabs.buf])

        def proj_rope(w, sl, outtile, outbufs, out_period, t0):
            for j in range(4):
                apA, bufA = ps_a.slice(0, SL)
                apB, bufB = ps_o.slice(0, SL)
                for c in range(DC):
                    lo, hi = fm_cols(c, sl * SL, (sl + 1) * SL, T + 1, pad=1)
                    gr.op("tensor", lambda e, c=c, j=j, lo=lo, hi=hi, ap=apA:
                          e.matmul(ap, w.t[:, c * 1024 + 128 * j: c * 1024 + 128 * j + 128],
                                   xn.t[:, lo:hi], start=(c == 0), stop=(c == DC - 1)),
                          reads=[w.buf, xn.bufs[0]], writes=bufA)
                for c in range(DC):
                    lo, hi = fm_cols(c, sl * SL, (sl + 1) * SL, T + 1, pad=1)
                    gr.op("tensor", lambda e, c=c, j=j, lo=lo, hi=hi, ap=apB:
                          e.matmul(ap, w.t[:, c * 1024 + 128 * (j + 4): c * 1024 + 128 * (j + 4) + 128],
                                   xn.t[:, lo:hi], start=(c == 0), stop=(c == DC - 1)),
                          reads=[w.buf, xn.bufs[0]], writes=bufB)
                ctab = tabc.t[:, j * SL:(j + 1) * SL]
                stab = tabs.t[:, j * SL:(j + 1) * SL]
                def ocols(cc):
                    return (cc * out_period + t0, cc * out_period + t0 + SL)
                r0, r1 = ocols(j)
                o0c, o1c = ocols(j + 4)
                gr.op("vector", lambda e, ap=apA, ctab=ctab: e.tensor_tensor(t1.t[:, :], ap, ctab, ALU.mult),
                      reads=bufA + [tabc.buf], writes=[t1.buf])
                gr.op("vector", lambda e, ap=apB, stab=stab: e.tensor_tensor(t2.t[:, :], ap, stab, ALU.mult),
                      reads=bufB + [tabs.buf], writes=[t2.buf])
                gr.op("vector", lambda e, r0=r0, r1=r1: e.tensor_add(outtile.t[:, r0:r1], t1.t[:, :], t2.t[:, :]),
                      reads=[t1.buf, t2.buf], writes=[outbufs[j]])
                gr.op("vector", lambda e, ap=apB, ctab=ctab: e.tensor_tensor(t1.t[:, :], ap, ctab, ALU.mult),
                      reads=bufB + [tabc.buf], writes=[t1.buf])
                gr.op("vector", lambda e, ap=apA, stab=stab: e.tensor_tensor(t2.t[:, :], ap, stab, ALU.mult),
                      reads=bufA + [tabs.buf], writes=[t2.buf])
                gr.op("vector", lambda e, o0c=o0c, o1c=o1c: e.tensor_tensor(outtile.t[:, o0c:o1c], t1.t[:, :],
                                                                            t2.t[:, :], ALU.subtract),
                      reads=[t1.buf, t2.buf], writes=[outbufs[j + 4]])

        for sl in range(NSL):
            load_tables(sl)
            proj_rope(wA, sl, krot, krot.bufs, T, sl * SL)

        load_w(wB, wv_d)
        for i in range(16):
            for hhalf in range(2):
                apV, bufV = ps_a.slice(0, SL)
                for c in range(DC):
                    lo, hi = fm_cols(c, i * 128, (i + 1) * 128, T + 1, pad=1)
                    gr.op("tensor", lambda e, c=c, lo=lo, hi=hi, hh=hhalf, ap=apV:
                          e.matmul(ap, xn.t[:, lo:hi], wB.t[:, c * 1024 + 512 * hh: c * 1024 + 512 * hh + 512],
                                   start=(c == 0), stop=(c == DC - 1)),
                          reads=[wB.buf, xn.bufs[0]], writes=bufV)
                gr.op("vector", lambda e, i=i, hh=hhalf, ap=apV:
                      e.tensor_copy(vtok.t[:, i * 1024 + 512 * hh: i * 1024 + 512 * hh + 512], ap),
                      reads=bufV, writes=[vtok.bufs[i]])

        load_w(wA, wq_d)
        for sl in range(NSL):
            load_tables(sl)
            proj_rope(wA, sl, qs, [qs.buf] * 8, SL, 0)
            for qq in range(4):
                qi = sl * 4 + qq
                W = (qi + 1) * 128
                ngroups = (W + 511) // 512
                s_bufs_all = []
                for gg in range(ngroups):
                    c0, c1 = gg * 512, min(W, (gg + 1) * 512)
                    apS, bufS = ps_s.slice(c0, c1)
                    s_bufs_all.extend(bufS)
                    for c in range(DC):
                        gr.op("tensor", lambda e, c=c, qq=qq, c0=c0, c1=c1, ap=apS:
                              e.matmul(ap, qs.t[:, c * SL + 128 * qq: c * SL + 128 * qq + 128],
                                       krot.t[:, c * T + c0: c * T + c1],
                                       start=(c == 0), stop=(c == DC - 1)),
                              reads=[qs.buf, krot.bufs[c]], writes=bufS)
                apD, bufD = ps_s.slice(W - 128, W)
                gr.op("tensor", lambda e, ap=apD: e.matmul(ap, C[("ident", "bf16")].t[:, :],
                                                           C[("causal_neg", "bf16")].t[:, :],
                                                           start=False, stop=True),
                      reads=[C[("ident", "bf16")].buf, C[("causal_neg", "bf16")].buf], writes=bufD)
                s_bufs_all = list({id(b): b for b in s_bufs_all}.values())
                apSfull, _ = ps_s.slice(0, W)
                gr.op("vector", lambda e, ap=apSfull: e.reduce_max(rowred.t[:, 0:1], ap, axis=AX.X, negate=True),
                      reads=s_bufs_all, writes=[rowred.buf])
                gr.op("vector", lambda e: e.tensor_scalar(rowred.t[:, 1:2], rowred.t[:, 0:1], 1.0 / 32.0, None, ALU.mult),
                      reads=[rowred.buf], writes=[rowred.buf])
                gr.op("scalar", lambda e, W=W, ap=apSfull: e.activation(pb.t[:, 0:W], ap, AF.Exp,
                                                                        bias=rowred.t[:, 1:2], scale=1.0 / 32.0),
                      reads=s_bufs_all + [rowred.buf], writes=[pb.buf])
                gr.op("vector", lambda e, W=W: e.reduce_sum(rowred.t[:, 2:3], pb.t[:, 0:W], axis=AX.X),
                      reads=[pb.buf], writes=[rowred.buf])
                gr.op("vector", lambda e: e.reciprocal(rowred.t[:, 3:4], rowred.t[:, 2:3]),
                      reads=[rowred.buf], writes=[rowred.buf])
                gr.op("vector", lambda e, W=W: e.tensor_scalar(pbn.t[:, 0:W], pb.t[:, 0:W], rowred.t[:, 3:4],
                                                               None, ALU.mult),
                      reads=[pb.buf, rowred.buf], writes=[pbn.buf])
                nt = qi + 1
                for j0 in range(0, nt, 8):
                    jn = min(8, nt - j0)
                    apT, bufT = ps_t.slice(0, jn * 128)
                    for j in range(j0, j0 + jn):
                        ap1, _ = ps_t.slice((j - j0) * 128, (j - j0 + 1) * 128)
                        gr.op("tensor", lambda e, j=j, ap1=ap1: e.transpose(ap1, pbn.t[:, j * 128:(j + 1) * 128],
                                                                            C[("ident", "bf16")].t[:, :]),
                              reads=[pbn.buf, C[("ident", "bf16")].buf], writes=bufT)
                    gr.op("vector", lambda e, j0=j0, jn=jn, ap=apT:
                          e.tensor_copy(pbt.t[:, j0 * 128:(j0 + jn) * 128], ap),
                          reads=bufT, writes=[pbt.buf])
                for c in range(DC):
                    apO, bufO = ps_o.slice(c * 128, (c + 1) * 128)
                    for j in range(nt):
                        gr.op("tensor", lambda e, c=c, j=j, nt=nt, ap=apO:
                              e.matmul(ap, vtok.t[:, j * 1024 + 128 * c: j * 1024 + 128 * c + 128],
                                       pbt.t[:, j * 128:(j + 1) * 128],
                                       start=(j == 0), stop=(j == nt - 1)),
                              reads=[vtok.bufs[j], pbt.buf], writes=bufO)
                    gr.op("vector", lambda e, c=c, qq=qq, ap=apO:
                          e.tensor_copy(ost.t[:, c * SL + 128 * qq: c * SL + 128 * qq + 128], ap),
                          reads=bufO, writes=[ost.buf])
            for c in range(DC):
                gr.dma("sync", o_d[c * 128:(c + 1) * 128, sl * SL:(sl + 1) * SL],
                       ost.t[:, c * SL:(c + 1) * SL], reads=[ost.buf])

        if debug:
            gr.dma("gpsimd", dbg_xn[:, :], xn.t[:, :], reads=[xn.bufs[0]])
            for c in range(DC):
                gr.dma("gpsimd", dbg_krot[:, c * T:(c + 1) * T], krot.t[:, c * T:(c + 1) * T],
                       reads=[krot.bufs[c]])
            gr.dma("gpsimd", dbg_qs[:, :], qs.t[:, :], reads=[qs.buf])
            gr.dma("gpsimd", dbg_pb[:, :], pb.t[:, :], reads=[pb.buf])
            for i in range(16):
                gr.dma("gpsimd", dbg_vtok[:, i * 1024:(i + 1) * 1024],
                       vtok.t[:, i * 1024:(i + 1) * 1024], reads=[vtok.bufs[i]])



def build_l0_attn(debug=False):
    nc = bass.Bass("TRN2", target_bir_lowering=False)
    gr = G(nc)
    consts = host_consts()
    x_d = nc.dram_tensor("xT", [D, T], F32, kind="ExternalInput")
    wq_d = nc.dram_tensor("wq_l", [128, DC * D], F32, kind="ExternalInput")
    wk_d = nc.dram_tensor("wk_l", [128, DC * D], F32, kind="ExternalInput")
    wv_d = nc.dram_tensor("wv_l", [128, DC * D], F32, kind="ExternalInput")
    cos_d = nc.dram_tensor("rope_cos", [NSL * 128, 4 * SL], F32, kind="ExternalInput")
    sin_d = nc.dram_tensor("rope_sin", [NSL * 128, 4 * SL], F32, kind="ExternalInput")
    o_d = nc.dram_tensor("o0T", [D, T], F32, kind="ExternalOutput")
    dr = {"xT": x_d, "wq_l": wq_d, "wk_l": wk_d, "wv_l": wv_d, "rope_cos": cos_d, "rope_sin": sin_d, "o0T": o_d}
    with contextlib.ExitStack() as ctx0:
        cells = Cells(nc, ctx0, gr)
        emit_l0_attn(nc, gr, consts, cells, dr, debug=debug)
        gr.finalize_and_emit(ctx0)
    return nc, consts


def emit_l0_ffn(nc, gr, consts, cells, dr, debug=False, xn_shared=None):
    if debug:
        dbgf = {}
        for nm, shp in [("xr", [128, DC * SL]), ("xk", [128, DC * SL]), ("r", [128, DC * SL]),
                        ("kact", [128, DC * SL]), ("xn", [128, DC * (T + 1)])]:
            dbgf[nm] = nc.dram_tensor("dbgf_" + nm, shp, F32, kind="ExternalOutput")
    x_d = dr["xT"]
    o_d = dr["o0T"]
    wr_d = dr["wr_l"]
    wk_d = dr["wk2_l"]
    wv_d = dr["wv2_l"]
    mur_d = dr["mu_r"]
    muk_d = dr["mu_k"]
    x1_d = dr["x1T"]
    ctx = contextlib.ExitStack()
    with ctx:
        C = const_tensors(nc, gr, ctx, consts, {"ones_col": ["bf16"], "ones_row": ["f32"], "allones": ["bf16"]})
        xn = xn_shared if xn_shared is not None else Tile(nc, ctx, "xn", [128, DC * (T + 1)], BF16)
        wR = Tile(nc, ctx, "wR", [128, DC * 1024], BF16)
        wK = Tile(nc, ctx, "wK", [128, DC * 1024], BF16)
        wV = Tile(nc, ctx, "wV", [128, DC * 1024], BF16)
        mur = Tile(nc, ctx, "mur", [128, DC], F32)
        muk = Tile(nc, ctx, "muk", [128, DC], F32)
        t1 = Tile(nc, ctx, "t1", [128, SL], F32)
        if xn_shared is None:
            xbf = Tile(nc, ctx, "xbf", [128, DC * SL], BF16, nbufs=DC)
            sq = Tile(nc, ctx, "sqs", [128, SL], BF16)
            t2 = Tile(nc, ctx, "t2", [128, SL], F32)
            mb_t = Tile(nc, ctx, "mb_t", [128, SL], F32)
            rb_t = Tile(nc, ctx, "rb_t", [128, SL], F32)
        xr = Tile(nc, ctx, "xr", [128, DC * SL], BF16)
        xk = Tile(nc, ctx, "xk", [128, DC * SL], BF16)
        rbuf = Tile(nc, ctx, "rbuf", [128, DC * SL], BF16)
        kact = Tile(nc, ctx, "kact", [128, DC * SL], BF16)
        relu = Tile(nc, ctx, "relu", [128, SL], BF16)
        xf32 = Tile(nc, ctx, "xf32", [128, DC * SL], F32, nbufs=DC)
        of32 = Tile(nc, ctx, "of32", [128, DC * SL], F32, nbufs=DC)
        x1st = Tile(nc, ctx, "x1st", [128, DC * SL], F32)

        ps_s = Psum(nc, ctx, "ps_s", [128, 2048], F32)
        ps_b = Psum(nc, ctx, "ps_b", [128, 1024], F32)
        ps_m = Psum(nc, ctx, "ps_m", [128, 1024], F32)

        if xn_shared is None:
            for c in range(DC):
                gr.op("vector", lambda e, c=c: e.memset(xn.t[:, c * (T + 1):c * (T + 1) + 1], 0.0),
                      writes=[xn.bufs[0]])
            emit_ln_stats_and_apply(nc, gr, C, x_d, xbf, xn, ps_b, sq, t1, t2, mb_t, rb_t, cells)

        def load_w(dst, src):
            gr.dma("gpsimd", dst.t[:, :], src[:, :], writes=[dst.buf])
        load_w(wR, wr_d)
        load_w(wK, wk_d)
        load_w(wV, wv_d)
        for c in range(DC):
            gr.dma("sync", mur.t[:, c:c + 1], mur_d[c * 128:(c + 1) * 128, :], writes=[mur.buf])
            gr.dma("sync", muk.t[:, c:c + 1], muk_d[c * 128:(c + 1) * 128, :], writes=[muk.buf])

        for sl in range(NSL):
            for c in range(DC):
                lo, hi = fm_cols(c, sl * SL, (sl + 1) * SL, T + 1, pad=1)
                cur = xn.t[:, lo:hi]
                prev = xn.t[:, lo - 1:hi - 1]
                gr.op("vector", lambda e, cur=cur, prev=prev: e.tensor_tensor(t1.t[:, :], cur, prev, ALU.subtract),
                      reads=[xn.bufs[0]], writes=[t1.buf])
                gr.op("vector", lambda e, c=c, prev=prev: e.scalar_tensor_tensor(
                    xr.t[:, c * SL:(c + 1) * SL], t1.t[:, :], mur.t[:, c:c + 1], prev, ALU.mult, ALU.add),
                      reads=[t1.buf, mur.buf, xn.bufs[0]], writes=[xr.buf])
                gr.op("vector", lambda e, c=c, prev=prev: e.scalar_tensor_tensor(
                    xk.t[:, c * SL:(c + 1) * SL], t1.t[:, :], muk.t[:, c:c + 1], prev, ALU.mult, ALU.add),
                      reads=[t1.buf, muk.buf, xn.bufs[0]], writes=[xk.buf])
            for j in range(DC):
                apR, bufR = ps_s.slice((j % 4) * 512, (j % 4) * 512 + 512)
                for c in range(DC):
                    gr.op("tensor", lambda e, c=c, j=j, ap=apR:
                          e.matmul(ap, wR.t[:, c * 1024 + 128 * j: c * 1024 + 128 * j + 128],
                                   xr.t[:, c * SL:(c + 1) * SL], start=(c == 0), stop=(c == DC - 1)),
                          reads=[wR.buf, xr.buf], writes=bufR)
                gr.op("scalar", lambda e, j=j, ap=apR: e.activation(rbuf.t[:, j * SL:(j + 1) * SL], ap, AF.Sigmoid),
                      reads=bufR, writes=[rbuf.buf])
            for j in range(DC):
                apK, bufK = ps_s.slice((j % 4) * 512, (j % 4) * 512 + 512)
                for c in range(DC):
                    gr.op("tensor", lambda e, c=c, j=j, ap=apK:
                          e.matmul(ap, wK.t[:, c * 1024 + 128 * j: c * 1024 + 128 * j + 128],
                                   xk.t[:, c * SL:(c + 1) * SL], start=(c == 0), stop=(c == DC - 1)),
                          reads=[wK.buf, xk.buf], writes=bufK)
                gr.op("scalar", lambda e, j=j, ap=apK: e.activation(relu.t[:, :], ap, AF.Relu),
                      reads=bufK, writes=[relu.buf])
                gr.op("vector", lambda e, j=j, ap=apK: e.tensor_tensor(kact.t[:, j * SL:(j + 1) * SL],
                                                                       relu.t[:, :], ap, ALU.mult),
                      reads=[relu.buf] + bufK, writes=[kact.buf])
            for c in range(DC):
                gr.dma("sync", xf32.t[:, c * SL:(c + 1) * SL],
                       x_d[c * 128:(c + 1) * 128, sl * SL:(sl + 1) * SL], writes=[xf32.bufs[c]])
                gr.dma("scalar", of32.t[:, c * SL:(c + 1) * SL],
                       o_d[c * 128:(c + 1) * 128, sl * SL:(sl + 1) * SL], writes=[of32.bufs[c]])
            if debug and sl == NSL - 1:
                gr.dma("gpsimd", dbgf["xr"][:, :], xr.t[:, :], reads=[xr.buf])
                gr.dma("gpsimd", dbgf["xk"][:, :], xk.t[:, :], reads=[xk.buf])
                gr.dma("gpsimd", dbgf["r"][:, :], rbuf.t[:, :], reads=[rbuf.buf])
                gr.dma("gpsimd", dbgf["kact"][:, :], kact.t[:, :], reads=[kact.buf])
                gr.dma("gpsimd", dbgf["xn"][:, :], xn.t[:, :], reads=[xn.bufs[0]])
            for j in range(DC):
                apV, bufV = ps_m.slice((j % 2) * 512, (j % 2) * 512 + 512)
                for c in range(DC):
                    gr.op("tensor", lambda e, c=c, j=j, ap=apV:
                          e.matmul(ap, wV.t[:, c * 1024 + 128 * j: c * 1024 + 128 * j + 128],
                                   kact.t[:, c * SL:(c + 1) * SL], start=(c == 0), stop=(c == DC - 1)),
                          reads=[wV.buf, kact.buf], writes=bufV)
                gr.op("vector", lambda e, j=j, ap=apV: e.tensor_tensor(t1.t[:, :], rbuf.t[:, j * SL:(j + 1) * SL],
                                                                       ap, ALU.mult),
                      reads=[rbuf.buf] + bufV, writes=[t1.buf])
                gr.op("vector", lambda e, j=j: e.tensor_tensor(x1st.t[:, j * SL:(j + 1) * SL],
                                                               xf32.t[:, j * SL:(j + 1) * SL],
                                                               of32.t[:, j * SL:(j + 1) * SL], ALU.add),
                      reads=[xf32.bufs[j], of32.bufs[j]], writes=[x1st.buf])
                gr.op("vector", lambda e, j=j: e.tensor_tensor(x1st.t[:, j * SL:(j + 1) * SL],
                                                               x1st.t[:, j * SL:(j + 1) * SL],
                                                               t1.t[:, :], ALU.add),
                      reads=[t1.buf, x1st.buf], writes=[x1st.buf])
                gr.dma("sync", x1_d[j * 128:(j + 1) * 128, sl * SL:(sl + 1) * SL],
                       x1st.t[:, j * SL:(j + 1) * SL], reads=[x1st.buf])




def build_l0_ffn(debug=False):
    nc = bass.Bass("TRN2", target_bir_lowering=False)
    gr = G(nc)
    consts = host_consts()
    x_d = nc.dram_tensor("xT", [D, T], F32, kind="ExternalInput")
    o_d = nc.dram_tensor("o0T", [D, T], F32, kind="ExternalInput")
    wr_d = nc.dram_tensor("wr_l", [128, DC * D], F32, kind="ExternalInput")
    wk_d = nc.dram_tensor("wk2_l", [128, DC * D], F32, kind="ExternalInput")
    wv_d = nc.dram_tensor("wv2_l", [128, DC * D], F32, kind="ExternalInput")
    mur_d = nc.dram_tensor("mu_r", [D, 1], F32, kind="ExternalInput")
    muk_d = nc.dram_tensor("mu_k", [D, 1], F32, kind="ExternalInput")
    x1_d = nc.dram_tensor("x1T", [D, T], F32, kind="ExternalOutput")
    dr = {"xT": x_d, "o0T": o_d, "wr_l": wr_d, "wk2_l": wk_d, "wv2_l": wv_d, "mu_r": mur_d, "mu_k": muk_d, "x1T": x1_d}
    with contextlib.ExitStack() as ctx0:
        cells = Cells(nc, ctx0, gr)
        emit_l0_ffn(nc, gr, consts, cells, dr, debug=debug)
        gr.finalize_and_emit(ctx0)
    return nc


def emit_l1_attn(nc, gr, consts, cells, dr, debug=False, LS=512):
    x1_d = dr["x1T"]
    wq_d = dr["Wq"]
    wk_d = dr["Wk"]
    wv_d = dr["Wv"]
    wg_d = dr["Wg"]
    wo_d = dr["Wo"]
    w2_d = dr["W2"]
    wa_d = dr["Wa"]
    wb_d = dr["Wb"]
    cq_d = dr["conv_q"]
    ck_d = dr["conv_k"]
    cv_d = dr["conv_v"]
    ln1g_d = dr["ln1_g"]
    ln1b_d = dr["ln1_b"]
    ln2g_d = dr["ln2_g"]
    ln2b_d = dr["ln2_b"]
    rmsw_d = dr["rms_in_w"]
    b2_d = dr["b2"]
    onw_d = dr["o_norm_w"]
    dtb_d = dr["dt_bias"]
    alog_d = dr["A_log"]
    y_d = dr["yT"]
    if debug:
        dbg = {}
        for nm, shp in [("h", [128, DC * 256]), ("qn", [128, DC * 256]), ("kn", [128, DC * 256]),
                        ("vn", [128, 16 * 256]), ("gb", [128, 8]), ("bb", [128, 8]),
                        ("btok", [128, 4]), ("Ot", [128, 2048]), ("U", [128, 2048]),
                        ("xa", [128, DC * 256]), ("ofm", [128, 16 * 256]),
                        ("Dlow", [128, 512]), ("DlowT", [128, 512]), ("DincT", [128, 512]),
                        ("Lb", [128, 512]), ("Ltb", [128, 512]), ("Rfin", [128, 512]),
                        ("AttT", [128, 512]), ("Wt", [128, 2048]), ("Vtok", [128, 2048]),
                        ("Snew", [128, 4096]), ("bT", [1, 512]), ("ogc", [128, 2048]),
                        ("gsil", [128, 2048])]:
            dbg[nm] = nc.dram_tensor("dbg_" + nm, shp, F32, kind="ExternalOutput")

    NCH = LS // 128
    LNS = T // LS
    RP = LS + 3

    ctx = contextlib.ExitStack()
    with ctx:
        C = const_tensors(nc, gr, ctx, consts, {"ident": ["bf16", "f32"], "ones_col": ["bf16"], "ones_row": ["f32", "bf16"], "neg_ones_row": ["f32"], "tri_incl": ["bf16"], "mask1": ["f32"], "mask2": ["f32"], "mask3": ["f32"], "allones": ["bf16"]})
        x1bf = Tile(nc, ctx, "x1bf", [128, DC * LS], BF16, nbufs=DC)
        xn1 = Tile(nc, ctx, "xn1", [128, DC * LS], BF16, nbufs=DC)
        a1t = Tile(nc, ctx, "a1t", [128, LS], BF16)
        h = Tile(nc, ctx, "h", [128, DC * LS], BF16, nbufs=DC)
        qraw = Tile(nc, ctx, "qraw", [128, DC * RP], BF16)
        kraw = Tile(nc, ctx, "kraw", [128, DC * RP], BF16)
        vraw = Tile(nc, ctx, "vraw", [128, 16 * RP], BF16)
        qn = Tile(nc, ctx, "qn", [128, DC * LS], BF16)
        kn = Tile(nc, ctx, "kn", [128, DC * LS], BF16)
        vn = Tile(nc, ctx, "vn", [128, 16 * LS], BF16)
        gate_sil = Tile(nc, ctx, "gate_sil", [128, (NCH // 2) * 2048], BF16)
        o_fm = Tile(nc, ctx, "o_fm", [128, 16 * LS], BF16, alias=vraw)
        h2t = Tile(nc, ctx, "h2t", [128, DC * LS], BF16, alias=x1bf)
        WPN = 4
        wps = [Tile(nc, ctx, f"wp{i}", [128, 2048], BF16) for i in range(WPN)]
        wab = Tile(nc, ctx, "wab", [128, DC * 8], BF16)
        cwq = Tile(nc, ctx, "cwq", [128, DC * 4], F32)
        cwk = Tile(nc, ctx, "cwk", [128, DC * 4], F32)
        cwv = Tile(nc, ctx, "cwv", [128, 16 * 4], F32)
        dtb_row = Tile(nc, ctx, "dtb_row", [1, 4], BF16)
        alog_row = Tile(nc, ctx, "alog_row", [1, 4], F32)
        negexpA = Tile(nc, ctx, "negexpA", [128, 4], F32)
        onw_row = Tile(nc, ctx, "onw_row", [1, Vd], F32)
        normw_bc = Tile(nc, ctx, "normw_bc", [128, Vd], BF16)
        vcols = {}
        for nm in ["ln1g", "ln1b", "ln2g", "ln2b", "rmsw", "b2a"]:
            vcols[nm] = Tile(nc, ctx, "vc_" + nm, [128, DC], F32)
        g_bf = Tile(nc, ctx, "g_bf", [128, NCH * 4], BF16)
        beta_bf = Tile(nc, ctx, "beta_bf", [128, NCH * 4], BF16)
        beta_f = Tile(nc, ctx, "beta_f", [128, NCH * 4], F32)
        nbeta_f = Tile(nc, ctx, "nbeta_f", [128, NCH * 4], F32)
        sc4a = Tile(nc, ctx, "sc4a", [128, 4], F32)
        sc4b = Tile(nc, ctx, "sc4b", [128, 4], F32)
        sc4c = Tile(nc, ctx, "sc4c", [128, 4], F32)
        b_tok = Tile(nc, ctx, "b_tok", [128, 4], F32)
        Acol = Tile(nc, ctx, "Acol", [128, 4], F32)
        aC_col = Tile(nc, ctx, "aC_col", [128, 4], F32)
        bC_col = Tile(nc, ctx, "bC_col", [128, 4], F32)
        ebar = Tile(nc, ctx, "ebar", [128, 4], F32)
        btT_sb = Tile(nc, ctx, "btT_sb", [1, 512], BF16)
        Dlow = Tile(nc, ctx, "Dlow", [128, 512], BF16)
        DlowT = Tile(nc, ctx, "DlowT", [128, 512], BF16)
        DincT = Tile(nc, ctx, "DincT", [128, 512], BF16)
        zs = Tile(nc, ctx, "zs", [128, 512], F32)
        Lb = Tile(nc, ctx, "Lb", [128, 512], BF16)
        Ltb = Tile(nc, ctx, "Ltb", [128, 512], BF16)
        Apw = [Tile(nc, ctx, f"Apw{i}", [128, 512], BF16) for i in range(2)]
        Atw = [Tile(nc, ctx, f"Atw{i}", [128, 512], BF16) for i in range(2)]
        Rw = [Tile(nc, ctx, f"Rw{i}", [128, 512], BF16) for i in range(2)]
        AttT = Tile(nc, ctx, "AttT", [128, 512], BF16)
        Vtok = Tile(nc, ctx, "Vtok", [128, 2048], BF16)
        Ktok = Tile(nc, ctx, "Ktok", [128, 1024], BF16)
        Kbar = Tile(nc, ctx, "Kbar", [128, 1024], BF16)
        Wt = Tile(nc, ctx, "Wt", [128, 2048], BF16)
        W1t = Tile(nc, ctx, "W1t", [128, 512], F32)
        Ut = Tile(nc, ctx, "Ut", [128, 2048], BF16)
        attu = Tile(nc, ctx, "attu", [128, 512], BF16)
        O_t = Tile(nc, ctx, "O_t", [128, 2048], BF16)
        ogc = Tile(nc, ctx, "ogc", [128, 2048], BF16, alias=O_t)
        ostd = Tile(nc, ctx, "ostd", [128, 4], F32)
        Sb = [Tile(nc, ctx, f"Sb{i}", [128, DC * SL], BF16) for i in range(2)]
        ca = Tile(nc, ctx, "ca", [128, LS], BF16)
        cb = Tile(nc, ctx, "cb", [128, LS], BF16)
        m0b = Tile(nc, ctx, "m0b", [128, LS], F32)
        r0b = Tile(nc, ctx, "r0b", [128, LS], F32)
        r01b = Tile(nc, ctx, "r01b", [128, LS], F32)
        vbb = Tile(nc, ctx, "vbb", [128, LS], F32)
        lnb = Tile(nc, ctx, "lnb", [128, LS], F32)
        bT_sb = Tile(nc, ctx, "bT_sb", [1, 512], F32)
        sq = Tile(nc, ctx, "sqs", [128, LS], BF16)
        t1 = Tile(nc, ctx, "t1", [128, LS], F32)
        t2 = Tile(nc, ctx, "t2", [128, LS], F32)
        x1f = Tile(nc, ctx, "x1f", [128, 2 * LS], F32, nbufs=2)
        yst = Tile(nc, ctx, "yst", [128, 2 * LS], F32, nbufs=2)
        gelc = Tile(nc, ctx, "gelc", [128, LS], BF16, alias=ca)

        P1 = Psum(nc, ctx, "P1", [128, 512], F32)
        P2 = Psum(nc, ctx, "P2", [128, 512], F32)
        P3 = Psum(nc, ctx, "P3", [128, 1024], F32)
        P4 = Psum(nc, ctx, "P4", [128, 1024], F32)
        P5 = Psum(nc, ctx, "P5", [128, 512], F32)
        PT = Psum(nc, ctx, "PT", [128, 1024], BF16)

        ID_BF = C[("ident", "bf16")]
        ID_F = C[("ident", "f32")]
        OC_BF = C[("ones_col", "bf16")]
        OR_F = C[("ones_row", "f32")]
        OR_BF = C[("ones_row", "bf16")]
        NOR_F = C[("neg_ones_row", "f32")]
        TRI = C[("tri_incl", "bf16")]
        MASKS = [C[("mask1", "f32")], C[("mask2", "f32")], C[("mask3", "f32")]]
        eps5, eps5_b = cells.cell(LN_EPS)
        eps5c, eps5c_b = cells.col(LN_EPS)
        eps6, eps6_b = cells.cell(EPS_L2)
        eps6c, eps6c_b = cells.col(EPS_L2)
        lnq, lnq_b = cells.col(float(-np.log(16.0)))
        AO_BF = C[("allones", "bf16")]

        for c in range(DC):
            gr.dma("gpsimd", wab.t[:, c * 8:c * 8 + 4], wa_d[c * 128:(c + 1) * 128, :], writes=[wab.buf])
            gr.dma("gpsimd", wab.t[:, c * 8 + 4:c * 8 + 8], wb_d[c * 128:(c + 1) * 128, :], writes=[wab.buf])
            gr.dma("sync", cwq.t[:, c * 4:(c + 1) * 4], cq_d[c * 128:(c + 1) * 128, :], writes=[cwq.buf])
            gr.dma("sync", cwk.t[:, c * 4:(c + 1) * 4], ck_d[c * 128:(c + 1) * 128, :], writes=[cwk.buf])
        for c in range(16):
            gr.dma("sync", cwv.t[:, c * 4:(c + 1) * 4], cv_d[c * 128:(c + 1) * 128, :], writes=[cwv.buf])
        for nm, dsrc in [("ln1g", ln1g_d), ("ln1b", ln1b_d), ("ln2g", ln2g_d),
                         ("ln2b", ln2b_d), ("rmsw", rmsw_d), ("b2a", b2_d)]:
            for c in range(DC):
                gr.dma("sync", vcols[nm].t[:, c:c + 1], dsrc[c * 128:(c + 1) * 128, :], writes=[vcols[nm].buf])
        gr.dma("gpsimd", dtb_row.t[:, :], dtb_d[:, :], writes=[dtb_row.buf])
        gr.dma("sync", alog_row.t[:, :], alog_d[:, :], writes=[alog_row.buf])
        gr.dma("sync", onw_row.t[:, :], onw_d[:, :], writes=[onw_row.buf])
        gr.op("scalar", lambda e: e.activation(bT_sb.t[0:1, 0:4], alog_row.t[:, :], AF.Exp),
              reads=[alog_row.buf], writes=[bT_sb.buf])
        gr.op("vector", lambda e: e.tensor_scalar(bT_sb.t[0:1, 4:8], bT_sb.t[0:1, 0:4], -1.0, None, ALU.mult),
              reads=[bT_sb.buf], writes=[bT_sb.buf])
        apN, bufN = P5.slice(0, 4)
        gr.op("tensor", lambda e, ap=apN: e.matmul(ap, OR_F.t[:, :], bT_sb.t[0:1, 4:8], start=True, stop=True),
              reads=[bT_sb.buf, OR_F.buf], writes=bufN)
        gr.op("vector", lambda e, ap=apN: e.tensor_copy(negexpA.t[:, :], ap),
              reads=bufN, writes=[negexpA.buf])
        apW, bufW = P5.slice(0, Vd)
        gr.op("tensor", lambda e, ap=apW: e.matmul(ap, OR_F.t[:, :], onw_row.t[:, :], start=True, stop=True),
              reads=[onw_row.buf, OR_F.buf], writes=bufW)
        gr.op("vector", lambda e, ap=apW: e.tensor_copy(normw_bc.t[:, :], ap),
              reads=bufW, writes=[normw_bc.buf])
        for i in range(2):
            gr.op("vector", lambda e, i=i: e.memset(Sb[i].t[:, :], 0.0), writes=[Sb[i].buf])
        for c in range(DC):
            gr.op("vector", lambda e, c=c: e.memset(qraw.t[:, c * RP:c * RP + 3], 0.0), writes=[qraw.buf])
            gr.op("vector", lambda e, c=c: e.memset(kraw.t[:, c * RP:c * RP + 3], 0.0), writes=[kraw.buf])
        for c in range(16):
            gr.op("vector", lambda e, c=c: e.memset(vraw.t[:, c * RP:c * RP + 3], 0.0), writes=[vraw.buf])

        piece_idx = [0]

        def wpiece():
            wp = wps[piece_idx[0] % WPN]
            piece_idx[0] += 1
            return wp

        def load_piece(wdram, pidx):
            wp = wpiece()
            gr.dma("gpsimd", wp.t[:, :], wdram[pidx * 128:(pidx + 1) * 128, :], writes=[wp.buf])
            return wp

        def psum_pick(j):
            return (P1, P2)[j % 2]

        for sl in range(LNS):
            tsl = slice(sl * LS, (sl + 1) * LS)
            for c in range(DC):
                gr.dma("gpsimd", x1bf.t[:, c * LS:(c + 1) * LS],
                       x1_d[c * 128:(c + 1) * 128, sl * LS:(sl + 1) * LS], writes=[x1bf.bufs[c]])
            sum_ap, sum_bufs = P3.slice(0, LS)
            sq_ap, sq_bufs = P3.slice(512, 512 + LS)
            for c in range(DC):
                gr.op("tensor", lambda e, c=c, ap=sum_ap: e.matmul(ap, AO_BF.t[:, :],
                                                                   x1bf.t[:, c * LS:(c + 1) * LS],
                                                                   start=(c == 0), stop=(c == DC - 1)),
                      reads=[x1bf.bufs[c], AO_BF.buf], writes=sum_bufs)
            for c in range(DC):
                gr.op("vector", lambda e, c=c: e.tensor_mul(sq.t[:, :], x1bf.t[:, c * LS:(c + 1) * LS],
                                                            x1bf.t[:, c * LS:(c + 1) * LS]),
                      reads=[x1bf.bufs[c]], writes=[sq.buf])
                gr.op("tensor", lambda e, c=c, ap=sq_ap: e.matmul(ap, AO_BF.t[:, :], sq.t[:, :],
                                                                  start=(c == 0), stop=(c == DC - 1)),
                      reads=[sq.buf, AO_BF.buf], writes=sq_bufs)
            gr.op("vector", lambda e, ap=sum_ap: e.tensor_scalar(m0b.t[:, :], ap, 1.0 / D, None, ALU.mult),
                  reads=sum_bufs, writes=[m0b.buf])
            gr.op("vector", lambda e: e.tensor_mul(t1.t[:, :], m0b.t[:, :], m0b.t[:, :]),
                  reads=[m0b.buf], writes=[t1.buf])
            gr.op("vector", lambda e, ap=sq_ap: e.scalar_tensor_tensor(vbb.t[:, :], ap, 1.0 / D,
                                                                       t1.t[:, :], ALU.mult, ALU.subtract),
                  reads=sq_bufs + [t1.buf], writes=[vbb.buf])
            gr.op("scalar", lambda e: e.activation(lnb.t[:, :], vbb.t[:, :], AF.Ln, bias=eps5c),
                  reads=[vbb.buf, eps5c_b], writes=[lnb.buf])
            gr.op("scalar", lambda e: e.activation(r0b.t[:, :], lnb.t[:, :], AF.Exp, scale=-0.5),
                  reads=[lnb.buf], writes=[r0b.buf])
            gr.op("vector", lambda e: e.tensor_mul(t1.t[:, :], r0b.t[:, :], r0b.t[:, :]),
                  reads=[r0b.buf], writes=[t1.buf])
            gr.op("vector", lambda e: e.tensor_mul(t1.t[:, :], t1.t[:, :], vbb.t[:, :]),
                  reads=[t1.buf, vbb.buf], writes=[t1.buf])
            gr.op("scalar", lambda e: e.activation(t1.t[:, :], t1.t[:, :], AF.Ln, bias=eps5c),
                  reads=[t1.buf, eps5c_b], writes=[t1.buf])
            gr.op("scalar", lambda e: e.activation(t1.t[:, :], t1.t[:, :], AF.Exp, scale=-0.5),
                  reads=[t1.buf], writes=[t1.buf])
            gr.op("vector", lambda e: e.tensor_mul(r01b.t[:, :], t1.t[:, :], r0b.t[:, :]),
                  reads=[t1.buf, r0b.buf], writes=[r01b.buf])
            for c in range(DC):
                csl = slice(c * LS, (c + 1) * LS)
                gr.op("vector", lambda e, csl=csl: e.tensor_tensor(t1.t[:, :], x1bf.t[:, csl],
                                                                   m0b.t[:, :], ALU.subtract),
                      reads=[x1bf.bufs[c], m0b.buf], writes=[t1.buf])
                gr.op("vector", lambda e, csl=csl: e.tensor_tensor(xn1.t[:, csl], t1.t[:, :],
                                                                   r0b.t[:, :], ALU.mult),
                      reads=[t1.buf, r0b.buf], writes=[xn1.bufs[c]])
                gr.op("vector", lambda e: e.tensor_tensor(t2.t[:, :], t1.t[:, :], r01b.t[:, :], ALU.mult),
                      reads=[t1.buf, r01b.buf], writes=[t2.buf])
                gr.op("vector", lambda e, c=c: e.tensor_scalar(a1t.t[:, :], t2.t[:, :],
                                                               vcols["ln1g"].t[:, c:c + 1],
                                                               vcols["ln1b"].t[:, c:c + 1],
                                                               ALU.mult, ALU.add),
                      reads=[t2.buf, vcols["ln1g"].buf, vcols["ln1b"].buf], writes=[a1t.buf])
                rs_ap, rs_bufs = P5.slice(0, LS)
                gr.op("vector", lambda e: e.tensor_mul(sq.t[:, :], a1t.t[:, :], a1t.t[:, :]),
                      reads=[a1t.buf], writes=[sq.buf])
                gr.op("tensor", lambda e, c=c, ap=rs_ap: e.matmul(ap, AO_BF.t[:, :], sq.t[:, :],
                                                                  start=(c == 0), stop=(c == DC - 1)),
                      reads=[sq.buf, AO_BF.buf], writes=rs_bufs)
                gr.op("vector", lambda e, csl=csl: e.tensor_copy(h.t[:, csl], a1t.t[:, :]),
                      reads=[a1t.buf], writes=[h.bufs[c]])
            rs_ap, rs_bufs = P5.slice(0, LS)
            gr.op("scalar", lambda e, ap=rs_ap: e.activation(lnb.t[:, :], ap, AF.Ln, bias=eps6c,
                                                             scale=1.0 / D),
                  reads=rs_bufs + [eps6c_b], writes=[lnb.buf])
            gr.op("scalar", lambda e: e.activation(lnb.t[:, :], lnb.t[:, :], AF.Exp, scale=-0.5),
                  reads=[lnb.buf], writes=[lnb.buf])
            for c in range(DC):
                csl = slice(c * LS, (c + 1) * LS)
                gr.op("vector", lambda e, csl=csl: e.tensor_tensor(t1.t[:, :], h.t[:, csl],
                                                                   lnb.t[:, :], ALU.mult),
                      reads=[h.bufs[c], lnb.buf], writes=[t1.buf])
                gr.op("vector", lambda e, c=c, csl=csl: e.tensor_scalar(h.t[:, csl], t1.t[:, :],
                                                                        vcols["rmsw"].t[:, c:c + 1], None,
                                                                        ALU.mult),
                      reads=[t1.buf, vcols["rmsw"].buf], writes=[h.bufs[c]])

            def proj(wdram, ncols, dst, dstper):
                for p in range(ncols // 256):
                    wp = load_piece(wdram, p)
                    for jj in range(2):
                        j = 2 * p + jj
                        PS = psum_pick(j)
                        ap, bufs = PS.slice(0, LS)
                        for c in range(DC):
                            gr.op("tensor", lambda e, c=c, jj=jj, wp=wp, ap=ap:
                                  e.matmul(ap, wp.t[:, c * 256 + 128 * jj:c * 256 + 128 * jj + 128],
                                           h.t[:, c * LS:(c + 1) * LS], start=(c == 0), stop=(c == DC - 1)),
                                  reads=[wp.buf, h.bufs[c]], writes=bufs)
                        lo = j * dstper + 3
                        gr.op("vector", lambda e, lo=lo, ap=ap: e.tensor_copy(dst.t[:, lo:lo + LS], ap),
                              reads=bufs, writes=[dst.buf])
            proj(wq_d, D, qraw, RP)
            proj(wk_d, D, kraw, RP)
            proj(wv_d, VAL_DIM, vraw, RP)

            for tt_ in range(NCH):
                ab_ap, ab_bufs = P5.slice(0, 8)
                for c in range(DC):
                    gr.op("tensor", lambda e, c=c, tt_=tt_, ap=ab_ap:
                          e.matmul(ap, h.t[:, c * LS + 128 * tt_:c * LS + 128 * tt_ + 128],
                                   wab.t[:, c * 8:(c + 1) * 8], start=(c == 0), stop=False),
                          reads=[h.bufs[c], wab.buf], writes=ab_bufs)
                ab4_ap, _ = P5.slice(0, 4)
                gr.op("tensor", lambda e, ap=ab4_ap: e.matmul(ap, OR_BF.t[:, :], dtb_row.t[:, :],
                                                              start=False, stop=True),
                      reads=[dtb_row.buf, OR_BF.buf], writes=ab_bufs)
                gr.op("scalar", lambda e, ap=ab4_ap: e.activation(sc4a.t[:, :], ap, AF.Abs),
                      reads=ab_bufs, writes=[sc4a.buf])
                gr.op("scalar", lambda e: e.activation(sc4b.t[:, :], sc4a.t[:, :], AF.Exp, scale=-1.0),
                      reads=[sc4a.buf], writes=[sc4b.buf])
                gr.op("scalar", lambda e: e.activation(sc4b.t[:, :], sc4b.t[:, :], AF.Ln, bias=1.0),
                      reads=[sc4b.buf], writes=[sc4b.buf])
                gr.op("scalar", lambda e, ap=ab4_ap: e.activation(sc4c.t[:, :], ap, AF.Relu),
                      reads=ab_bufs, writes=[sc4c.buf])
                gr.op("vector", lambda e: e.tensor_add(sc4a.t[:, :], sc4b.t[:, :], sc4c.t[:, :]),
                      reads=[sc4b.buf, sc4c.buf], writes=[sc4a.buf])
                gr.op("vector", lambda e, tt_=tt_: e.tensor_mul(g_bf.t[:, 4 * tt_:4 * tt_ + 4],
                                                                sc4a.t[:, :], negexpA.t[:, :]),
                      reads=[sc4a.buf, negexpA.buf], writes=[g_bf.buf])
                b_ap = P5.t[0:128, 4:8]
                gr.op("scalar", lambda e, tt_=tt_, ap=b_ap: e.activation(beta_f.t[:, 4 * tt_:4 * tt_ + 4],
                                                                         ap, AF.Sigmoid),
                      reads=ab_bufs, writes=[beta_f.buf])
                gr.op("vector", lambda e, tt_=tt_: e.tensor_copy(beta_bf.t[:, 4 * tt_:4 * tt_ + 4],
                                                                 beta_f.t[:, 4 * tt_:4 * tt_ + 4]),
                      reads=[beta_f.buf], writes=[beta_bf.buf])
                gr.op("vector", lambda e, tt_=tt_: e.tensor_scalar(nbeta_f.t[:, 4 * tt_:4 * tt_ + 4],
                                                                   beta_f.t[:, 4 * tt_:4 * tt_ + 4],
                                                                   -1.0, None, ALU.mult),
                      reads=[beta_f.buf], writes=[nbeta_f.buf])

            def conv(raw, cw, nch, dst):
                for c in range(nch):
                    base = c * RP
                    gr.op("vector", lambda e, base=base, c=c, raw=raw, cw=cw:
                          e.tensor_scalar(ca.t[:, :], raw.t[:, base:base + LS], cw.t[:, c * 4:c * 4 + 1],
                                          None, ALU.mult),
                          reads=[raw.buf, cw.buf], writes=[ca.buf])
                    gr.op("vector", lambda e, base=base, c=c, raw=raw, cw=cw:
                          e.scalar_tensor_tensor(cb.t[:, :], raw.t[:, base + 1:base + 1 + LS],
                                                 cw.t[:, c * 4 + 1:c * 4 + 2], ca.t[:, :], ALU.mult, ALU.add),
                          reads=[raw.buf, cw.buf, ca.buf], writes=[cb.buf])
                    gr.op("vector", lambda e, base=base, c=c, raw=raw, cw=cw:
                          e.scalar_tensor_tensor(ca.t[:, :], raw.t[:, base + 2:base + 2 + LS],
                                                 cw.t[:, c * 4 + 2:c * 4 + 3], cb.t[:, :], ALU.mult, ALU.add),
                          reads=[raw.buf, cw.buf, cb.buf], writes=[ca.buf])
                    gr.op("vector", lambda e, base=base, c=c, raw=raw, cw=cw:
                          e.scalar_tensor_tensor(cb.t[:, :], raw.t[:, base + 3:base + 3 + LS],
                                                 cw.t[:, c * 4 + 3:c * 4 + 4], ca.t[:, :], ALU.mult, ALU.add),
                          reads=[raw.buf, cw.buf, ca.buf], writes=[cb.buf])
                    gr.op("scalar", lambda e, c=c, dst=dst: e.activation(dst.t[:, c * LS:(c + 1) * LS],
                                                                         cb.t[:, :], AF.Silu),
                          reads=[cb.buf], writes=[dst.buf])
            conv(qraw, cwq, DC, qn)
            conv(kraw, cwk, DC, kn)
            conv(vraw, cwv, 16, vn)
            if sl < LNS - 1:
                for c in range(DC):
                    gr.op("vector", lambda e, c=c: e.tensor_copy(qraw.t[:, c * RP:c * RP + 3],
                                                                 qraw.t[:, c * RP + LS:c * RP + LS + 3]),
                          reads=[qraw.buf], writes=[qraw.buf])
                    gr.op("vector", lambda e, c=c: e.tensor_copy(kraw.t[:, c * RP:c * RP + 3],
                                                                 kraw.t[:, c * RP + LS:c * RP + LS + 3]),
                          reads=[kraw.buf], writes=[kraw.buf])
                for c in range(16):
                    gr.op("vector", lambda e, c=c: e.tensor_copy(vraw.t[:, c * RP:c * RP + 3],
                                                                 vraw.t[:, c * RP + LS:c * RP + LS + 3]),
                          reads=[vraw.buf], writes=[vraw.buf])

            for cc0 in range(0, NCH, 2):
                for p in range(VAL_DIM // 256):
                    wp = load_piece(wg_d, p)
                    for jj in range(2):
                        jhv = 2 * p + jj
                        PS = psum_pick(jj)
                        for cc in range(cc0, cc0 + 2):
                            ap = PS.t[0:128, 128 * (cc - cc0):128 * (cc - cc0) + 128]
                            _, bufs = PS.slice(128 * (cc - cc0), 128 * (cc - cc0) + 128)
                            for c in range(DC):
                                gr.op("tensor", lambda e, c=c, cc=cc, jj=jj, wp=wp, ap=ap:
                                      e.matmul(ap, h.t[:, c * LS + 128 * cc:c * LS + 128 * cc + 128],
                                               wp.t[:, c * 256 + 128 * jj:c * 256 + 128 * jj + 128],
                                               start=(c == 0), stop=(c == DC - 1)),
                                      reads=[wp.buf, h.bufs[c]], writes=bufs)
                            gr.op("scalar", lambda e, cc=cc, cc0=cc0, jhv=jhv, ap=ap:
                                  e.activation(gate_sil.t[:, (cc - cc0) * 2048 + jhv * 128: (cc - cc0) * 2048 + jhv * 128 + 128],
                                               ap, AF.Silu),
                                  reads=bufs, writes=[gate_sil.buf])

                if cc0 == 0:
                    def l2n(dst, qscale):
                        for hh in range(H):
                            rs_ap, rs_bufs = P2.slice(0, LS)
                            for kc in range(2):
                                csl = slice((2 * hh + kc) * LS, (2 * hh + kc + 1) * LS)
                                gr.op("vector", lambda e, csl=csl, dst=dst: e.tensor_mul(sq.t[:, :], dst.t[:, csl],
                                                                                         dst.t[:, csl]),
                                      reads=[dst.buf], writes=[sq.buf])
                                gr.op("tensor", lambda e, kc=kc, ap=rs_ap: e.matmul(ap, AO_BF.t[:, :], sq.t[:, :],
                                                                                    start=(kc == 0), stop=(kc == 1)),
                                      reads=[sq.buf, AO_BF.buf], writes=rs_bufs)
                            gr.op("scalar", lambda e, ap=rs_ap: e.activation(lnb.t[:, :], ap, AF.Ln, bias=eps6c),
                                  reads=rs_bufs + [eps6c_b], writes=[lnb.buf])
                            if qscale:
                                gr.op("scalar", lambda e: e.activation(lnb.t[:, :], lnb.t[:, :], AF.Exp,
                                                                       scale=-0.5, bias=lnq),
                                      reads=[lnb.buf, lnq_b], writes=[lnb.buf])
                            else:
                                gr.op("scalar", lambda e: e.activation(lnb.t[:, :], lnb.t[:, :], AF.Exp, scale=-0.5),
                                      reads=[lnb.buf], writes=[lnb.buf])
                            for kc in range(2):
                                csl = slice((2 * hh + kc) * LS, (2 * hh + kc + 1) * LS)
                                gr.op("vector", lambda e, csl=csl, dst=dst:
                                      e.tensor_tensor(dst.t[:, csl], dst.t[:, csl], lnb.t[:, :], ALU.mult),
                                      reads=[dst.buf, lnb.buf], writes=[dst.buf])
                    l2n(qn, True)
                    l2n(kn, False)
                for cc in (cc0, cc0 + 1):
                    gsl = slice(4 * cc, 4 * cc + 4)
                    bt_ap, bt_bufs = P1.slice(0, 4)
                    gr.op("tensor", lambda e, gsl=gsl, ap=bt_ap: e.matmul(ap, TRI.t[:, :], g_bf.t[:, gsl],
                                                                          start=True, stop=True),
                          reads=[g_bf.buf, TRI.buf], writes=bt_bufs)
                    gr.op("vector", lambda e, ap=bt_ap: e.tensor_copy(b_tok.t[:, :], ap),
                          reads=bt_bufs, writes=[b_tok.buf])
                    gr.op("scalar", lambda e: e.activation(Acol.t[:, :], b_tok.t[:, :], AF.Exp),
                          reads=[b_tok.buf], writes=[Acol.buf])
                    for hh in range(H):
                        tp_ap = P5.t[0:1, 128 * hh:128 * hh + 128]
                        _, tp_bufs = P5.slice(128 * hh, 128 * hh + 128)
                        gr.op("tensor", lambda e, hh=hh, ap=tp_ap: e.transpose(ap, b_tok.t[:, hh:hh + 1],
                                                                               ID_F.t[:, :]),
                              reads=[b_tok.buf, ID_F.buf], writes=tp_bufs)
                    ball_ap, ball_bufs = P5.slice(0, 512, 0, 1)
                    gr.op("vector", lambda e, ap=ball_ap: e.tensor_copy(bT_sb.t[:, :], ap),
                          reads=ball_bufs, writes=[bT_sb.buf])
                    bC_ap, bC_bufs = P5.slice(0, 4)
                    lastv = bass.AP(bT_sb.t, 127, [[512, 1], [128, 4]])
                    gr.op("tensor", lambda e, ap=bC_ap, lastv=lastv: e.matmul(ap, OR_F.t[:, :], lastv,
                                                                              start=True, stop=True),
                          reads=[bT_sb.buf, OR_F.buf], writes=bC_bufs)
                    gr.op("vector", lambda e, ap=bC_ap: e.tensor_copy(bC_col.t[:, :], ap),
                          reads=bC_bufs, writes=[bC_col.buf])
                    gr.op("scalar", lambda e: e.activation(aC_col.t[:, :], bC_col.t[:, :], AF.Exp),
                          reads=[bC_col.buf], writes=[aC_col.buf])
                    for hh in range(H):
                        gr.op("scalar", lambda e, hh=hh: e.activation(ebar.t[:, hh:hh + 1], b_tok.t[:, hh:hh + 1],
                                                                      AF.Exp, scale=-1.0,
                                                                      bias=bC_col.t[:, hh:hh + 1]),
                              reads=[b_tok.buf, bC_col.buf], writes=[ebar.buf])
                    zp_ap, zp_bufs = P2.slice(0, 512)
                    for hh in range(H):
                        blk = P2.t[0:128, 128 * hh:128 * hh + 128]
                        bseg = bT_sb.t[0:1, 128 * hh:128 * hh + 128]
                        gr.op("tensor", lambda e, blk=blk, bseg=bseg, hh=hh:
                              e.matmul(blk, bseg, OR_F.t[:, :], start=(hh == 0), stop=False),
                              reads=[bT_sb.buf, OR_F.buf], writes=zp_bufs)
                    gr.op("tensor", lambda e, ap=zp_ap: e.matmul(ap, NOR_F.t[:, :], bT_sb.t[:, :],
                                                                 start=False, stop=True),
                          reads=[bT_sb.buf, NOR_F.buf], writes=zp_bufs)
                    for Dt, mk, sc in [(Dlow, MASKS[0], 1.0), (DlowT, MASKS[1], -1.0), (DincT, MASKS[2], -1.0)]:
                        gr.op("vector", lambda e, mk=mk, ap=zp_ap: e.tensor_tensor(zs.t[:, :], ap, mk.t[:, :],
                                                                                   ALU.add),
                              reads=zp_bufs + [mk.buf], writes=[zs.buf])
                        gr.op("scalar", lambda e, Dt=Dt, sc=sc: e.activation(Dt.t[:, :], zs.t[:, :], AF.Exp,
                                                                             scale=sc),
                              reads=[zs.buf], writes=[Dt.buf])
                    g_ap, g_bufs = P1.slice(0, 512)
                    for hh in range(H):
                        blk = P1.t[0:128, 128 * hh:128 * hh + 128]
                        for kc in range(2):
                            kcol = slice((2 * hh + kc) * LS + 128 * cc, (2 * hh + kc) * LS + 128 * cc + 128)
                            gr.op("tensor", lambda e, blk=blk, kcol=kcol, hh=hh, kc=kc:
                                  e.matmul(blk, kn.t[:, kcol], kn.t[:, kcol],
                                           start=(hh == 0 and kc == 0), stop=False),
                                  reads=[kn.buf], writes=g_bufs)
                    bcolv = bass.AP(beta_bf.t, 4 * cc, [[NCH * 4, 128], [1, 4], [0, 128]])
                    gr.op("vector", lambda e, ap=g_ap: e.tensor_tensor(Lb.t[:, :], ap, Dlow.t[:, :], ALU.mult),
                          reads=g_bufs + [Dlow.buf], writes=[Lb.buf])
                    gr.op("vector", lambda e, bcolv=bcolv: e.tensor_tensor(
                        Lb.t[:, :].rearrange("p (a b) -> p a b", a=4), Lb.t[:, :].rearrange("p (a b) -> p a b", a=4),
                        bcolv, ALU.mult),
                          reads=[Lb.buf, beta_bf.buf], writes=[Lb.buf])
                    for hh in range(H):
                        tp_ap = P5.t[0:1, 128 * hh:128 * hh + 128]
                        _, tp_bufs = P5.slice(128 * hh, 128 * hh + 128)
                        gr.op("tensor", lambda e, hh=hh, cc=cc, ap=tp_ap:
                              e.matmul(ap, beta_bf.t[:, 4 * cc + hh:4 * cc + hh + 1], ID_BF.t[:, :],
                                       start=True, stop=True),
                              reads=[beta_bf.buf, ID_BF.buf], writes=tp_bufs)
                    btall_ap, btall_bufs = P5.slice(0, 512, 0, 1)
                    gr.op("vector", lambda e, ap=btall_ap: e.tensor_copy(btT_sb.t[:, :], ap),
                          reads=btall_bufs, writes=[btT_sb.buf])
                    brow_ap, brow_bufs = P4.slice(0, 512)
                    gr.op("tensor", lambda e, ap=brow_ap: e.matmul(ap, OR_BF.t[:, :], btT_sb.t[:, :],
                                                                   start=True, stop=True),
                          reads=[btT_sb.buf, OR_BF.buf], writes=brow_bufs)
                    gr.op("vector", lambda e, ap=g_ap: e.tensor_tensor(Ltb.t[:, :], ap, DlowT.t[:, :], ALU.mult),
                          reads=g_bufs + [DlowT.buf], writes=[Ltb.buf])
                    gr.op("vector", lambda e, ap=brow_ap: e.tensor_tensor(Ltb.t[:, :], Ltb.t[:, :], ap, ALU.mult),
                          reads=[Ltb.buf] + brow_bufs, writes=[Ltb.buf])
                    kq_ap, kq_bufs = P1.slice(0, 512)
                    for hh in range(H):
                        blk = P1.t[0:128, 128 * hh:128 * hh + 128]
                        for kc in range(2):
                            kcol = slice((2 * hh + kc) * LS + 128 * cc, (2 * hh + kc) * LS + 128 * cc + 128)
                            gr.op("tensor", lambda e, blk=blk, kcol=kcol, hh=hh, kc=kc:
                                  e.matmul(blk, kn.t[:, kcol], qn.t[:, kcol],
                                           start=(hh == 0 and kc == 0), stop=False),
                                  reads=[kn.buf, qn.buf], writes=kq_bufs)
                    gr.op("vector", lambda e, ap=kq_ap: e.tensor_tensor(AttT.t[:, :], ap, DincT.t[:, :], ALU.mult),
                          reads=kq_bufs + [DincT.buf], writes=[AttT.buf])
                    Sold, Snew = Sb[cc % 2], Sb[(cc + 1) % 2]
                    for hh in range(H):
                        pt_ap4, pt_bufs4 = PT.slice(0, 512)
                        for vc in range(4):
                            vcol = slice((4 * hh + vc) * LS + 128 * cc, (4 * hh + vc) * LS + 128 * cc + 128)
                            tp = PT.t[0:128, 128 * vc:128 * vc + 128]
                            gr.op("tensor", lambda e, tp=tp, vcol=vcol: e.transpose(tp, vn.t[:, vcol],
                                                                                    ID_BF.t[:, :]),
                                  reads=[vn.buf, ID_BF.buf], writes=pt_bufs4)
                        gr.op("vector", lambda e, hh=hh, ap=pt_ap4: e.tensor_copy(
                            Vtok.t[:, 512 * hh:512 * hh + 512], ap),
                              reads=pt_bufs4, writes=[Vtok.buf])
                    for hh in range(H):
                        ks_ap, ks_bufs = P3.slice(0, 512)
                        for kc in range(2):
                            kcol = slice((2 * hh + kc) * LS + 128 * cc, (2 * hh + kc) * LS + 128 * cc + 128)
                            gr.op("tensor", lambda e, kcol=kcol, kc=kc, hh=hh, ap=ks_ap, Sold=Sold:
                                  e.matmul(ap, kn.t[:, kcol], Sold.t[:, (2 * hh + kc) * 512:(2 * hh + kc + 1) * 512],
                                           start=(kc == 0), stop=(kc == 1)),
                                  reads=[kn.buf, Sold.buf], writes=ks_bufs)
                        gr.op("vector", lambda e, hh=hh, ap=ks_ap: e.scalar_tensor_tensor(
                            W1t.t[:, :], ap, Acol.t[:, hh:hh + 1], Vtok.t[:, 512 * hh:512 * hh + 512],
                            ALU.mult, ALU.subtract),
                              reads=ks_bufs + [Acol.buf, Vtok.buf], writes=[W1t.buf])
                        gr.op("vector", lambda e, hh=hh, cc=cc: e.tensor_scalar(
                            Wt.t[:, 512 * hh:512 * hh + 512], W1t.t[:, :],
                            nbeta_f.t[:, 4 * cc + hh:4 * cc + hh + 1], None, ALU.mult),
                              reads=[W1t.buf, nbeta_f.buf], writes=[Wt.buf])
                    gr.op("vector", lambda e: e.tensor_tensor(
                        Rw[0].t[:, :].rearrange("p (a b) -> p a b", a=4),
                        bass.AP(ID_BF.t, 0, [[128, 128], [0, 4], [1, 128]]),
                        Ltb.t[:, :].rearrange("p (a b) -> p a b", a=4), ALU.subtract),
                          reads=[Ltb.buf, ID_BF.buf], writes=[Rw[0].buf])
                    Acur, Atcur, Rcur = Lb, Ltb, Rw[0]
                    Anew, Atnew, Rnew = Apw[0], Atw[0], Rw[1]
                    Aspare, Atspare = Apw[1], Atw[1]
                    for k in range(1, 7):
                        a_ap, a_bufs = P1.slice(0, 512)
                        for hh in range(H):
                            blk = P1.t[0:128, 128 * hh:128 * hh + 128]
                            hs = slice(128 * hh, 128 * hh + 128)
                            gr.op("tensor", lambda e, blk=blk, hs=hs, hh=hh, Atcur=Atcur, Acur=Acur:
                                  e.matmul(blk, Atcur.t[:, hs], Acur.t[:, hs], start=(hh == 0), stop=(hh == 3)),
                                  reads=[Atcur.buf, Acur.buf], writes=a_bufs)
                        gr.op("vector", lambda e, ap=a_ap, Anew=Anew: e.tensor_copy(Anew.t[:, :], ap),
                              reads=a_bufs, writes=[Anew.buf])
                        if k < 6:
                            at_ap, at_bufs = P2.slice(0, 512)
                            for hh in range(H):
                                blk = P2.t[0:128, 128 * hh:128 * hh + 128]
                                hs = slice(128 * hh, 128 * hh + 128)
                                gr.op("tensor", lambda e, blk=blk, hs=hs, hh=hh, Atcur=Atcur, Acur=Acur:
                                      e.matmul(blk, Acur.t[:, hs], Atcur.t[:, hs], start=(hh == 0), stop=(hh == 3)),
                                      reads=[Atcur.buf, Acur.buf], writes=at_bufs)
                            gr.op("vector", lambda e, ap=at_ap, Atnew=Atnew: e.tensor_copy(Atnew.t[:, :], ap),
                                  reads=at_bufs, writes=[Atnew.buf])
                        r_ap, r_bufs = P1.slice(0, 512)
                        for hh in range(H):
                            blk = P1.t[0:128, 128 * hh:128 * hh + 128]
                            hs = slice(128 * hh, 128 * hh + 128)
                            gr.op("tensor", lambda e, blk=blk, hs=hs, hh=hh, Anew=Anew, Rcur=Rcur:
                                  e.matmul(blk, Anew.t[:, hs], Rcur.t[:, hs], start=(hh == 0), stop=(hh == 3)),
                                  reads=[Anew.buf, Rcur.buf], writes=r_bufs)
                        gr.op("vector", lambda e, ap=r_ap, Rcur=Rcur, Rnew=Rnew:
                              e.tensor_tensor(Rnew.t[:, :], Rcur.t[:, :], ap, ALU.add),
                              reads=[Rcur.buf] + r_bufs, writes=[Rnew.buf])
                        if k == 1:
                            Acur, Anew = Anew, Aspare
                            Atcur, Atnew = Atnew, Atspare
                        else:
                            Acur, Anew = Anew, Acur
                            Atcur, Atnew = Atnew, Atcur
                        Rcur, Rnew = Rnew, Rcur
                    Rfin = Rcur
                    for hh in range(H):
                        u_ap, u_bufs = P3.slice(0, 512) if hh % 2 == 0 else P3.slice(512, 1024)
                        hs = slice(128 * hh, 128 * hh + 128)
                        gr.op("tensor", lambda e, hs=hs, hh=hh, ap=u_ap, Rfin=Rfin:
                              e.matmul(ap, Rfin.t[:, hs], Wt.t[:, 512 * hh:512 * hh + 512], start=True, stop=True),
                              reads=[Rfin.buf, Wt.buf], writes=u_bufs)
                        gr.op("vector", lambda e, hh=hh, ap=u_ap: e.tensor_copy(Ut.t[:, 512 * hh:512 * hh + 512],
                                                                                ap),
                              reads=u_bufs, writes=[Ut.buf])
                    for hh in range(H):
                        qs_ap, qs_bufs = P3.slice(0, 512)
                        au_ap, au_bufs = P3.slice(512, 1024)
                        for kc in range(2):
                            kcol = slice((2 * hh + kc) * LS + 128 * cc, (2 * hh + kc) * LS + 128 * cc + 128)
                            gr.op("tensor", lambda e, kcol=kcol, kc=kc, hh=hh, ap=qs_ap, Sold=Sold:
                                  e.matmul(ap, qn.t[:, kcol], Sold.t[:, (2 * hh + kc) * 512:(2 * hh + kc + 1) * 512],
                                           start=(kc == 0), stop=(kc == 1)),
                                  reads=[qn.buf, Sold.buf], writes=qs_bufs)
                        hs = slice(128 * hh, 128 * hh + 128)
                        gr.op("tensor", lambda e, hs=hs, hh=hh, ap=au_ap:
                              e.matmul(ap, AttT.t[:, hs], Ut.t[:, 512 * hh:512 * hh + 512], start=True, stop=True),
                              reads=[AttT.buf, Ut.buf], writes=au_bufs)
                        gr.op("vector", lambda e, ap=au_ap: e.tensor_copy(attu.t[:, :], ap),
                              reads=au_bufs, writes=[attu.buf])
                        gr.op("vector", lambda e, hh=hh, ap=qs_ap: e.scalar_tensor_tensor(
                            O_t.t[:, 512 * hh:512 * hh + 512], ap, Acol.t[:, hh:hh + 1], attu.t[:, :],
                            ALU.mult, ALU.add),
                              reads=qs_bufs + [Acol.buf, attu.buf], writes=[O_t.buf])
                    for hh in range(H):
                        pt_ap2, pt_bufs2 = PT.slice(0, 256)
                        for kc in range(2):
                            kcol = slice((2 * hh + kc) * LS + 128 * cc, (2 * hh + kc) * LS + 128 * cc + 128)
                            tp = PT.t[0:128, 128 * kc:128 * kc + 128]
                            gr.op("tensor", lambda e, tp=tp, kcol=kcol: e.transpose(tp, kn.t[:, kcol],
                                                                                    ID_BF.t[:, :]),
                                  reads=[kn.buf, ID_BF.buf], writes=pt_bufs2)
                        gr.op("vector", lambda e, hh=hh, ap=PT.t[0:128, 0:256]: e.tensor_copy(
                            Ktok.t[:, 256 * hh:256 * hh + 256], ap),
                              reads=pt_bufs2, writes=[Ktok.buf])
                        gr.op("vector", lambda e, hh=hh: e.tensor_scalar(Kbar.t[:, 256 * hh:256 * hh + 256],
                                                                         Ktok.t[:, 256 * hh:256 * hh + 256],
                                                                         ebar.t[:, hh:hh + 1], None, ALU.mult),
                              reads=[Ktok.buf, ebar.buf], writes=[Kbar.buf])
                        for kc in range(2):
                            st_ap, st_bufs = P4.slice(512 * kc, 512 * kc + 512)
                            gr.op("tensor", lambda e, hh=hh, kc=kc, ap=st_ap:
                                  e.matmul(ap, Kbar.t[:, 256 * hh + 128 * kc:256 * hh + 128 * kc + 128],
                                           Ut.t[:, 512 * hh:512 * hh + 512], start=True, stop=True),
                                  reads=[Kbar.buf, Ut.buf], writes=st_bufs)
                            scol = slice((2 * hh + kc) * 512, (2 * hh + kc + 1) * 512)
                            gr.op("vector", lambda e, hh=hh, scol=scol, ap=st_ap, Sold=Sold, Snew=Snew:
                                  e.scalar_tensor_tensor(Snew.t[:, scol], Sold.t[:, scol],
                                                         aC_col.t[:, hh:hh + 1], ap, ALU.mult, ALU.add),
                                  reads=[Sold.buf, aC_col.buf] + st_bufs, writes=[Snew.buf])
                    gr.op("vector", lambda e: e.tensor_mul(Wt.t[:, :], O_t.t[:, :], O_t.t[:, :]),
                          reads=[O_t.buf], writes=[Wt.buf])
                    gr.op("vector", lambda e: e.reduce_sum(ostd.t[:, :],
                                                           Wt.t[:, :].rearrange("p (a b) -> p a b", a=4),
                                                           axis=AX.X),
                          reads=[Wt.buf], writes=[ostd.buf])
                    gr.op("scalar", lambda e: e.activation(ostd.t[:, :], ostd.t[:, :], AF.Ln, scale=1.0 / Vd,
                                                           bias=eps6c),
                          reads=[ostd.buf, eps6c_b], writes=[ostd.buf])
                    gr.op("scalar", lambda e: e.activation(ostd.t[:, :], ostd.t[:, :], AF.Exp, scale=-0.5),
                          reads=[ostd.buf], writes=[ostd.buf])
                    for hh in range(H):
                        gr.op("vector", lambda e, hh=hh: e.tensor_scalar(
                            ogc.t[:, 512 * hh:512 * hh + 512], O_t.t[:, 512 * hh:512 * hh + 512],
                            ostd.t[:, hh:hh + 1], None, ALU.mult),
                              reads=[O_t.buf, ostd.buf], writes=[ogc.buf])
                    nwv = bass.AP(normw_bc.t, 0, [[512, 128], [0, 4], [1, 512]])
                    gr.op("vector", lambda e, nwv=nwv: e.tensor_tensor(
                        ogc.t[:, :].rearrange("p (a b) -> p a b", a=4),
                        ogc.t[:, :].rearrange("p (a b) -> p a b", a=4), nwv, ALU.mult),
                          reads=[ogc.buf, normw_bc.buf], writes=[ogc.buf])
                    gr.op("vector", lambda e, cc=cc, cc0=cc0: e.tensor_tensor(ogc.t[:, :], ogc.t[:, :],
                                                                     gate_sil.t[:, (cc - cc0) * 2048:(cc - cc0 + 1) * 2048],
                                                                     ALU.mult),
                          reads=[ogc.buf, gate_sil.buf], writes=[ogc.buf])
                    for b8 in range(2):
                        pt8_ap, pt8_bufs = PT.slice(0, 1024)
                        for j in range(8):
                            jhv = b8 * 8 + j
                            tp = PT.t[0:128, 128 * j:128 * j + 128]
                            gr.op("tensor", lambda e, tp=tp, jhv=jhv: e.transpose(
                                tp, ogc.t[:, 128 * jhv:128 * jhv + 128], ID_BF.t[:, :]),
                                  reads=[ogc.buf, ID_BF.buf], writes=pt8_bufs)
                        for j in range(8):
                            jhv = b8 * 8 + j
                            gr.op("vector", lambda e, j=j, jhv=jhv, cc=cc: e.tensor_copy(
                                o_fm.t[:, jhv * LS + 128 * cc: jhv * LS + 128 * cc + 128],
                                PT.t[0:128, 128 * j:128 * j + 128]),
                                  reads=pt8_bufs, writes=[o_fm.buf])

                    if debug and sl == 0 and cc == 1:
                        for nm, tl in [("Dlow", Dlow), ("DlowT", DlowT), ("DincT", DincT),
                                       ("Lb", Lb), ("Ltb", Ltb), ("AttT", AttT), ("Wt", Wt),
                                       ("Vtok", Vtok), ("ogc", ogc)]:
                            gr.dma("gpsimd", dbg[nm][:, :], tl.t[:, :], reads=[tl.buf])
                        gr.dma("gpsimd", dbg["Rfin"][:, :], Rfin.t[:, :], reads=[Rfin.buf])
                        gr.dma("gpsimd", dbg["Snew"][:, :], Snew.t[:, :], reads=[Snew.buf])
                        gr.dma("gpsimd", dbg["bT"][:, :], bT_sb.t[:, :], reads=[bT_sb.buf])
                        gr.dma("gpsimd", dbg["gsil"][:, :], gate_sil.t[:, (cc - cc0) * 2048:(cc - cc0 + 1) * 2048],
                               reads=[gate_sil.buf])


            for grp in range(2):
                psl = [P3.slice(0, LS), P3.slice(512, 512 + LS), P4.slice(0, LS), P4.slice(512, 512 + LS)]
                for p in range(8):
                    wp = load_piece(wo_d, p)
                    for i in range(2):
                        jhv = 2 * p + i
                        for dci in range(4):
                            dc = grp * 4 + dci
                            ap, bufs = psl[dci]
                            gr.op("tensor", lambda e, wp=wp, i=i, dc=dc, jhv=jhv, ap=ap:
                                  e.matmul(ap, wp.t[:, i * D + 128 * dc:i * D + 128 * dc + 128],
                                           o_fm.t[:, jhv * LS:(jhv + 1) * LS],
                                           start=(jhv == 0), stop=(jhv == 15)),
                                  reads=[wp.buf, o_fm.buf], writes=bufs)
                for dci in range(4):
                    dc = grp * 4 + dci
                    ap, bufs = psl[dci]
                    csl = slice(dc * LS, (dc + 1) * LS)
                    gr.op("vector", lambda e, csl=csl, ap=ap: e.tensor_tensor(xn1.t[:, csl], xn1.t[:, csl],
                                                                              ap, ALU.add),
                          reads=[xn1.bufs[dc]] + bufs, writes=[xn1.bufs[dc]])
            s2_ap, s2_bufs = P3.slice(0, LS)
            q2_ap, q2_bufs = P3.slice(512, 512 + LS)
            for c in range(DC):
                gr.op("tensor", lambda e, c=c, ap=s2_ap: e.matmul(ap, AO_BF.t[:, :],
                                                                  xn1.t[:, c * LS:(c + 1) * LS],
                                                                  start=(c == 0), stop=(c == DC - 1)),
                      reads=[xn1.bufs[c], AO_BF.buf], writes=s2_bufs)
            for c in range(DC):
                gr.op("vector", lambda e, c=c: e.tensor_mul(sq.t[:, :], xn1.t[:, c * LS:(c + 1) * LS],
                                                            xn1.t[:, c * LS:(c + 1) * LS]),
                      reads=[xn1.bufs[c]], writes=[sq.buf])
                gr.op("tensor", lambda e, c=c, ap=q2_ap: e.matmul(ap, AO_BF.t[:, :], sq.t[:, :],
                                                                  start=(c == 0), stop=(c == DC - 1)),
                      reads=[sq.buf, AO_BF.buf], writes=q2_bufs)
            gr.op("vector", lambda e, ap=s2_ap: e.tensor_scalar(m0b.t[:, :], ap, 1.0 / D, None, ALU.mult),
                  reads=s2_bufs, writes=[m0b.buf])
            gr.op("vector", lambda e: e.tensor_mul(t1.t[:, :], m0b.t[:, :], m0b.t[:, :]),
                  reads=[m0b.buf], writes=[t1.buf])
            gr.op("vector", lambda e, ap=q2_ap: e.scalar_tensor_tensor(t2.t[:, :], ap, 1.0 / D,
                                                                       t1.t[:, :], ALU.mult, ALU.subtract),
                  reads=q2_bufs + [t1.buf], writes=[t2.buf])
            gr.op("scalar", lambda e: e.activation(t2.t[:, :], t2.t[:, :], AF.Ln, bias=eps5c),
                  reads=[t2.buf, eps5c_b], writes=[t2.buf])
            gr.op("scalar", lambda e: e.activation(r0b.t[:, :], t2.t[:, :], AF.Exp, scale=-0.5),
                  reads=[t2.buf], writes=[r0b.buf])
            for c in range(DC):
                csl = slice(c * LS, (c + 1) * LS)
                gr.op("vector", lambda e, csl=csl: e.tensor_tensor(t1.t[:, :], xn1.t[:, csl],
                                                                   m0b.t[:, :], ALU.subtract),
                      reads=[xn1.bufs[c], m0b.buf], writes=[t1.buf])
                gr.op("vector", lambda e: e.tensor_tensor(t2.t[:, :], t1.t[:, :], r0b.t[:, :], ALU.mult),
                      reads=[t1.buf, r0b.buf], writes=[t2.buf])
                gr.op("vector", lambda e, c=c, csl=csl: e.tensor_scalar(h2t.t[:, csl], t2.t[:, :],
                                                                        vcols["ln2g"].t[:, c:c + 1],
                                                                        vcols["ln2b"].t[:, c:c + 1],
                                                                        ALU.mult, ALU.add),
                      reads=[t2.buf, vcols["ln2g"].buf, vcols["ln2b"].buf], writes=[h2t.bufs[c]])
            for p in range(4):
                wp = load_piece(w2_d, p)
                for jj in range(2):
                    j = 2 * p + jj
                    PS = psum_pick(jj)
                    ap, bufs = PS.slice(0, LS)
                    for c in range(DC):
                        gr.op("tensor", lambda e, c=c, jj=jj, wp=wp, ap=ap:
                              e.matmul(ap, wp.t[:, c * 256 + 128 * jj:c * 256 + 128 * jj + 128],
                                       h2t.t[:, c * LS:(c + 1) * LS], start=(c == 0), stop=(c == DC - 1)),
                              reads=[wp.buf, h2t.bufs[c]], writes=bufs)
                    gr.op("scalar", lambda e, j=j, ap=ap: e.activation(gelc.t[:, :], ap, AF.Gelu,
                                                                       bias=vcols["b2a"].t[:, j:j + 1]),
                          reads=bufs + [vcols["b2a"].buf], writes=[gelc.buf])
                    xbuf = x1f.bufs[j % 2]
                    xap = x1f.t[:, (j % 2) * LS:(j % 2 + 1) * LS]
                    gr.dma("sync", xap, x1_d[j * 128:(j + 1) * 128, sl * LS:(sl + 1) * LS], writes=[xbuf])
                    ybuf = yst.bufs[j % 2]
                    yap = yst.t[:, (j % 2) * LS:(j % 2 + 1) * LS]
                    jsl = slice(j * LS, (j + 1) * LS)
                    gr.op("vector", lambda e, jsl=jsl, yap=yap: e.tensor_tensor(yap, xn1.t[:, jsl],
                                                                                gelc.t[:, :], ALU.add),
                          reads=[xn1.bufs[j], gelc.buf], writes=[ybuf])
                    gr.op("vector", lambda e, yap=yap, xap=xap: e.tensor_tensor(yap, yap, xap, ALU.add),
                          reads=[xbuf, ybuf], writes=[ybuf])
                    gr.dma("sync", y_d[j * 128:(j + 1) * 128, sl * LS:(sl + 1) * LS], yap, reads=[ybuf])

            if debug and sl == 0:
                gr.dma("gpsimd", dbg["h"][:, :], h.t[:, :], reads=list(h.bufs))
                gr.dma("gpsimd", dbg["qn"][:, :], qn.t[:, :], reads=[qn.buf])
                gr.dma("gpsimd", dbg["kn"][:, :], kn.t[:, :], reads=[kn.buf])
                gr.dma("gpsimd", dbg["vn"][:, :], vn.t[:, :], reads=[vn.buf])
                gr.dma("gpsimd", dbg["gb"][:, :], g_bf.t[:, :], reads=[g_bf.buf])
                gr.dma("gpsimd", dbg["bb"][:, :], beta_bf.t[:, :], reads=[beta_bf.buf])
                gr.dma("gpsimd", dbg["btok"][:, :], b_tok.t[:, :], reads=[b_tok.buf])
                gr.dma("gpsimd", dbg["Ot"][:, :], O_t.t[:, :], reads=[O_t.buf])
                gr.dma("gpsimd", dbg["U"][:, :], Ut.t[:, :], reads=[Ut.buf])
                gr.dma("gpsimd", dbg["xa"][:, :], xn1.t[:, :], reads=list(xn1.bufs))
                gr.dma("gpsimd", dbg["ofm"][:, :], o_fm.t[:, :], reads=[o_fm.buf])




def build_l1_attn(debug=False, LS=512):
    nc = bass.Bass("TRN2", target_bir_lowering=False)
    gr = G(nc)
    consts = host_consts()
    x1_d = nc.dram_tensor("x1T", [D, T], F32, kind="ExternalInput")
    wq_d = nc.dram_tensor("Wq", [4 * 128, 2048], F32, kind="ExternalInput")
    wk_d = nc.dram_tensor("Wk", [4 * 128, 2048], F32, kind="ExternalInput")
    wv_d = nc.dram_tensor("Wv", [8 * 128, 2048], F32, kind="ExternalInput")
    wg_d = nc.dram_tensor("Wg", [8 * 128, 2048], F32, kind="ExternalInput")
    wo_d = nc.dram_tensor("Wo", [8 * 128, 2048], F32, kind="ExternalInput")
    w2_d = nc.dram_tensor("W2", [4 * 128, 2048], F32, kind="ExternalInput")
    wa_d = nc.dram_tensor("Wa", [D, 4], F32, kind="ExternalInput")
    wb_d = nc.dram_tensor("Wb", [D, 4], F32, kind="ExternalInput")
    cq_d = nc.dram_tensor("conv_q", [D, 4], F32, kind="ExternalInput")
    ck_d = nc.dram_tensor("conv_k", [D, 4], F32, kind="ExternalInput")
    cv_d = nc.dram_tensor("conv_v", [VAL_DIM, 4], F32, kind="ExternalInput")
    ln1g_d = nc.dram_tensor("ln1_g", [D, 1], F32, kind="ExternalInput")
    ln1b_d = nc.dram_tensor("ln1_b", [D, 1], F32, kind="ExternalInput")
    ln2g_d = nc.dram_tensor("ln2_g", [D, 1], F32, kind="ExternalInput")
    ln2b_d = nc.dram_tensor("ln2_b", [D, 1], F32, kind="ExternalInput")
    rmsw_d = nc.dram_tensor("rms_in_w", [D, 1], F32, kind="ExternalInput")
    b2_d = nc.dram_tensor("b2", [D, 1], F32, kind="ExternalInput")
    onw_d = nc.dram_tensor("o_norm_w", [1, Vd], F32, kind="ExternalInput")
    dtb_d = nc.dram_tensor("dt_bias", [1, 4], F32, kind="ExternalInput")
    alog_d = nc.dram_tensor("A_log", [1, 4], F32, kind="ExternalInput")
    y_d = nc.dram_tensor("yT", [D, T], F32, kind="ExternalOutput")
    dr = {"x1T": x1_d, "Wq": wq_d, "Wk": wk_d, "Wv": wv_d, "Wg": wg_d, "Wo": wo_d, "W2": w2_d, "Wa": wa_d, "Wb": wb_d, "conv_q": cq_d, "conv_k": ck_d, "conv_v": cv_d, "ln1_g": ln1g_d, "ln1_b": ln1b_d, "ln2_g": ln2g_d, "ln2_b": ln2b_d, "rms_in_w": rmsw_d, "b2": b2_d, "o_norm_w": onw_d, "dt_bias": dtb_d, "A_log": alog_d, "yT": y_d}
    with contextlib.ExitStack() as ctx0:
        cells = Cells(nc, ctx0, gr)
        emit_l1_attn(nc, gr, consts, cells, dr, debug=debug, LS=LS)
        gr.finalize_and_emit(ctx0)
    return nc


def emit_l1_moe(nc, gr, consts, cells, dr, debug=False):
    x1_d = dr["x1T"]
    y_d = dr["yT"]
    gw_d = dr["gate_W"]
    w1_d = dr["W1e"]
    b1_d = dr["b1e"]
    w2_d = dr["W2e"]
    b2_d = dr["b2e"]
    ws1_d = dr["Ws1"]
    bs1_d = dr["bs1"]
    ws2_d = dr["Ws2"]
    bs2_d = dr["bs2"]
    rs_d = dr["route_scale"]
    out_d = dr["outT"]
    E = 8
    if debug:
        dbg = {}
        for nm, shp in [("gf", [128, E]), ("sc", [128, E]), ("acc", [128, DC * SL]),
                        ("xn1", [128, DC * SL])]:
            dbg[nm] = nc.dram_tensor("dbg_" + nm, shp, F32, kind="ExternalOutput")

    ctx = contextlib.ExitStack()
    with ctx:
        C = const_tensors(nc, gr, ctx, consts, {"ident": ["f32"], "ones_col": ["bf16"], "ones_row": ["f32", "bf16"]})
        OC_BF = C[("ones_col", "bf16")]
        OR_F = C[("ones_row", "f32")]
        OR_BF = C[("ones_row", "bf16")]
        ID_F = C[("ident", "f32")]
        eps5, eps5_b = cells.cell(LN_EPS)

        x1bf = Tile(nc, ctx, "x1bf", [128, DC * SL], BF16, nbufs=DC)
        xn1 = Tile(nc, ctx, "xn1", [128, DC * SL], BF16)
        acc = Tile(nc, ctx, "acc", [128, DC * SL], F32)
        Hg = Tile(nc, ctx, "Hg", [128, SL], F32)
        Hw = Tile(nc, ctx, "Hw", [128, 4 * SL], BF16)
        WPN = 6
        wps = [Tile(nc, ctx, f"wp{i}", [128, 2048], BF16) for i in range(WPN)]
        gw_sb = Tile(nc, ctx, "gw_sb", [128, DC * E], BF16)
        b2_sb = Tile(nc, ctx, "b2_sb", [8, D], BF16)
        b1col = Tile(nc, ctx, "b1col", [128, E * 4], F32)
        bs1col = Tile(nc, ctx, "bs1col", [128, 4], F32)
        bs2col = Tile(nc, ctx, "bs2col", [128, DC], F32)
        rs_sb = Tile(nc, ctx, "rs_sb", [1, 1], F32)
        rs_col = Tile(nc, ctx, "rs_col", [128, 1], F32)
        sc = Tile(nc, ctx, "sc", [128, E], F32)
        gf = Tile(nc, ctx, "gf", [128, E], F32)
        mm1 = Tile(nc, ctx, "mm1", [128, 4], F32)
        eqm = Tile(nc, ctx, "eqm", [128, E], F32)
        gfT8 = Tile(nc, ctx, "gfT8", [8, 4 * 128], BF16)
        ge_row = Tile(nc, ctx, "ge_row", [1, SL], BF16)
        yf = Tile(nc, ctx, "yf", [128, SL], F32)
        ost = Tile(nc, ctx, "ost", [128, SL], F32)
        sq = Tile(nc, ctx, "sqs", [128, SL], BF16)
        t1 = Tile(nc, ctx, "t1", [128, SL], F32)
        rows_all = Tile(nc, ctx, "rows_all", [1, 2048], F32)
        m_row = RowTile(rows_all, 0, "m_row", 0, SL)
        rstd0_row = RowTile(rows_all, 0, "rstd0_row", 512, SL)
        rows = RowTile(rows_all, 0, "rows", 1024, 2 * SL)

        PH = Psum(nc, ctx, "PH", [128, 1024], F32)
        PW = Psum(nc, ctx, "PW", [128, 1024], F32)
        PG = Psum(nc, ctx, "PG", [128, 512], F32)
        PS5 = Psum(nc, ctx, "PS5", [128, 512], F32)

        for c in range(DC):
            gr.dma("gpsimd", gw_sb.t[:, c * E:(c + 1) * E], gw_d[c * 128:(c + 1) * 128, :],
                   writes=[gw_sb.buf])
        gr.dma("gpsimd", b2_sb.t[:, :], b2_d[:, :], writes=[b2_sb.buf])
        for e in range(E):
            for hc in range(4):
                ap_src = bass.AP(b1_d, e * 512 + hc * 128, [[1, 128], [1, 1]])
                gr.dma("sync", b1col.t[:, e * 4 + hc:e * 4 + hc + 1], ap_src, writes=[b1col.buf])
        for hc in range(4):
            gr.dma("sync", bs1col.t[:, hc:hc + 1], bs1_d[hc * 128:(hc + 1) * 128, :], writes=[bs1col.buf])
        for c in range(DC):
            gr.dma("sync", bs2col.t[:, c:c + 1], bs2_d[c * 128:(c + 1) * 128, :], writes=[bs2col.buf])
        gr.dma("sync", rs_sb.t[:, :], rs_d[:, :], writes=[rs_sb.buf])
        rsb_ap, rsb_bufs = PS5.slice(0, 1)
        gr.op("tensor", lambda e, ap=rsb_ap: e.matmul(ap, OR_F.t[:, :], rs_sb.t[0:1, 0:1],
                                                      start=True, stop=True),
              reads=[rs_sb.buf, OR_F.buf], writes=rsb_bufs)
        gr.op("vector", lambda e, ap=rsb_ap: e.tensor_copy(rs_col.t[:, :], ap),
              reads=rsb_bufs, writes=[rs_col.buf])

        piece_idx = [0]

        def wpiece():
            wp = wps[piece_idx[0] % WPN]
            piece_idx[0] += 1
            return wp

        def load_piece(wdram, pidx):
            wp = wpiece()
            gr.dma("gpsimd", wp.t[:, :], wdram[pidx * 128:(pidx + 1) * 128, :], writes=[wp.buf])
            return wp

        for sl in range(NSL):
            for c in range(DC):
                gr.dma("gpsimd", x1bf.t[:, c * SL:(c + 1) * SL],
                       x1_d[c * 128:(c + 1) * 128, sl * SL:(sl + 1) * SL], writes=[x1bf.bufs[c]])
            sum_ap, sum_bufs = PH.slice(0, SL, 0, 1)
            sq_ap, sq_bufs = PH.slice(512, 1024, 0, 1)
            for c in range(DC):
                gr.op("tensor", lambda e, c=c, ap=sum_ap: e.matmul(ap, OC_BF.t[:, 0:1],
                                                                   x1bf.t[:, c * SL:(c + 1) * SL],
                                                                   start=(c == 0), stop=(c == DC - 1)),
                      reads=[x1bf.bufs[c], OC_BF.buf], writes=sum_bufs)
            for c in range(DC):
                gr.op("vector", lambda e, c=c: e.tensor_mul(sq.t[:, :], x1bf.t[:, c * SL:(c + 1) * SL],
                                                            x1bf.t[:, c * SL:(c + 1) * SL]),
                      reads=[x1bf.bufs[c]], writes=[sq.buf])
                gr.op("tensor", lambda e, c=c, ap=sq_ap: e.matmul(ap, OC_BF.t[:, 0:1], sq.t[:, :],
                                                                  start=(c == 0), stop=(c == DC - 1)),
                      reads=[sq.buf, OC_BF.buf], writes=sq_bufs)
            gr.op("vector", lambda e, ap=sum_ap: e.tensor_scalar(m_row.t[0:1, :], ap, 1.0 / D, None, ALU.mult),
                  reads=sum_bufs, writes=[m_row.buf])
            gr.op("vector", lambda e: e.tensor_mul(rows.t[0:1, 0:SL], m_row.t[0:1, :], m_row.t[0:1, :]),
                  reads=[m_row.buf], writes=[rows.buf])
            gr.op("vector", lambda e, ap=sq_ap: e.scalar_tensor_tensor(rows.t[0:1, SL:2 * SL], ap, 1.0 / D,
                                                                       rows.t[0:1, 0:SL], ALU.mult,
                                                                       ALU.subtract),
                  reads=sq_bufs + [rows.buf], writes=[rows.buf])
            gr.op("scalar", lambda e: e.activation(rows.t[0:1, 0:SL], rows.t[0:1, SL:2 * SL], AF.Ln,
                                                   bias=eps5),
                  reads=[rows.buf, eps5_b], writes=[rows.buf])
            gr.op("scalar", lambda e: e.activation(rstd0_row.t[0:1, :], rows.t[0:1, 0:SL], AF.Exp,
                                                   scale=-0.5),
                  reads=[rows.buf], writes=[rstd0_row.buf])
            mb_ap, mb_bufs = PW.slice(0, SL)
            rb_ap, rb_bufs = PW.slice(512, 1024)
            gr.op("tensor", lambda e, ap=mb_ap: e.matmul(ap, OR_F.t[:, :], m_row.t[0:1, :],
                                                         start=True, stop=True),
                  reads=[m_row.buf, OR_F.buf], writes=mb_bufs)
            gr.op("tensor", lambda e, ap=rb_ap: e.matmul(ap, OR_F.t[:, :], rstd0_row.t[0:1, :],
                                                         start=True, stop=True),
                  reads=[rstd0_row.buf, OR_F.buf], writes=rb_bufs)
            for c in range(DC):
                csl = slice(c * SL, (c + 1) * SL)
                gr.op("vector", lambda e, c=c, csl=csl, ap=mb_ap: e.tensor_tensor(
                    t1.t[:, :], x1bf.t[:, csl], ap, ALU.subtract),
                      reads=[x1bf.bufs[c]] + mb_bufs, writes=[t1.buf])
                gr.op("vector", lambda e, csl=csl, ap=rb_ap: e.tensor_tensor(xn1.t[:, csl], t1.t[:, :],
                                                                             ap, ALU.mult),
                      reads=[t1.buf] + rb_bufs, writes=[xn1.buf])
            gr.op("vector", lambda e: e.memset(acc.t[:, :], 0.0), writes=[acc.buf])

            for tt_ in range(4):
                sc_ap, sc_bufs = PS5.slice(0, E)
                for c in range(DC):
                    gr.op("tensor", lambda e, c=c, tt_=tt_, ap=sc_ap:
                          e.matmul(ap, xn1.t[:, c * SL + 128 * tt_:c * SL + 128 * tt_ + 128],
                                   gw_sb.t[:, c * E:(c + 1) * E], start=(c == 0), stop=(c == DC - 1)),
                          reads=[xn1.buf, gw_sb.buf], writes=sc_bufs)
                gr.op("scalar", lambda e, ap=sc_ap: e.activation(sc.t[:, :], ap, AF.Sigmoid),
                      reads=sc_bufs, writes=[sc.buf])
                gr.op("vector", lambda e: e.tensor_scalar(sc.t[:, :], sc.t[:, :], rs_col.t[:, 0:1],
                                                          None, ALU.mult),
                      reads=[sc.buf, rs_col.buf], writes=[sc.buf])
                gr.op("vector", lambda e: e.reduce_max(mm1.t[:, 0:1], sc.t[:, :], axis=AX.X),
                      reads=[sc.buf], writes=[mm1.buf])
                gr.op("vector", lambda e: e.tensor_scalar(eqm.t[:, :], sc.t[:, :], mm1.t[:, 0:1],
                                                          None, ALU.is_equal),
                      reads=[sc.buf, mm1.buf], writes=[eqm.buf])
                gr.op("vector", lambda e: e.scalar_tensor_tensor(eqm.t[:, :], eqm.t[:, :], -1e30,
                                                                 sc.t[:, :], ALU.mult, ALU.add),
                      reads=[eqm.buf, sc.buf], writes=[eqm.buf])
                gr.op("vector", lambda e: e.reduce_max(mm1.t[:, 1:2], eqm.t[:, :], axis=AX.X),
                      reads=[eqm.buf], writes=[mm1.buf])
                gr.op("vector", lambda e: e.tensor_add(mm1.t[:, 2:3], mm1.t[:, 0:1], mm1.t[:, 1:2]),
                      reads=[mm1.buf], writes=[mm1.buf])
                gr.op("vector", lambda e: e.reciprocal(mm1.t[:, 3:4], mm1.t[:, 2:3]),
                      reads=[mm1.buf], writes=[mm1.buf])
                gr.op("vector", lambda e: e.tensor_scalar(gf.t[:, :], sc.t[:, :], mm1.t[:, 1:2],
                                                          None, ALU.is_ge),
                      reads=[sc.buf, mm1.buf], writes=[gf.buf])
                gr.op("vector", lambda e: e.tensor_tensor(gf.t[:, :], gf.t[:, :], sc.t[:, :], ALU.mult),
                      reads=[gf.buf, sc.buf], writes=[gf.buf])
                gr.op("vector", lambda e: e.tensor_scalar(gf.t[:, :], gf.t[:, :], mm1.t[:, 3:4],
                                                          None, ALU.mult),
                      reads=[gf.buf, mm1.buf], writes=[gf.buf])
                gt_ap, gt_bufs = PS5.slice(0, 128, 0, 8)
                gr.op("tensor", lambda e, ap=gt_ap: e.transpose(ap, gf.t[:, :], ID_F.t[:, :]),
                      reads=[gf.buf, ID_F.buf], writes=gt_bufs)
                gr.op("vector", lambda e, tt_=tt_, ap=gt_ap: e.tensor_copy(
                    gfT8.t[:, 128 * tt_:128 * tt_ + 128], ap),
                      reads=gt_bufs, writes=[gfT8.buf])
                if debug and sl == 0 and tt_ == 0:
                    gr.dma("gpsimd", dbg["sc"][:, :], sc.t[:, :], reads=[sc.buf])
                    gr.dma("gpsimd", dbg["gf"][:, :], gf.t[:, :], reads=[gf.buf])

            for e_i in range(E):
                gr.dma("sync", ge_row.t[0:1, :],
                       bass.AP(gfT8.t, e_i * 4 * 128, [[4 * 128, 1], [1, SL]]),
                       reads=[gfT8.buf], writes=[ge_row.buf])
                gb_ap, gb_bufs = PG.slice(0, SL)
                gr.op("tensor", lambda e, ap=gb_ap: e.matmul(ap, OR_BF.t[:, :], ge_row.t[0:1, :],
                                                             start=True, stop=True),
                      reads=[ge_row.buf, OR_BF.buf], writes=gb_bufs)
                for p in range(2):
                    wp = load_piece(w1_d, e_i * 2 + p)
                    for jj in range(2):
                        hc = 2 * p + jj
                        h_ap = PH.t[0:128, 512 * (hc % 2):512 * (hc % 2) + 512]
                        _, h_bufs = PH.slice(512 * (hc % 2), 512 * (hc % 2) + 512)
                        for c in range(DC):
                            gr.op("tensor", lambda e, c=c, jj=jj, wp=wp, ap=h_ap:
                                  e.matmul(ap, wp.t[:, c * 256 + 128 * jj:c * 256 + 128 * jj + 128],
                                           xn1.t[:, c * SL:(c + 1) * SL], start=(c == 0),
                                           stop=(c == DC - 1)),
                                  reads=[wp.buf, xn1.buf], writes=h_bufs)
                        gr.op("scalar", lambda e, e_i=e_i, hc=hc, ap=h_ap: e.activation(
                            Hg.t[:, :], ap, AF.Gelu, bias=b1col.t[:, e_i * 4 + hc:e_i * 4 + hc + 1]),
                              reads=h_bufs + [b1col.buf], writes=[Hg.buf])
                        gr.op("vector", lambda e, hc=hc, ap=gb_ap: e.tensor_tensor(
                            Hw.t[:, hc * SL:(hc + 1) * SL], Hg.t[:, :], ap, ALU.mult),
                              reads=[Hg.buf] + gb_bufs, writes=[Hw.buf])
                for p in range(2):
                    wp = load_piece(w2_d, e_i * 2 + p)
                    for c in range(DC):
                        w_ap = PW.t[0:128, 512 * (c % 2):512 * (c % 2) + 512]
                        _, w_bufs = PW.slice(512 * (c % 2), 512 * (c % 2) + 512)
                        for i in range(2):
                            hc = 2 * p + i
                            gr.op("tensor", lambda e, wp=wp, i=i, c=c, hc=hc, ap=w_ap:
                                  e.matmul(ap, wp.t[:, i * D + 128 * c:i * D + 128 * c + 128],
                                           Hw.t[:, hc * SL:(hc + 1) * SL],
                                           start=(i == 0), stop=(i == 1)),
                                  reads=[wp.buf, Hw.buf], writes=w_bufs)
                        csl = slice(c * SL, (c + 1) * SL)
                        gr.op("vector", lambda e, csl=csl, ap=w_ap: e.tensor_tensor(
                            acc.t[:, csl], acc.t[:, csl], ap, ALU.add),
                              reads=gb_bufs + w_bufs + [acc.buf], writes=[acc.buf])
            for c in range(DC):
                b2_ap = PW.t[0:128, 512 * (c % 2):512 * (c % 2) + 512]
                _, b2_bufs = PW.slice(512 * (c % 2), 512 * (c % 2) + 512)
                gr.op("tensor", lambda e, c=c, ap=b2_ap: e.matmul(
                    ap, b2_sb.t[:, c * 128:(c + 1) * 128], gfT8.t[:, :], start=True, stop=True),
                      reads=[b2_sb.buf, gfT8.buf], writes=b2_bufs)
                csl = slice(c * SL, (c + 1) * SL)
                gr.op("vector", lambda e, csl=csl, ap=b2_ap: e.tensor_tensor(acc.t[:, csl],
                                                                             acc.t[:, csl], ap, ALU.add),
                      reads=b2_bufs + [acc.buf], writes=[acc.buf])
            for p in range(2):
                wp = load_piece(ws1_d, p)
                for jj in range(2):
                    hc = 2 * p + jj
                    h_ap = PH.t[0:128, 512 * (hc % 2):512 * (hc % 2) + 512]
                    _, h_bufs = PH.slice(512 * (hc % 2), 512 * (hc % 2) + 512)
                    for c in range(DC):
                        gr.op("tensor", lambda e, c=c, jj=jj, wp=wp, ap=h_ap:
                              e.matmul(ap, wp.t[:, c * 256 + 128 * jj:c * 256 + 128 * jj + 128],
                                       xn1.t[:, c * SL:(c + 1) * SL], start=(c == 0), stop=(c == DC - 1)),
                              reads=[wp.buf, xn1.buf], writes=h_bufs)
                    gr.op("scalar", lambda e, hc=hc, ap=h_ap: e.activation(
                        Hg.t[:, :], ap, AF.Gelu, bias=bs1col.t[:, hc:hc + 1]),
                          reads=h_bufs + [bs1col.buf], writes=[Hg.buf])
                    gr.op("vector", lambda e, hc=hc: e.tensor_copy(Hw.t[:, hc * SL:(hc + 1) * SL],
                                                                   Hg.t[:, :]),
                          reads=[Hg.buf], writes=[Hw.buf])
            for p in range(2):
                wp = load_piece(ws2_d, p)
                for c in range(DC):
                    w_ap = PW.t[0:128, 512 * (c % 2):512 * (c % 2) + 512]
                    _, w_bufs = PW.slice(512 * (c % 2), 512 * (c % 2) + 512)
                    for i in range(2):
                        hc = 2 * p + i
                        gr.op("tensor", lambda e, wp=wp, i=i, c=c, hc=hc, ap=w_ap:
                              e.matmul(ap, wp.t[:, i * D + 128 * c:i * D + 128 * c + 128],
                                       Hw.t[:, hc * SL:(hc + 1) * SL],
                                       start=(i == 0), stop=(i == 1)),
                              reads=[wp.buf, Hw.buf], writes=w_bufs)
                    csl = slice(c * SL, (c + 1) * SL)
                    if p == 0:
                        gr.op("vector", lambda e, c=c, csl=csl, ap=w_ap: e.scalar_tensor_tensor(
                            acc.t[:, csl], ap, bs2col.t[:, c:c + 1], acc.t[:, csl], ALU.add, ALU.add),
                              reads=w_bufs + [bs2col.buf, acc.buf], writes=[acc.buf])
                    else:
                        gr.op("vector", lambda e, csl=csl, ap=w_ap: e.tensor_tensor(
                            acc.t[:, csl], acc.t[:, csl], ap, ALU.add),
                              reads=w_bufs + [acc.buf], writes=[acc.buf])
            if debug and sl == 0:
                gr.dma("gpsimd", dbg["acc"][:, :], acc.t[:, :], reads=[acc.buf])
                gr.dma("gpsimd", dbg["xn1"][:, :], xn1.t[:, :], reads=[xn1.buf])
            for c in range(DC):
                gr.dma("sync", yf.t[:, :], y_d[c * 128:(c + 1) * 128, sl * SL:(sl + 1) * SL],
                       writes=[yf.buf])
                gr.op("vector", lambda e, c=c: e.tensor_tensor(ost.t[:, :], yf.t[:, :],
                                                               acc.t[:, c * SL:(c + 1) * SL], ALU.add),
                      reads=[yf.buf, acc.buf], writes=[ost.buf])
                gr.dma("sync", out_d[c * 128:(c + 1) * 128, sl * SL:(sl + 1) * SL], ost.t[:, :],
                       reads=[ost.buf])




def build_l1_moe(debug=False):
    nc = bass.Bass("TRN2", target_bir_lowering=False)
    gr = G(nc)
    consts = host_consts()
    x1_d = nc.dram_tensor("x1T", [D, T], F32, kind="ExternalInput")
    y_d = nc.dram_tensor("yT", [D, T], F32, kind="ExternalInput")
    gw_d = nc.dram_tensor("gate_W", [D, E], F32, kind="ExternalInput")
    w1_d = nc.dram_tensor("W1e", [16 * 128, 2048], F32, kind="ExternalInput")
    b1_d = nc.dram_tensor("b1e", [E, 512], F32, kind="ExternalInput")
    w2_d = nc.dram_tensor("W2e", [16 * 128, 2048], F32, kind="ExternalInput")
    b2_d = nc.dram_tensor("b2e", [E, D], F32, kind="ExternalInput")
    ws1_d = nc.dram_tensor("Ws1", [2 * 128, 2048], F32, kind="ExternalInput")
    bs1_d = nc.dram_tensor("bs1", [512, 1], F32, kind="ExternalInput")
    ws2_d = nc.dram_tensor("Ws2", [2 * 128, 2048], F32, kind="ExternalInput")
    bs2_d = nc.dram_tensor("bs2", [D, 1], F32, kind="ExternalInput")
    rs_d = nc.dram_tensor("route_scale", [1, 1], F32, kind="ExternalInput")
    out_d = nc.dram_tensor("outT", [D, T], F32, kind="ExternalOutput")
    dr = {"x1T": x1_d, "yT": y_d, "gate_W": gw_d, "W1e": w1_d, "b1e": b1_d, "W2e": w2_d, "b2e": b2_d, "Ws1": ws1_d, "bs1": bs1_d, "Ws2": ws2_d, "bs2": bs2_d, "route_scale": rs_d, "outT": out_d}
    with contextlib.ExitStack() as ctx0:
        cells = Cells(nc, ctx0, gr)
        emit_l1_moe(nc, gr, consts, cells, dr, debug=debug)
        gr.finalize_and_emit(ctx0)
    return nc



def build_fused(debug=False):
    nc = bass.Bass("TRN2", target_bir_lowering=False)
    gr = G(nc)
    consts = host_consts()
    E = 8
    dr = {}
    for name, shape in [("xT", [D, T]), ("wq_l", [128, DC * D]), ("wk_l", [128, DC * D]), ("wv_l", [128, DC * D]),
                        ("rope_cos", [NSL * 128, 4 * SL]), ("rope_sin", [NSL * 128, 4 * SL]),
                        ("wr_l", [128, DC * D]), ("wk2_l", [128, DC * D]), ("wv2_l", [128, DC * D]),
                        ("mu_r", [D, 1]), ("mu_k", [D, 1]),
                        ("Wq", [4 * 128, 2048]), ("Wk", [4 * 128, 2048]), ("Wv", [8 * 128, 2048]), ("Wg", [8 * 128, 2048]),
                        ("Wo", [8 * 128, 2048]), ("W2", [4 * 128, 2048]), ("Wa", [D, 4]), ("Wb", [D, 4]),
                        ("conv_q", [D, 4]), ("conv_k", [D, 4]), ("conv_v", [VAL_DIM, 4]),
                        ("ln1_g", [D, 1]), ("ln1_b", [D, 1]), ("ln2_g", [D, 1]), ("ln2_b", [D, 1]),
                        ("rms_in_w", [D, 1]), ("b2", [D, 1]), ("o_norm_w", [1, Vd]),
                        ("dt_bias", [1, 4]), ("A_log", [1, 4]),
                        ("gate_W", [D, E]), ("W1e", [16 * 128, 2048]), ("b1e", [E, 512]),
                        ("W2e", [16 * 128, 2048]), ("b2e", [E, D]), ("Ws1", [2 * 128, 2048]),
                        ("bs1", [512, 1]), ("Ws2", [2 * 128, 2048]), ("bs2", [D, 1]),
                        ("route_scale", [1, 1])]:
        dr[name] = nc.dram_tensor(name, shape, F32, kind="ExternalInput")
    ikind = "ExternalOutput" if debug else Ellipsis
    for name, shape in [("o0T", [D, T]), ("x1T", [D, T]), ("yT", [D, T])]:
        if debug:
            dr[name] = nc.dram_tensor(name, shape, F32, kind="ExternalOutput")
        else:
            dr[name] = nc.dram_tensor(name, shape, F32)
    dr["outT"] = nc.dram_tensor("outT", [D, T], F32, kind="ExternalOutput")

    with contextlib.ExitStack() as ctx0:
        cells = Cells(nc, ctx0, gr)
        with contextlib.ExitStack() as ctx01:
            xn_shared = Tile(nc, ctx01, "xn_shared", [128, DC * (T + 1)], BF16)
            emit_l0_attn(nc, gr, consts, cells, dr, xn_shared=xn_shared)
            gr.barrier()
            emit_l0_ffn(nc, gr, consts, cells, dr, xn_shared=xn_shared)
        gr.barrier()
        emit_l1_attn(nc, gr, consts, cells, dr)
        gr.barrier()
        emit_l1_moe(nc, gr, consts, cells, dr)
        gr.finalize_and_emit(ctx0)
    return nc, consts


def _host_inputs(inputs, consts):
    f32 = lambda k: np.ascontiguousarray(np.asarray(inputs[k], np.float32))
    def _tab_p(t):
        a = t.reshape(4, 128, 4, SL)
        return np.ascontiguousarray(a.transpose(2, 1, 0, 3).reshape(4 * 128, 4 * SL))

    base = {
        "wq_l": _whole_w(np.ascontiguousarray(f32("L0_attn_w_q")[_PERM].T)),
        "wk_l": _whole_w(np.ascontiguousarray(f32("L0_attn_w_k")[_PERM].T)),
        "wv_l": _whole_w(np.ascontiguousarray(f32("L0_attn_w_v").T)),
        "rope_cos": _tab_p(consts["rope_cos"]), "rope_sin": _tab_p(consts["rope_sin"]),
        "wr_l": _whole_w(np.ascontiguousarray(f32("L0_ffn_W_r").T)),
        "wk2_l": _whole_w(np.ascontiguousarray(f32("L0_ffn_W_k").T)),
        "wv2_l": _whole_w(np.ascontiguousarray(f32("L0_ffn_W_v").T)),
        "mu_r": f32("L0_ffn_mu_r").reshape(D, 1), "mu_k": f32("L0_ffn_mu_k").reshape(D, 1),
        "Wq": _pieces_cols(f32("L1_attn_Wq"), 256), "Wk": _pieces_cols(f32("L1_attn_Wk"), 256),
        "Wv": _pieces_cols(f32("L1_attn_Wv"), 256), "Wg": _pieces_cols(f32("L1_attn_Wg"), 256),
        "Wo": _pieces_rows(f32("L1_attn_Wo"), 2), "W2": _pieces_cols(f32("L1_attn_W2"), 256),
        "Wa": f32("L1_attn_Wa"), "Wb": f32("L1_attn_Wb"),
        "conv_q": f32("L1_attn_conv_q_w"), "conv_k": f32("L1_attn_conv_k_w"),
        "conv_v": f32("L1_attn_conv_v_w"),
        "ln1_g": f32("L1_attn_ln1_g").reshape(-1, 1), "ln1_b": f32("L1_attn_ln1_b").reshape(-1, 1),
        "ln2_g": f32("L1_attn_ln2_g").reshape(-1, 1), "ln2_b": f32("L1_attn_ln2_b").reshape(-1, 1),
        "rms_in_w": f32("L1_attn_rms_in_w").reshape(-1, 1), "b2": f32("L1_attn_b2").reshape(-1, 1),
        "o_norm_w": f32("L1_attn_o_norm_w").reshape(1, -1),
        "dt_bias": f32("L1_attn_dt_bias").reshape(1, -1), "A_log": f32("L1_attn_A_log").reshape(1, -1),
        "gate_W": f32("L1_ffn_gate_W"),
        "W1e": np.concatenate([_pieces_cols(f32("L1_ffn_W1")[e], 256) for e in range(8)], 0),
        "b1e": f32("L1_ffn_b1"),
        "W2e": np.concatenate([_pieces_rows(f32("L1_ffn_W2")[e], 2) for e in range(8)], 0),
        "b2e": f32("L1_ffn_b2"),
        "Ws1": _pieces_cols(f32("L1_ffn_Ws1"), 256), "bs1": f32("L1_ffn_bs1").reshape(-1, 1),
        "Ws2": _pieces_rows(f32("L1_ffn_Ws2"), 2), "bs2": f32("L1_ffn_bs2").reshape(-1, 1),
        "route_scale": f32("L1_ffn_route_scale").reshape(1, 1),
    }
    for name in ["ident", "ones_col", "ones_row", "neg_ones_row", "causal_neg", "tri_incl",
                 "mask1", "mask2", "mask3", "allones"]:
        base["c_" + name] = consts[name]
    return base


def kernel(**inputs):
    consts = host_consts()
    x = np.ascontiguousarray(np.asarray(inputs["L0_attn_x"], np.float32))
    B = x.shape[0]
    xT = np.ascontiguousarray(x.transpose(0, 2, 1))
    nc, _ = build_fused()
    base = _host_inputs(inputs, consts)
    res = run_bass_kernel_spmd(nc, [dict(base, xT=xT[b]) for b in range(B)],
                               core_ids=list(range(B)))
    out = np.stack([res.results[b]["outT"] for b in range(B)])
    return np.ascontiguousarray(out.transpose(0, 2, 1))



def _pieces_cols(w, ewid):
    K, N = w.shape
    npc, dck = N // ewid, K // 128
    a = w.reshape(dck, 128, npc, ewid)
    return np.ascontiguousarray(a.transpose(2, 1, 0, 3).reshape(npc * 128, dck * ewid))


def _pieces_rows(w, nrch):
    K, N = w.shape
    npc = K // (nrch * 128)
    a = w.reshape(npc, nrch, 128, N)
    return np.ascontiguousarray(a.transpose(0, 2, 1, 3).reshape(npc * 128, nrch * N))


def _whole_w(w):
    K, N = w.shape
    dck = K // 128
    return np.ascontiguousarray(w.reshape(dck, 128, N).transpose(1, 0, 2).reshape(128, dck * N))


_PERM = np.concatenate([np.arange(0, D, 2), np.arange(1, D, 2)])


def kernel_unfused(**inputs):
    consts = host_consts()
    x = np.ascontiguousarray(np.asarray(inputs["L0_attn_x"], np.float32))
    B = x.shape[0]
    xT = np.ascontiguousarray(x.transpose(0, 2, 1))
    f32 = lambda k: np.ascontiguousarray(np.asarray(inputs[k], np.float32))

    def cmap(base, names):
        for name in names:
            base["c_" + name] = consts[name]
        return base

    nc1, _ = build_l0_attn()
    base = cmap({"wq_l": np.ascontiguousarray(f32("L0_attn_w_q")[_PERM].T),
                 "wk_l": np.ascontiguousarray(f32("L0_attn_w_k")[_PERM].T),
                 "wv_l": np.ascontiguousarray(f32("L0_attn_w_v").T),
                 "rope_cos": consts["rope_cos"], "rope_sin": consts["rope_sin"]},
                ["ident", "ones_col", "ones_row", "causal_neg"])
    res1 = run_bass_kernel_spmd(nc1, [dict(base, xT=xT[b]) for b in range(B)],
                                core_ids=list(range(B)))
    o0 = [res1.results[b]["o0T"] for b in range(B)]

    nc2 = build_l0_ffn()
    base2 = cmap({"wr_l": np.ascontiguousarray(f32("L0_ffn_W_r").T),
                  "wk2_l": np.ascontiguousarray(f32("L0_ffn_W_k").T),
                  "wv2_l": np.ascontiguousarray(f32("L0_ffn_W_v").T),
                  "mu_r": f32("L0_ffn_mu_r").reshape(D, 1),
                  "mu_k": f32("L0_ffn_mu_k").reshape(D, 1)},
                 ["ones_col", "ones_row"])
    res2 = run_bass_kernel_spmd(nc2, [dict(base2, xT=xT[b], o0T=o0[b]) for b in range(B)],
                                core_ids=list(range(B)))
    x1 = [res2.results[b]["x1T"] for b in range(B)]

    nc3 = build_l1_attn()
    base3 = cmap({
        "Wq": _pieces_cols(f32("L1_attn_Wq"), 256), "Wk": _pieces_cols(f32("L1_attn_Wk"), 256),
        "Wv": _pieces_cols(f32("L1_attn_Wv"), 256), "Wg": _pieces_cols(f32("L1_attn_Wg"), 256),
        "Wo": _pieces_rows(f32("L1_attn_Wo"), 2), "W2": _pieces_cols(f32("L1_attn_W2"), 256),
        "Wa": f32("L1_attn_Wa"), "Wb": f32("L1_attn_Wb"),
        "conv_q": f32("L1_attn_conv_q_w"), "conv_k": f32("L1_attn_conv_k_w"),
        "conv_v": f32("L1_attn_conv_v_w"),
        "ln1_g": f32("L1_attn_ln1_g").reshape(-1, 1), "ln1_b": f32("L1_attn_ln1_b").reshape(-1, 1),
        "ln2_g": f32("L1_attn_ln2_g").reshape(-1, 1), "ln2_b": f32("L1_attn_ln2_b").reshape(-1, 1),
        "rms_in_w": f32("L1_attn_rms_in_w").reshape(-1, 1), "b2": f32("L1_attn_b2").reshape(-1, 1),
        "o_norm_w": f32("L1_attn_o_norm_w").reshape(1, -1),
        "dt_bias": f32("L1_attn_dt_bias").reshape(1, -1),
        "A_log": f32("L1_attn_A_log").reshape(1, -1)},
        ["ident", "ones_col", "ones_row", "neg_ones_row", "tri_incl", "mask1", "mask2", "mask3"])
    res3 = run_bass_kernel_spmd(nc3, [dict(base3, x1T=x1[b]) for b in range(B)],
                                core_ids=list(range(B)))
    y = [res3.results[b]["yT"] for b in range(B)]

    nc4 = build_l1_moe()
    base4 = cmap({
        "gate_W": f32("L1_ffn_gate_W"),
        "W1e": np.ascontiguousarray(f32("L1_ffn_W1").reshape(8 * D, 512)),
        "b1e": f32("L1_ffn_b1"),
        "W2e": np.ascontiguousarray(f32("L1_ffn_W2").reshape(8 * 512, D)),
        "b2e": f32("L1_ffn_b2"),
        "Ws1": f32("L1_ffn_Ws1"), "bs1": f32("L1_ffn_bs1").reshape(-1, 1),
        "Ws2": f32("L1_ffn_Ws2"), "bs2": f32("L1_ffn_bs2").reshape(-1, 1),
        "route_scale": f32("L1_ffn_route_scale").reshape(1, 1)},
        ["ident", "ones_col", "ones_row"])
    res4 = run_bass_kernel_spmd(nc4, [dict(base4, x1T=x1[b], yT=y[b]) for b in range(B)],
                                core_ids=list(range(B)))
    out = np.stack([res4.results[b]["outT"] for b in range(B)])
    return np.ascontiguousarray(out.transpose(0, 2, 1))


# version 20
# speedup vs baseline: 1.1423x; 1.0019x over previous
import contextlib
import math
import numpy as np
import ml_dtypes
import concourse.bass as bass
import concourse.mybir as mybir
from concourse.bass_utils import run_bass_kernel_spmd

F32 = mybir.dt.float32
BF16 = mybir.dt.bfloat16
AF = mybir.ActivationFunctionType
ALU = mybir.AluOpType
AX = mybir.AxisListType

T = 2048
D = 1024
DC = D // 128
NSL = 4
SL = 512
H, Kd, Vd = 4, 256, 512
VAL_DIM = H * Vd
EPS_L2 = 1e-6
LN_EPS = 1e-5

ENGINES = ("sync", "tensor", "vector", "scalar", "gpsimd")



DMA_POOL = 8


class Op:
    __slots__ = ("eng", "fn", "deps", "is_dma", "pos", "needs_inc", "inc_value", "waits",
                 "dma_sem", "dma_val")

    def __init__(self, eng, fn, deps, is_dma):
        self.eng = eng
        self.fn = fn
        self.deps = deps
        self.is_dma = is_dma
        self.needs_inc = is_dma
        self.inc_value = None
        self.waits = []
        self.dma_sem = None
        self.dma_val = None


class Buf:
    __slots__ = ("name", "last_writer", "readers")

    def __init__(self, name):
        self.name = name
        self.last_writer = None
        self.readers = []


class G:
    def __init__(self, nc):
        self.nc = nc
        self.ops = {e: [] for e in ENGINES}
        self.order = []
        self._fence = None
        self._fence_pending = {e: False for e in ENGINES}
        self._last_dma = {}
        self._dma_counts = {}
        self._all_dmas = {}

    def barrier(self):
        join = self.op("vector", lambda e: e.engine_nop(), deps=self.barrier_deps())
        self._fence = join
        self._fence_pending = {e: True for e in ENGINES}
        self._fence_pending["vector"] = False

    def op(self, eng, fn, reads=(), writes=(), deps=(), dma=False):
        dd = []
        if self._fence is not None and self._fence_pending.get(eng):
            dd.append(self._fence)
            self._fence_pending[eng] = False
        for b in reads:
            if b.last_writer is not None:
                dd.append(b.last_writer)
        for b in writes:
            if b.last_writer is not None:
                dd.append(b.last_writer)
            dd.extend(b.readers)
        dd.extend(deps)
        seen, deps_u = set(), []
        for d in dd:
            if d is not None and id(d) not in seen:
                seen.add(id(d))
                deps_u.append(d)
        o = Op(eng, fn, deps_u, dma)
        o.pos = len(self.order)
        self.order.append(o)
        self.ops[eng].append(o)
        if dma:
            i = self._dma_counts.get(eng, 0)
            self._dma_counts[eng] = i + 1
            o.dma_sem = f"dq_{eng}_{i % DMA_POOL}"
            o.dma_val = 16 * (i // DMA_POOL + 1)
            self._last_dma[eng] = o
        for b in reads:
            b.readers.append(o)
        for b in writes:
            b.last_writer = o
            b.readers = []
        return o

    def dma(self, queue, out_ap_fn, in_ap_fn, reads=(), writes=(), deps=()):
        def fn(eng):
            return eng.dma_start(out_ap_fn() if callable(out_ap_fn) else out_ap_fn,
                                 in_ap_fn() if callable(in_ap_fn) else in_ap_fn)
        return self.op(queue, fn, reads=reads, writes=writes, deps=deps, dma=True)

    def barrier_deps(self):
        dd = [self.ops[e][-1] for e in ENGINES if self.ops[e]]
        for e in ENGINES:
            dmas = [o for o in self.ops[e] if o.is_dma][-DMA_POOL:]
            dd.extend(dmas)
        return dd

    @staticmethod
    def _skip_edge(d, o):
        return d.eng == o.eng == "tensor" and not d.is_dma

    def finalize_and_emit(self, ctx):
        for o in self.order:
            for d in o.deps:
                if not self._skip_edge(d, o):
                    d.needs_inc = True
        counts = {e: 0 for e in ENGINES}
        for o in self.order:
            if o.needs_inc and not o.is_dma:
                counts[o.eng] += 1
                o.inc_value = counts[o.eng]
        for o in self.order:
            need = {}
            for d in o.deps:
                if self._skip_edge(d, o):
                    continue
                if d.is_dma:
                    need[d.dma_sem] = max(need.get(d.dma_sem, 0), d.dma_val)
                else:
                    assert d.inc_value is not None
                    need[d.eng] = max(need.get(d.eng, 0), d.inc_value)
            if o.is_dma and o.dma_val > 16:
                need[o.dma_sem] = max(need.get(o.dma_sem, 0), o.dma_val - 16)
            o.waits = sorted(need.items())
        final_counts = dict(counts)
        for e, n in self._dma_counts.items():
            for i in range(min(n, DMA_POOL)):
                nm = f"dq_{e}_{i}"
                nuses = (n - i + DMA_POOL - 1) // DMA_POOL
                final_counts[nm] = 16 * nuses

        sems = {e: ctx.enter_context(self.nc.semaphore("sem_" + e)) for e in ENGINES}
        for e, n in self._dma_counts.items():
            for i in range(min(n, DMA_POOL)):
                nm = f"dq_{e}_{i}"
                sems[nm] = ctx.enter_context(self.nc.semaphore(nm))

        block = ctx.enter_context(self.nc.Block())

        def make_body(ename):
            def body(eng):
                watermark = {k: 0 for k in sems}
                for o in self.ops[ename]:
                    for se, v in o.waits:
                        if v > watermark[se]:
                            eng.wait_ge(sems[se], v)
                            watermark[se] = v
                    inst = o.fn(eng)
                    if o.is_dma:
                        inst.then_inc(sems[o.dma_sem], 16)
                    elif o.needs_inc:
                        inst.then_inc(sems[o.eng], 1)
                if ename == "sync":
                    for se, v in final_counts.items():
                        if v > watermark.get(se, 0) and v > 0:
                            eng.wait_ge(sems[se], v)
            return body

        for ename in ENGINES:
            getattr(block, ename)(make_body(ename))



_NAME_SEQ = [0]


def _uniq(name):
    _NAME_SEQ[0] += 1
    return f"{name}_{_NAME_SEQ[0]}"


class Tile:

    def __init__(self, nc, ctx, name, shape, dtype, nbufs=1, alias=None):
        if alias is not None:
            assert alias.shape[1] >= shape[1] and alias.t.dtype == dtype
            self.t = alias.t
            self.shape = shape
            self.buf = alias.buf
            self.bufs = alias.bufs
            return
        name = _uniq(name)
        self.t = ctx.enter_context(nc.sbuf_tensor(name, shape, dtype))
        self.shape = shape
        if nbufs == 1:
            self.buf = Buf(name)
            self.bufs = [self.buf]
        else:
            self.bufs = [Buf(f"{name}#{i}") for i in range(nbufs)]
            self.buf = self.bufs[0]

    def __getitem__(self, idx):
        return self.t[idx]

    @property
    def ap(self):
        return self.t[:, :]


class _RowIndexer:
    def __init__(self, t, part, c0, width):
        self._t = t
        self._part = part
        self._c0 = c0
        self._w = width

    def __getitem__(self, idx):
        rows, cols = idx
        a = cols.start if cols.start is not None else 0
        b = cols.stop if cols.stop is not None else self._w
        return self._t[self._part:self._part + 1, self._c0 + a:self._c0 + b]


class RowTile:

    def __init__(self, parent, part, name, c0=0, width=None):
        self.buf = Buf(name)
        if width is None:
            width = parent.shape[1] - c0
        self.t = _RowIndexer(parent.t, part, c0, width)


class Psum:

    def __init__(self, nc, ctx, name, shape, dtype):
        name = _uniq(name)
        self.t = ctx.enter_context(nc.psum_tensor(name, shape, dtype))
        self.shape = shape
        elem = 4 if dtype == F32 else 2
        self.bank_cols = 2048 // elem
        self.nbanks = (shape[1] * elem + 2047) // 2048
        self.bufs = [Buf(f"{name}@b{i}") for i in range(self.nbanks)]

    def slice(self, c0, c1, p0=0, p1=128):
        ap = self.t[p0:p1, c0:c1]
        b0 = c0 // self.bank_cols
        b1 = (c1 - 1) // self.bank_cols
        return ap, self.bufs[b0:b1 + 1]


class Cells:

    def __init__(self, nc, ctx, g):
        self.tile = Tile(nc, ctx, "cells", [128, 16], F32)
        self.g = g
        self.map = {}
        self.n = 0

    def _col(self, value):
        v = float(value)
        if v not in self.map:
            k = self.n
            self.n += 1
            assert k < 16
            self.g.op("vector", lambda e, k=k, v=v: e.memset(self.tile.t[:, k:k + 1], v),
                      writes=[self.tile.buf])
            self.map[v] = k
        return self.map[v]

    def col(self, value):
        k = self._col(value)
        return self.tile.t[:, k:k + 1], self.tile.buf

    def cell(self, value):
        k = self._col(value)
        return self.tile.t[0:1, k:k + 1], self.tile.buf


def fm_cols(c, t0, t1, period, pad=0):
    return (c * period + pad + t0, c * period + pad + t1)



def host_consts():
    c = {}
    c["ident"] = np.eye(128, dtype=np.float32)
    c["allones"] = np.ones((128, 128), np.float32)
    c["ones_col"] = np.ones((128, 1), np.float32)
    c["ones_row"] = np.ones((1, 128), np.float32)
    c["neg_ones_row"] = -np.ones((1, 128), np.float32)
    i = np.arange(128)
    c["causal_neg"] = np.where(i[:, None] >= i[None, :], 0.0, -1e30).astype(np.float32)
    cn512 = np.zeros((128, 512), np.float32)
    cn512[:, 384:] = c["causal_neg"]
    c["causal_neg512"] = cn512
    ii = np.arange(D // 2, dtype=np.float64)
    theta = np.power(10000.0, -2.0 * (ii - 1.0) / D)
    pos = np.arange(T, dtype=np.float64)
    ang = theta[:, None] * pos[None, :]
    c["rope_cos"] = np.cos(ang).astype(np.float32)
    c["rope_sin"] = np.sin(ang).astype(np.float32)
    j = np.arange(128)
    cc, cp = j[:, None], j[None, :]
    c["tri_incl"] = (cc <= cp).astype(np.float32)
    big = np.float32(1e30)
    c["mask1"] = np.tile((cp >= cc) * -big, (1, 4)).astype(np.float32)
    c["mask2"] = np.tile((cp <= cc) * big, (1, 4)).astype(np.float32)
    c["mask3"] = np.tile((cp < cc) * big, (1, 4)).astype(np.float32)
    return c


def const_tensors(nc, g, ctx, consts, names):
    out = {}
    if not hasattr(g, "_const_drams"):
        g._const_drams = {}
    for name, want in names.items():
        arr = consts[name]
        if name in g._const_drams:
            dt = g._const_drams[name]
        else:
            dt = nc.dram_tensor("c_" + name, list(arr.shape), F32, kind="ExternalInput")
            g._const_drams[name] = dt
        for w in want:
            dtype = F32 if w == "f32" else BF16
            tl = Tile(nc, ctx, f"ct_{name}_{w}", list(arr.shape), dtype)
            q = "gpsimd" if w == "bf16" else "sync"
            g.dma(q, tl.t[:, :], dt[:, :], writes=[tl.buf])
            out[(name, w)] = tl
    return out



def emit_ln_stats_and_apply(nc, g, C, xsrc_dram, xbf, xn, ps_b, sq, t1, t2, mb_t, rb_t,
                            cells, eps=LN_EPS):
    ones_m = C[("allones", "bf16")]
    eps_col, eps_col_b = cells.col(eps)
    for sl in range(NSL):
        for c in range(DC):
            g.dma("gpsimd",
                  xbf.t[:, c * SL:(c + 1) * SL],
                  xsrc_dram[c * 128:(c + 1) * 128, sl * SL:(sl + 1) * SL],
                  writes=[xbf.bufs[c]])
        sum_ap, sum_bufs = ps_b.slice(0, SL)
        sq_ap, sq_bufs = ps_b.slice(512, 512 + SL)
        for c in range(DC):
            g.op("tensor",
                 lambda e, c=c, ap=sum_ap: e.matmul(ap, ones_m.t[:, :], xbf.t[:, c * SL:(c + 1) * SL],
                                                    start=(c == 0), stop=(c == DC - 1)),
                 reads=[xbf.bufs[c], ones_m.buf], writes=sum_bufs)
        for c in range(DC):
            g.op("vector",
                 lambda e, c=c: e.tensor_mul(sq.t[:, :], xbf.t[:, c * SL:(c + 1) * SL],
                                             xbf.t[:, c * SL:(c + 1) * SL]),
                 reads=[xbf.bufs[c]], writes=[sq.buf])
            g.op("tensor",
                 lambda e, c=c, ap=sq_ap: e.matmul(ap, ones_m.t[:, :], sq.t[:, :],
                                                   start=(c == 0), stop=(c == DC - 1)),
                 reads=[sq.buf, ones_m.buf], writes=sq_bufs)
        g.op("vector", lambda e, ap=sum_ap: e.tensor_scalar(mb_t.t[:, :], ap, 1.0 / D, None, ALU.mult),
             reads=sum_bufs, writes=[mb_t.buf])
        g.op("vector", lambda e: e.tensor_mul(t1.t[:, :], mb_t.t[:, :], mb_t.t[:, :]),
             reads=[mb_t.buf], writes=[t1.buf])
        g.op("vector", lambda e, ap=sq_ap: e.scalar_tensor_tensor(t2.t[:, :], ap, 1.0 / D,
                                                                  t1.t[:, :], ALU.mult, ALU.subtract),
             reads=sq_bufs + [t1.buf], writes=[t2.buf])
        g.op("scalar", lambda e: e.activation(t2.t[:, :], t2.t[:, :], AF.Ln, bias=eps_col),
             reads=[t2.buf, eps_col_b], writes=[t2.buf])
        g.op("scalar", lambda e: e.activation(rb_t.t[:, :], t2.t[:, :], AF.Exp, scale=-0.5),
             reads=[t2.buf], writes=[rb_t.buf])
        for c in range(DC):
            lo, hi = fm_cols(c, sl * SL, (sl + 1) * SL, T + 1, pad=1)
            g.op("vector", lambda e, c=c: e.tensor_tensor(t1.t[:, :], xbf.t[:, c * SL:(c + 1) * SL],
                                                          mb_t.t[:, :], ALU.subtract),
                 reads=[xbf.bufs[c], mb_t.buf], writes=[t1.buf])
            g.op("vector", lambda e, lo=lo, hi=hi: e.tensor_tensor(xn.t[:, lo:hi], t1.t[:, :],
                                                                   rb_t.t[:, :], ALU.mult),
                 reads=[t1.buf, rb_t.buf], writes=[xn.bufs[0]])



def emit_l0_attn(nc, gr, consts, cells, dr, debug=False, xn_shared=None):
    x_d = dr["xT"]
    wq_d = dr["wq_l"]
    wk_d = dr["wk_l"]
    wv_d = dr["wv_l"]
    cos_d = dr["rope_cos"]
    sin_d = dr["rope_sin"]
    o_d = dr["o0T"]
    if debug:
        dbg_xn = nc.dram_tensor("dbg_xn", [128, DC * (T + 1)], F32, kind="ExternalOutput")
        dbg_krot = nc.dram_tensor("dbg_krot", [128, DC * T], F32, kind="ExternalOutput")
        dbg_qs = nc.dram_tensor("dbg_qs", [128, DC * SL], F32, kind="ExternalOutput")
        dbg_pb = nc.dram_tensor("dbg_pb", [128, T], F32, kind="ExternalOutput")
        dbg_vtok = nc.dram_tensor("dbg_vtok", [128, 16 * 1024], F32, kind="ExternalOutput")

    ctx = contextlib.ExitStack()
    with ctx:
        C = const_tensors(nc, gr, ctx, consts, {"ident": ["bf16"], "ones_col": ["bf16"], "ones_row": ["f32"], "causal_neg": ["bf16"], "causal_neg512": ["bf16"], "allones": ["bf16"]})
        xn = xn_shared if xn_shared is not None else Tile(nc, ctx, "xn", [128, DC * (T + 1)], BF16)
        krot = Tile(nc, ctx, "krot", [128, DC * T], BF16, nbufs=DC)
        vtok = Tile(nc, ctx, "vtok", [128, 16 * 1024], BF16, nbufs=16)
        qs = Tile(nc, ctx, "qs", [128, DC * SL], BF16)
        xbf = Tile(nc, ctx, "xbf", [128, DC * SL], BF16, nbufs=DC)
        wA = Tile(nc, ctx, "wA", [128, DC * 1024], BF16)
        wB = Tile(nc, ctx, "wB", [128, DC * 1024], BF16)
        tabc = Tile(nc, ctx, "tabc", [128, 4 * SL], BF16)
        tabs = Tile(nc, ctx, "tabs", [128, 4 * SL], BF16)
        sq = Tile(nc, ctx, "sqs", [128, SL], BF16)
        t1 = Tile(nc, ctx, "t1", [128, SL], F32)
        t2 = Tile(nc, ctx, "t2", [128, SL], F32)
        mb_t = Tile(nc, ctx, "mb_t", [128, SL], F32)
        rb_t = Tile(nc, ctx, "rb_t", [128, SL], F32)
        pb = Tile(nc, ctx, "pb", [128, T], BF16)
        pbn = Tile(nc, ctx, "pbn", [128, T], BF16)
        pbt = Tile(nc, ctx, "pbt", [128, T], BF16)
        ost = Tile(nc, ctx, "ost", [128, DC * SL], F32)
        rowred = Tile(nc, ctx, "rowred", [128, 4], F32)

        ps_s = Psum(nc, ctx, "ps_s", [128, 2048], F32)
        ps_o = Psum(nc, ctx, "ps_o", [128, 1024], F32)
        ps_a = Psum(nc, ctx, "ps_a", [128, 512], F32)
        ps_t = Psum(nc, ctx, "ps_t", [128, 1024], BF16)

        for c in range(DC):
            gr.op("vector", lambda e, c=c: e.memset(xn.t[:, c * (T + 1):c * (T + 1) + 1], 0.0),
                  writes=[xn.bufs[0]])
        emit_ln_stats_and_apply(nc, gr, C, x_d, xbf, xn, ps_o, sq, t1, t2, mb_t, rb_t, cells)

        def load_w(dst, src):
            gr.dma("gpsimd", dst.t[:, :], src[:, :], writes=[dst.buf])

        load_w(wA, wk_d)

        def load_tables(sl):
            gr.dma("gpsimd", tabc.t[:, :], cos_d[sl * 128:(sl + 1) * 128, :], writes=[tabc.buf])
            gr.dma("gpsimd", tabs.t[:, :], sin_d[sl * 128:(sl + 1) * 128, :], writes=[tabs.buf])

        def proj_rope(w, sl, outtile, outbufs, out_period, t0):
            for j in range(4):
                apA, bufA = ps_a.slice(0, SL)
                apB, bufB = ps_o.slice(0, SL)
                for c in range(DC):
                    lo, hi = fm_cols(c, sl * SL, (sl + 1) * SL, T + 1, pad=1)
                    gr.op("tensor", lambda e, c=c, j=j, lo=lo, hi=hi, ap=apA:
                          e.matmul(ap, w.t[:, c * 1024 + 128 * j: c * 1024 + 128 * j + 128],
                                   xn.t[:, lo:hi], start=(c == 0), stop=(c == DC - 1)),
                          reads=[w.buf, xn.bufs[0]], writes=bufA)
                for c in range(DC):
                    lo, hi = fm_cols(c, sl * SL, (sl + 1) * SL, T + 1, pad=1)
                    gr.op("tensor", lambda e, c=c, j=j, lo=lo, hi=hi, ap=apB:
                          e.matmul(ap, w.t[:, c * 1024 + 128 * (j + 4): c * 1024 + 128 * (j + 4) + 128],
                                   xn.t[:, lo:hi], start=(c == 0), stop=(c == DC - 1)),
                          reads=[w.buf, xn.bufs[0]], writes=bufB)
                ctab = tabc.t[:, j * SL:(j + 1) * SL]
                stab = tabs.t[:, j * SL:(j + 1) * SL]
                def ocols(cc):
                    return (cc * out_period + t0, cc * out_period + t0 + SL)
                r0, r1 = ocols(j)
                o0c, o1c = ocols(j + 4)
                gr.op("vector", lambda e, ap=apA, ctab=ctab: e.tensor_tensor(t1.t[:, :], ap, ctab, ALU.mult),
                      reads=bufA + [tabc.buf], writes=[t1.buf])
                gr.op("vector", lambda e, ap=apB, stab=stab: e.tensor_tensor(t2.t[:, :], ap, stab, ALU.mult),
                      reads=bufB + [tabs.buf], writes=[t2.buf])
                gr.op("vector", lambda e, r0=r0, r1=r1: e.tensor_add(outtile.t[:, r0:r1], t1.t[:, :], t2.t[:, :]),
                      reads=[t1.buf, t2.buf], writes=[outbufs[j]])
                gr.op("vector", lambda e, ap=apB, ctab=ctab: e.tensor_tensor(t1.t[:, :], ap, ctab, ALU.mult),
                      reads=bufB + [tabc.buf], writes=[t1.buf])
                gr.op("vector", lambda e, ap=apA, stab=stab: e.tensor_tensor(t2.t[:, :], ap, stab, ALU.mult),
                      reads=bufA + [tabs.buf], writes=[t2.buf])
                gr.op("vector", lambda e, o0c=o0c, o1c=o1c: e.tensor_tensor(outtile.t[:, o0c:o1c], t1.t[:, :],
                                                                            t2.t[:, :], ALU.subtract),
                      reads=[t1.buf, t2.buf], writes=[outbufs[j + 4]])

        for sl in range(NSL):
            load_tables(sl)
            proj_rope(wA, sl, krot, krot.bufs, T, sl * SL)

        load_w(wB, wv_d)
        for i in range(16):
            for hhalf in range(2):
                apV, bufV = ps_a.slice(0, SL)
                for c in range(DC):
                    lo, hi = fm_cols(c, i * 128, (i + 1) * 128, T + 1, pad=1)
                    gr.op("tensor", lambda e, c=c, lo=lo, hi=hi, hh=hhalf, ap=apV:
                          e.matmul(ap, xn.t[:, lo:hi], wB.t[:, c * 1024 + 512 * hh: c * 1024 + 512 * hh + 512],
                                   start=(c == 0), stop=(c == DC - 1)),
                          reads=[wB.buf, xn.bufs[0]], writes=bufV)
                gr.op("vector", lambda e, i=i, hh=hhalf, ap=apV:
                      e.tensor_copy(vtok.t[:, i * 1024 + 512 * hh: i * 1024 + 512 * hh + 512], ap),
                      reads=bufV, writes=[vtok.bufs[i]])

        load_w(wA, wq_d)
        for sl in range(NSL):
            load_tables(sl)
            proj_rope(wA, sl, qs, [qs.buf] * 8, SL, 0)
            for qq in range(4):
                qi = sl * 4 + qq
                W = (qi + 1) * 128
                ngroups = (W + 511) // 512
                s_bufs_all = []
                for gg in range(ngroups):
                    c0, c1 = gg * 512, min(W, (gg + 1) * 512)
                    last = (gg == ngroups - 1)
                    apS, bufS = ps_s.slice(c0, c1)
                    s_bufs_all.extend(bufS)
                    for c in range(DC):
                        gr.op("tensor", lambda e, c=c, qq=qq, c0=c0, c1=c1, ap=apS, last=last:
                              e.matmul(ap, qs.t[:, c * SL + 128 * qq: c * SL + 128 * qq + 128],
                                       krot.t[:, c * T + c0: c * T + c1],
                                       start=(c == 0), stop=(c == DC - 1) and not last),
                              reads=[qs.buf, krot.bufs[c]], writes=bufS)
                    if last:
                        wgrp = c1 - c0
                        gr.op("tensor", lambda e, ap=apS, wgrp=wgrp: e.matmul(
                            ap, C[("ident", "bf16")].t[:, :],
                            C[("causal_neg512", "bf16")].t[:, 512 - wgrp:512],
                            start=False, stop=True),
                              reads=[C[("ident", "bf16")].buf, C[("causal_neg512", "bf16")].buf],
                              writes=bufS)
                s_bufs_all = list({id(b): b for b in s_bufs_all}.values())
                apSfull, _ = ps_s.slice(0, W)
                gr.op("vector", lambda e, ap=apSfull: e.reduce_max(rowred.t[:, 0:1], ap, axis=AX.X, negate=True),
                      reads=s_bufs_all, writes=[rowred.buf])
                gr.op("vector", lambda e: e.tensor_scalar(rowred.t[:, 1:2], rowred.t[:, 0:1], 1.0 / 32.0, None, ALU.mult),
                      reads=[rowred.buf], writes=[rowred.buf])
                gr.op("scalar", lambda e, W=W, ap=apSfull: e.activation(pb.t[:, 0:W], ap, AF.Exp,
                                                                        bias=rowred.t[:, 1:2], scale=1.0 / 32.0),
                      reads=s_bufs_all + [rowred.buf], writes=[pb.buf])
                gr.op("vector", lambda e, W=W: e.reduce_sum(rowred.t[:, 2:3], pb.t[:, 0:W], axis=AX.X),
                      reads=[pb.buf], writes=[rowred.buf])
                gr.op("vector", lambda e: e.reciprocal(rowred.t[:, 3:4], rowred.t[:, 2:3]),
                      reads=[rowred.buf], writes=[rowred.buf])
                gr.op("vector", lambda e, W=W: e.tensor_scalar(pbn.t[:, 0:W], pb.t[:, 0:W], rowred.t[:, 3:4],
                                                               None, ALU.mult),
                      reads=[pb.buf, rowred.buf], writes=[pbn.buf])
                nt = qi + 1
                for j0 in range(0, nt, 8):
                    jn = min(8, nt - j0)
                    apT, bufT = ps_t.slice(0, jn * 128)
                    for j in range(j0, j0 + jn):
                        ap1, _ = ps_t.slice((j - j0) * 128, (j - j0 + 1) * 128)
                        gr.op("tensor", lambda e, j=j, ap1=ap1: e.transpose(ap1, pbn.t[:, j * 128:(j + 1) * 128],
                                                                            C[("ident", "bf16")].t[:, :]),
                              reads=[pbn.buf, C[("ident", "bf16")].buf], writes=bufT)
                    gr.op("vector", lambda e, j0=j0, jn=jn, ap=apT:
                          e.tensor_copy(pbt.t[:, j0 * 128:(j0 + jn) * 128], ap),
                          reads=bufT, writes=[pbt.buf])
                for c in range(DC):
                    apO, bufO = ps_o.slice(c * 128, (c + 1) * 128)
                    for j in range(nt):
                        gr.op("tensor", lambda e, c=c, j=j, nt=nt, ap=apO:
                              e.matmul(ap, vtok.t[:, j * 1024 + 128 * c: j * 1024 + 128 * c + 128],
                                       pbt.t[:, j * 128:(j + 1) * 128],
                                       start=(j == 0), stop=(j == nt - 1)),
                              reads=[vtok.bufs[j], pbt.buf], writes=bufO)
                    gr.op("vector", lambda e, c=c, qq=qq, ap=apO:
                          e.tensor_copy(ost.t[:, c * SL + 128 * qq: c * SL + 128 * qq + 128], ap),
                          reads=bufO, writes=[ost.buf])
            for c in range(DC):
                gr.dma("sync", o_d[c * 128:(c + 1) * 128, sl * SL:(sl + 1) * SL],
                       ost.t[:, c * SL:(c + 1) * SL], reads=[ost.buf])

        if debug:
            gr.dma("gpsimd", dbg_xn[:, :], xn.t[:, :], reads=[xn.bufs[0]])
            for c in range(DC):
                gr.dma("gpsimd", dbg_krot[:, c * T:(c + 1) * T], krot.t[:, c * T:(c + 1) * T],
                       reads=[krot.bufs[c]])
            gr.dma("gpsimd", dbg_qs[:, :], qs.t[:, :], reads=[qs.buf])
            gr.dma("gpsimd", dbg_pb[:, :], pb.t[:, :], reads=[pb.buf])
            for i in range(16):
                gr.dma("gpsimd", dbg_vtok[:, i * 1024:(i + 1) * 1024],
                       vtok.t[:, i * 1024:(i + 1) * 1024], reads=[vtok.bufs[i]])



def build_l0_attn(debug=False):
    nc = bass.Bass("TRN2", target_bir_lowering=False)
    gr = G(nc)
    consts = host_consts()
    x_d = nc.dram_tensor("xT", [D, T], F32, kind="ExternalInput")
    wq_d = nc.dram_tensor("wq_l", [128, DC * D], F32, kind="ExternalInput")
    wk_d = nc.dram_tensor("wk_l", [128, DC * D], F32, kind="ExternalInput")
    wv_d = nc.dram_tensor("wv_l", [128, DC * D], F32, kind="ExternalInput")
    cos_d = nc.dram_tensor("rope_cos", [NSL * 128, 4 * SL], F32, kind="ExternalInput")
    sin_d = nc.dram_tensor("rope_sin", [NSL * 128, 4 * SL], F32, kind="ExternalInput")
    o_d = nc.dram_tensor("o0T", [D, T], F32, kind="ExternalOutput")
    dr = {"xT": x_d, "wq_l": wq_d, "wk_l": wk_d, "wv_l": wv_d, "rope_cos": cos_d, "rope_sin": sin_d, "o0T": o_d}
    with contextlib.ExitStack() as ctx0:
        cells = Cells(nc, ctx0, gr)
        emit_l0_attn(nc, gr, consts, cells, dr, debug=debug)
        gr.finalize_and_emit(ctx0)
    return nc, consts


def emit_l0_ffn(nc, gr, consts, cells, dr, debug=False, xn_shared=None):
    if debug:
        dbgf = {}
        for nm, shp in [("xr", [128, DC * SL]), ("xk", [128, DC * SL]), ("r", [128, DC * SL]),
                        ("kact", [128, DC * SL]), ("xn", [128, DC * (T + 1)])]:
            dbgf[nm] = nc.dram_tensor("dbgf_" + nm, shp, F32, kind="ExternalOutput")
    x_d = dr["xT"]
    o_d = dr["o0T"]
    wr_d = dr["wr_l"]
    wk_d = dr["wk2_l"]
    wv_d = dr["wv2_l"]
    mur_d = dr["mu_r"]
    muk_d = dr["mu_k"]
    x1_d = dr["x1T"]
    ctx = contextlib.ExitStack()
    with ctx:
        C = const_tensors(nc, gr, ctx, consts, {"ones_col": ["bf16"], "ones_row": ["f32"], "allones": ["bf16"]})
        xn = xn_shared if xn_shared is not None else Tile(nc, ctx, "xn", [128, DC * (T + 1)], BF16)
        wR = Tile(nc, ctx, "wR", [128, DC * 1024], BF16)
        wK = Tile(nc, ctx, "wK", [128, DC * 1024], BF16)
        wV = Tile(nc, ctx, "wV", [128, DC * 1024], BF16)
        mur = Tile(nc, ctx, "mur", [128, DC], F32)
        muk = Tile(nc, ctx, "muk", [128, DC], F32)
        t1 = Tile(nc, ctx, "t1", [128, SL], F32)
        if xn_shared is None:
            xbf = Tile(nc, ctx, "xbf", [128, DC * SL], BF16, nbufs=DC)
            sq = Tile(nc, ctx, "sqs", [128, SL], BF16)
            t2 = Tile(nc, ctx, "t2", [128, SL], F32)
            mb_t = Tile(nc, ctx, "mb_t", [128, SL], F32)
            rb_t = Tile(nc, ctx, "rb_t", [128, SL], F32)
        xr = Tile(nc, ctx, "xr", [128, DC * SL], BF16)
        xk = Tile(nc, ctx, "xk", [128, DC * SL], BF16)
        rbuf = Tile(nc, ctx, "rbuf", [128, DC * SL], BF16)
        kact = Tile(nc, ctx, "kact", [128, DC * SL], BF16)
        relu = Tile(nc, ctx, "relu", [128, SL], BF16)
        xf32 = Tile(nc, ctx, "xf32", [128, DC * SL], F32, nbufs=DC)
        of32 = Tile(nc, ctx, "of32", [128, DC * SL], F32, nbufs=DC)
        x1st = Tile(nc, ctx, "x1st", [128, DC * SL], F32)

        ps_s = Psum(nc, ctx, "ps_s", [128, 2048], F32)
        ps_b = Psum(nc, ctx, "ps_b", [128, 1024], F32)
        ps_m = Psum(nc, ctx, "ps_m", [128, 1024], F32)

        if xn_shared is None:
            for c in range(DC):
                gr.op("vector", lambda e, c=c: e.memset(xn.t[:, c * (T + 1):c * (T + 1) + 1], 0.0),
                      writes=[xn.bufs[0]])
            emit_ln_stats_and_apply(nc, gr, C, x_d, xbf, xn, ps_b, sq, t1, t2, mb_t, rb_t, cells)

        def load_w(dst, src):
            gr.dma("gpsimd", dst.t[:, :], src[:, :], writes=[dst.buf])
        load_w(wR, wr_d)
        load_w(wK, wk_d)
        load_w(wV, wv_d)
        for c in range(DC):
            gr.dma("sync", mur.t[:, c:c + 1], mur_d[c * 128:(c + 1) * 128, :], writes=[mur.buf])
            gr.dma("sync", muk.t[:, c:c + 1], muk_d[c * 128:(c + 1) * 128, :], writes=[muk.buf])

        for sl in range(NSL):
            for c in range(DC):
                lo, hi = fm_cols(c, sl * SL, (sl + 1) * SL, T + 1, pad=1)
                cur = xn.t[:, lo:hi]
                prev = xn.t[:, lo - 1:hi - 1]
                gr.op("vector", lambda e, cur=cur, prev=prev: e.tensor_tensor(t1.t[:, :], cur, prev, ALU.subtract),
                      reads=[xn.bufs[0]], writes=[t1.buf])
                gr.op("vector", lambda e, c=c, prev=prev: e.scalar_tensor_tensor(
                    xr.t[:, c * SL:(c + 1) * SL], t1.t[:, :], mur.t[:, c:c + 1], prev, ALU.mult, ALU.add),
                      reads=[t1.buf, mur.buf, xn.bufs[0]], writes=[xr.buf])
                gr.op("vector", lambda e, c=c, prev=prev: e.scalar_tensor_tensor(
                    xk.t[:, c * SL:(c + 1) * SL], t1.t[:, :], muk.t[:, c:c + 1], prev, ALU.mult, ALU.add),
                      reads=[t1.buf, muk.buf, xn.bufs[0]], writes=[xk.buf])
            for j in range(DC):
                apR, bufR = ps_s.slice((j % 4) * 512, (j % 4) * 512 + 512)
                for c in range(DC):
                    gr.op("tensor", lambda e, c=c, j=j, ap=apR:
                          e.matmul(ap, wR.t[:, c * 1024 + 128 * j: c * 1024 + 128 * j + 128],
                                   xr.t[:, c * SL:(c + 1) * SL], start=(c == 0), stop=(c == DC - 1)),
                          reads=[wR.buf, xr.buf], writes=bufR)
                gr.op("scalar", lambda e, j=j, ap=apR: e.activation(rbuf.t[:, j * SL:(j + 1) * SL], ap, AF.Sigmoid),
                      reads=bufR, writes=[rbuf.buf])
            for j in range(DC):
                apK, bufK = ps_s.slice((j % 4) * 512, (j % 4) * 512 + 512)
                for c in range(DC):
                    gr.op("tensor", lambda e, c=c, j=j, ap=apK:
                          e.matmul(ap, wK.t[:, c * 1024 + 128 * j: c * 1024 + 128 * j + 128],
                                   xk.t[:, c * SL:(c + 1) * SL], start=(c == 0), stop=(c == DC - 1)),
                          reads=[wK.buf, xk.buf], writes=bufK)
                gr.op("scalar", lambda e, j=j, ap=apK: e.activation(relu.t[:, :], ap, AF.Relu),
                      reads=bufK, writes=[relu.buf])
                gr.op("vector", lambda e, j=j, ap=apK: e.tensor_tensor(kact.t[:, j * SL:(j + 1) * SL],
                                                                       relu.t[:, :], ap, ALU.mult),
                      reads=[relu.buf] + bufK, writes=[kact.buf])
            for c in range(DC):
                gr.dma("sync", xf32.t[:, c * SL:(c + 1) * SL],
                       x_d[c * 128:(c + 1) * 128, sl * SL:(sl + 1) * SL], writes=[xf32.bufs[c]])
                gr.dma("scalar", of32.t[:, c * SL:(c + 1) * SL],
                       o_d[c * 128:(c + 1) * 128, sl * SL:(sl + 1) * SL], writes=[of32.bufs[c]])
            if debug and sl == NSL - 1:
                gr.dma("gpsimd", dbgf["xr"][:, :], xr.t[:, :], reads=[xr.buf])
                gr.dma("gpsimd", dbgf["xk"][:, :], xk.t[:, :], reads=[xk.buf])
                gr.dma("gpsimd", dbgf["r"][:, :], rbuf.t[:, :], reads=[rbuf.buf])
                gr.dma("gpsimd", dbgf["kact"][:, :], kact.t[:, :], reads=[kact.buf])
                gr.dma("gpsimd", dbgf["xn"][:, :], xn.t[:, :], reads=[xn.bufs[0]])
            for j in range(DC):
                apV, bufV = ps_m.slice((j % 2) * 512, (j % 2) * 512 + 512)
                for c in range(DC):
                    gr.op("tensor", lambda e, c=c, j=j, ap=apV:
                          e.matmul(ap, wV.t[:, c * 1024 + 128 * j: c * 1024 + 128 * j + 128],
                                   kact.t[:, c * SL:(c + 1) * SL], start=(c == 0), stop=(c == DC - 1)),
                          reads=[wV.buf, kact.buf], writes=bufV)
                gr.op("vector", lambda e, j=j, ap=apV: e.tensor_tensor(t1.t[:, :], rbuf.t[:, j * SL:(j + 1) * SL],
                                                                       ap, ALU.mult),
                      reads=[rbuf.buf] + bufV, writes=[t1.buf])
                gr.op("vector", lambda e, j=j: e.tensor_tensor(x1st.t[:, j * SL:(j + 1) * SL],
                                                               xf32.t[:, j * SL:(j + 1) * SL],
                                                               of32.t[:, j * SL:(j + 1) * SL], ALU.add),
                      reads=[xf32.bufs[j], of32.bufs[j]], writes=[x1st.buf])
                gr.op("vector", lambda e, j=j: e.tensor_tensor(x1st.t[:, j * SL:(j + 1) * SL],
                                                               x1st.t[:, j * SL:(j + 1) * SL],
                                                               t1.t[:, :], ALU.add),
                      reads=[t1.buf, x1st.buf], writes=[x1st.buf])
                gr.dma("sync", x1_d[j * 128:(j + 1) * 128, sl * SL:(sl + 1) * SL],
                       x1st.t[:, j * SL:(j + 1) * SL], reads=[x1st.buf])




def build_l0_ffn(debug=False):
    nc = bass.Bass("TRN2", target_bir_lowering=False)
    gr = G(nc)
    consts = host_consts()
    x_d = nc.dram_tensor("xT", [D, T], F32, kind="ExternalInput")
    o_d = nc.dram_tensor("o0T", [D, T], F32, kind="ExternalInput")
    wr_d = nc.dram_tensor("wr_l", [128, DC * D], F32, kind="ExternalInput")
    wk_d = nc.dram_tensor("wk2_l", [128, DC * D], F32, kind="ExternalInput")
    wv_d = nc.dram_tensor("wv2_l", [128, DC * D], F32, kind="ExternalInput")
    mur_d = nc.dram_tensor("mu_r", [D, 1], F32, kind="ExternalInput")
    muk_d = nc.dram_tensor("mu_k", [D, 1], F32, kind="ExternalInput")
    x1_d = nc.dram_tensor("x1T", [D, T], F32, kind="ExternalOutput")
    dr = {"xT": x_d, "o0T": o_d, "wr_l": wr_d, "wk2_l": wk_d, "wv2_l": wv_d, "mu_r": mur_d, "mu_k": muk_d, "x1T": x1_d}
    with contextlib.ExitStack() as ctx0:
        cells = Cells(nc, ctx0, gr)
        emit_l0_ffn(nc, gr, consts, cells, dr, debug=debug)
        gr.finalize_and_emit(ctx0)
    return nc


def emit_l1_attn(nc, gr, consts, cells, dr, debug=False, LS=512):
    x1_d = dr["x1T"]
    wq_d = dr["Wq"]
    wk_d = dr["Wk"]
    wv_d = dr["Wv"]
    wg_d = dr["Wg"]
    wo_d = dr["Wo"]
    w2_d = dr["W2"]
    wa_d = dr["Wa"]
    wb_d = dr["Wb"]
    cq_d = dr["conv_q"]
    ck_d = dr["conv_k"]
    cv_d = dr["conv_v"]
    ln1g_d = dr["ln1_g"]
    ln1b_d = dr["ln1_b"]
    ln2g_d = dr["ln2_g"]
    ln2b_d = dr["ln2_b"]
    rmsw_d = dr["rms_in_w"]
    b2_d = dr["b2"]
    onw_d = dr["o_norm_w"]
    dtb_d = dr["dt_bias"]
    alog_d = dr["A_log"]
    y_d = dr["yT"]
    if debug:
        dbg = {}
        for nm, shp in [("h", [128, DC * 256]), ("qn", [128, DC * 256]), ("kn", [128, DC * 256]),
                        ("vn", [128, 16 * 256]), ("gb", [128, 8]), ("bb", [128, 8]),
                        ("btok", [128, 4]), ("Ot", [128, 2048]), ("U", [128, 2048]),
                        ("xa", [128, DC * 256]), ("ofm", [128, 16 * 256]),
                        ("Dlow", [128, 512]), ("DlowT", [128, 512]), ("DincT", [128, 512]),
                        ("Lb", [128, 512]), ("Ltb", [128, 512]), ("Rfin", [128, 512]),
                        ("AttT", [128, 512]), ("Wt", [128, 2048]), ("Vtok", [128, 2048]),
                        ("Snew", [128, 4096]), ("bT", [1, 512]), ("ogc", [128, 2048]),
                        ("gsil", [128, 2048])]:
            dbg[nm] = nc.dram_tensor("dbg_" + nm, shp, F32, kind="ExternalOutput")

    NCH = LS // 128
    LNS = T // LS
    RP = LS + 3

    ctx = contextlib.ExitStack()
    with ctx:
        C = const_tensors(nc, gr, ctx, consts, {"ident": ["bf16", "f32"], "ones_col": ["bf16"], "ones_row": ["f32", "bf16"], "neg_ones_row": ["f32"], "tri_incl": ["bf16"], "mask1": ["f32"], "mask2": ["f32"], "mask3": ["f32"], "allones": ["bf16"]})
        x1bf = Tile(nc, ctx, "x1bf", [128, DC * LS], BF16, nbufs=DC)
        xn1 = Tile(nc, ctx, "xn1", [128, DC * LS], BF16, nbufs=DC)
        a1t = Tile(nc, ctx, "a1t", [128, LS], BF16)
        h = Tile(nc, ctx, "h", [128, DC * LS], BF16, nbufs=DC)
        qraw = Tile(nc, ctx, "qraw", [128, DC * RP], BF16)
        kraw = Tile(nc, ctx, "kraw", [128, DC * RP], BF16)
        vraw = Tile(nc, ctx, "vraw", [128, 16 * RP], BF16)
        qn = Tile(nc, ctx, "qn", [128, DC * LS], BF16)
        kn = Tile(nc, ctx, "kn", [128, DC * LS], BF16)
        vn = Tile(nc, ctx, "vn", [128, 16 * LS], BF16)
        gate_sil = Tile(nc, ctx, "gate_sil", [128, (NCH // 2) * 2048], BF16)
        o_fm = Tile(nc, ctx, "o_fm", [128, 16 * LS], BF16, alias=vraw)
        h2t = Tile(nc, ctx, "h2t", [128, DC * LS], BF16, alias=x1bf)
        WPN = 4
        wps = [Tile(nc, ctx, f"wp{i}", [128, 2048], BF16) for i in range(WPN)]
        wab = Tile(nc, ctx, "wab", [128, DC * 8], BF16)
        cwq = Tile(nc, ctx, "cwq", [128, DC * 4], F32)
        cwk = Tile(nc, ctx, "cwk", [128, DC * 4], F32)
        cwv = Tile(nc, ctx, "cwv", [128, 16 * 4], F32)
        dtb_row = Tile(nc, ctx, "dtb_row", [1, 8], BF16)
        alog_row = Tile(nc, ctx, "alog_row", [1, 4], F32)
        negexpA = Tile(nc, ctx, "negexpA", [128, 4], F32)
        onw_row = Tile(nc, ctx, "onw_row", [1, Vd], F32)
        normw_bc = Tile(nc, ctx, "normw_bc", [128, Vd], BF16)
        vcols = {}
        for nm in ["ln1g", "ln1b", "ln2g", "ln2b", "rmsw", "b2a"]:
            vcols[nm] = Tile(nc, ctx, "vc_" + nm, [128, DC], F32)
        g_bf = Tile(nc, ctx, "g_bf", [128, NCH * 4], BF16)
        beta_bf = Tile(nc, ctx, "beta_bf", [128, NCH * 4], BF16)
        beta_f = Tile(nc, ctx, "beta_f", [128, NCH * 4], F32)
        nbeta_f = Tile(nc, ctx, "nbeta_f", [128, NCH * 4], F32)
        sc4a = Tile(nc, ctx, "sc4a", [128, 4], F32)
        sc4b = Tile(nc, ctx, "sc4b", [128, 4], F32)
        sc4c = Tile(nc, ctx, "sc4c", [128, 4], F32)
        b_tok = Tile(nc, ctx, "b_tok", [128, 4], F32)
        Acol = Tile(nc, ctx, "Acol", [128, 4], F32)
        aC_col = Tile(nc, ctx, "aC_col", [128, 4], F32)
        bC_col = Tile(nc, ctx, "bC_col", [128, 4], F32)
        ebar = Tile(nc, ctx, "ebar", [128, 4], F32)
        btT_sb = Tile(nc, ctx, "btT_sb", [1, 512], BF16)
        Dlow = Tile(nc, ctx, "Dlow", [128, 512], BF16)
        DlowT = Tile(nc, ctx, "DlowT", [128, 512], BF16)
        DincT = Tile(nc, ctx, "DincT", [128, 512], BF16)
        zs = Tile(nc, ctx, "zs", [128, 512], F32)
        Lb = Tile(nc, ctx, "Lb", [128, 512], BF16)
        Ltb = Tile(nc, ctx, "Ltb", [128, 512], BF16)
        Apw = [Tile(nc, ctx, f"Apw{i}", [128, 512], BF16) for i in range(2)]
        Atw = [Tile(nc, ctx, f"Atw{i}", [128, 512], BF16) for i in range(2)]
        Rw = [Tile(nc, ctx, f"Rw{i}", [128, 512], BF16) for i in range(2)]
        AttT = Tile(nc, ctx, "AttT", [128, 512], BF16)
        Vtok = Tile(nc, ctx, "Vtok", [128, 2048], BF16)
        Ktok = Tile(nc, ctx, "Ktok", [128, 1024], BF16)
        Kbar = Tile(nc, ctx, "Kbar", [128, 1024], BF16)
        Wt = Tile(nc, ctx, "Wt", [128, 2048], BF16)
        W1t = Tile(nc, ctx, "W1t", [128, 512], F32)
        Ut = Tile(nc, ctx, "Ut", [128, 2048], BF16)
        attu = Tile(nc, ctx, "attu", [128, 512], BF16)
        O_t = Tile(nc, ctx, "O_t", [128, 2048], BF16)
        ogc = Tile(nc, ctx, "ogc", [128, 2048], BF16, alias=O_t)
        ostd = Tile(nc, ctx, "ostd", [128, 4], F32)
        Sb = [Tile(nc, ctx, f"Sb{i}", [128, DC * SL], BF16) for i in range(2)]
        ca = Tile(nc, ctx, "ca", [128, LS], BF16)
        cb = Tile(nc, ctx, "cb", [128, LS], BF16)
        m0b = Tile(nc, ctx, "m0b", [128, LS], F32)
        r0b = Tile(nc, ctx, "r0b", [128, LS], F32)
        r01b = Tile(nc, ctx, "r01b", [128, LS], F32)
        vbb = Tile(nc, ctx, "vbb", [128, LS], F32)
        lnb = Tile(nc, ctx, "lnb", [128, LS], F32)
        bT_sb = Tile(nc, ctx, "bT_sb", [1, 512], F32)
        sq = Tile(nc, ctx, "sqs", [128, LS], BF16)
        t1 = Tile(nc, ctx, "t1", [128, LS], F32)
        t2 = Tile(nc, ctx, "t2", [128, LS], F32)
        x1f = Tile(nc, ctx, "x1f", [128, 2 * LS], F32, nbufs=2)
        yst = Tile(nc, ctx, "yst", [128, 2 * LS], F32, nbufs=2)
        gelc = Tile(nc, ctx, "gelc", [128, LS], BF16, alias=ca)

        P1 = Psum(nc, ctx, "P1", [128, 512], F32)
        P2 = Psum(nc, ctx, "P2", [128, 512], F32)
        P3 = Psum(nc, ctx, "P3", [128, 1024], F32)
        P4 = Psum(nc, ctx, "P4", [128, 1024], F32)
        P5 = Psum(nc, ctx, "P5", [128, 512], F32)
        PT = Psum(nc, ctx, "PT", [128, 1024], BF16)

        ID_BF = C[("ident", "bf16")]
        ID_F = C[("ident", "f32")]
        OC_BF = C[("ones_col", "bf16")]
        OR_F = C[("ones_row", "f32")]
        OR_BF = C[("ones_row", "bf16")]
        NOR_F = C[("neg_ones_row", "f32")]
        TRI = C[("tri_incl", "bf16")]
        MASKS = [C[("mask1", "f32")], C[("mask2", "f32")], C[("mask3", "f32")]]
        eps5, eps5_b = cells.cell(LN_EPS)
        eps5c, eps5c_b = cells.col(LN_EPS)
        eps6, eps6_b = cells.cell(EPS_L2)
        eps6c, eps6c_b = cells.col(EPS_L2)
        lnq, lnq_b = cells.col(float(-np.log(16.0)))
        AO_BF = C[("allones", "bf16")]

        for c in range(DC):
            gr.dma("gpsimd", wab.t[:, c * 8:c * 8 + 4], wa_d[c * 128:(c + 1) * 128, :], writes=[wab.buf])
            gr.dma("gpsimd", wab.t[:, c * 8 + 4:c * 8 + 8], wb_d[c * 128:(c + 1) * 128, :], writes=[wab.buf])
            gr.dma("sync", cwq.t[:, c * 4:(c + 1) * 4], cq_d[c * 128:(c + 1) * 128, :], writes=[cwq.buf])
            gr.dma("sync", cwk.t[:, c * 4:(c + 1) * 4], ck_d[c * 128:(c + 1) * 128, :], writes=[cwk.buf])
        for c in range(16):
            gr.dma("sync", cwv.t[:, c * 4:(c + 1) * 4], cv_d[c * 128:(c + 1) * 128, :], writes=[cwv.buf])
        for nm, dsrc in [("ln1g", ln1g_d), ("ln1b", ln1b_d), ("ln2g", ln2g_d),
                         ("ln2b", ln2b_d), ("rmsw", rmsw_d), ("b2a", b2_d)]:
            for c in range(DC):
                gr.dma("sync", vcols[nm].t[:, c:c + 1], dsrc[c * 128:(c + 1) * 128, :], writes=[vcols[nm].buf])
        gr.op("vector", lambda e: e.memset(dtb_row.t[0:1, 4:8], 0.0), writes=[dtb_row.buf])
        gr.dma("gpsimd", dtb_row.t[0:1, 0:4], dtb_d[:, :], writes=[dtb_row.buf])
        gr.dma("sync", alog_row.t[:, :], alog_d[:, :], writes=[alog_row.buf])
        gr.dma("sync", onw_row.t[:, :], onw_d[:, :], writes=[onw_row.buf])
        gr.op("scalar", lambda e: e.activation(bT_sb.t[0:1, 0:4], alog_row.t[:, :], AF.Exp),
              reads=[alog_row.buf], writes=[bT_sb.buf])
        gr.op("vector", lambda e: e.tensor_scalar(bT_sb.t[0:1, 4:8], bT_sb.t[0:1, 0:4], -1.0, None, ALU.mult),
              reads=[bT_sb.buf], writes=[bT_sb.buf])
        apN, bufN = P5.slice(0, 4)
        gr.op("tensor", lambda e, ap=apN: e.matmul(ap, OR_F.t[:, :], bT_sb.t[0:1, 4:8], start=True, stop=True),
              reads=[bT_sb.buf, OR_F.buf], writes=bufN)
        gr.op("vector", lambda e, ap=apN: e.tensor_copy(negexpA.t[:, :], ap),
              reads=bufN, writes=[negexpA.buf])
        apW, bufW = P5.slice(0, Vd)
        gr.op("tensor", lambda e, ap=apW: e.matmul(ap, OR_F.t[:, :], onw_row.t[:, :], start=True, stop=True),
              reads=[onw_row.buf, OR_F.buf], writes=bufW)
        gr.op("vector", lambda e, ap=apW: e.tensor_copy(normw_bc.t[:, :], ap),
              reads=bufW, writes=[normw_bc.buf])
        for i in range(2):
            gr.op("vector", lambda e, i=i: e.memset(Sb[i].t[:, :], 0.0), writes=[Sb[i].buf])
        for c in range(DC):
            gr.op("vector", lambda e, c=c: e.memset(qraw.t[:, c * RP:c * RP + 3], 0.0), writes=[qraw.buf])
            gr.op("vector", lambda e, c=c: e.memset(kraw.t[:, c * RP:c * RP + 3], 0.0), writes=[kraw.buf])
        for c in range(16):
            gr.op("vector", lambda e, c=c: e.memset(vraw.t[:, c * RP:c * RP + 3], 0.0), writes=[vraw.buf])

        piece_idx = [0]

        def wpiece():
            wp = wps[piece_idx[0] % WPN]
            piece_idx[0] += 1
            return wp

        def load_piece(wdram, pidx):
            wp = wpiece()
            gr.dma("gpsimd", wp.t[:, :], wdram[pidx * 128:(pidx + 1) * 128, :], writes=[wp.buf])
            return wp

        def psum_pick(j):
            return (P1, P2)[j % 2]

        for sl in range(LNS):
            tsl = slice(sl * LS, (sl + 1) * LS)
            for c in range(DC):
                gr.dma("gpsimd", x1bf.t[:, c * LS:(c + 1) * LS],
                       x1_d[c * 128:(c + 1) * 128, sl * LS:(sl + 1) * LS], writes=[x1bf.bufs[c]])
            sum_ap, sum_bufs = P3.slice(0, LS)
            sq_ap, sq_bufs = P3.slice(512, 512 + LS)
            for c in range(DC):
                gr.op("tensor", lambda e, c=c, ap=sum_ap: e.matmul(ap, AO_BF.t[:, :],
                                                                   x1bf.t[:, c * LS:(c + 1) * LS],
                                                                   start=(c == 0), stop=(c == DC - 1)),
                      reads=[x1bf.bufs[c], AO_BF.buf], writes=sum_bufs)
            for c in range(DC):
                gr.op("vector", lambda e, c=c: e.tensor_mul(sq.t[:, :], x1bf.t[:, c * LS:(c + 1) * LS],
                                                            x1bf.t[:, c * LS:(c + 1) * LS]),
                      reads=[x1bf.bufs[c]], writes=[sq.buf])
                gr.op("tensor", lambda e, c=c, ap=sq_ap: e.matmul(ap, AO_BF.t[:, :], sq.t[:, :],
                                                                  start=(c == 0), stop=(c == DC - 1)),
                      reads=[sq.buf, AO_BF.buf], writes=sq_bufs)
            gr.op("vector", lambda e, ap=sum_ap: e.tensor_scalar(m0b.t[:, :], ap, 1.0 / D, None, ALU.mult),
                  reads=sum_bufs, writes=[m0b.buf])
            gr.op("vector", lambda e: e.tensor_mul(t1.t[:, :], m0b.t[:, :], m0b.t[:, :]),
                  reads=[m0b.buf], writes=[t1.buf])
            gr.op("vector", lambda e, ap=sq_ap: e.scalar_tensor_tensor(vbb.t[:, :], ap, 1.0 / D,
                                                                       t1.t[:, :], ALU.mult, ALU.subtract),
                  reads=sq_bufs + [t1.buf], writes=[vbb.buf])
            gr.op("scalar", lambda e: e.activation(lnb.t[:, :], vbb.t[:, :], AF.Ln, bias=eps5c),
                  reads=[vbb.buf, eps5c_b], writes=[lnb.buf])
            gr.op("scalar", lambda e: e.activation(r0b.t[:, :], lnb.t[:, :], AF.Exp, scale=-0.5),
                  reads=[lnb.buf], writes=[r0b.buf])
            gr.op("vector", lambda e: e.tensor_mul(t1.t[:, :], r0b.t[:, :], r0b.t[:, :]),
                  reads=[r0b.buf], writes=[t1.buf])
            gr.op("vector", lambda e: e.tensor_mul(t1.t[:, :], t1.t[:, :], vbb.t[:, :]),
                  reads=[t1.buf, vbb.buf], writes=[t1.buf])
            gr.op("scalar", lambda e: e.activation(t1.t[:, :], t1.t[:, :], AF.Ln, bias=eps5c),
                  reads=[t1.buf, eps5c_b], writes=[t1.buf])
            gr.op("scalar", lambda e: e.activation(t1.t[:, :], t1.t[:, :], AF.Exp, scale=-0.5),
                  reads=[t1.buf], writes=[t1.buf])
            gr.op("vector", lambda e: e.tensor_mul(r01b.t[:, :], t1.t[:, :], r0b.t[:, :]),
                  reads=[t1.buf, r0b.buf], writes=[r01b.buf])
            for c in range(DC):
                csl = slice(c * LS, (c + 1) * LS)
                gr.op("vector", lambda e, csl=csl: e.tensor_tensor(t1.t[:, :], x1bf.t[:, csl],
                                                                   m0b.t[:, :], ALU.subtract),
                      reads=[x1bf.bufs[c], m0b.buf], writes=[t1.buf])
                gr.op("vector", lambda e, csl=csl: e.tensor_tensor(xn1.t[:, csl], t1.t[:, :],
                                                                   r0b.t[:, :], ALU.mult),
                      reads=[t1.buf, r0b.buf], writes=[xn1.bufs[c]])
                gr.op("vector", lambda e: e.tensor_tensor(t2.t[:, :], t1.t[:, :], r01b.t[:, :], ALU.mult),
                      reads=[t1.buf, r01b.buf], writes=[t2.buf])
                gr.op("vector", lambda e, c=c: e.tensor_scalar(a1t.t[:, :], t2.t[:, :],
                                                               vcols["ln1g"].t[:, c:c + 1],
                                                               vcols["ln1b"].t[:, c:c + 1],
                                                               ALU.mult, ALU.add),
                      reads=[t2.buf, vcols["ln1g"].buf, vcols["ln1b"].buf], writes=[a1t.buf])
                rs_ap, rs_bufs = P5.slice(0, LS)
                gr.op("vector", lambda e: e.tensor_mul(sq.t[:, :], a1t.t[:, :], a1t.t[:, :]),
                      reads=[a1t.buf], writes=[sq.buf])
                gr.op("tensor", lambda e, c=c, ap=rs_ap: e.matmul(ap, AO_BF.t[:, :], sq.t[:, :],
                                                                  start=(c == 0), stop=(c == DC - 1)),
                      reads=[sq.buf, AO_BF.buf], writes=rs_bufs)
                gr.op("vector", lambda e, csl=csl: e.tensor_copy(h.t[:, csl], a1t.t[:, :]),
                      reads=[a1t.buf], writes=[h.bufs[c]])
            rs_ap, rs_bufs = P5.slice(0, LS)
            gr.op("scalar", lambda e, ap=rs_ap: e.activation(lnb.t[:, :], ap, AF.Ln, bias=eps6c,
                                                             scale=1.0 / D),
                  reads=rs_bufs + [eps6c_b], writes=[lnb.buf])
            gr.op("scalar", lambda e: e.activation(lnb.t[:, :], lnb.t[:, :], AF.Exp, scale=-0.5),
                  reads=[lnb.buf], writes=[lnb.buf])
            for c in range(DC):
                csl = slice(c * LS, (c + 1) * LS)
                gr.op("vector", lambda e, csl=csl: e.tensor_tensor(t1.t[:, :], h.t[:, csl],
                                                                   lnb.t[:, :], ALU.mult),
                      reads=[h.bufs[c], lnb.buf], writes=[t1.buf])
                gr.op("vector", lambda e, c=c, csl=csl: e.tensor_scalar(h.t[:, csl], t1.t[:, :],
                                                                        vcols["rmsw"].t[:, c:c + 1], None,
                                                                        ALU.mult),
                      reads=[t1.buf, vcols["rmsw"].buf], writes=[h.bufs[c]])

            def proj(wdram, ncols, dst, dstper):
                for p in range(ncols // 256):
                    wp = load_piece(wdram, p)
                    for jj in range(2):
                        j = 2 * p + jj
                        PS = psum_pick(j)
                        ap, bufs = PS.slice(0, LS)
                        for c in range(DC):
                            gr.op("tensor", lambda e, c=c, jj=jj, wp=wp, ap=ap:
                                  e.matmul(ap, wp.t[:, c * 256 + 128 * jj:c * 256 + 128 * jj + 128],
                                           h.t[:, c * LS:(c + 1) * LS], start=(c == 0), stop=(c == DC - 1)),
                                  reads=[wp.buf, h.bufs[c]], writes=bufs)
                        lo = j * dstper + 3
                        gr.op("vector", lambda e, lo=lo, ap=ap: e.tensor_copy(dst.t[:, lo:lo + LS], ap),
                              reads=bufs, writes=[dst.buf])
            proj(wq_d, D, qraw, RP)
            proj(wk_d, D, kraw, RP)
            proj(wv_d, VAL_DIM, vraw, RP)

            for tt_ in range(NCH):
                ab_ap, ab_bufs = P5.slice(0, 8)
                for c in range(DC):
                    gr.op("tensor", lambda e, c=c, tt_=tt_, ap=ab_ap:
                          e.matmul(ap, h.t[:, c * LS + 128 * tt_:c * LS + 128 * tt_ + 128],
                                   wab.t[:, c * 8:(c + 1) * 8], start=(c == 0), stop=False),
                          reads=[h.bufs[c], wab.buf], writes=ab_bufs)
                ab4_ap, _ = P5.slice(0, 4)
                gr.op("tensor", lambda e, ap=ab_ap: e.matmul(ap, OR_BF.t[:, :], dtb_row.t[0:1, 0:8],
                                                             start=False, stop=True),
                      reads=[dtb_row.buf, OR_BF.buf], writes=ab_bufs)
                gr.op("scalar", lambda e, ap=ab4_ap: e.activation(sc4a.t[:, :], ap, AF.Abs),
                      reads=ab_bufs, writes=[sc4a.buf])
                gr.op("scalar", lambda e: e.activation(sc4b.t[:, :], sc4a.t[:, :], AF.Exp, scale=-1.0),
                      reads=[sc4a.buf], writes=[sc4b.buf])
                gr.op("scalar", lambda e: e.activation(sc4b.t[:, :], sc4b.t[:, :], AF.Ln, bias=1.0),
                      reads=[sc4b.buf], writes=[sc4b.buf])
                gr.op("scalar", lambda e, ap=ab4_ap: e.activation(sc4c.t[:, :], ap, AF.Relu),
                      reads=ab_bufs, writes=[sc4c.buf])
                gr.op("vector", lambda e: e.tensor_add(sc4a.t[:, :], sc4b.t[:, :], sc4c.t[:, :]),
                      reads=[sc4b.buf, sc4c.buf], writes=[sc4a.buf])
                gr.op("vector", lambda e, tt_=tt_: e.tensor_mul(g_bf.t[:, 4 * tt_:4 * tt_ + 4],
                                                                sc4a.t[:, :], negexpA.t[:, :]),
                      reads=[sc4a.buf, negexpA.buf], writes=[g_bf.buf])
                b_ap = P5.t[0:128, 4:8]
                gr.op("scalar", lambda e, tt_=tt_, ap=b_ap: e.activation(beta_f.t[:, 4 * tt_:4 * tt_ + 4],
                                                                         ap, AF.Sigmoid),
                      reads=ab_bufs, writes=[beta_f.buf])
                gr.op("vector", lambda e, tt_=tt_: e.tensor_copy(beta_bf.t[:, 4 * tt_:4 * tt_ + 4],
                                                                 beta_f.t[:, 4 * tt_:4 * tt_ + 4]),
                      reads=[beta_f.buf], writes=[beta_bf.buf])
                gr.op("vector", lambda e, tt_=tt_: e.tensor_scalar(nbeta_f.t[:, 4 * tt_:4 * tt_ + 4],
                                                                   beta_f.t[:, 4 * tt_:4 * tt_ + 4],
                                                                   -1.0, None, ALU.mult),
                      reads=[beta_f.buf], writes=[nbeta_f.buf])

            def conv(raw, cw, nch, dst):
                for c in range(nch):
                    base = c * RP
                    gr.op("vector", lambda e, base=base, c=c, raw=raw, cw=cw:
                          e.tensor_scalar(ca.t[:, :], raw.t[:, base:base + LS], cw.t[:, c * 4:c * 4 + 1],
                                          None, ALU.mult),
                          reads=[raw.buf, cw.buf], writes=[ca.buf])
                    gr.op("vector", lambda e, base=base, c=c, raw=raw, cw=cw:
                          e.scalar_tensor_tensor(cb.t[:, :], raw.t[:, base + 1:base + 1 + LS],
                                                 cw.t[:, c * 4 + 1:c * 4 + 2], ca.t[:, :], ALU.mult, ALU.add),
                          reads=[raw.buf, cw.buf, ca.buf], writes=[cb.buf])
                    gr.op("vector", lambda e, base=base, c=c, raw=raw, cw=cw:
                          e.scalar_tensor_tensor(ca.t[:, :], raw.t[:, base + 2:base + 2 + LS],
                                                 cw.t[:, c * 4 + 2:c * 4 + 3], cb.t[:, :], ALU.mult, ALU.add),
                          reads=[raw.buf, cw.buf, cb.buf], writes=[ca.buf])
                    gr.op("vector", lambda e, base=base, c=c, raw=raw, cw=cw:
                          e.scalar_tensor_tensor(cb.t[:, :], raw.t[:, base + 3:base + 3 + LS],
                                                 cw.t[:, c * 4 + 3:c * 4 + 4], ca.t[:, :], ALU.mult, ALU.add),
                          reads=[raw.buf, cw.buf, ca.buf], writes=[cb.buf])
                    gr.op("scalar", lambda e, c=c, dst=dst: e.activation(dst.t[:, c * LS:(c + 1) * LS],
                                                                         cb.t[:, :], AF.Silu),
                          reads=[cb.buf], writes=[dst.buf])
            conv(qraw, cwq, DC, qn)
            conv(kraw, cwk, DC, kn)
            conv(vraw, cwv, 16, vn)
            if sl < LNS - 1:
                for c in range(DC):
                    gr.op("vector", lambda e, c=c: e.tensor_copy(qraw.t[:, c * RP:c * RP + 3],
                                                                 qraw.t[:, c * RP + LS:c * RP + LS + 3]),
                          reads=[qraw.buf], writes=[qraw.buf])
                    gr.op("vector", lambda e, c=c: e.tensor_copy(kraw.t[:, c * RP:c * RP + 3],
                                                                 kraw.t[:, c * RP + LS:c * RP + LS + 3]),
                          reads=[kraw.buf], writes=[kraw.buf])
                for c in range(16):
                    gr.op("vector", lambda e, c=c: e.tensor_copy(vraw.t[:, c * RP:c * RP + 3],
                                                                 vraw.t[:, c * RP + LS:c * RP + LS + 3]),
                          reads=[vraw.buf], writes=[vraw.buf])

            for cc0 in range(0, NCH, 2):
                for p in range(VAL_DIM // 256):
                    wp = load_piece(wg_d, p)
                    for jj in range(2):
                        jhv = 2 * p + jj
                        PS = psum_pick(jj)
                        for cc in range(cc0, cc0 + 2):
                            ap = PS.t[0:128, 128 * (cc - cc0):128 * (cc - cc0) + 128]
                            _, bufs = PS.slice(128 * (cc - cc0), 128 * (cc - cc0) + 128)
                            for c in range(DC):
                                gr.op("tensor", lambda e, c=c, cc=cc, jj=jj, wp=wp, ap=ap:
                                      e.matmul(ap, h.t[:, c * LS + 128 * cc:c * LS + 128 * cc + 128],
                                               wp.t[:, c * 256 + 128 * jj:c * 256 + 128 * jj + 128],
                                               start=(c == 0), stop=(c == DC - 1)),
                                      reads=[wp.buf, h.bufs[c]], writes=bufs)
                            gr.op("scalar", lambda e, cc=cc, cc0=cc0, jhv=jhv, ap=ap:
                                  e.activation(gate_sil.t[:, (cc - cc0) * 2048 + jhv * 128: (cc - cc0) * 2048 + jhv * 128 + 128],
                                               ap, AF.Silu),
                                  reads=bufs, writes=[gate_sil.buf])

                if cc0 == 0:
                    def l2n(dst, qscale):
                        for hh in range(H):
                            rs_ap, rs_bufs = P2.slice(0, LS)
                            for kc in range(2):
                                csl = slice((2 * hh + kc) * LS, (2 * hh + kc + 1) * LS)
                                gr.op("vector", lambda e, csl=csl, dst=dst: e.tensor_mul(sq.t[:, :], dst.t[:, csl],
                                                                                         dst.t[:, csl]),
                                      reads=[dst.buf], writes=[sq.buf])
                                gr.op("tensor", lambda e, kc=kc, ap=rs_ap: e.matmul(ap, AO_BF.t[:, :], sq.t[:, :],
                                                                                    start=(kc == 0), stop=(kc == 1)),
                                      reads=[sq.buf, AO_BF.buf], writes=rs_bufs)
                            gr.op("scalar", lambda e, ap=rs_ap: e.activation(lnb.t[:, :], ap, AF.Ln, bias=eps6c),
                                  reads=rs_bufs + [eps6c_b], writes=[lnb.buf])
                            if qscale:
                                gr.op("scalar", lambda e: e.activation(lnb.t[:, :], lnb.t[:, :], AF.Exp,
                                                                       scale=-0.5, bias=lnq),
                                      reads=[lnb.buf, lnq_b], writes=[lnb.buf])
                            else:
                                gr.op("scalar", lambda e: e.activation(lnb.t[:, :], lnb.t[:, :], AF.Exp, scale=-0.5),
                                      reads=[lnb.buf], writes=[lnb.buf])
                            for kc in range(2):
                                csl = slice((2 * hh + kc) * LS, (2 * hh + kc + 1) * LS)
                                gr.op("vector", lambda e, csl=csl, dst=dst:
                                      e.tensor_tensor(dst.t[:, csl], dst.t[:, csl], lnb.t[:, :], ALU.mult),
                                      reads=[dst.buf, lnb.buf], writes=[dst.buf])
                    l2n(qn, True)
                    l2n(kn, False)
                for cc in (cc0, cc0 + 1):
                    gsl = slice(4 * cc, 4 * cc + 4)
                    bt_ap, bt_bufs = P1.slice(0, 4)
                    gr.op("tensor", lambda e, gsl=gsl, ap=bt_ap: e.matmul(ap, TRI.t[:, :], g_bf.t[:, gsl],
                                                                          start=True, stop=True),
                          reads=[g_bf.buf, TRI.buf], writes=bt_bufs)
                    gr.op("vector", lambda e, ap=bt_ap: e.tensor_copy(b_tok.t[:, :], ap),
                          reads=bt_bufs, writes=[b_tok.buf])
                    gr.op("scalar", lambda e: e.activation(Acol.t[:, :], b_tok.t[:, :], AF.Exp),
                          reads=[b_tok.buf], writes=[Acol.buf])
                    for hh in range(H):
                        tp_ap = P5.t[0:1, 128 * hh:128 * hh + 128]
                        _, tp_bufs = P5.slice(128 * hh, 128 * hh + 128)
                        gr.op("tensor", lambda e, hh=hh, ap=tp_ap: e.transpose(ap, b_tok.t[:, hh:hh + 1],
                                                                               ID_F.t[:, :]),
                              reads=[b_tok.buf, ID_F.buf], writes=tp_bufs)
                    ball_ap, ball_bufs = P5.slice(0, 512, 0, 1)
                    gr.op("vector", lambda e, ap=ball_ap: e.tensor_copy(bT_sb.t[:, :], ap),
                          reads=ball_bufs, writes=[bT_sb.buf])
                    bC_ap, bC_bufs = P5.slice(0, 4)
                    lastv = bass.AP(bT_sb.t, 127, [[512, 1], [128, 4]])
                    gr.op("tensor", lambda e, ap=bC_ap, lastv=lastv: e.matmul(ap, OR_F.t[:, :], lastv,
                                                                              start=True, stop=True),
                          reads=[bT_sb.buf, OR_F.buf], writes=bC_bufs)
                    gr.op("vector", lambda e, ap=bC_ap: e.tensor_copy(bC_col.t[:, :], ap),
                          reads=bC_bufs, writes=[bC_col.buf])
                    gr.op("scalar", lambda e: e.activation(aC_col.t[:, :], bC_col.t[:, :], AF.Exp),
                          reads=[bC_col.buf], writes=[aC_col.buf])
                    for hh in range(H):
                        gr.op("scalar", lambda e, hh=hh: e.activation(ebar.t[:, hh:hh + 1], b_tok.t[:, hh:hh + 1],
                                                                      AF.Exp, scale=-1.0,
                                                                      bias=bC_col.t[:, hh:hh + 1]),
                              reads=[b_tok.buf, bC_col.buf], writes=[ebar.buf])
                    zp_ap, zp_bufs = P2.slice(0, 512)
                    for hh in range(H):
                        blk = P2.t[0:128, 128 * hh:128 * hh + 128]
                        bseg = bT_sb.t[0:1, 128 * hh:128 * hh + 128]
                        gr.op("tensor", lambda e, blk=blk, bseg=bseg:
                              e.matmul(blk, bseg, OR_F.t[:, :], start=True, stop=False),
                              reads=[bT_sb.buf, OR_F.buf], writes=zp_bufs)
                        gr.op("tensor", lambda e, blk=blk, bseg=bseg:
                              e.matmul(blk, NOR_F.t[:, :], bseg, start=False, stop=True),
                              reads=[bT_sb.buf, NOR_F.buf], writes=zp_bufs)
                    for Dt, mk, sc in [(Dlow, MASKS[0], 1.0), (DlowT, MASKS[1], -1.0), (DincT, MASKS[2], -1.0)]:
                        gr.op("vector", lambda e, mk=mk, ap=zp_ap: e.tensor_tensor(zs.t[:, :], ap, mk.t[:, :],
                                                                                   ALU.add),
                              reads=zp_bufs + [mk.buf], writes=[zs.buf])
                        gr.op("scalar", lambda e, Dt=Dt, sc=sc: e.activation(Dt.t[:, :], zs.t[:, :], AF.Exp,
                                                                             scale=sc),
                              reads=[zs.buf], writes=[Dt.buf])
                    g_ap, g_bufs = P1.slice(0, 512)
                    for hh in range(H):
                        blk = P1.t[0:128, 128 * hh:128 * hh + 128]
                        for kc in range(2):
                            kcol = slice((2 * hh + kc) * LS + 128 * cc, (2 * hh + kc) * LS + 128 * cc + 128)
                            gr.op("tensor", lambda e, blk=blk, kcol=kcol, hh=hh, kc=kc:
                                  e.matmul(blk, kn.t[:, kcol], kn.t[:, kcol],
                                           start=(hh == 0 and kc == 0), stop=False),
                                  reads=[kn.buf], writes=g_bufs)
                    bcolv = bass.AP(beta_bf.t, 4 * cc, [[NCH * 4, 128], [1, 4], [0, 128]])
                    gr.op("vector", lambda e, ap=g_ap: e.tensor_tensor(Lb.t[:, :], ap, Dlow.t[:, :], ALU.mult),
                          reads=g_bufs + [Dlow.buf], writes=[Lb.buf])
                    gr.op("vector", lambda e, bcolv=bcolv: e.tensor_tensor(
                        Lb.t[:, :].rearrange("p (a b) -> p a b", a=4), Lb.t[:, :].rearrange("p (a b) -> p a b", a=4),
                        bcolv, ALU.mult),
                          reads=[Lb.buf, beta_bf.buf], writes=[Lb.buf])
                    for hh in range(H):
                        tp_ap = P5.t[0:1, 128 * hh:128 * hh + 128]
                        _, tp_bufs = P5.slice(128 * hh, 128 * hh + 128)
                        gr.op("tensor", lambda e, hh=hh, cc=cc, ap=tp_ap:
                              e.matmul(ap, beta_bf.t[:, 4 * cc + hh:4 * cc + hh + 1], ID_BF.t[:, :],
                                       start=True, stop=True),
                              reads=[beta_bf.buf, ID_BF.buf], writes=tp_bufs)
                    btall_ap, btall_bufs = P5.slice(0, 512, 0, 1)
                    gr.op("vector", lambda e, ap=btall_ap: e.tensor_copy(btT_sb.t[:, :], ap),
                          reads=btall_bufs, writes=[btT_sb.buf])
                    brow_ap, brow_bufs = P4.slice(0, 512)
                    gr.op("tensor", lambda e, ap=brow_ap: e.matmul(ap, OR_BF.t[:, :], btT_sb.t[:, :],
                                                                   start=True, stop=True),
                          reads=[btT_sb.buf, OR_BF.buf], writes=brow_bufs)
                    gr.op("vector", lambda e, ap=g_ap: e.tensor_tensor(Ltb.t[:, :], ap, DlowT.t[:, :], ALU.mult),
                          reads=g_bufs + [DlowT.buf], writes=[Ltb.buf])
                    gr.op("vector", lambda e, ap=brow_ap: e.tensor_tensor(Ltb.t[:, :], Ltb.t[:, :], ap, ALU.mult),
                          reads=[Ltb.buf] + brow_bufs, writes=[Ltb.buf])
                    kq_ap, kq_bufs = P1.slice(0, 512)
                    for hh in range(H):
                        blk = P1.t[0:128, 128 * hh:128 * hh + 128]
                        for kc in range(2):
                            kcol = slice((2 * hh + kc) * LS + 128 * cc, (2 * hh + kc) * LS + 128 * cc + 128)
                            gr.op("tensor", lambda e, blk=blk, kcol=kcol, hh=hh, kc=kc:
                                  e.matmul(blk, kn.t[:, kcol], qn.t[:, kcol],
                                           start=(hh == 0 and kc == 0), stop=False),
                                  reads=[kn.buf, qn.buf], writes=kq_bufs)
                    gr.op("vector", lambda e, ap=kq_ap: e.tensor_tensor(AttT.t[:, :], ap, DincT.t[:, :], ALU.mult),
                          reads=kq_bufs + [DincT.buf], writes=[AttT.buf])
                    Sold, Snew = Sb[cc % 2], Sb[(cc + 1) % 2]
                    for hh in range(H):
                        pt_ap4, pt_bufs4 = PT.slice(0, 512)
                        for vc in range(4):
                            vcol = slice((4 * hh + vc) * LS + 128 * cc, (4 * hh + vc) * LS + 128 * cc + 128)
                            tp = PT.t[0:128, 128 * vc:128 * vc + 128]
                            gr.op("tensor", lambda e, tp=tp, vcol=vcol: e.transpose(tp, vn.t[:, vcol],
                                                                                    ID_BF.t[:, :]),
                                  reads=[vn.buf, ID_BF.buf], writes=pt_bufs4)
                        gr.op("vector", lambda e, hh=hh, ap=pt_ap4: e.tensor_copy(
                            Vtok.t[:, 512 * hh:512 * hh + 512], ap),
                              reads=pt_bufs4, writes=[Vtok.buf])
                    for hh in range(H):
                        ks_ap, ks_bufs = P3.slice(0, 512)
                        for kc in range(2):
                            kcol = slice((2 * hh + kc) * LS + 128 * cc, (2 * hh + kc) * LS + 128 * cc + 128)
                            gr.op("tensor", lambda e, kcol=kcol, kc=kc, hh=hh, ap=ks_ap, Sold=Sold:
                                  e.matmul(ap, kn.t[:, kcol], Sold.t[:, (2 * hh + kc) * 512:(2 * hh + kc + 1) * 512],
                                           start=(kc == 0), stop=(kc == 1)),
                                  reads=[kn.buf, Sold.buf], writes=ks_bufs)
                        gr.op("vector", lambda e, hh=hh, ap=ks_ap: e.scalar_tensor_tensor(
                            W1t.t[:, :], ap, Acol.t[:, hh:hh + 1], Vtok.t[:, 512 * hh:512 * hh + 512],
                            ALU.mult, ALU.subtract),
                              reads=ks_bufs + [Acol.buf, Vtok.buf], writes=[W1t.buf])
                        gr.op("vector", lambda e, hh=hh, cc=cc: e.tensor_scalar(
                            Wt.t[:, 512 * hh:512 * hh + 512], W1t.t[:, :],
                            nbeta_f.t[:, 4 * cc + hh:4 * cc + hh + 1], None, ALU.mult),
                              reads=[W1t.buf, nbeta_f.buf], writes=[Wt.buf])
                    gr.op("vector", lambda e: e.tensor_tensor(
                        Rw[0].t[:, :].rearrange("p (a b) -> p a b", a=4),
                        bass.AP(ID_BF.t, 0, [[128, 128], [0, 4], [1, 128]]),
                        Ltb.t[:, :].rearrange("p (a b) -> p a b", a=4), ALU.subtract),
                          reads=[Ltb.buf, ID_BF.buf], writes=[Rw[0].buf])
                    Acur, Atcur, Rcur = Lb, Ltb, Rw[0]
                    Anew, Atnew, Rnew = Apw[0], Atw[0], Rw[1]
                    Aspare, Atspare = Apw[1], Atw[1]
                    for k in range(1, 7):
                        a_ap, a_bufs = P1.slice(0, 512)
                        for hh in range(H):
                            blk = P1.t[0:128, 128 * hh:128 * hh + 128]
                            hs = slice(128 * hh, 128 * hh + 128)
                            gr.op("tensor", lambda e, blk=blk, hs=hs, hh=hh, Atcur=Atcur, Acur=Acur:
                                  e.matmul(blk, Atcur.t[:, hs], Acur.t[:, hs], start=(hh == 0), stop=(hh == 3)),
                                  reads=[Atcur.buf, Acur.buf], writes=a_bufs)
                        gr.op("vector", lambda e, ap=a_ap, Anew=Anew: e.tensor_copy(Anew.t[:, :], ap),
                              reads=a_bufs, writes=[Anew.buf])
                        if k < 6:
                            at_ap, at_bufs = P2.slice(0, 512)
                            for hh in range(H):
                                blk = P2.t[0:128, 128 * hh:128 * hh + 128]
                                hs = slice(128 * hh, 128 * hh + 128)
                                gr.op("tensor", lambda e, blk=blk, hs=hs, hh=hh, Atcur=Atcur, Acur=Acur:
                                      e.matmul(blk, Acur.t[:, hs], Atcur.t[:, hs], start=(hh == 0), stop=(hh == 3)),
                                      reads=[Atcur.buf, Acur.buf], writes=at_bufs)
                            gr.op("vector", lambda e, ap=at_ap, Atnew=Atnew: e.tensor_copy(Atnew.t[:, :], ap),
                                  reads=at_bufs, writes=[Atnew.buf])
                        r_ap, r_bufs = P1.slice(0, 512)
                        for hh in range(H):
                            blk = P1.t[0:128, 128 * hh:128 * hh + 128]
                            hs = slice(128 * hh, 128 * hh + 128)
                            gr.op("tensor", lambda e, blk=blk, hs=hs, hh=hh, Anew=Anew, Rcur=Rcur:
                                  e.matmul(blk, Anew.t[:, hs], Rcur.t[:, hs], start=(hh == 0), stop=(hh == 3)),
                                  reads=[Anew.buf, Rcur.buf], writes=r_bufs)
                        gr.op("vector", lambda e, ap=r_ap, Rcur=Rcur, Rnew=Rnew:
                              e.tensor_tensor(Rnew.t[:, :], Rcur.t[:, :], ap, ALU.add),
                              reads=[Rcur.buf] + r_bufs, writes=[Rnew.buf])
                        if k == 1:
                            Acur, Anew = Anew, Aspare
                            Atcur, Atnew = Atnew, Atspare
                        else:
                            Acur, Anew = Anew, Acur
                            Atcur, Atnew = Atnew, Atcur
                        Rcur, Rnew = Rnew, Rcur
                    Rfin = Rcur
                    for hh in range(H):
                        u_ap, u_bufs = P3.slice(0, 512) if hh % 2 == 0 else P3.slice(512, 1024)
                        hs = slice(128 * hh, 128 * hh + 128)
                        gr.op("tensor", lambda e, hs=hs, hh=hh, ap=u_ap, Rfin=Rfin:
                              e.matmul(ap, Rfin.t[:, hs], Wt.t[:, 512 * hh:512 * hh + 512], start=True, stop=True),
                              reads=[Rfin.buf, Wt.buf], writes=u_bufs)
                        gr.op("vector", lambda e, hh=hh, ap=u_ap: e.tensor_copy(Ut.t[:, 512 * hh:512 * hh + 512],
                                                                                ap),
                              reads=u_bufs, writes=[Ut.buf])
                    for hh in range(H):
                        qs_ap, qs_bufs = P3.slice(0, 512)
                        au_ap, au_bufs = P3.slice(512, 1024)
                        for kc in range(2):
                            kcol = slice((2 * hh + kc) * LS + 128 * cc, (2 * hh + kc) * LS + 128 * cc + 128)
                            gr.op("tensor", lambda e, kcol=kcol, kc=kc, hh=hh, ap=qs_ap, Sold=Sold:
                                  e.matmul(ap, qn.t[:, kcol], Sold.t[:, (2 * hh + kc) * 512:(2 * hh + kc + 1) * 512],
                                           start=(kc == 0), stop=(kc == 1)),
                                  reads=[qn.buf, Sold.buf], writes=qs_bufs)
                        hs = slice(128 * hh, 128 * hh + 128)
                        gr.op("tensor", lambda e, hs=hs, hh=hh, ap=au_ap:
                              e.matmul(ap, AttT.t[:, hs], Ut.t[:, 512 * hh:512 * hh + 512], start=True, stop=True),
                              reads=[AttT.buf, Ut.buf], writes=au_bufs)
                        gr.op("vector", lambda e, ap=au_ap: e.tensor_copy(attu.t[:, :], ap),
                              reads=au_bufs, writes=[attu.buf])
                        gr.op("vector", lambda e, hh=hh, ap=qs_ap: e.scalar_tensor_tensor(
                            O_t.t[:, 512 * hh:512 * hh + 512], ap, Acol.t[:, hh:hh + 1], attu.t[:, :],
                            ALU.mult, ALU.add),
                              reads=qs_bufs + [Acol.buf, attu.buf], writes=[O_t.buf])
                    for hh in range(H):
                        pt_ap2, pt_bufs2 = PT.slice(0, 256)
                        for kc in range(2):
                            kcol = slice((2 * hh + kc) * LS + 128 * cc, (2 * hh + kc) * LS + 128 * cc + 128)
                            tp = PT.t[0:128, 128 * kc:128 * kc + 128]
                            gr.op("tensor", lambda e, tp=tp, kcol=kcol: e.transpose(tp, kn.t[:, kcol],
                                                                                    ID_BF.t[:, :]),
                                  reads=[kn.buf, ID_BF.buf], writes=pt_bufs2)
                        gr.op("vector", lambda e, hh=hh, ap=PT.t[0:128, 0:256]: e.tensor_copy(
                            Ktok.t[:, 256 * hh:256 * hh + 256], ap),
                              reads=pt_bufs2, writes=[Ktok.buf])
                        gr.op("vector", lambda e, hh=hh: e.tensor_scalar(Kbar.t[:, 256 * hh:256 * hh + 256],
                                                                         Ktok.t[:, 256 * hh:256 * hh + 256],
                                                                         ebar.t[:, hh:hh + 1], None, ALU.mult),
                              reads=[Ktok.buf, ebar.buf], writes=[Kbar.buf])
                        for kc in range(2):
                            st_ap, st_bufs = P4.slice(512 * kc, 512 * kc + 512)
                            gr.op("tensor", lambda e, hh=hh, kc=kc, ap=st_ap:
                                  e.matmul(ap, Kbar.t[:, 256 * hh + 128 * kc:256 * hh + 128 * kc + 128],
                                           Ut.t[:, 512 * hh:512 * hh + 512], start=True, stop=True),
                                  reads=[Kbar.buf, Ut.buf], writes=st_bufs)
                            scol = slice((2 * hh + kc) * 512, (2 * hh + kc + 1) * 512)
                            gr.op("vector", lambda e, hh=hh, scol=scol, ap=st_ap, Sold=Sold, Snew=Snew:
                                  e.scalar_tensor_tensor(Snew.t[:, scol], Sold.t[:, scol],
                                                         aC_col.t[:, hh:hh + 1], ap, ALU.mult, ALU.add),
                                  reads=[Sold.buf, aC_col.buf] + st_bufs, writes=[Snew.buf])
                    gr.op("vector", lambda e: e.tensor_mul(Wt.t[:, :], O_t.t[:, :], O_t.t[:, :]),
                          reads=[O_t.buf], writes=[Wt.buf])
                    gr.op("vector", lambda e: e.reduce_sum(ostd.t[:, :],
                                                           Wt.t[:, :].rearrange("p (a b) -> p a b", a=4),
                                                           axis=AX.X),
                          reads=[Wt.buf], writes=[ostd.buf])
                    gr.op("scalar", lambda e: e.activation(ostd.t[:, :], ostd.t[:, :], AF.Ln, scale=1.0 / Vd,
                                                           bias=eps6c),
                          reads=[ostd.buf, eps6c_b], writes=[ostd.buf])
                    gr.op("scalar", lambda e: e.activation(ostd.t[:, :], ostd.t[:, :], AF.Exp, scale=-0.5),
                          reads=[ostd.buf], writes=[ostd.buf])
                    for hh in range(H):
                        gr.op("vector", lambda e, hh=hh: e.tensor_scalar(
                            ogc.t[:, 512 * hh:512 * hh + 512], O_t.t[:, 512 * hh:512 * hh + 512],
                            ostd.t[:, hh:hh + 1], None, ALU.mult),
                              reads=[O_t.buf, ostd.buf], writes=[ogc.buf])
                    nwv = bass.AP(normw_bc.t, 0, [[512, 128], [0, 4], [1, 512]])
                    gr.op("vector", lambda e, nwv=nwv: e.tensor_tensor(
                        ogc.t[:, :].rearrange("p (a b) -> p a b", a=4),
                        ogc.t[:, :].rearrange("p (a b) -> p a b", a=4), nwv, ALU.mult),
                          reads=[ogc.buf, normw_bc.buf], writes=[ogc.buf])
                    gr.op("vector", lambda e, cc=cc, cc0=cc0: e.tensor_tensor(ogc.t[:, :], ogc.t[:, :],
                                                                     gate_sil.t[:, (cc - cc0) * 2048:(cc - cc0 + 1) * 2048],
                                                                     ALU.mult),
                          reads=[ogc.buf, gate_sil.buf], writes=[ogc.buf])
                    for b8 in range(2):
                        pt8_ap, pt8_bufs = PT.slice(0, 1024)
                        for j in range(8):
                            jhv = b8 * 8 + j
                            tp = PT.t[0:128, 128 * j:128 * j + 128]
                            gr.op("tensor", lambda e, tp=tp, jhv=jhv: e.transpose(
                                tp, ogc.t[:, 128 * jhv:128 * jhv + 128], ID_BF.t[:, :]),
                                  reads=[ogc.buf, ID_BF.buf], writes=pt8_bufs)
                        for j in range(8):
                            jhv = b8 * 8 + j
                            gr.op("vector", lambda e, j=j, jhv=jhv, cc=cc: e.tensor_copy(
                                o_fm.t[:, jhv * LS + 128 * cc: jhv * LS + 128 * cc + 128],
                                PT.t[0:128, 128 * j:128 * j + 128]),
                                  reads=pt8_bufs, writes=[o_fm.buf])

                    if debug and sl == 0 and cc == 1:
                        for nm, tl in [("Dlow", Dlow), ("DlowT", DlowT), ("DincT", DincT),
                                       ("Lb", Lb), ("Ltb", Ltb), ("AttT", AttT), ("Wt", Wt),
                                       ("Vtok", Vtok), ("ogc", ogc)]:
                            gr.dma("gpsimd", dbg[nm][:, :], tl.t[:, :], reads=[tl.buf])
                        gr.dma("gpsimd", dbg["Rfin"][:, :], Rfin.t[:, :], reads=[Rfin.buf])
                        gr.dma("gpsimd", dbg["Snew"][:, :], Snew.t[:, :], reads=[Snew.buf])
                        gr.dma("gpsimd", dbg["bT"][:, :], bT_sb.t[:, :], reads=[bT_sb.buf])
                        gr.dma("gpsimd", dbg["gsil"][:, :], gate_sil.t[:, (cc - cc0) * 2048:(cc - cc0 + 1) * 2048],
                               reads=[gate_sil.buf])


            for grp in range(2):
                psl = [P3.slice(0, LS), P3.slice(512, 512 + LS), P4.slice(0, LS), P4.slice(512, 512 + LS)]
                for p in range(8):
                    wp = load_piece(wo_d, p)
                    for i in range(2):
                        jhv = 2 * p + i
                        for dci in range(4):
                            dc = grp * 4 + dci
                            ap, bufs = psl[dci]
                            gr.op("tensor", lambda e, wp=wp, i=i, dc=dc, jhv=jhv, ap=ap:
                                  e.matmul(ap, wp.t[:, i * D + 128 * dc:i * D + 128 * dc + 128],
                                           o_fm.t[:, jhv * LS:(jhv + 1) * LS],
                                           start=(jhv == 0), stop=(jhv == 15)),
                                  reads=[wp.buf, o_fm.buf], writes=bufs)
                for dci in range(4):
                    dc = grp * 4 + dci
                    ap, bufs = psl[dci]
                    csl = slice(dc * LS, (dc + 1) * LS)
                    gr.op("vector", lambda e, csl=csl, ap=ap: e.tensor_tensor(xn1.t[:, csl], xn1.t[:, csl],
                                                                              ap, ALU.add),
                          reads=[xn1.bufs[dc]] + bufs, writes=[xn1.bufs[dc]])
            s2_ap, s2_bufs = P3.slice(0, LS)
            q2_ap, q2_bufs = P3.slice(512, 512 + LS)
            for c in range(DC):
                gr.op("tensor", lambda e, c=c, ap=s2_ap: e.matmul(ap, AO_BF.t[:, :],
                                                                  xn1.t[:, c * LS:(c + 1) * LS],
                                                                  start=(c == 0), stop=(c == DC - 1)),
                      reads=[xn1.bufs[c], AO_BF.buf], writes=s2_bufs)
            for c in range(DC):
                gr.op("vector", lambda e, c=c: e.tensor_mul(sq.t[:, :], xn1.t[:, c * LS:(c + 1) * LS],
                                                            xn1.t[:, c * LS:(c + 1) * LS]),
                      reads=[xn1.bufs[c]], writes=[sq.buf])
                gr.op("tensor", lambda e, c=c, ap=q2_ap: e.matmul(ap, AO_BF.t[:, :], sq.t[:, :],
                                                                  start=(c == 0), stop=(c == DC - 1)),
                      reads=[sq.buf, AO_BF.buf], writes=q2_bufs)
            gr.op("vector", lambda e, ap=s2_ap: e.tensor_scalar(m0b.t[:, :], ap, 1.0 / D, None, ALU.mult),
                  reads=s2_bufs, writes=[m0b.buf])
            gr.op("vector", lambda e: e.tensor_mul(t1.t[:, :], m0b.t[:, :], m0b.t[:, :]),
                  reads=[m0b.buf], writes=[t1.buf])
            gr.op("vector", lambda e, ap=q2_ap: e.scalar_tensor_tensor(t2.t[:, :], ap, 1.0 / D,
                                                                       t1.t[:, :], ALU.mult, ALU.subtract),
                  reads=q2_bufs + [t1.buf], writes=[t2.buf])
            gr.op("scalar", lambda e: e.activation(t2.t[:, :], t2.t[:, :], AF.Ln, bias=eps5c),
                  reads=[t2.buf, eps5c_b], writes=[t2.buf])
            gr.op("scalar", lambda e: e.activation(r0b.t[:, :], t2.t[:, :], AF.Exp, scale=-0.5),
                  reads=[t2.buf], writes=[r0b.buf])
            for c in range(DC):
                csl = slice(c * LS, (c + 1) * LS)
                gr.op("vector", lambda e, csl=csl: e.tensor_tensor(t1.t[:, :], xn1.t[:, csl],
                                                                   m0b.t[:, :], ALU.subtract),
                      reads=[xn1.bufs[c], m0b.buf], writes=[t1.buf])
                gr.op("vector", lambda e: e.tensor_tensor(t2.t[:, :], t1.t[:, :], r0b.t[:, :], ALU.mult),
                      reads=[t1.buf, r0b.buf], writes=[t2.buf])
                gr.op("vector", lambda e, c=c, csl=csl: e.tensor_scalar(h2t.t[:, csl], t2.t[:, :],
                                                                        vcols["ln2g"].t[:, c:c + 1],
                                                                        vcols["ln2b"].t[:, c:c + 1],
                                                                        ALU.mult, ALU.add),
                      reads=[t2.buf, vcols["ln2g"].buf, vcols["ln2b"].buf], writes=[h2t.bufs[c]])
            for p in range(4):
                wp = load_piece(w2_d, p)
                for jj in range(2):
                    j = 2 * p + jj
                    PS = psum_pick(jj)
                    ap, bufs = PS.slice(0, LS)
                    for c in range(DC):
                        gr.op("tensor", lambda e, c=c, jj=jj, wp=wp, ap=ap:
                              e.matmul(ap, wp.t[:, c * 256 + 128 * jj:c * 256 + 128 * jj + 128],
                                       h2t.t[:, c * LS:(c + 1) * LS], start=(c == 0), stop=(c == DC - 1)),
                              reads=[wp.buf, h2t.bufs[c]], writes=bufs)
                    gr.op("scalar", lambda e, j=j, ap=ap: e.activation(gelc.t[:, :], ap, AF.Gelu,
                                                                       bias=vcols["b2a"].t[:, j:j + 1]),
                          reads=bufs + [vcols["b2a"].buf], writes=[gelc.buf])
                    xbuf = x1f.bufs[j % 2]
                    xap = x1f.t[:, (j % 2) * LS:(j % 2 + 1) * LS]
                    gr.dma("sync", xap, x1_d[j * 128:(j + 1) * 128, sl * LS:(sl + 1) * LS], writes=[xbuf])
                    ybuf = yst.bufs[j % 2]
                    yap = yst.t[:, (j % 2) * LS:(j % 2 + 1) * LS]
                    jsl = slice(j * LS, (j + 1) * LS)
                    gr.op("vector", lambda e, jsl=jsl, yap=yap: e.tensor_tensor(yap, xn1.t[:, jsl],
                                                                                gelc.t[:, :], ALU.add),
                          reads=[xn1.bufs[j], gelc.buf], writes=[ybuf])
                    gr.op("vector", lambda e, yap=yap, xap=xap: e.tensor_tensor(yap, yap, xap, ALU.add),
                          reads=[xbuf, ybuf], writes=[ybuf])
                    gr.dma("sync", y_d[j * 128:(j + 1) * 128, sl * LS:(sl + 1) * LS], yap, reads=[ybuf])

            if debug and sl == 0:
                gr.dma("gpsimd", dbg["h"][:, :], h.t[:, :], reads=list(h.bufs))
                gr.dma("gpsimd", dbg["qn"][:, :], qn.t[:, :], reads=[qn.buf])
                gr.dma("gpsimd", dbg["kn"][:, :], kn.t[:, :], reads=[kn.buf])
                gr.dma("gpsimd", dbg["vn"][:, :], vn.t[:, :], reads=[vn.buf])
                gr.dma("gpsimd", dbg["gb"][:, :], g_bf.t[:, :], reads=[g_bf.buf])
                gr.dma("gpsimd", dbg["bb"][:, :], beta_bf.t[:, :], reads=[beta_bf.buf])
                gr.dma("gpsimd", dbg["btok"][:, :], b_tok.t[:, :], reads=[b_tok.buf])
                gr.dma("gpsimd", dbg["Ot"][:, :], O_t.t[:, :], reads=[O_t.buf])
                gr.dma("gpsimd", dbg["U"][:, :], Ut.t[:, :], reads=[Ut.buf])
                gr.dma("gpsimd", dbg["xa"][:, :], xn1.t[:, :], reads=list(xn1.bufs))
                gr.dma("gpsimd", dbg["ofm"][:, :], o_fm.t[:, :], reads=[o_fm.buf])




def build_l1_attn(debug=False, LS=512):
    nc = bass.Bass("TRN2", target_bir_lowering=False)
    gr = G(nc)
    consts = host_consts()
    x1_d = nc.dram_tensor("x1T", [D, T], F32, kind="ExternalInput")
    wq_d = nc.dram_tensor("Wq", [4 * 128, 2048], F32, kind="ExternalInput")
    wk_d = nc.dram_tensor("Wk", [4 * 128, 2048], F32, kind="ExternalInput")
    wv_d = nc.dram_tensor("Wv", [8 * 128, 2048], F32, kind="ExternalInput")
    wg_d = nc.dram_tensor("Wg", [8 * 128, 2048], F32, kind="ExternalInput")
    wo_d = nc.dram_tensor("Wo", [8 * 128, 2048], F32, kind="ExternalInput")
    w2_d = nc.dram_tensor("W2", [4 * 128, 2048], F32, kind="ExternalInput")
    wa_d = nc.dram_tensor("Wa", [D, 4], F32, kind="ExternalInput")
    wb_d = nc.dram_tensor("Wb", [D, 4], F32, kind="ExternalInput")
    cq_d = nc.dram_tensor("conv_q", [D, 4], F32, kind="ExternalInput")
    ck_d = nc.dram_tensor("conv_k", [D, 4], F32, kind="ExternalInput")
    cv_d = nc.dram_tensor("conv_v", [VAL_DIM, 4], F32, kind="ExternalInput")
    ln1g_d = nc.dram_tensor("ln1_g", [D, 1], F32, kind="ExternalInput")
    ln1b_d = nc.dram_tensor("ln1_b", [D, 1], F32, kind="ExternalInput")
    ln2g_d = nc.dram_tensor("ln2_g", [D, 1], F32, kind="ExternalInput")
    ln2b_d = nc.dram_tensor("ln2_b", [D, 1], F32, kind="ExternalInput")
    rmsw_d = nc.dram_tensor("rms_in_w", [D, 1], F32, kind="ExternalInput")
    b2_d = nc.dram_tensor("b2", [D, 1], F32, kind="ExternalInput")
    onw_d = nc.dram_tensor("o_norm_w", [1, Vd], F32, kind="ExternalInput")
    dtb_d = nc.dram_tensor("dt_bias", [1, 4], F32, kind="ExternalInput")
    alog_d = nc.dram_tensor("A_log", [1, 4], F32, kind="ExternalInput")
    y_d = nc.dram_tensor("yT", [D, T], F32, kind="ExternalOutput")
    dr = {"x1T": x1_d, "Wq": wq_d, "Wk": wk_d, "Wv": wv_d, "Wg": wg_d, "Wo": wo_d, "W2": w2_d, "Wa": wa_d, "Wb": wb_d, "conv_q": cq_d, "conv_k": ck_d, "conv_v": cv_d, "ln1_g": ln1g_d, "ln1_b": ln1b_d, "ln2_g": ln2g_d, "ln2_b": ln2b_d, "rms_in_w": rmsw_d, "b2": b2_d, "o_norm_w": onw_d, "dt_bias": dtb_d, "A_log": alog_d, "yT": y_d}
    with contextlib.ExitStack() as ctx0:
        cells = Cells(nc, ctx0, gr)
        emit_l1_attn(nc, gr, consts, cells, dr, debug=debug, LS=LS)
        gr.finalize_and_emit(ctx0)
    return nc


def emit_l1_moe(nc, gr, consts, cells, dr, debug=False):
    x1_d = dr["x1T"]
    y_d = dr["yT"]
    gw_d = dr["gate_W"]
    w1_d = dr["W1e"]
    b1_d = dr["b1e"]
    w2_d = dr["W2e"]
    b2_d = dr["b2e"]
    ws1_d = dr["Ws1"]
    bs1_d = dr["bs1"]
    ws2_d = dr["Ws2"]
    bs2_d = dr["bs2"]
    rs_d = dr["route_scale"]
    out_d = dr["outT"]
    E = 8
    if debug:
        dbg = {}
        for nm, shp in [("gf", [128, E]), ("sc", [128, E]), ("acc", [128, DC * SL]),
                        ("xn1", [128, DC * SL])]:
            dbg[nm] = nc.dram_tensor("dbg_" + nm, shp, F32, kind="ExternalOutput")

    ctx = contextlib.ExitStack()
    with ctx:
        C = const_tensors(nc, gr, ctx, consts, {"ident": ["f32"], "ones_col": ["bf16"], "ones_row": ["f32", "bf16"]})
        OC_BF = C[("ones_col", "bf16")]
        OR_F = C[("ones_row", "f32")]
        OR_BF = C[("ones_row", "bf16")]
        ID_F = C[("ident", "f32")]
        eps5, eps5_b = cells.cell(LN_EPS)

        x1bf = Tile(nc, ctx, "x1bf", [128, DC * SL], BF16, nbufs=DC)
        xn1 = Tile(nc, ctx, "xn1", [128, DC * SL], BF16)
        acc = Tile(nc, ctx, "acc", [128, DC * SL], F32)
        Hg = Tile(nc, ctx, "Hg", [128, SL], F32)
        Hw = Tile(nc, ctx, "Hw", [128, 4 * SL], BF16)
        WPN = 6
        wps = [Tile(nc, ctx, f"wp{i}", [128, 2048], BF16) for i in range(WPN)]
        gw_sb = Tile(nc, ctx, "gw_sb", [128, DC * E], BF16)
        b2_sb = Tile(nc, ctx, "b2_sb", [8, D], BF16)
        b1col = Tile(nc, ctx, "b1col", [128, E * 4], F32)
        bs1col = Tile(nc, ctx, "bs1col", [128, 4], F32)
        bs2col = Tile(nc, ctx, "bs2col", [128, DC], F32)
        rs_sb = Tile(nc, ctx, "rs_sb", [1, 1], F32)
        rs_col = Tile(nc, ctx, "rs_col", [128, 1], F32)
        sc = Tile(nc, ctx, "sc", [128, E], F32)
        gf = Tile(nc, ctx, "gf", [128, E], F32)
        mm1 = Tile(nc, ctx, "mm1", [128, 4], F32)
        eqm = Tile(nc, ctx, "eqm", [128, E], F32)
        gfT8 = Tile(nc, ctx, "gfT8", [8, 4 * 128], BF16)
        ge_row = Tile(nc, ctx, "ge_row", [1, SL], BF16)
        yf = Tile(nc, ctx, "yf", [128, SL], F32)
        ost = Tile(nc, ctx, "ost", [128, SL], F32)
        sq = Tile(nc, ctx, "sqs", [128, SL], BF16)
        t1 = Tile(nc, ctx, "t1", [128, SL], F32)
        rows_all = Tile(nc, ctx, "rows_all", [1, 2048], F32)
        m_row = RowTile(rows_all, 0, "m_row", 0, SL)
        rstd0_row = RowTile(rows_all, 0, "rstd0_row", 512, SL)
        rows = RowTile(rows_all, 0, "rows", 1024, 2 * SL)

        PH = Psum(nc, ctx, "PH", [128, 1024], F32)
        PW = Psum(nc, ctx, "PW", [128, 1024], F32)
        PG = Psum(nc, ctx, "PG", [128, 512], F32)
        PS5 = Psum(nc, ctx, "PS5", [128, 512], F32)

        for c in range(DC):
            gr.dma("gpsimd", gw_sb.t[:, c * E:(c + 1) * E], gw_d[c * 128:(c + 1) * 128, :],
                   writes=[gw_sb.buf])
        gr.dma("gpsimd", b2_sb.t[:, :], b2_d[:, :], writes=[b2_sb.buf])
        for e in range(E):
            for hc in range(4):
                ap_src = bass.AP(b1_d, e * 512 + hc * 128, [[1, 128], [1, 1]])
                gr.dma("sync", b1col.t[:, e * 4 + hc:e * 4 + hc + 1], ap_src, writes=[b1col.buf])
        for hc in range(4):
            gr.dma("sync", bs1col.t[:, hc:hc + 1], bs1_d[hc * 128:(hc + 1) * 128, :], writes=[bs1col.buf])
        for c in range(DC):
            gr.dma("sync", bs2col.t[:, c:c + 1], bs2_d[c * 128:(c + 1) * 128, :], writes=[bs2col.buf])
        gr.dma("sync", rs_sb.t[:, :], rs_d[:, :], writes=[rs_sb.buf])
        rsb_ap, rsb_bufs = PS5.slice(0, 1)
        gr.op("tensor", lambda e, ap=rsb_ap: e.matmul(ap, OR_F.t[:, :], rs_sb.t[0:1, 0:1],
                                                      start=True, stop=True),
              reads=[rs_sb.buf, OR_F.buf], writes=rsb_bufs)
        gr.op("vector", lambda e, ap=rsb_ap: e.tensor_copy(rs_col.t[:, :], ap),
              reads=rsb_bufs, writes=[rs_col.buf])

        piece_idx = [0]

        def wpiece():
            wp = wps[piece_idx[0] % WPN]
            piece_idx[0] += 1
            return wp

        def load_piece(wdram, pidx):
            wp = wpiece()
            gr.dma("gpsimd", wp.t[:, :], wdram[pidx * 128:(pidx + 1) * 128, :], writes=[wp.buf])
            return wp

        for sl in range(NSL):
            for c in range(DC):
                gr.dma("gpsimd", x1bf.t[:, c * SL:(c + 1) * SL],
                       x1_d[c * 128:(c + 1) * 128, sl * SL:(sl + 1) * SL], writes=[x1bf.bufs[c]])
            sum_ap, sum_bufs = PH.slice(0, SL, 0, 1)
            sq_ap, sq_bufs = PH.slice(512, 1024, 0, 1)
            for c in range(DC):
                gr.op("tensor", lambda e, c=c, ap=sum_ap: e.matmul(ap, OC_BF.t[:, 0:1],
                                                                   x1bf.t[:, c * SL:(c + 1) * SL],
                                                                   start=(c == 0), stop=(c == DC - 1)),
                      reads=[x1bf.bufs[c], OC_BF.buf], writes=sum_bufs)
            for c in range(DC):
                gr.op("vector", lambda e, c=c: e.tensor_mul(sq.t[:, :], x1bf.t[:, c * SL:(c + 1) * SL],
                                                            x1bf.t[:, c * SL:(c + 1) * SL]),
                      reads=[x1bf.bufs[c]], writes=[sq.buf])
                gr.op("tensor", lambda e, c=c, ap=sq_ap: e.matmul(ap, OC_BF.t[:, 0:1], sq.t[:, :],
                                                                  start=(c == 0), stop=(c == DC - 1)),
                      reads=[sq.buf, OC_BF.buf], writes=sq_bufs)
            gr.op("vector", lambda e, ap=sum_ap: e.tensor_scalar(m_row.t[0:1, :], ap, 1.0 / D, None, ALU.mult),
                  reads=sum_bufs, writes=[m_row.buf])
            gr.op("vector", lambda e: e.tensor_mul(rows.t[0:1, 0:SL], m_row.t[0:1, :], m_row.t[0:1, :]),
                  reads=[m_row.buf], writes=[rows.buf])
            gr.op("vector", lambda e, ap=sq_ap: e.scalar_tensor_tensor(rows.t[0:1, SL:2 * SL], ap, 1.0 / D,
                                                                       rows.t[0:1, 0:SL], ALU.mult,
                                                                       ALU.subtract),
                  reads=sq_bufs + [rows.buf], writes=[rows.buf])
            gr.op("scalar", lambda e: e.activation(rows.t[0:1, 0:SL], rows.t[0:1, SL:2 * SL], AF.Ln,
                                                   bias=eps5),
                  reads=[rows.buf, eps5_b], writes=[rows.buf])
            gr.op("scalar", lambda e: e.activation(rstd0_row.t[0:1, :], rows.t[0:1, 0:SL], AF.Exp,
                                                   scale=-0.5),
                  reads=[rows.buf], writes=[rstd0_row.buf])
            mb_ap, mb_bufs = PW.slice(0, SL)
            rb_ap, rb_bufs = PW.slice(512, 1024)
            gr.op("tensor", lambda e, ap=mb_ap: e.matmul(ap, OR_F.t[:, :], m_row.t[0:1, :],
                                                         start=True, stop=True),
                  reads=[m_row.buf, OR_F.buf], writes=mb_bufs)
            gr.op("tensor", lambda e, ap=rb_ap: e.matmul(ap, OR_F.t[:, :], rstd0_row.t[0:1, :],
                                                         start=True, stop=True),
                  reads=[rstd0_row.buf, OR_F.buf], writes=rb_bufs)
            for c in range(DC):
                csl = slice(c * SL, (c + 1) * SL)
                gr.op("vector", lambda e, c=c, csl=csl, ap=mb_ap: e.tensor_tensor(
                    t1.t[:, :], x1bf.t[:, csl], ap, ALU.subtract),
                      reads=[x1bf.bufs[c]] + mb_bufs, writes=[t1.buf])
                gr.op("vector", lambda e, csl=csl, ap=rb_ap: e.tensor_tensor(xn1.t[:, csl], t1.t[:, :],
                                                                             ap, ALU.mult),
                      reads=[t1.buf] + rb_bufs, writes=[xn1.buf])
            gr.op("vector", lambda e: e.memset(acc.t[:, :], 0.0), writes=[acc.buf])

            for tt_ in range(4):
                sc_ap, sc_bufs = PS5.slice(0, E)
                for c in range(DC):
                    gr.op("tensor", lambda e, c=c, tt_=tt_, ap=sc_ap:
                          e.matmul(ap, xn1.t[:, c * SL + 128 * tt_:c * SL + 128 * tt_ + 128],
                                   gw_sb.t[:, c * E:(c + 1) * E], start=(c == 0), stop=(c == DC - 1)),
                          reads=[xn1.buf, gw_sb.buf], writes=sc_bufs)
                gr.op("scalar", lambda e, ap=sc_ap: e.activation(sc.t[:, :], ap, AF.Sigmoid),
                      reads=sc_bufs, writes=[sc.buf])
                gr.op("vector", lambda e: e.tensor_scalar(sc.t[:, :], sc.t[:, :], rs_col.t[:, 0:1],
                                                          None, ALU.mult),
                      reads=[sc.buf, rs_col.buf], writes=[sc.buf])
                gr.op("vector", lambda e: e.reduce_max(mm1.t[:, 0:1], sc.t[:, :], axis=AX.X),
                      reads=[sc.buf], writes=[mm1.buf])
                gr.op("vector", lambda e: e.tensor_scalar(eqm.t[:, :], sc.t[:, :], mm1.t[:, 0:1],
                                                          None, ALU.is_equal),
                      reads=[sc.buf, mm1.buf], writes=[eqm.buf])
                gr.op("vector", lambda e: e.scalar_tensor_tensor(eqm.t[:, :], eqm.t[:, :], -1e30,
                                                                 sc.t[:, :], ALU.mult, ALU.add),
                      reads=[eqm.buf, sc.buf], writes=[eqm.buf])
                gr.op("vector", lambda e: e.reduce_max(mm1.t[:, 1:2], eqm.t[:, :], axis=AX.X),
                      reads=[eqm.buf], writes=[mm1.buf])
                gr.op("vector", lambda e: e.tensor_add(mm1.t[:, 2:3], mm1.t[:, 0:1], mm1.t[:, 1:2]),
                      reads=[mm1.buf], writes=[mm1.buf])
                gr.op("vector", lambda e: e.reciprocal(mm1.t[:, 3:4], mm1.t[:, 2:3]),
                      reads=[mm1.buf], writes=[mm1.buf])
                gr.op("vector", lambda e: e.tensor_scalar(gf.t[:, :], sc.t[:, :], mm1.t[:, 1:2],
                                                          None, ALU.is_ge),
                      reads=[sc.buf, mm1.buf], writes=[gf.buf])
                gr.op("vector", lambda e: e.tensor_tensor(gf.t[:, :], gf.t[:, :], sc.t[:, :], ALU.mult),
                      reads=[gf.buf, sc.buf], writes=[gf.buf])
                gr.op("vector", lambda e: e.tensor_scalar(gf.t[:, :], gf.t[:, :], mm1.t[:, 3:4],
                                                          None, ALU.mult),
                      reads=[gf.buf, mm1.buf], writes=[gf.buf])
                gt_ap, gt_bufs = PS5.slice(0, 128, 0, 8)
                gr.op("tensor", lambda e, ap=gt_ap: e.transpose(ap, gf.t[:, :], ID_F.t[:, :]),
                      reads=[gf.buf, ID_F.buf], writes=gt_bufs)
                gr.op("vector", lambda e, tt_=tt_, ap=gt_ap: e.tensor_copy(
                    gfT8.t[:, 128 * tt_:128 * tt_ + 128], ap),
                      reads=gt_bufs, writes=[gfT8.buf])
                if debug and sl == 0 and tt_ == 0:
                    gr.dma("gpsimd", dbg["sc"][:, :], sc.t[:, :], reads=[sc.buf])
                    gr.dma("gpsimd", dbg["gf"][:, :], gf.t[:, :], reads=[gf.buf])

            for e_i in range(E):
                gr.dma("sync", ge_row.t[0:1, :],
                       bass.AP(gfT8.t, e_i * 4 * 128, [[4 * 128, 1], [1, SL]]),
                       reads=[gfT8.buf], writes=[ge_row.buf])
                gb_ap, gb_bufs = PG.slice(0, SL)
                gr.op("tensor", lambda e, ap=gb_ap: e.matmul(ap, OR_BF.t[:, :], ge_row.t[0:1, :],
                                                             start=True, stop=True),
                      reads=[ge_row.buf, OR_BF.buf], writes=gb_bufs)
                for p in range(2):
                    wp = load_piece(w1_d, e_i * 2 + p)
                    for jj in range(2):
                        hc = 2 * p + jj
                        h_ap = PH.t[0:128, 512 * (hc % 2):512 * (hc % 2) + 512]
                        _, h_bufs = PH.slice(512 * (hc % 2), 512 * (hc % 2) + 512)
                        for c in range(DC):
                            gr.op("tensor", lambda e, c=c, jj=jj, wp=wp, ap=h_ap:
                                  e.matmul(ap, wp.t[:, c * 256 + 128 * jj:c * 256 + 128 * jj + 128],
                                           xn1.t[:, c * SL:(c + 1) * SL], start=(c == 0),
                                           stop=(c == DC - 1)),
                                  reads=[wp.buf, xn1.buf], writes=h_bufs)
                        gr.op("scalar", lambda e, e_i=e_i, hc=hc, ap=h_ap: e.activation(
                            Hg.t[:, :], ap, AF.Gelu, bias=b1col.t[:, e_i * 4 + hc:e_i * 4 + hc + 1]),
                              reads=h_bufs + [b1col.buf], writes=[Hg.buf])
                        gr.op("vector", lambda e, hc=hc, ap=gb_ap: e.tensor_tensor(
                            Hw.t[:, hc * SL:(hc + 1) * SL], Hg.t[:, :], ap, ALU.mult),
                              reads=[Hg.buf] + gb_bufs, writes=[Hw.buf])
                for p in range(2):
                    wp = load_piece(w2_d, e_i * 2 + p)
                    for c in range(DC):
                        w_ap = PW.t[0:128, 512 * (c % 2):512 * (c % 2) + 512]
                        _, w_bufs = PW.slice(512 * (c % 2), 512 * (c % 2) + 512)
                        for i in range(2):
                            hc = 2 * p + i
                            gr.op("tensor", lambda e, wp=wp, i=i, c=c, hc=hc, ap=w_ap:
                                  e.matmul(ap, wp.t[:, i * D + 128 * c:i * D + 128 * c + 128],
                                           Hw.t[:, hc * SL:(hc + 1) * SL],
                                           start=(i == 0), stop=(i == 1)),
                                  reads=[wp.buf, Hw.buf], writes=w_bufs)
                        csl = slice(c * SL, (c + 1) * SL)
                        gr.op("vector", lambda e, csl=csl, ap=w_ap: e.tensor_tensor(
                            acc.t[:, csl], acc.t[:, csl], ap, ALU.add),
                              reads=gb_bufs + w_bufs + [acc.buf], writes=[acc.buf])
            for c in range(DC):
                b2_ap = PW.t[0:128, 512 * (c % 2):512 * (c % 2) + 512]
                _, b2_bufs = PW.slice(512 * (c % 2), 512 * (c % 2) + 512)
                gr.op("tensor", lambda e, c=c, ap=b2_ap: e.matmul(
                    ap, b2_sb.t[:, c * 128:(c + 1) * 128], gfT8.t[:, :], start=True, stop=True),
                      reads=[b2_sb.buf, gfT8.buf], writes=b2_bufs)
                csl = slice(c * SL, (c + 1) * SL)
                gr.op("vector", lambda e, csl=csl, ap=b2_ap: e.tensor_tensor(acc.t[:, csl],
                                                                             acc.t[:, csl], ap, ALU.add),
                      reads=b2_bufs + [acc.buf], writes=[acc.buf])
            for p in range(2):
                wp = load_piece(ws1_d, p)
                for jj in range(2):
                    hc = 2 * p + jj
                    h_ap = PH.t[0:128, 512 * (hc % 2):512 * (hc % 2) + 512]
                    _, h_bufs = PH.slice(512 * (hc % 2), 512 * (hc % 2) + 512)
                    for c in range(DC):
                        gr.op("tensor", lambda e, c=c, jj=jj, wp=wp, ap=h_ap:
                              e.matmul(ap, wp.t[:, c * 256 + 128 * jj:c * 256 + 128 * jj + 128],
                                       xn1.t[:, c * SL:(c + 1) * SL], start=(c == 0), stop=(c == DC - 1)),
                              reads=[wp.buf, xn1.buf], writes=h_bufs)
                    gr.op("scalar", lambda e, hc=hc, ap=h_ap: e.activation(
                        Hg.t[:, :], ap, AF.Gelu, bias=bs1col.t[:, hc:hc + 1]),
                          reads=h_bufs + [bs1col.buf], writes=[Hg.buf])
                    gr.op("vector", lambda e, hc=hc: e.tensor_copy(Hw.t[:, hc * SL:(hc + 1) * SL],
                                                                   Hg.t[:, :]),
                          reads=[Hg.buf], writes=[Hw.buf])
            for p in range(2):
                wp = load_piece(ws2_d, p)
                for c in range(DC):
                    w_ap = PW.t[0:128, 512 * (c % 2):512 * (c % 2) + 512]
                    _, w_bufs = PW.slice(512 * (c % 2), 512 * (c % 2) + 512)
                    for i in range(2):
                        hc = 2 * p + i
                        gr.op("tensor", lambda e, wp=wp, i=i, c=c, hc=hc, ap=w_ap:
                              e.matmul(ap, wp.t[:, i * D + 128 * c:i * D + 128 * c + 128],
                                       Hw.t[:, hc * SL:(hc + 1) * SL],
                                       start=(i == 0), stop=(i == 1)),
                              reads=[wp.buf, Hw.buf], writes=w_bufs)
                    csl = slice(c * SL, (c + 1) * SL)
                    if p == 0:
                        gr.op("vector", lambda e, c=c, csl=csl, ap=w_ap: e.scalar_tensor_tensor(
                            acc.t[:, csl], ap, bs2col.t[:, c:c + 1], acc.t[:, csl], ALU.add, ALU.add),
                              reads=w_bufs + [bs2col.buf, acc.buf], writes=[acc.buf])
                    else:
                        gr.op("vector", lambda e, csl=csl, ap=w_ap: e.tensor_tensor(
                            acc.t[:, csl], acc.t[:, csl], ap, ALU.add),
                              reads=w_bufs + [acc.buf], writes=[acc.buf])
            if debug and sl == 0:
                gr.dma("gpsimd", dbg["acc"][:, :], acc.t[:, :], reads=[acc.buf])
                gr.dma("gpsimd", dbg["xn1"][:, :], xn1.t[:, :], reads=[xn1.buf])
            for c in range(DC):
                gr.dma("sync", yf.t[:, :], y_d[c * 128:(c + 1) * 128, sl * SL:(sl + 1) * SL],
                       writes=[yf.buf])
                gr.op("vector", lambda e, c=c: e.tensor_tensor(ost.t[:, :], yf.t[:, :],
                                                               acc.t[:, c * SL:(c + 1) * SL], ALU.add),
                      reads=[yf.buf, acc.buf], writes=[ost.buf])
                gr.dma("sync", out_d[c * 128:(c + 1) * 128, sl * SL:(sl + 1) * SL], ost.t[:, :],
                       reads=[ost.buf])




def build_l1_moe(debug=False):
    nc = bass.Bass("TRN2", target_bir_lowering=False)
    gr = G(nc)
    consts = host_consts()
    x1_d = nc.dram_tensor("x1T", [D, T], F32, kind="ExternalInput")
    y_d = nc.dram_tensor("yT", [D, T], F32, kind="ExternalInput")
    gw_d = nc.dram_tensor("gate_W", [D, E], F32, kind="ExternalInput")
    w1_d = nc.dram_tensor("W1e", [16 * 128, 2048], F32, kind="ExternalInput")
    b1_d = nc.dram_tensor("b1e", [E, 512], F32, kind="ExternalInput")
    w2_d = nc.dram_tensor("W2e", [16 * 128, 2048], F32, kind="ExternalInput")
    b2_d = nc.dram_tensor("b2e", [E, D], F32, kind="ExternalInput")
    ws1_d = nc.dram_tensor("Ws1", [2 * 128, 2048], F32, kind="ExternalInput")
    bs1_d = nc.dram_tensor("bs1", [512, 1], F32, kind="ExternalInput")
    ws2_d = nc.dram_tensor("Ws2", [2 * 128, 2048], F32, kind="ExternalInput")
    bs2_d = nc.dram_tensor("bs2", [D, 1], F32, kind="ExternalInput")
    rs_d = nc.dram_tensor("route_scale", [1, 1], F32, kind="ExternalInput")
    out_d = nc.dram_tensor("outT", [D, T], F32, kind="ExternalOutput")
    dr = {"x1T": x1_d, "yT": y_d, "gate_W": gw_d, "W1e": w1_d, "b1e": b1_d, "W2e": w2_d, "b2e": b2_d, "Ws1": ws1_d, "bs1": bs1_d, "Ws2": ws2_d, "bs2": bs2_d, "route_scale": rs_d, "outT": out_d}
    with contextlib.ExitStack() as ctx0:
        cells = Cells(nc, ctx0, gr)
        emit_l1_moe(nc, gr, consts, cells, dr, debug=debug)
        gr.finalize_and_emit(ctx0)
    return nc



def build_fused(debug=False):
    nc = bass.Bass("TRN2", target_bir_lowering=False)
    gr = G(nc)
    consts = host_consts()
    E = 8
    dr = {}
    for name, shape in [("xT", [D, T]), ("wq_l", [128, DC * D]), ("wk_l", [128, DC * D]), ("wv_l", [128, DC * D]),
                        ("rope_cos", [NSL * 128, 4 * SL]), ("rope_sin", [NSL * 128, 4 * SL]),
                        ("wr_l", [128, DC * D]), ("wk2_l", [128, DC * D]), ("wv2_l", [128, DC * D]),
                        ("mu_r", [D, 1]), ("mu_k", [D, 1]),
                        ("Wq", [4 * 128, 2048]), ("Wk", [4 * 128, 2048]), ("Wv", [8 * 128, 2048]), ("Wg", [8 * 128, 2048]),
                        ("Wo", [8 * 128, 2048]), ("W2", [4 * 128, 2048]), ("Wa", [D, 4]), ("Wb", [D, 4]),
                        ("conv_q", [D, 4]), ("conv_k", [D, 4]), ("conv_v", [VAL_DIM, 4]),
                        ("ln1_g", [D, 1]), ("ln1_b", [D, 1]), ("ln2_g", [D, 1]), ("ln2_b", [D, 1]),
                        ("rms_in_w", [D, 1]), ("b2", [D, 1]), ("o_norm_w", [1, Vd]),
                        ("dt_bias", [1, 4]), ("A_log", [1, 4]),
                        ("gate_W", [D, E]), ("W1e", [16 * 128, 2048]), ("b1e", [E, 512]),
                        ("W2e", [16 * 128, 2048]), ("b2e", [E, D]), ("Ws1", [2 * 128, 2048]),
                        ("bs1", [512, 1]), ("Ws2", [2 * 128, 2048]), ("bs2", [D, 1]),
                        ("route_scale", [1, 1])]:
        dr[name] = nc.dram_tensor(name, shape, F32, kind="ExternalInput")
    ikind = "ExternalOutput" if debug else Ellipsis
    for name, shape in [("o0T", [D, T]), ("x1T", [D, T]), ("yT", [D, T])]:
        if debug:
            dr[name] = nc.dram_tensor(name, shape, F32, kind="ExternalOutput")
        else:
            dr[name] = nc.dram_tensor(name, shape, F32)
    dr["outT"] = nc.dram_tensor("outT", [D, T], F32, kind="ExternalOutput")

    with contextlib.ExitStack() as ctx0:
        cells = Cells(nc, ctx0, gr)
        with contextlib.ExitStack() as ctx01:
            xn_shared = Tile(nc, ctx01, "xn_shared", [128, DC * (T + 1)], BF16)
            emit_l0_attn(nc, gr, consts, cells, dr, xn_shared=xn_shared)
            gr.barrier()
            emit_l0_ffn(nc, gr, consts, cells, dr, xn_shared=xn_shared)
        gr.barrier()
        emit_l1_attn(nc, gr, consts, cells, dr)
        gr.barrier()
        emit_l1_moe(nc, gr, consts, cells, dr)
        gr.finalize_and_emit(ctx0)
    return nc, consts


def _host_inputs(inputs, consts):
    f32 = lambda k: np.ascontiguousarray(np.asarray(inputs[k], np.float32))
    def _tab_p(t):
        a = t.reshape(4, 128, 4, SL)
        return np.ascontiguousarray(a.transpose(2, 1, 0, 3).reshape(4 * 128, 4 * SL))

    base = {
        "wq_l": _whole_w(np.ascontiguousarray(f32("L0_attn_w_q")[_PERM].T)),
        "wk_l": _whole_w(np.ascontiguousarray(f32("L0_attn_w_k")[_PERM].T)),
        "wv_l": _whole_w(np.ascontiguousarray(f32("L0_attn_w_v").T)),
        "rope_cos": _tab_p(consts["rope_cos"]), "rope_sin": _tab_p(consts["rope_sin"]),
        "wr_l": _whole_w(np.ascontiguousarray(f32("L0_ffn_W_r").T)),
        "wk2_l": _whole_w(np.ascontiguousarray(f32("L0_ffn_W_k").T)),
        "wv2_l": _whole_w(np.ascontiguousarray(f32("L0_ffn_W_v").T)),
        "mu_r": f32("L0_ffn_mu_r").reshape(D, 1), "mu_k": f32("L0_ffn_mu_k").reshape(D, 1),
        "Wq": _pieces_cols(f32("L1_attn_Wq"), 256), "Wk": _pieces_cols(f32("L1_attn_Wk"), 256),
        "Wv": _pieces_cols(f32("L1_attn_Wv"), 256), "Wg": _pieces_cols(f32("L1_attn_Wg"), 256),
        "Wo": _pieces_rows(f32("L1_attn_Wo"), 2), "W2": _pieces_cols(f32("L1_attn_W2"), 256),
        "Wa": f32("L1_attn_Wa"), "Wb": f32("L1_attn_Wb"),
        "conv_q": f32("L1_attn_conv_q_w"), "conv_k": f32("L1_attn_conv_k_w"),
        "conv_v": f32("L1_attn_conv_v_w"),
        "ln1_g": f32("L1_attn_ln1_g").reshape(-1, 1), "ln1_b": f32("L1_attn_ln1_b").reshape(-1, 1),
        "ln2_g": f32("L1_attn_ln2_g").reshape(-1, 1), "ln2_b": f32("L1_attn_ln2_b").reshape(-1, 1),
        "rms_in_w": f32("L1_attn_rms_in_w").reshape(-1, 1), "b2": f32("L1_attn_b2").reshape(-1, 1),
        "o_norm_w": f32("L1_attn_o_norm_w").reshape(1, -1),
        "dt_bias": f32("L1_attn_dt_bias").reshape(1, -1), "A_log": f32("L1_attn_A_log").reshape(1, -1),
        "gate_W": f32("L1_ffn_gate_W"),
        "W1e": np.concatenate([_pieces_cols(f32("L1_ffn_W1")[e], 256) for e in range(8)], 0),
        "b1e": f32("L1_ffn_b1"),
        "W2e": np.concatenate([_pieces_rows(f32("L1_ffn_W2")[e], 2) for e in range(8)], 0),
        "b2e": f32("L1_ffn_b2"),
        "Ws1": _pieces_cols(f32("L1_ffn_Ws1"), 256), "bs1": f32("L1_ffn_bs1").reshape(-1, 1),
        "Ws2": _pieces_rows(f32("L1_ffn_Ws2"), 2), "bs2": f32("L1_ffn_bs2").reshape(-1, 1),
        "route_scale": f32("L1_ffn_route_scale").reshape(1, 1),
    }
    for name in ["ident", "ones_col", "ones_row", "neg_ones_row", "causal_neg", "causal_neg512",
                 "tri_incl", "mask1", "mask2", "mask3", "allones"]:
        base["c_" + name] = consts[name]
    return base


def kernel(**inputs):
    consts = host_consts()
    x = np.ascontiguousarray(np.asarray(inputs["L0_attn_x"], np.float32))
    B = x.shape[0]
    xT = np.ascontiguousarray(x.transpose(0, 2, 1))
    nc, _ = build_fused()
    base = _host_inputs(inputs, consts)
    res = run_bass_kernel_spmd(nc, [dict(base, xT=xT[b]) for b in range(B)],
                               core_ids=list(range(B)))
    out = np.stack([res.results[b]["outT"] for b in range(B)])
    return np.ascontiguousarray(out.transpose(0, 2, 1))



def _pieces_cols(w, ewid):
    K, N = w.shape
    npc, dck = N // ewid, K // 128
    a = w.reshape(dck, 128, npc, ewid)
    return np.ascontiguousarray(a.transpose(2, 1, 0, 3).reshape(npc * 128, dck * ewid))


def _pieces_rows(w, nrch):
    K, N = w.shape
    npc = K // (nrch * 128)
    a = w.reshape(npc, nrch, 128, N)
    return np.ascontiguousarray(a.transpose(0, 2, 1, 3).reshape(npc * 128, nrch * N))


def _whole_w(w):
    K, N = w.shape
    dck = K // 128
    return np.ascontiguousarray(w.reshape(dck, 128, N).transpose(1, 0, 2).reshape(128, dck * N))


_PERM = np.concatenate([np.arange(0, D, 2), np.arange(1, D, 2)])


def kernel_unfused(**inputs):
    consts = host_consts()
    x = np.ascontiguousarray(np.asarray(inputs["L0_attn_x"], np.float32))
    B = x.shape[0]
    xT = np.ascontiguousarray(x.transpose(0, 2, 1))
    f32 = lambda k: np.ascontiguousarray(np.asarray(inputs[k], np.float32))

    def cmap(base, names):
        for name in names:
            base["c_" + name] = consts[name]
        return base

    nc1, _ = build_l0_attn()
    base = cmap({"wq_l": np.ascontiguousarray(f32("L0_attn_w_q")[_PERM].T),
                 "wk_l": np.ascontiguousarray(f32("L0_attn_w_k")[_PERM].T),
                 "wv_l": np.ascontiguousarray(f32("L0_attn_w_v").T),
                 "rope_cos": consts["rope_cos"], "rope_sin": consts["rope_sin"]},
                ["ident", "ones_col", "ones_row", "causal_neg"])
    res1 = run_bass_kernel_spmd(nc1, [dict(base, xT=xT[b]) for b in range(B)],
                                core_ids=list(range(B)))
    o0 = [res1.results[b]["o0T"] for b in range(B)]

    nc2 = build_l0_ffn()
    base2 = cmap({"wr_l": np.ascontiguousarray(f32("L0_ffn_W_r").T),
                  "wk2_l": np.ascontiguousarray(f32("L0_ffn_W_k").T),
                  "wv2_l": np.ascontiguousarray(f32("L0_ffn_W_v").T),
                  "mu_r": f32("L0_ffn_mu_r").reshape(D, 1),
                  "mu_k": f32("L0_ffn_mu_k").reshape(D, 1)},
                 ["ones_col", "ones_row"])
    res2 = run_bass_kernel_spmd(nc2, [dict(base2, xT=xT[b], o0T=o0[b]) for b in range(B)],
                                core_ids=list(range(B)))
    x1 = [res2.results[b]["x1T"] for b in range(B)]

    nc3 = build_l1_attn()
    base3 = cmap({
        "Wq": _pieces_cols(f32("L1_attn_Wq"), 256), "Wk": _pieces_cols(f32("L1_attn_Wk"), 256),
        "Wv": _pieces_cols(f32("L1_attn_Wv"), 256), "Wg": _pieces_cols(f32("L1_attn_Wg"), 256),
        "Wo": _pieces_rows(f32("L1_attn_Wo"), 2), "W2": _pieces_cols(f32("L1_attn_W2"), 256),
        "Wa": f32("L1_attn_Wa"), "Wb": f32("L1_attn_Wb"),
        "conv_q": f32("L1_attn_conv_q_w"), "conv_k": f32("L1_attn_conv_k_w"),
        "conv_v": f32("L1_attn_conv_v_w"),
        "ln1_g": f32("L1_attn_ln1_g").reshape(-1, 1), "ln1_b": f32("L1_attn_ln1_b").reshape(-1, 1),
        "ln2_g": f32("L1_attn_ln2_g").reshape(-1, 1), "ln2_b": f32("L1_attn_ln2_b").reshape(-1, 1),
        "rms_in_w": f32("L1_attn_rms_in_w").reshape(-1, 1), "b2": f32("L1_attn_b2").reshape(-1, 1),
        "o_norm_w": f32("L1_attn_o_norm_w").reshape(1, -1),
        "dt_bias": f32("L1_attn_dt_bias").reshape(1, -1),
        "A_log": f32("L1_attn_A_log").reshape(1, -1)},
        ["ident", "ones_col", "ones_row", "neg_ones_row", "tri_incl", "mask1", "mask2", "mask3"])
    res3 = run_bass_kernel_spmd(nc3, [dict(base3, x1T=x1[b]) for b in range(B)],
                                core_ids=list(range(B)))
    y = [res3.results[b]["yT"] for b in range(B)]

    nc4 = build_l1_moe()
    base4 = cmap({
        "gate_W": f32("L1_ffn_gate_W"),
        "W1e": np.ascontiguousarray(f32("L1_ffn_W1").reshape(8 * D, 512)),
        "b1e": f32("L1_ffn_b1"),
        "W2e": np.ascontiguousarray(f32("L1_ffn_W2").reshape(8 * 512, D)),
        "b2e": f32("L1_ffn_b2"),
        "Ws1": f32("L1_ffn_Ws1"), "bs1": f32("L1_ffn_bs1").reshape(-1, 1),
        "Ws2": f32("L1_ffn_Ws2"), "bs2": f32("L1_ffn_bs2").reshape(-1, 1),
        "route_scale": f32("L1_ffn_route_scale").reshape(1, 1)},
        ["ident", "ones_col", "ones_row"])
    res4 = run_bass_kernel_spmd(nc4, [dict(base4, x1T=x1[b], yT=y[b]) for b in range(B)],
                                core_ids=list(range(B)))
    out = np.stack([res4.results[b]["outT"] for b in range(B)])
    return np.ascontiguousarray(out.transpose(0, 2, 1))
